# version 01
# baseline (speedup 1.0000x reference)
import contextlib
import numpy as np
import concourse.bass as bass
import concourse.mybir as mybir
from concourse.bass_utils import run_bass_kernel_spmd

KW_HOST = 4
F32 = mybir.dt.float32
F32R = mybir.dt.float32r
BF16 = mybir.dt.bfloat16
AF = mybir.ActivationFunctionType
ALU = mybir.AluOpType
AX = mybir.AxisListType

D = 1024
T = 8192
H = 16
DKH = 64
DVH = 128
KW = 4
NT = 256
CK = 128
NCH = D // 128
EPS = 1e-6
LN_EPS = 1e-5


def _consts_host():
    cols = []
    cmap = {}

    def add(name, tile, which):
        tile = np.asarray(tile, np.float32)
        t = np.zeros((128, tile.shape[1]), np.float32)
        t[: tile.shape[0]] = tile
        cmap[name] = (sum(c.shape[1] for c in cols), tile.shape[1], which)
        cols.append(t)

    i128 = np.arange(128)
    add("ident", np.eye(128, dtype=np.float32), "b")
    incl_u = (i128[:, None] <= i128[None, :]).astype(np.float32)
    strict_l = (i128[:, None] > i128[None, :]).astype(np.float32)
    add("incl_u", incl_u, "b")
    add("strict_l", strict_l, "b")
    blk8 = i128 // 8
    m8 = ((blk8[:, None] == blk8[None, :]) & (i128[:, None] < i128[None, :])).astype(np.float32)
    add("m8u", m8, "b")
    add("m8l", m8.T, "b")
    for m in (8, 16, 32, 64):
        b2 = i128 // (2 * m)
        half = (i128 // m) % 2
        r = ((b2[:, None] == b2[None, :]) & (half[:, None] == 0) & (half[None, :] == 1)).astype(np.float32)
        add(f"r{m}u", r, "b")
        add(f"r{m}l", r.T, "b")
    bd = np.zeros((128, 2), np.float32)
    bd[:64, 0] = 1.0
    bd[64:, 1] = 1.0
    add("bdones", bd, "b")
    add("ones_col", np.ones((128, 1), np.float32), "b")
    add("ones_row", np.ones((1, 128), np.float32), "b")
    add("incl_u_r", incl_u, "r")
    add("ones_row_r", np.ones((1, 128), np.float32), "r")
    oh = np.zeros((16, 16 * 128), np.float32)
    for h in range(16):
        oh[h, h * 128:(h + 1) * 128] = 1.0
    add("onehot", oh, "r")
    offs = {"b": 0, "r": 0}
    cmap2 = {}
    for name, (o, w, which) in cmap.items():
        cmap2[name] = (o, offs[which], w, which)
        offs[which] += w
    return (np.ascontiguousarray(np.concatenate(cols, axis=1)), cmap2,
            offs["b"], offs["r"])


_CONSTS, _CMAP, _CWB, _CWR = _consts_host()



class P:
    def __init__(self, nc):
        self.nc = nc
        self.q = {e: [] for e in ("t", "v", "s", "g", "y")}
        self.n = {k: 0 for k in ("t", "v", "s", "g", "y", "dl", "dg",
                                 "dxa", "dxb", "dca", "dcb", "dga", "dgb")}
        self.hw = {e: {} for e in self.q}
        self.sems = {}

    def _wait(self, e, tok):
        if tok is None:
            return
        s, c = tok
        if c <= 0 or c <= self.hw[e].get(s, 0):
            return
        self.hw[e][s] = c
        self.q[e].append(lambda eng, s=s, c=c: eng.wait_ge(self.sems[s], c))

    def op(self, e, emitfn, deps=(), sem=None, inc=0):
        for t in deps:
            self._wait(e, t)
        if inc:
            sname = sem or e
            self.n[sname] += inc
            tok = (sname, self.n[sname])

            def run(eng, f=emitfn, sn=sname, k=inc):
                f(eng).then_inc(self.sems[sn], k)
            self.q[e].append(run)
            return tok
        self.q[e].append(lambda eng, f=emitfn: f(eng))
        return None


class Guard:
    def __init__(self, ap=None):
        self.ap = ap
        self.w = None
        self.rs = []

    def rd(self):
        return [self.w] if self.w else []

    def wr(self):
        return ([self.w] if self.w else []) + self.rs

    def wrote(self, tok):
        if tok is not None:
            self.w = tok
            self.rs = []

    def read(self, tok):
        if tok is not None:
            self.rs.append(tok)


def tmax(*toks):
    best = {}
    for t in toks:
        if t is None:
            continue
        if isinstance(t, list):
            for u in t:
                if u is not None and u[1] > best.get(u[0], -1):
                    best[u[0]] = u[1]
        elif t[1] > best.get(t[0], -1):
            best[t[0]] = t[1]
    return [(s, c) for s, c in best.items()]



def build_program(n_tiles=T // NT, debug=(), phases=("A", "B"), x1_out=False,
                  stop_after=None):
    nc = bass.Bass("TRN2", target_bir_lowering=False)
    NTI = n_tiles

    dx = nc.dram_tensor("x", [D, T], F32, kind="ExternalInput").ap()
    dwq = nc.dram_tensor("wq", [D, D], F32, kind="ExternalInput").ap()
    dwk = nc.dram_tensor("wk", [D, D], F32, kind="ExternalInput").ap()
    dwv = nc.dram_tensor("wv", [D, 2048], F32, kind="ExternalInput").ap()
    dwo = nc.dram_tensor("wo", [2048, D], F32, kind="ExternalInput").ap()
    dwab = nc.dram_tensor("wab", [128, NCH * 32], F32, kind="ExternalInput").ap()
    dcq = nc.dram_tensor("convq", [128, 8 * KW], F32, kind="ExternalInput").ap()
    dck = nc.dram_tensor("convk", [128, 8 * KW], F32, kind="ExternalInput").ap()
    dcv = nc.dram_tensor("convv", [128, 16 * KW], F32, kind="ExternalInput").ap()
    dalog = nc.dram_tensor("alog", [1, H], F32, kind="ExternalInput").ap()
    ddtb = nc.dram_tensor("dtb", [1, H], F32, kind="ExternalInput").ap()
    dwr = nc.dram_tensor("wr", [D, D], F32, kind="ExternalInput").ap()
    dwkf = nc.dram_tensor("wkf", [D, D], F32, kind="ExternalInput").ap()
    dwvf = nc.dram_tensor("wvf", [D, D], F32, kind="ExternalInput").ap()
    dmur = nc.dram_tensor("mur", [D, 1], F32, kind="ExternalInput").ap()
    dmuk = nc.dram_tensor("muk", [D, 1], F32, kind="ExternalInput").ap()
    dconst = nc.dram_tensor("const", list(_CONSTS.shape), F32, kind="ExternalInput").ap()
    dout = nc.dram_tensor("out", [D, T], F32, kind="ExternalOutput").ap()
    dx1 = nc.dram_tensor("x1s", [D, T], F32,
                         kind="ExternalOutput" if x1_out else "Internal").ap()
    dcdiag = nc.dram_tensor("cdiag", [128, 32 * KW * 128], BF16, kind="Internal").ap()

    ddbg = {}
    for name, shape in debug:
        ddbg[name] = nc.dram_tensor("dbg_" + name, list(shape), F32, kind="ExternalOutput").ap()

    p = P(nc)
    es = contextlib.ExitStack()

    def sb(name, shape, dt):
        return es.enter_context(nc.sbuf_tensor(name, list(shape), dt))

    with es:
        for s in ("t", "v", "s", "g", "y", "dl", "dg",
                  "dxa", "dxb", "dca", "dcb", "dga", "dgb"):
            p.sems[s] = es.enter_context(nc.semaphore("sem_" + s))

        cst_b = sb("cst_b", (128, _CWB), BF16)
        cst_r = sb("cst_r", (128, _CWR), F32R)
        epsc = sb("epsc", (128, 2), F32)
        p.op("v", lambda e: e.memset(epsc[:, 0:1], LN_EPS), inc=1)
        p.op("v", lambda e: e.memset(epsc[:, 1:2], EPS), inc=1)
        nc.const_aps.register(F32, LN_EPS, epsc[:, 0:1])
        nc.const_aps.register(F32, EPS, epsc[:, 1:2])

        def cb(nm):
            _, o, w, which = _CMAP[nm]
            assert which == "b"
            return cst_b[:, o:o + w]

        def cr(nm):
            _, o, w, which = _CMAP[nm]
            assert which == "r"
            return cst_r[:, o:o + w]

        bank = [Guard(es.enter_context(nc.psum_tensor(f"ps{i}", [128, 512], F32)))
                for i in range(5)]
        bank5 = Guard(es.enter_context(nc.psum_tensor("ps5", [128, 1024], BF16)))
        bank6 = Guard(es.enter_context(nc.psum_tensor("ps6", [128, 512], F32)))
        bank7 = Guard(es.enter_context(nc.psum_tensor("ps7", [128, 512], F32)))

        wpool = sb("wpool", (128, 48 * 1024), BF16)
        wq_sb = [wpool[:, (k) * 1024:(k + 1) * 1024] for k in range(8)]
        wk_sb = [wpool[:, (8 + k) * 1024:(9 + k) * 1024] for k in range(8)]
        wv_sb = [wpool[:, (16 + 2 * k) * 1024:(16 + 2 * k + 2) * 1024] for k in range(8)]
        wo_sb = [wpool[:, (32 + k) * 1024:(33 + k) * 1024] for k in range(16)]
        wr_sb = [wpool[:, (k) * 1024:(k + 1) * 1024] for k in range(8)]
        wkf_sb = [wpool[:, (8 + k) * 1024:(9 + k) * 1024] for k in range(8)]
        wvf_sb = [wpool[:, (16 + k) * 1024:(17 + k) * 1024] for k in range(8)]
        wg = Guard()

        wab_sb = sb("wab_sb", (128, NCH * 32), BF16)
        wbar = sb("wbar", (1, 2 * D + 32 + 2048), BF16)
        WBQ, WBK, WBAB, WBV = 0, D, 2 * D, 2 * D + 32
        convw = sb("convw", (128, 32 * KW), F32)
        negA = sb("negA", (128, H), F32)
        dtbb = sb("dtbb", (128, H), F32)
        alog_r = sb("alog_r", (1, 2 * H), F32R)
        wstg = [Guard(sb("wstg0", (128, 1024), F32))]

        xbuf = [[Guard(sb(f"x{b}_{c}", (128, NT), F32)) for c in range(NCH)] for b in range(2)]
        apool = sb("apool", (128, 12 * 1024), BF16)
        qkv = [Guard(apool[:, pc * NT:(pc + 1) * NT]) for pc in range(32)]
        on_fm = [Guard(apool[:, 8 * 1024 + hv * NT:8 * 1024 + (hv + 1) * NT]) for hv in range(16)]
        z_sb = [Guard(apool[:, c * (NT + 1):(c + 1) * (NT + 1)]) for c in range(8)]
        xr_sb = [Guard(apool[:, 2080 + c * NT:2080 + (c + 1) * NT]) for c in range(8)]
        xk_sb = [Guard(apool[:, 4128 + c * NT:4128 + (c + 1) * NT]) for c in range(8)]
        r_sb = [Guard(apool[:, 6176 + c * NT:6176 + (c + 1) * NT]) for c in range(8)]
        ka_sb = [Guard(apool[:, 8224 + c * NT:8224 + (c + 1) * NT]) for c in range(8)]

        ao_sb = [Guard(sb(f"ao{c}", (128, NT), F32)) for c in range(NCH)]
        ybuf = [Guard(sb(f"y_{c}", (128, NT), BF16)) for c in range(NCH)]
        xwork = Guard(sb("xwork", (128, 2 * NT), BF16))
        xsq = Guard(sb("xsq", (128, NT), BF16))
        stat = Guard(sb("stat", (1, 2 * NT), F32))
        mr_bf = Guard(sb("mr_bf", (1, NT), BF16))
        rstd_bf = Guard(sb("rstd_bf", (1, NT), BF16))
        pbuf = [Guard(sb(f"pb{i}", (128, NT + KW - 1), BF16)) for i in range(2)]
        ptail = Guard(sb("ptail", (128, 32 * (KW - 1)), BF16))
        cdg = [Guard(sb(f"cdg{i}", (128, KW * 128), BF16)) for i in range(2)]

        ab_sb = Guard(sb("ab_sb", (128, 32), F32))
        loga = Guard(sb("loga", (128, H), F32R))
        beta_g = Guard(sb("beta", (128, H), F32))
        ell_tm = Guard(sb("ell_tm", (128, H), F32))
        ell_fm = Guard(sb("ell_fm", (16, CK), F32R))
        gam_tm = Guard(sb("gam_tm", (128, H), F32))
        rnorm = Guard(sb("rnorm", (128, 32), F32))
        scl = Guard(sb("scl", (128, 6 * H), F32))
        negbg = Guard(sb("negbg", (128, H), F32))

        S_f = [Guard(None) for _ in range(H)]
        S_b = [Guard(None) for _ in range(H)]
        S_sb = sb("S", (128, 8 * DVH), F32)
        S_bf = sb("S_bf", (128, 8 * DVH), BF16)
        for h in range(H):
            hb, hp_ = 64 * (h % 2), h // 2
            S_f[h].ap = S_sb[hb:hb + 64, hp_ * DVH:(hp_ + 1) * DVH]
            S_b[h].ap = S_bf[hb:hb + 64, hp_ * DVH:(hp_ + 1) * DVH]

        Ebat = Guard(sb("Ebat", (128, 2 * CK), F32))
        Enb = Guard(sb("Enb", (128, 2 * CK), F32))
        expE = Guard(sb("expE", (128, 2 * CK), BF16))
        dinc = Guard(sb("dinc", (128, 2 * CK), BF16))
        lfac = Guard(sb("lfac", (128, 2 * CK), BF16))
        mmt = [Guard(sb(f"mmt{i}", (128, CK), BF16)) for i in range(2)]
        Lt = [Guard(sb(f"Lt{i}", (128, CK), BF16)) for i in range(2)]
        LTt = [Guard(sb(f"LTt{i}", (128, CK), BF16)) for i in range(2)]
        Ngt = Guard(sb("Ngt", (128, 2 * CK), BF16))
        h2 = Guard(sb("h2", (128, 2 * CK), BF16))
        h4 = Guard(sb("h4", (128, 2 * CK), BF16))
        h24 = Guard(sb("h24", (128, 2 * CK), BF16))
        t0a = Guard(sb("t0a", (128, 2 * CK), BF16))
        Tg = [Guard(sb(f"Tg{i}", (128, CK), BF16)) for i in range(2)]
        TTg = [Guard(sb(f"TTg{i}", (128, CK), BF16)) for i in range(2)]
        Pg = Guard(sb("Pg", (128, 2 * CK), BF16))
        kqtm = Guard(sb("kqtm", (128, 2 * CK), BF16))
        khn_t = Guard(sb("khn", (128, 2 * CK), BF16))
        hnf = Guard(sb("hnf", (128, 2 * CK), BF16))
        khat = Guard(sb("khat", (128, 2 * DKH), BF16))
        vtm = Guard(sb("vtm", (128, 2 * DVH), BF16))
        Rtm = [Guard(sb(f"Rtm{i}", (128, DVH), BF16)) for i in range(2)]
        Utm = [Guard(sb(f"Utm{i}", (128, DVH), BF16)) for i in range(2)]
        osc = Guard(sb("osc", (128, DVH), F32))
        osq = Guard(sb("osq", (128, DVH), F32))
        oms = Guard(sb("oms", (128, 4), F32))
        ontm = Guard(sb("ontm", (128, 2 * DVH), BF16))

        mu_sb = sb("mu_sb", (128, 16), F32)
        zt_g = Guard(None)
        fsc = Guard(sb("fsc", (128, NT), F32))

        cw_all = _CONSTS.shape[1]
        tkb = None
        for go in range(0, cw_all, 1024):
            w = min(1024, cw_all - go)
            g = wstg[0]
            td = p.op("y", lambda e, go=go, w=w, g=g:
                      e.dma_start(out=g.ap[:, :w], in_=dconst[:, go:go + w]),
                      deps=g.wr(), sem="dl", inc=16)
            g.wrote(td)
            for nm, (osrc, odst, wd, which) in _CMAP.items():
                a, b = max(osrc, go), min(osrc + wd, go + w)
                if a >= b:
                    continue
                dstt = cst_b if which == "b" else cst_r
                tkb = p.op("v", lambda e, a=a, b=b, go=go, osrc=osrc, odst=odst, dstt=dstt, g=g:
                           e.tensor_copy(dstt[:, odst + a - osrc:odst + b - osrc],
                                         g.ap[:, a - go:b - go]),
                           deps=g.rd(), inc=1)
                g.read(tkb)
        tkr = tkb

        def stage_load(src_view, dst_view, gidx, extra_deps=()):
            g = wstg[0]
            td = p.op("y", lambda e: e.dma_start(out=g.ap[:, :src_view.shape[1]], in_=src_view),
                      deps=tmax(g.wr(), *extra_deps), sem="dl", inc=16)
            g.wrote(td)
            tc = p.op("v", lambda e: e.tensor_copy(dst_view, g.ap[:, :src_view.shape[1]]),
                      deps=tmax(g.rd(), wg.w), inc=1)
            g.read(tc)
            return tc

        def load_phase_a_weights():
            toks = []
            for k in range(8):
                toks.append(stage_load(dwq[k * 128:(k + 1) * 128, :], wq_sb[k], k % 2))
                toks.append(stage_load(dwk[k * 128:(k + 1) * 128, :], wk_sb[k], k % 2))
                for hf in range(2):
                    toks.append(stage_load(dwv[k * 128:(k + 1) * 128, hf * 1024:(hf + 1) * 1024],
                                           wv_sb[k][:, hf * 1024:(hf + 1) * 1024], k % 2))
            for k in range(16):
                toks.append(stage_load(dwo[k * 128:(k + 1) * 128, :], wo_sb[k], k % 2))
            return toks

        wtoks = load_phase_a_weights()
        wg.wrote(wtoks[-1])
        wA_tok = tmax(*wtoks)

        g = wstg[0]
        td = p.op("y", lambda e: e.dma_start(out=g.ap[:, :32 * NCH], in_=dwab),
                  deps=g.wr(), sem="dl", inc=16)
        g.wrote(td)
        tab_w = p.op("v", lambda e: e.tensor_copy(wab_sb[:], g.ap[:, :32 * NCH]),
                     deps=g.rd(), inc=1)
        g.read(tab_w)

        tcw = ("v", p.n["v"])
        tcw = tab_w
        for (srcd, nch_, off) in ((dcq, 8, 0), (dck, 8, 8), (dcv, 16, 16)):
            tcw = p.op("y", lambda e, srcd=srcd, nch_=nch_, off=off:
                       e.dma_start(out=convw[:, off * KW:(off + nch_) * KW], in_=srcd),
                       deps=[tcw], sem="dl", inc=16)

        t_al = p.op("g", lambda e: e.dma_start(out=alog_r[:, :H], in_=dalog),
                    deps=[tcw], sem="dg", inc=16)
        t_db = p.op("g", lambda e: e.dma_start(out=alog_r[:, H:], in_=ddtb),
                    deps=[t_al], sem="dg", inc=16)
        t_bc = p.op("t", lambda e: e.matmul(bank6.ap[:, :2 * H], cr("ones_row_r")[:1, :], alog_r[:],
                                            start=True, stop=True),
                    deps=tmax(t_al, t_db, tkr, bank6.wr()), inc=1)
        bank6.wrote(t_bc)
        tA1 = p.op("s", lambda e: e.activation(negA[:], bank6.ap[:, :H], AF.Exp),
                   deps=bank6.rd(), inc=1)
        t_negA = p.op("s", lambda e: e.mul(negA[:], negA[:], -1.0), deps=[tA1], inc=1)
        t_dtb = p.op("v", lambda e: e.tensor_copy(dtbb[:], bank6.ap[:, H:2 * H]),
                     deps=bank6.rd(), inc=1)
        bank6.read(t_negA)
        bank6.read(t_dtb)

        def build_wbar(dst_off, slabs, width, dep):
            for part in range(0, width, 512):
                w = min(512, width - part)
                tprev = None
                for ki, slab in enumerate(slabs):
                    tprev = p.op("t", lambda e, slab=slab, part=part, w=w, ki=ki:
                                 e.matmul(bank7.ap[:1, :w], cb("ones_col"),
                                          slab[:, part:part + w],
                                          start=(ki == 0), stop=(ki == len(slabs) - 1)),
                                 deps=tmax(tkb, dep, bank7.wr()) if ki == 0 else (), inc=1)
                bank7.wrote(tprev)
                te = p.op("v", lambda e, part=part, w=w, dst_off=dst_off:
                          e.tensor_scalar(wbar[:1, dst_off + part:dst_off + part + w],
                                          bank7.ap[:1, :w], -1.0, None, ALU.mult),
                          deps=bank7.rd(), inc=1)
                bank7.read(te)

        build_wbar(WBQ, wq_sb, D, wA_tok)
        build_wbar(WBK, wk_sb, D, wA_tok)
        build_wbar(WBV, wv_sb, 2048, wA_tok)
        build_wbar(WBAB, [wab_sb[:, k * 32:(k + 1) * 32] for k in range(NCH)], 32, [tab_w])
        wbar_tok = p.n["v"]

        cstg = Guard(sb("cstg", (128, 2 * KW * 128), BF16))
        for c in range(32):
            half = (c % 2) * KW * 128
            tb = None
            for j in range(KW):
                tb = p.op("v", lambda e, c=c, j=j, half=half:
                          e.tensor_scalar(cstg.ap[:, half + j * 128:half + (j + 1) * 128],
                                          cb("ident"), convw[:, c * KW + j:c * KW + j + 1],
                                          None, ALU.mult),
                          deps=tmax(tcw, tkb, cstg.wr()) if j == 0 else (), inc=1)
            cstg.wrote(tb)
            ts_ = p.op("g", lambda e, c=c, half=half:
                       e.dma_start(out=dcdiag[:, c * KW * 128:(c + 1) * KW * 128],
                                   in_=cstg.ap[:, half:half + KW * 128]),
                       deps=tmax(cstg.rd(), t_db if c == 0 else []), sem="dg", inc=16)
            cstg.read(ts_)
        cdiag_tok = ("dg", p.n["dg"])

        def ln_stats(src, eps):
            t2 = None
            for c in range(NCH):
                tq = p.op("v", lambda e, c=c: e.tensor_tensor(
                    xwork.ap[:, NT:2 * NT], src[c].ap[:], src[c].ap[:], ALU.mult),
                    deps=tmax(src[c].w, xwork.wr()), inc=1)
                src[c].read(tq)
                tb_ = p.op("v", lambda e, c=c: e.tensor_copy(xwork.ap[:, 0:NT], src[c].ap[:]),
                           deps=[tq], inc=1)
                xwork.wrote(tb_)
                src[c].read(tb_)
                t2 = p.op("t", lambda e, c=c: e.matmul(bank7.ap[:1, 0:2 * NT], cb("ones_col"),
                                                  xwork.ap[:],
                                                  start=(c == 0), stop=(c == NCH - 1)),
                          deps=tmax(tb_, tkb, bank7.wr() if c == 0 else []), inc=1)
                xwork.read(t2)
            bank7.wrote(t2)
            tm = p.op("v", lambda e: e.tensor_scalar(stat.ap[:1, 0:NT], bank7.ap[:1, 0:NT],
                                                     1.0 / D, None, ALU.mult),
                      deps=tmax(t2, stat.wr()), inc=1)
            tmsq = p.op("v", lambda e: e.tensor_tensor(stat.ap[:1, NT:2 * NT], stat.ap[:1, 0:NT],
                                                       stat.ap[:1, 0:NT], ALU.mult),
                        deps=[tm], inc=1)
            tvar = p.op("v", lambda e: e.scalar_tensor_tensor(
                stat.ap[:1, NT:2 * NT], bank7.ap[:1, NT:2 * NT], 1.0 / D,
                stat.ap[:1, NT:2 * NT], ALU.mult, ALU.subtract), deps=[tmsq], inc=1)
            bank7.read(tvar)
            tl = p.op("s", lambda e: e.activation(stat.ap[:1, NT:2 * NT],
                                                  stat.ap[:1, NT:2 * NT], AF.Ln,
                                                  bias=float(eps)),
                      deps=[tvar], inc=1)
            tr = p.op("s", lambda e: e.activation(stat.ap[:1, NT:2 * NT],
                                                  stat.ap[:1, NT:2 * NT], AF.Exp, scale=-0.5),
                      deps=[tl], inc=1)
            stat.wrote(tr)
            trb = p.op("v", lambda e: e.tensor_copy(rstd_bf.ap[:], stat.ap[:1, NT:2 * NT]),
                       deps=tmax(tr, rstd_bf.wr()), inc=1)
            rstd_bf.wrote(trb)
            tmr2 = p.op("v", lambda e: e.tensor_tensor(stat.ap[:1, 0:NT], stat.ap[:1, 0:NT],
                                                       stat.ap[:1, NT:2 * NT], ALU.mult),
                        deps=[tr], inc=1)
            tmrb = p.op("v", lambda e: e.tensor_copy(mr_bf.ap[:], stat.ap[:1, 0:NT]),
                        deps=tmax(tmr2, mr_bf.wr()), inc=1)
            mr_bf.wrote(tmrb)
            trbc = p.op("t", lambda e: e.matmul(bank6.ap[:, 0:NT], cb("ones_row")[:1, :],
                                                rstd_bf.ap[:], start=True, stop=True),
                        deps=tmax(trb, bank6.wr()), inc=1)
            bank6.wrote(trbc)
            rstd_bf.read(trbc)
            return trb, tmrb, trbc

        def bcast2(rowg, width):
            pass

        def load_x_tile(ti, dst, src_dram, extra=()):
            sem = "dxa" if ti % 2 == 0 else "dxb"
            tk_ = None
            for c in range(NCH):
                g = dst[c]
                tk_ = p.op("y", lambda e, c=c, ti=ti, g=g, src_dram=src_dram:
                           e.dma_start(out=g.ap[:], in_=src_dram[c * 128:(c + 1) * 128,
                                                             ti * NT:(ti + 1) * NT]),
                           deps=tmax(g.wr(), *extra), sem=sem, inc=16)
            for c in range(NCH):
                dst[c].wrote(tk_)
            return [tk_]

        def wslab(pc):
            if pc < 8:
                return [wq_sb[k][:, pc * 128:(pc + 1) * 128] for k in range(8)], WBQ + pc * 128
            if pc < 16:
                o = pc - 8
                return [wk_sb[k][:, o * 128:(o + 1) * 128] for k in range(8)], WBK + o * 128
            o = pc - 16
            return [wv_sb[k][:, o * 128:(o + 1) * 128] for k in range(8)], WBV + o * 128

        def attn_tile(ti):
            xb = xbuf[ti % 2]
            if ti + 1 < NTI:
                load_x_tile(ti + 1, xbuf[(ti + 1) % 2], dx)

            trb, tmr, trbc = ln_stats(xb, LN_EPS)

            ytoks = []
            for c in range(NCH):
                ty = p.op("v", lambda e, c=c: e.tensor_tensor(
                    ybuf[c].ap[:], xb[c].ap[:], bank6.ap[:, 0:NT], ALU.mult),
                    deps=tmax(xb[c].w, bank6.w, ybuf[c].wr()), inc=1)
                ybuf[c].wrote(ty)
                bank6.read(ty)
                xb[c].read(ty)
                ytoks.append(ty)

            for pc in range(32):
                pb = bank[pc % 2]
                slabs, wb_off = wslab(pc)
                tmm = None
                for k in range(NCH):
                    tmm = p.op("t", lambda e, k=k, slabs=slabs, pb=pb: e.matmul(
                        pb.ap[:, 0:NT], slabs[k], ybuf[k].ap[:],
                        start=(k == 0), stop=False),
                        deps=tmax(ytoks[k], wA_tok, pb.wr()) if k == 0 else [ytoks[k]],
                        inc=1)
                tmm = p.op("t", lambda e, wb_off=wb_off, pb=pb: e.matmul(
                    pb.ap[:, 0:NT], wbar[:1, wb_off:wb_off + 128], mr_bf.ap[:],
                    start=False, stop=True),
                    deps=tmax(mr_bf.w, ("v", wbar_tok)), inc=1)
                pb.wrote(tmm)
                mr_bf.read(tmm)
                for k in range(NCH):
                    ybuf[k].read(tmm)
                cd = cdg[pc % 2]
                tcd = p.op("y", lambda e, pc=pc, cd=cd:
                           e.dma_start(out=cd.ap[:],
                                       in_=dcdiag[:, pc * KW * 128:(pc + 1) * KW * 128]),
                           deps=tmax(cd.wr(), cdiag_tok),
                           sem="dca" if pc % 2 == 0 else "dcb", inc=16)
                cd.wrote(tcd)
                pg = pbuf[pc % 2]
                tp0 = p.op("v", lambda e, pc=pc, pg=pg: e.tensor_copy(
                    pg.ap[:, 0:KW - 1], ptail.ap[:, pc * (KW - 1):(pc + 1) * (KW - 1)]),
                    deps=tmax(ptail.w, pg.wr()), inc=1)
                ptail.read(tp0)
                tp = p.op("s", lambda e, pg=pg, pb=pb: e.copy(pg.ap[:, KW - 1:], pb.ap[:, 0:NT]),
                          deps=tmax(tmm, tp0), inc=1)
                pb.read(tp)
                pg.wrote(tp)
                tt = p.op("v", lambda e, pc=pc, pg=pg: e.tensor_copy(
                    ptail.ap[:, pc * (KW - 1):(pc + 1) * (KW - 1)], pg.ap[:, NT:NT + KW - 1]),
                    deps=[tp], inc=1)
                ptail.wrote(tt)
                pg.read(tt)
                cbnk = bank[2 + (pc % 2)]
                tc_ = None
                for j in range(KW):
                    tc_ = p.op("t", lambda e, j=j, cd=cd, pg=pg, cbnk=cbnk: e.matmul(
                        cbnk.ap[:, 0:NT], cd.ap[:, j * 128:(j + 1) * 128], pg.ap[:, j:j + NT],
                        start=(j == 0), stop=(j == KW - 1)),
                        deps=tmax(tcd, tp, cbnk.wr()) if j == 0 else (), inc=1)
                cbnk.wrote(tc_)
                cd.read(tc_)
                pg.read(tc_)
                tsl = p.op("s", lambda e, pc=pc, cbnk=cbnk: e.activation(
                    qkv[pc].ap[:], cbnk.ap[:, 0:NT], AF.Silu),
                    deps=tmax(tc_, qkv[pc].wr()), inc=1)
                cbnk.read(tsl)
                qkv[pc].wrote(tsl)

            if stop_after == "proj":
                return []
            for cc in range(NT // CK):
                chunk_recurrence(ti, cc, ytoks)
                if isinstance(stop_after, str) and stop_after.startswith("cc0") and cc == 0:
                    break
            if stop_after is not None:
                return []

            for oc in range(NCH):
                pb = bank[oc % 2]
                tmm = None
                for hv in range(16):
                    tmm = p.op("t", lambda e, hv=hv, oc=oc, pb=pb: e.matmul(
                        pb.ap[:, 0:NT], wo_sb[hv][:, oc * 128:(oc + 1) * 128],
                        apool[:, 8 * 1024 + hv * NT:8 * 1024 + (hv + 1) * NT],
                        start=(hv == 0), stop=(hv == 15)),
                        deps=tmax(on_fm[hv].w, pb.wr()) if hv == 0 else [on_fm[hv].w],
                        inc=1)
                pb.wrote(tmm)
                for hv in range(16):
                    on_fm[hv].read(tmm)
                te = p.op("s", lambda e, oc=oc, pb=pb: e.copy(ao_sb[oc].ap[:], pb.ap[:, 0:NT]),
                          deps=tmax(tmm, ao_sb[oc].wr()), inc=1)
                pb.read(te)
                ao_sb[oc].wrote(te)

            trb2, tmr2, trbc2 = ln_stats(ao_sb, LN_EPS)
            tmrbc = p.op("t", lambda e: e.matmul(bank6.ap[:, NT:2 * NT], cb("ones_row")[:1, :],
                                                 mr_bf.ap[:], start=True, stop=True),
                         deps=tmax(tmr2, mr_bf.w), inc=1)
            bank6.wrote(tmrbc)
            mr_bf.read(tmrbc)
            st_toks = []
            for c in range(NCH):
                t1 = p.op("v", lambda e, c=c: e.tensor_tensor(
                    ao_sb[c].ap[:], ao_sb[c].ap[:], bank6.ap[:, 0:NT], ALU.mult),
                    deps=tmax(ao_sb[c].w, tmrbc), inc=1)
                t2 = p.op("v", lambda e, c=c: e.tensor_tensor(
                    ao_sb[c].ap[:], ao_sb[c].ap[:], bank6.ap[:, NT:2 * NT], ALU.subtract),
                    deps=[t1], inc=1)
                bank6.read(t2)
                t3 = p.op("v", lambda e, c=c: e.tensor_tensor(
                    xb[c].ap[:], xb[c].ap[:], ao_sb[c].ap[:], ALU.add),
                    deps=tmax(t2, xb[c].w, xb[c].rs), inc=1)
                ao_sb[c].wrote(t3)
                xb[c].wrote(t3)
                ts_ = p.op("g", lambda e, c=c, ti=ti: e.dma_start(
                    out=dx1[c * 128:(c + 1) * 128, ti * NT:(ti + 1) * NT], in_=xb[c].ap[:]),
                    deps=[t3], sem="dga" if ti % 2 == 0 else "dgb", inc=16)
                xb[c].read(ts_)
                st_toks.append(ts_)
            for c in range(NCH):
                xb[c].rs = [st_toks[-1]]
            return st_toks

        RB = [2, 3, 4]
        rb_state = [0]

        def nxt():
            b = bank[RB[rb_state[0] % 3]]
            rb_state[0] += 1
            return b

        def chunk_recurrence(ti, cc, ytoks):
            tof = cc * CK
            tab = None
            for k in range(NCH):
                tab = p.op("t", lambda e, k=k: e.matmul(
                    bank7.ap[:, 0:32], ybuf[k].ap[:, tof:tof + CK],
                    wab_sb[:, k * 32:(k + 1) * 32],
                    start=(k == 0), stop=False),
                    deps=tmax(ytoks[k], tab_w, bank7.wr()) if k == 0 else [ytoks[k]], inc=1)
            tab = p.op("t", lambda e: e.matmul(
                bank7.ap[:, 0:32], mr_bf.ap[:1, tof:tof + CK], wbar[:1, WBAB:WBAB + 32],
                start=False, stop=True),
                deps=tmax(mr_bf.w, ("v", wbar_tok)), inc=1)
            bank7.wrote(tab)
            mr_bf.read(tab)
            for k in range(NCH):
                ybuf[k].read(tab)
            tabe = p.op("v", lambda e: e.tensor_copy(ab_sb.ap[:], bank7.ap[:, 0:32]),
                        deps=tmax(tab, ab_sb.wr()), inc=1)
            bank7.read(tabe)
            ab_sb.wrote(tabe)

            t1 = p.op("v", lambda e: e.tensor_tensor(scl.ap[:, 0:H], ab_sb.ap[:, 0:H], dtbb[:],
                                                     ALU.add),
                      deps=tmax(tabe, t_dtb, scl.wr()), inc=1)
            t2 = p.op("s", lambda e: e.activation(scl.ap[:, H:2 * H], scl.ap[:, 0:H], AF.Abs),
                      deps=[t1], inc=1)
            t3 = p.op("s", lambda e: e.activation(scl.ap[:, 2 * H:3 * H], scl.ap[:, H:2 * H],
                                                  AF.Exp, scale=-1.0), deps=[t2], inc=1)
            t4 = p.op("s", lambda e: e.activation(scl.ap[:, 2 * H:3 * H], scl.ap[:, 2 * H:3 * H],
                                                  AF.Ln, bias=1.0), deps=[t3], inc=1)
            t5 = p.op("v", lambda e: e.tensor_scalar(scl.ap[:, 3 * H:4 * H], scl.ap[:, 0:H], 0.0,
                                                     None, ALU.max), deps=[t1], inc=1)
            t6 = p.op("v", lambda e: e.tensor_tensor(scl.ap[:, 2 * H:3 * H], scl.ap[:, 2 * H:3 * H],
                                                     scl.ap[:, 3 * H:4 * H], ALU.add),
                      deps=[t4, t5], inc=1)
            tlg = p.op("v", lambda e: e.tensor_tensor(loga.ap[:], scl.ap[:, 2 * H:3 * H], negA[:],
                                                      ALU.mult),
                       deps=tmax(t6, t_negA, loga.wr()), inc=1)
            loga.wrote(tlg)
            scl.wrote(t6)
            tbt = p.op("s", lambda e: e.activation(beta_g.ap[:], ab_sb.ap[:, H:2 * H], AF.Sigmoid),
                       deps=tmax(tabe, beta_g.wr()), inc=1)
            tbt = p.op("s", lambda e: e.mul(beta_g.ap[:], beta_g.ap[:], 2.0), deps=[tbt], inc=1)
            beta_g.wrote(tbt)
            ab_sb.read(tbt)

            tc1 = p.op("t", lambda e: e.matmul(bank7.ap[:, 64:64 + H], cr("incl_u_r"), loga.ap[:],
                                               start=True, stop=True),
                       deps=tmax(tlg, tkr, bank7.wr()), inc=1)
            tc2 = p.op("t", lambda e: e.matmul(bank7.ap[:16, 128:128 + CK], loga.ap[:],
                                               cr("incl_u_r"), start=True, stop=True),
                       deps=tmax(tlg), inc=1)
            bank7.wrote(tc2)
            loga.read(tc2)
            te1 = p.op("v", lambda e: e.tensor_copy(ell_tm.ap[:], bank7.ap[:, 64:64 + H]),
                       deps=tmax(tc1, tc2, ell_tm.wr()), inc=1)
            ell_tm.wrote(te1)
            te2 = p.op("v", lambda e: e.tensor_copy(ell_fm.ap[:], bank7.ap[:16, 128:128 + CK]),
                       deps=tmax(tc1, tc2, ell_fm.wr()), inc=1)
            ell_fm.wrote(te2)
            bank7.read(te1)
            bank7.read(te2)
            tg = p.op("s", lambda e: e.activation(gam_tm.ap[:], ell_tm.ap[:], AF.Exp),
                      deps=tmax(te1, gam_tm.wr()), inc=1)
            gam_tm.wrote(tg)
            tnb = p.op("v", lambda e: e.tensor_tensor(negbg.ap[:], beta_g.ap[:], gam_tm.ap[:],
                                                      ALU.mult),
                       deps=tmax(tbt, tg, negbg.wr()), inc=1)
            tnb = p.op("v", lambda e: e.tensor_scalar(negbg.ap[:], negbg.ap[:], -1.0, None,
                                                      ALU.mult), deps=[tnb], inc=1)
            negbg.wrote(tnb)
            beta_g.read(tnb)
            gam_tm.read(tnb)

            ntk = None
            for qc in range(16):
                tq = p.op("v", lambda e, qc=qc: e.tensor_tensor(
                    xsq.ap[:, 0:CK], qkv[qc].ap[:, tof:tof + CK], qkv[qc].ap[:, tof:tof + CK],
                    ALU.mult),
                    deps=tmax(qkv[qc].w, xsq.wr()), inc=1)
                xsq.wrote(tq)
                qkv[qc].read(tq)
                ntk = p.op("t", lambda e, qc=qc: e.matmul(
                    bank7.ap[:, 256 + qc * 2:256 + qc * 2 + 2], xsq.ap[:, 0:CK], cb("bdones"),
                    start=True, stop=True),
                    deps=tmax(tq, tkb, bank7.wr() if qc == 0 else []), inc=1)
                xsq.read(ntk)
            bank7.wrote(ntk)
            tn1 = p.op("s", lambda e: e.activation(rnorm.ap[:], bank7.ap[:, 256:288], AF.Ln),
                       deps=tmax(ntk, rnorm.wr()), inc=1)
            bank7.read(tn1)
            tn2 = p.op("s", lambda e: e.activation(rnorm.ap[:], rnorm.ap[:], AF.Exp, scale=-0.5),
                       deps=[tn1], inc=1)
            tn3 = p.op("v", lambda e: e.tensor_scalar(rnorm.ap[:], rnorm.ap[:], 1.0e6, None,
                                                      ALU.min), deps=[tn2], inc=1)
            rnorm.wrote(tn3)

            if stop_after == "decay" or stop_after == "cc0decay":
                return
            nhp = 8
            if isinstance(stop_after, str) and "hp" in stop_after:
                import re as _re
                nhp = int(_re.search(r"hp(\d+)", stop_after).group(1))
            for hp in range(nhp):
                head_pair(ti, cc, tof, hp, te1, te2)

        def head_pair(ti, cc, tof, hp, te1, te2):
            h0 = 2 * hp
            tb_ = p.op("t", lambda e: e.matmul(bank6.ap[:, 0:CK],
                                               cr("onehot")[:16, h0 * 128:(h0 + 1) * 128],
                                               ell_fm.ap[:], start=True, stop=True),
                       deps=tmax(te2, tkr, bank6.wr()), inc=1)
            tb1 = p.op("t", lambda e: e.matmul(bank6.ap[:, CK:2 * CK],
                                               cr("onehot")[:16, (h0 + 1) * 128:(h0 + 2) * 128],
                                               ell_fm.ap[:], start=True, stop=True), inc=1)
            bank6.wrote(tb1)
            ell_fm.read(tb1)
            if stop_after is not None and "xbc" in str(stop_after):
                return
            tE = None
            for i in range(2):
                tE = p.op("v", lambda e, i=i: e.tensor_scalar(
                    Ebat.ap[:, i * CK:(i + 1) * CK], bank6.ap[:, i * CK:(i + 1) * CK],
                    ell_tm.ap[:, h0 + i:h0 + i + 1], None, ALU.subtract),
                    deps=tmax(tb1, te1, Ebat.wr() if i == 0 else []), inc=1)
                ell_tm.read(tE)
            bank6.read(tE)
            Ebat.wrote(tE)
            tEm = p.op("v", lambda e: e.tensor_scalar(Enb.ap[:], Ebat.ap[:], 0.0, None, ALU.min),
                       deps=tmax(tE, Enb.wr()), inc=1)
            Enb.wrote(tEm)
            tx1 = p.op("s", lambda e: e.activation(expE.ap[:], Enb.ap[:], AF.Exp),
                       deps=tmax(tEm, expE.wr()), inc=1)
            expE.wrote(tx1)
            tEn = p.op("v", lambda e: e.tensor_tensor(Enb.ap[:], Enb.ap[:], Ebat.ap[:],
                                                      ALU.subtract), deps=[tEm, tx1], inc=1)
            Ebat.read(tEn)
            Enb.wrote(tEn)
            td_ = None
            for i in range(2):
                td_ = p.op("v", lambda e, i=i: e.tensor_tensor(
                    dinc.ap[:, i * CK:(i + 1) * CK], expE.ap[:, i * CK:(i + 1) * CK],
                    cb("incl_u"), ALU.mult),
                    deps=tmax(tx1, tkb, dinc.wr() if i == 0 else []), inc=1)
            dinc.wrote(td_)
            expE.read(td_)
            tx2 = p.op("s", lambda e: e.activation(expE.ap[:], Enb.ap[:], AF.Exp),
                       deps=tmax(tEn, expE.wr()), inc=1)
            expE.wrote(tx2)
            Enb.read(tx2)
            tl_ = None
            for i in range(2):
                tl_ = p.op("v", lambda e, i=i: e.tensor_tensor(
                    lfac.ap[:, i * CK:(i + 1) * CK], expE.ap[:, i * CK:(i + 1) * CK],
                    cb("strict_l"), ALU.mult),
                    deps=tmax(tx2, tkb, lfac.wr() if i == 0 else []), inc=1)
            lfac.wrote(tl_)
            expE.read(tl_)

            if stop_after is not None and "xee" in str(stop_after):
                return
            ttr = p.op("t", lambda e: e.transpose(
                bank5.ap[:, 0:CK], qkv[8 + hp].ap[:, tof:tof + CK], cb("ident")),
                deps=tmax(qkv[8 + hp].w, tkb, bank5.wr()), inc=1)
            qkv[8 + hp].read(ttr)
            ttr = p.op("t", lambda e: e.transpose(
                bank5.ap[:, CK:2 * CK], qkv[hp].ap[:, tof:tof + CK], cb("ident")), inc=1)
            qkv[hp].read(ttr)
            bank5.wrote(ttr)
            tkq = p.op("v", lambda e: e.tensor_copy(kqtm.ap[:], bank5.ap[:, 0:256]),
                       deps=tmax(ttr, kqtm.wr()), inc=1)
            bank5.read(tkq)
            kqtm.wrote(tkq)
            tnm = None
            for i in range(2):
                h = h0 + i
                tnm = p.op("v", lambda e, i=i, h=h: e.tensor_scalar(
                    khn_t.ap[:, i * 64:(i + 1) * 64], kqtm.ap[:, i * 64:(i + 1) * 64],
                    rnorm.ap[:, 16 + h:17 + h], None, ALU.mult),
                    deps=tmax(tkq, rnorm.w, khn_t.wr() if i == 0 else []), inc=1)
                tnm = p.op("v", lambda e, i=i, h=h: e.tensor_scalar(
                    khn_t.ap[:, 128 + i * 64:128 + (i + 1) * 64],
                    kqtm.ap[:, 128 + i * 64:128 + (i + 1) * 64],
                    rnorm.ap[:, h:h + 1], None, ALU.mult), inc=1)
                rnorm.read(tnm)
            khn_t.wrote(tnm)
            kqtm.read(tnm)
            ttr2 = None
            for i in range(2):
                for j in range(2):
                    ttr2 = p.op("t", lambda e, i=i, j=j: e.transpose(
                        bank5.ap[64 * i:64 * i + 64, 256 + j * 128:256 + (j + 1) * 128],
                        khn_t.ap[:, j * 128 + i * 64:j * 128 + (i + 1) * 64], cb("ident")),
                        deps=tmax(tnm, bank5.wr() if (i == 0 and j == 0) else []), inc=1)
            bank5.wrote(ttr2)
            khn_t.read(ttr2)
            thn = p.op("v", lambda e: e.tensor_copy(hnf.ap[:], bank5.ap[:, 256:256 + 256]),
                       deps=tmax(ttr2, hnf.wr()), inc=1)
            bank5.read(thn)
            hnf.wrote(thn)
            ttv = None
            for i in range(2):
                h = h0 + i
                ttv = p.op("t", lambda e, i=i, h=h: e.transpose(
                    bank5.ap[:, 768 + i * 128:768 + (i + 1) * 128],
                    qkv[16 + h].ap[:, tof:tof + CK], cb("ident")),
                    deps=tmax(qkv[16 + h].w, bank5.wr() if i == 0 else []), inc=1)
                qkv[16 + h].read(ttv)
            bank5.wrote(ttv)
            tve = p.op("v", lambda e: e.tensor_copy(vtm.ap[:], bank5.ap[:, 768:1024]),
                       deps=tmax(ttv, vtm.wr()), inc=1)
            bank5.read(tve)
            vtm.wrote(tve)
            tek = None
            for i in range(2):
                tek = p.op("s", lambda e, i=i: e.activation(
                    scl.ap[:, 4 * H + i:4 * H + i + 1],
                    Ebat.ap[:, i * CK + CK - 1:i * CK + CK], AF.Exp),
                    deps=tmax(tE, scl.wr() if i == 0 else []), inc=1)
                Ebat.read(tek)
            scl.wrote(tek)
            tkh = None
            for i in range(2):
                tkh = p.op("v", lambda e, i=i: e.tensor_scalar(
                    khat.ap[:, i * 64:(i + 1) * 64], khn_t.ap[:, i * 64:(i + 1) * 64],
                    scl.ap[:, 4 * H + i:4 * H + i + 1], None, ALU.mult),
                    deps=tmax(tnm, tek, khat.wr() if i == 0 else []), inc=1)
                scl.read(tkh)
            khat.wrote(tkh)
            khn_t.read(tkh)

            if stop_after is not None and "tp" in str(stop_after):
                return
            for i in range(2):
                h = h0 + i
                kn = hnf.ap[64 * i:64 * i + 64, 0:CK]
                qn = hnf.ap[64 * i:64 * i + 64, CK:2 * CK]
                bgr = nxt()
                tg1 = p.op("t", lambda e, kn=kn, bgr=bgr: e.matmul(bgr.ap[:, 0:CK], kn, kn,
                                                          start=True, stop=True),
                           deps=tmax(thn, bgr.wr()), inc=1)
                tg2 = p.op("t", lambda e, kn=kn, qn=qn, bgr=bgr: e.matmul(bgr.ap[:, CK:2 * CK], kn, qn,
                                                                 start=True, stop=True), inc=1)
                bgr.wrote(tg2)
                hnf.read(tg2)
                tMm = p.op("v", lambda e, i=i, bgr=bgr: e.tensor_tensor(
                    mmt[i].ap[:], dinc.ap[:, i * CK:(i + 1) * CK], bgr.ap[:, CK:2 * CK],
                    ALU.mult),
                    deps=tmax(td_, tg2, mmt[i].wr()), inc=1)
                mmt[i].wrote(tMm)
                dinc.read(tMm)
                bgr.read(tMm)
                tL = p.op("v", lambda e, i=i, bgr=bgr: e.tensor_tensor(
                    Lt[i].ap[:], lfac.ap[:, i * CK:(i + 1) * CK], bgr.ap[:, 0:CK], ALU.mult),
                    deps=tmax(tl_, tg1, Lt[i].wr()), inc=1)
                lfac.read(tL)
                bgr.read(tL)
                tL = p.op("v", lambda e, i=i, h=h: e.tensor_scalar(
                    Lt[i].ap[:], Lt[i].ap[:], beta_g.ap[:, h:h + 1], None, ALU.mult),
                    deps=[tL], inc=1)
                Lt[i].wrote(tL)
                beta_g.read(tL)
                tLT = p.op("t", lambda e, i=i: e.transpose(bank5.ap[:, 0:CK], Lt[i].ap[:],
                                                           cb("ident")),
                           deps=tmax(tL, tkb, bank5.wr()), inc=1)
                bank5.wrote(tLT)
                Lt[i].read(tLT)
                tLTe = p.op("v", lambda e, i=i: e.tensor_copy(LTt[i].ap[:], bank5.ap[:, 0:CK]),
                            deps=tmax(tLT, LTt[i].wr()), inc=1)
                bank5.read(tLTe)
                LTt[i].wrote(tLTe)

                tN = p.op("v", lambda e, i=i: e.tensor_tensor(Ngt.ap[:, 0:CK], LTt[i].ap[:],
                                                              cb("m8u"), ALU.mult),
                          deps=tmax(tLTe, Ngt.wr()), inc=1)
                LTt[i].read(tN)
                tNT = p.op("v", lambda e, i=i: e.tensor_tensor(Ngt.ap[:, CK:2 * CK], Lt[i].ap[:],
                                                               cb("m8l"), ALU.mult), inc=1)
                Lt[i].read(tNT)
                Ngt.wrote(tNT)
                bh = nxt()
                tm1 = p.op("t", lambda e, bh=bh: e.matmul(bh.ap[:, 0:CK], Ngt.ap[:, CK:2 * CK],
                                                   Ngt.ap[:, 0:CK], start=True, stop=True),
                           deps=tmax(tNT, bh.wr()), inc=1)
                tm2 = p.op("t", lambda e, bh=bh: e.matmul(bh.ap[:, CK:2 * CK], Ngt.ap[:, 0:CK],
                                                   Ngt.ap[:, CK:2 * CK], start=True, stop=True),
                           inc=1)
                bh.wrote(tm2)
                Ngt.read(tm2)
                th2 = p.op("v", lambda e, bh=bh: e.tensor_copy(h2.ap[:], bh.ap[:, 0:2 * CK]),
                           deps=tmax(tm2, h2.wr()), inc=1)
                bh.read(th2)
                h2.wrote(th2)
                bh = nxt()
                tm1 = p.op("t", lambda e, bh=bh: e.matmul(bh.ap[:, 0:CK], h2.ap[:, CK:2 * CK],
                                                   h2.ap[:, 0:CK], start=True, stop=True),
                           deps=tmax(th2, bh.wr()), inc=1)
                tm2 = p.op("t", lambda e, bh=bh: e.matmul(bh.ap[:, CK:2 * CK], h2.ap[:, 0:CK],
                                                   h2.ap[:, CK:2 * CK], start=True, stop=True),
                           inc=1)
                bh.wrote(tm2)
                h2.read(tm2)
                th4 = p.op("v", lambda e, bh=bh: e.tensor_copy(h4.ap[:], bh.ap[:, 0:2 * CK]),
                           deps=tmax(tm2, h4.wr()), inc=1)
                bh.read(th4)
                h4.wrote(th4)
                bh = nxt()
                tm1 = p.op("t", lambda e, bh=bh: e.matmul(bh.ap[:, 0:CK], h2.ap[:, CK:2 * CK],
                                                   h4.ap[:, 0:CK], start=True, stop=True),
                           deps=tmax(th4, bh.wr()), inc=1)
                tm2 = p.op("t", lambda e, bh=bh: e.matmul(bh.ap[:, CK:2 * CK], h4.ap[:, 0:CK],
                                                   h2.ap[:, CK:2 * CK], start=True, stop=True),
                           inc=1)
                bh.wrote(tm2)
                h2.read(tm2)
                h4.read(tm2)
                th24 = p.op("v", lambda e, bh=bh: e.tensor_copy(h24.ap[:], bh.ap[:, 0:2 * CK]),
                            deps=tmax(tm2, h24.wr()), inc=1)
                bh.read(th24)
                h24.wrote(th24)
                ta1 = p.op("v", lambda e: e.tensor_tensor(t0a.ap[:], h2.ap[:], h4.ap[:], ALU.add),
                           deps=tmax(th2, th4, t0a.wr()), inc=1)
                ta2 = p.op("v", lambda e: e.tensor_tensor(t0a.ap[:], t0a.ap[:], h24.ap[:],
                                                          ALU.add), deps=[ta1, th24], inc=1)
                h24.read(ta2)
                h2.read(ta2)
                h4.read(ta2)
                ta3 = None
                for j in range(2):
                    ta3 = p.op("v", lambda e, j=j: e.tensor_tensor(
                        t0a.ap[:, j * CK:(j + 1) * CK], t0a.ap[:, j * CK:(j + 1) * CK],
                        cb("ident"), ALU.add), deps=[ta2] if j == 0 else (), inc=1)
                t0a.wrote(ta3)
                bh = nxt()
                tmA = p.op("t", lambda e, bh=bh: e.matmul(bh.ap[:, 0:CK], Ngt.ap[:, CK:2 * CK],
                                                   t0a.ap[:, 0:CK], start=True, stop=True),
                           deps=tmax(ta3, bh.wr()), inc=1)
                tmB = p.op("t", lambda e, bh=bh: e.matmul(bh.ap[:, CK:2 * CK], t0a.ap[:, 0:CK],
                                                   Ngt.ap[:, CK:2 * CK], start=True, stop=True),
                           inc=1)
                bh.wrote(tmB)
                Ngt.read(tmB)
                t0a.read(tmB)
                tT0 = p.op("v", lambda e, i=i, bh=bh: e.tensor_tensor(Tg[i].ap[:], t0a.ap[:, 0:CK],
                                                               bh.ap[:, 0:CK], ALU.subtract),
                           deps=tmax(tmB, Tg[i].wr()), inc=1)
                Tg[i].wrote(tT0)
                tT0T = p.op("v", lambda e, i=i, bh=bh: e.tensor_tensor(TTg[i].ap[:], t0a.ap[:, CK:2 * CK],
                                                                bh.ap[:, CK:2 * CK], ALU.subtract),
                            deps=tmax(tmB, TTg[i].wr()), inc=1)
                TTg[i].wrote(tT0T)
                bh.read(tT0)
                bh.read(tT0T)
                t0a.read(tT0T)

                if stop_after is not None and "tbase" in str(stop_after):
                    return
                for li, m in enumerate((8, 16, 32, 64)):
                    last = (m == 64)
                    tRk = p.op("v", lambda e, i=i, m=m: e.tensor_tensor(
                        Ngt.ap[:, 0:CK], LTt[i].ap[:], cb(f"r{m}u"), ALU.mult),
                        deps=tmax(tLTe, Ngt.wr()), inc=1)
                    LTt[i].read(tRk)
                    tRkT = p.op("v", lambda e, i=i, m=m: e.tensor_tensor(
                        Ngt.ap[:, CK:2 * CK], Lt[i].ap[:], cb(f"r{m}l"), ALU.mult), inc=1)
                    Lt[i].read(tRkT)
                    Ngt.wrote(tRkT)
                    bh = nxt()
                    tmP = p.op("t", lambda e, i=i, bh=bh: e.matmul(bh.ap[:, 0:CK], TTg[i].ap[:],
                                                            Ngt.ap[:, 0:CK], start=True, stop=True),
                               deps=tmax(tRkT, TTg[i].w, bh.wr()), inc=1)
                    tmPT = p.op("t", lambda e, i=i, bh=bh: e.matmul(bh.ap[:, CK:2 * CK], Ngt.ap[:, 0:CK],
                                                             TTg[i].ap[:], start=True, stop=True),
                               inc=1)
                    bh.wrote(tmPT)
                    Ngt.read(tmPT)
                    TTg[i].read(tmPT)
                    tPe = p.op("v", lambda e, bh=bh: e.tensor_copy(Pg.ap[:], bh.ap[:, 0:2 * CK]),
                               deps=tmax(tmPT, Pg.wr()), inc=1)
                    bh.read(tPe)
                    Pg.wrote(tPe)
                    bh = nxt()
                    tmU1 = p.op("t", lambda e, i=i, bh=bh: e.matmul(bh.ap[:, 0:CK], Pg.ap[:, CK:2 * CK],
                                                             Tg[i].ap[:], start=True, stop=True),
                                deps=tmax(tPe, Tg[i].w, bh.wr()), inc=1)
                    if not last:
                        tmU2 = p.op("t", lambda e, i=i, bh=bh: e.matmul(bh.ap[:, CK:2 * CK], Tg[i].ap[:],
                                                                 Pg.ap[:, CK:2 * CK],
                                                                 start=True, stop=True), inc=1)
                    else:
                        tmU2 = tmU1
                    bh.wrote(tmU2)
                    Pg.read(tmU2)
                    Tg[i].read(tmU2)
                    tTn = p.op("v", lambda e, i=i, bh=bh: e.tensor_tensor(
                        Tg[i].ap[:], Tg[i].ap[:], bh.ap[:, 0:CK], ALU.subtract),
                        deps=tmax(tmU2, Tg[i].wr()), inc=1)
                    Tg[i].wrote(tTn)
                    bh.read(tTn)
                    if not last:
                        tTTn = p.op("v", lambda e, i=i, bh=bh: e.tensor_tensor(
                            TTg[i].ap[:], TTg[i].ap[:], bh.ap[:, CK:2 * CK], ALU.subtract),
                            deps=tmax(tmU2, TTg[i].wr()), inc=1)
                        TTg[i].wrote(tTTn)
                        bh.read(tTTn)

                if stop_after is not None and "tlvl" in str(stop_after):
                    return
                bks = nxt()
                tKS = p.op("t", lambda e, i=i, h=h, kn=kn, bks=bks: e.matmul(
                    bks.ap[:, 0:DVH], kn, S_b[h].ap, start=True, stop=True),
                    deps=tmax(thn, S_b[h].w, bks.wr()), inc=1)
                bks.wrote(tKS)
                S_b[h].read(tKS)
                hnf.read(tKS)
                tR = p.op("v", lambda e, i=i, h=h, bks=bks: e.scalar_tensor_tensor(
                    Rtm[i].ap[:], bks.ap[:, 0:DVH], negbg.ap[:, h:h + 1],
                    vtm.ap[:, i * DVH:(i + 1) * DVH], ALU.mult, ALU.add),
                    deps=tmax(tKS, tve, negbg.w, Rtm[i].wr()), inc=1)
                bks.read(tR)
                vtm.read(tR)
                negbg.read(tR)
                Rtm[i].wrote(tR)
                bu = nxt()
                tU = p.op("t", lambda e, i=i, bu=bu: e.matmul(bu.ap[:, 0:DVH], Tg[i].ap[:], Rtm[i].ap[:],
                                                       start=True, stop=True),
                          deps=tmax(tR, Tg[i].w, bu.wr()), inc=1)
                bu.wrote(tU)
                Tg[i].read(tU)
                Rtm[i].read(tU)
                tUe = p.op("v", lambda e, i=i, bu=bu: e.tensor_copy(Utm[i].ap[:], bu.ap[:, 0:DVH]),
                           deps=tmax(tU, Utm[i].wr()), inc=1)
                bu.read(tUe)
                Utm[i].wrote(tUe)

                bo1 = nxt()
                tO1 = p.op("t", lambda e, i=i, bo1=bo1: e.matmul(bo1.ap[:, 0:DVH], mmt[i].ap[:],
                                                        Utm[i].ap[:], start=True, stop=True),
                           deps=tmax(tUe, tMm, bo1.wr()), inc=1)
                bo1.wrote(tO1)
                mmt[i].read(tO1)
                Utm[i].read(tO1)
                bo2 = nxt()
                tO2 = p.op("t", lambda e, i=i, h=h, qn=qn, bo2=bo2: e.matmul(
                    bo2.ap[:, 0:DVH], qn, S_b[h].ap, start=True, stop=True),
                    deps=tmax(thn, S_b[h].w, bo2.wr()), inc=1)
                bo2.wrote(tO2)
                S_b[h].read(tO2)
                hnf.read(tO2)
                tsc2 = p.op("v", lambda e, h=h, bo2=bo2: e.tensor_scalar(
                    osc.ap[:], bo2.ap[:, 0:DVH], gam_tm.ap[:, h:h + 1], None, ALU.mult),
                    deps=tmax(tO2, gam_tm.w, osc.wr()), inc=1)
                bo2.read(tsc2)
                gam_tm.read(tsc2)
                osc.wrote(tsc2)
                tOa = p.op("v", lambda e, bo1=bo1: e.tensor_tensor(
                    osc.ap[:], osc.ap[:], bo1.ap[:, 0:DVH], ALU.add),
                    deps=tmax(tO1, tsc2), inc=1)
                bo1.read(tOa)
                osc.wrote(tOa)
                tq2 = p.op("s", lambda e: e.activation(osq.ap[:], osc.ap[:], AF.Square),
                           deps=tmax(tOa, osq.wr()), inc=1)
                osq.wrote(tq2)
                tms = p.op("v", lambda e: e.reduce_sum(oms.ap[:, 0:1], osq.ap[:], AX.X),
                           deps=tmax(tq2, oms.wr()), inc=1)
                osq.read(tms)
                tln = p.op("s", lambda e: e.activation(oms.ap[:, 1:2], oms.ap[:, 0:1], AF.Ln,
                                                       scale=1.0 / DVH, bias=EPS),
                           deps=[tms], inc=1)
                trs = p.op("s", lambda e: e.activation(oms.ap[:, 1:2], oms.ap[:, 1:2], AF.Exp,
                                                       scale=-0.5), deps=[tln], inc=1)
                oms.wrote(trs)
                tno = p.op("v", lambda e, i=i: e.tensor_scalar(
                    ontm.ap[:, i * DVH:(i + 1) * DVH], osc.ap[:], oms.ap[:, 1:2], None, ALU.mult),
                    deps=tmax(trs, ontm.wr() if i == 0 else []), inc=1)
                oms.read(tno)
                osc.read(tno)
                ontm.wrote(tno)

                bst = nxt()
                tS = p.op("t", lambda e, i=i, bst=bst: e.matmul(
                    bst.ap[64 * i:64 * i + 64, 0:DVH], khat.ap[:, i * 64:(i + 1) * 64],
                    Utm[i].ap[:], start=True, stop=True),
                    deps=tmax(tkh, tUe, bst.wr()), inc=1)
                bst.wrote(tS)
                khat.read(tS)
                Utm[i].read(tS)
                tgc = p.op("v", lambda e, i=i, h=h: e.tensor_tensor(
                    scl.ap[:, 4 * H + 2 + i:4 * H + 3 + i], scl.ap[:, 4 * H + i:4 * H + i + 1],
                    gam_tm.ap[:, h:h + 1], ALU.mult),
                    deps=tmax(tek, gam_tm.w, scl.wr()), inc=1)
                gam_tm.read(tgc)
                scl.wrote(tgc)
                hb = 64 * (h % 2)
                tSu = p.op("v", lambda e, i=i, h=h, hb=hb, bst=bst: e.scalar_tensor_tensor(
                    S_f[h].ap, S_f[h].ap, scl.ap[hb:hb + 64, 4 * H + 2 + i:4 * H + 3 + i],
                    bst.ap[hb:hb + 64, 0:DVH], ALU.mult, ALU.add),
                    deps=tmax(tS, tgc, S_f[h].wr()), inc=1)
                scl.read(tSu)
                bst.read(tSu)
                S_f[h].wrote(tSu)
                tSb = p.op("v", lambda e, h=h: e.tensor_copy(S_b[h].ap, S_f[h].ap),
                           deps=tmax(tSu, S_b[h].wr()), inc=1)
                S_f[h].read(tSb)
                S_b[h].wrote(tSb)

            ttn = None
            for i in range(2):
                ttn = p.op("t", lambda e, i=i: e.transpose(
                    bank5.ap[:, 256 + i * 128:256 + (i + 1) * 128],
                    ontm.ap[:, i * DVH:(i + 1) * DVH], cb("ident")),
                    deps=tmax(ontm.w, bank5.wr() if i == 0 else []), inc=1)
            bank5.wrote(ttn)
            ontm.read(ttn)
            toe = None
            for i in range(2):
                h = h0 + i
                toe = p.op("v", lambda e, i=i, h=h: e.tensor_copy(
                    on_fm[h].ap[:, tof:tof + CK], bank5.ap[:, 256 + i * 128:256 + (i + 1) * 128]),
                    deps=tmax(ttn, on_fm[h].wr()), inc=1)
                on_fm[h].wrote(toe)
            bank5.read(toe)

        def load_phase_b_weights(dep):
            toks = []
            for k in range(8):
                toks.append(stage_load(dwr[k * 128:(k + 1) * 128, :], wr_sb[k], k % 2, (dep,)))
                toks.append(stage_load(dwkf[k * 128:(k + 1) * 128, :], wkf_sb[k], k % 2))
                toks.append(stage_load(dwvf[k * 128:(k + 1) * 128, :], wvf_sb[k], k % 2))
            return tmax(*toks)

        def ffn_tile(ti, wB_tok):
            xb = xbuf[ti % 2]
            if ti + 1 < NTI:
                load_x_tile(ti + 1, xbuf[(ti + 1) % 2], dx1)

            trb, tmr, trbc = ln_stats(xb, LN_EPS)
            tmrbc = p.op("t", lambda e: e.matmul(bank6.ap[:, NT:2 * NT], cb("ones_row")[:1, :],
                                                 mr_bf.ap[:], start=True, stop=True),
                         deps=tmax(tmr, mr_bf.w), inc=1)
            bank6.wrote(tmrbc)
            mr_bf.read(tmrbc)
            ztoks = []
            for c in range(NCH):
                zt = p.op("v", lambda e, c=c: e.tensor_copy(
                    z_sb[c].ap[:, 0:1], z_sb[c].ap[:, NT:NT + 1]),
                    deps=tmax(z_sb[c].w, z_sb[c].rs), inc=1)
                t1 = p.op("v", lambda e, c=c: e.tensor_tensor(
                    ao_sb[c].ap[:], xb[c].ap[:], bank6.ap[:, 0:NT], ALU.mult),
                    deps=tmax(xb[c].w, tmrbc, ao_sb[c].wr()), inc=1)
                xb[c].read(t1)
                t2 = p.op("v", lambda e, c=c: e.tensor_tensor(
                    z_sb[c].ap[:, 1:NT + 1], ao_sb[c].ap[:], bank6.ap[:, NT:2 * NT],
                    ALU.subtract),
                    deps=tmax(t1, zt), inc=1)
                ao_sb[c].wrote(t1)
                ao_sb[c].read(t2)
                bank6.read(t2)
                z_sb[c].wrote(t2)
                ztoks.append(t2)
            mixtoks = []
            for c in range(NCH):
                td_ = p.op("v", lambda e, c=c: e.tensor_tensor(
                    fsc.ap[:], z_sb[c].ap[:, 1:NT + 1], z_sb[c].ap[:, 0:NT], ALU.subtract),
                    deps=tmax(ztoks[c], fsc.wr()), inc=1)
                fsc.wrote(td_)
                z_sb[c].read(td_)
                t1 = p.op("v", lambda e, c=c: e.scalar_tensor_tensor(
                    xr_sb[c].ap[:], fsc.ap[:], mu_sb[:, c:c + 1], z_sb[c].ap[:, 0:NT],
                    ALU.mult, ALU.add),
                    deps=tmax(td_, xr_sb[c].wr()), inc=1)
                xr_sb[c].wrote(t1)
                t2 = p.op("v", lambda e, c=c: e.scalar_tensor_tensor(
                    xk_sb[c].ap[:], fsc.ap[:], mu_sb[:, 8 + c:9 + c], z_sb[c].ap[:, 0:NT],
                    ALU.mult, ALU.add),
                    deps=tmax(td_, xk_sb[c].wr()), inc=1)
                xk_sb[c].wrote(t2)
                fsc.read(t2)
                z_sb[c].read(t2)
                mixtoks.append((t1, t2))
            for oc in range(NCH):
                pb = bank[oc % 2]
                tmm = None
                for k in range(NCH):
                    tmm = p.op("t", lambda e, k=k, oc=oc, pb=pb: e.matmul(
                        pb.ap[:, 0:NT], wr_sb[k][:, oc * 128:(oc + 1) * 128], xr_sb[k].ap[:],
                        start=(k == 0), stop=(k == NCH - 1)),
                        deps=tmax(mixtoks[k][0], wB_tok, pb.wr()) if k == 0 else [mixtoks[k][0]],
                        inc=1)
                pb.wrote(tmm)
                for k in range(NCH):
                    xr_sb[k].read(tmm)
                tr_ = p.op("s", lambda e, oc=oc, pb=pb: e.activation(
                    r_sb[oc].ap[:], pb.ap[:, 0:NT], AF.Sigmoid),
                    deps=tmax(tmm, r_sb[oc].wr()), inc=1)
                pb.read(tr_)
                r_sb[oc].wrote(tr_)
            for oc in range(NCH):
                pb = bank[2 + oc % 2]
                tmm = None
                for k in range(NCH):
                    tmm = p.op("t", lambda e, k=k, oc=oc, pb=pb: e.matmul(
                        pb.ap[:, 0:NT], wkf_sb[k][:, oc * 128:(oc + 1) * 128], xk_sb[k].ap[:],
                        start=(k == 0), stop=(k == NCH - 1)),
                        deps=tmax(mixtoks[k][1], wB_tok, pb.wr()) if k == 0 else [mixtoks[k][1]],
                        inc=1)
                pb.wrote(tmm)
                for k in range(NCH):
                    xk_sb[k].read(tmm)
                trl = p.op("s", lambda e, pb=pb: e.activation(fsc.ap[:], pb.ap[:, 0:NT], AF.Relu),
                           deps=tmax(tmm, fsc.wr()), inc=1)
                pb.read(trl)
                fsc.wrote(trl)
                tsq = p.op("v", lambda e, oc=oc: e.tensor_tensor(
                    ka_sb[oc].ap[:], fsc.ap[:], fsc.ap[:], ALU.mult),
                    deps=tmax(trl, ka_sb[oc].wr()), inc=1)
                fsc.read(tsq)
                ka_sb[oc].wrote(tsq)
            for oc in range(NCH):
                pb = bank[4] if oc % 2 == 0 else bank[2 + (oc + 1) % 2]
                pb = bank[4 - (oc % 2)]
                tmm = None
                for k in range(NCH):
                    tmm = p.op("t", lambda e, k=k, oc=oc, pb=pb: e.matmul(
                        pb.ap[:, 0:NT], wvf_sb[k][:, oc * 128:(oc + 1) * 128], ka_sb[k].ap[:],
                        start=(k == 0), stop=(k == NCH - 1)),
                        deps=tmax(ka_sb[k].w, pb.wr()) if k == 0 else [ka_sb[k].w], inc=1)
                pb.wrote(tmm)
                for k in range(NCH):
                    ka_sb[k].read(tmm)
                tev = p.op("v", lambda e, oc=oc, pb=pb: e.tensor_tensor(
                    ao_sb[oc].ap[:], pb.ap[:, 0:NT], r_sb[oc].ap[:], ALU.mult),
                    deps=tmax(tmm, r_sb[oc].w, ao_sb[oc].wr()), inc=1)
                pb.read(tev)
                r_sb[oc].read(tev)
                ao_sb[oc].wrote(tev)
            trb4, tmr4, trbc4 = ln_stats(ao_sb, LN_EPS)
            tmrbc4 = p.op("t", lambda e: e.matmul(bank6.ap[:, NT:2 * NT], cb("ones_row")[:1, :],
                                                  mr_bf.ap[:], start=True, stop=True),
                          deps=tmax(tmr4, mr_bf.w), inc=1)
            bank6.wrote(tmrbc4)
            mr_bf.read(tmrbc4)
            st_toks = []
            for c in range(NCH):
                t1 = p.op("v", lambda e, c=c: e.tensor_tensor(
                    ao_sb[c].ap[:], ao_sb[c].ap[:], bank6.ap[:, 0:NT], ALU.mult),
                    deps=tmax(ao_sb[c].w, tmrbc4), inc=1)
                t2 = p.op("v", lambda e, c=c: e.tensor_tensor(
                    ao_sb[c].ap[:], ao_sb[c].ap[:], bank6.ap[:, NT:2 * NT], ALU.subtract),
                    deps=[t1], inc=1)
                bank6.read(t2)
                t3 = p.op("v", lambda e, c=c: e.tensor_tensor(
                    xb[c].ap[:], xb[c].ap[:], ao_sb[c].ap[:], ALU.add),
                    deps=tmax(t2, xb[c].w, xb[c].rs), inc=1)
                ao_sb[c].wrote(t3)
                xb[c].wrote(t3)
                ts_ = p.op("g", lambda e, c=c, ti=ti: e.dma_start(
                    out=dout[c * 128:(c + 1) * 128, ti * NT:(ti + 1) * NT], in_=xb[c].ap[:]),
                    deps=[t3], sem="dga" if ti % 2 == 0 else "dgb", inc=16)
                xb[c].read(ts_)
                st_toks.append(ts_)
            for c in range(NCH):
                xb[c].rs = [st_toks[-1]]
            return st_toks

        if "A" in phases:
            tms_ = p.op("v", lambda e: e.memset(ptail.ap[:], 0.0), inc=1)
            ptail.wrote(tms_)
            tms2 = p.op("v", lambda e: e.memset(S_sb[:], 0.0), inc=1)
            tms3 = p.op("v", lambda e: e.memset(S_bf[:], 0.0), inc=1)
            for h in range(H):
                S_f[h].wrote(tms2)
                S_b[h].wrote(tms3)
            load_x_tile(0, xbuf[0], dx)
            for ti in range(NTI):
                attn_tile(ti)
            a_done = [("dga", p.n["dga"]), ("dgb", p.n["dgb"]), ("dg", p.n["dg"])]
        else:
            a_done = [("dga", p.n["dga"]), ("dgb", p.n["dgb"]), ("dg", p.n["dg"])]

        if "B" in phases:
            wB_tok = load_phase_b_weights(("t", p.n["t"]))
            tmu1 = wB_tok[0] if isinstance(wB_tok, list) else wB_tok
            for c in range(NCH):
                tmu1 = p.op("y", lambda e, c=c: e.dma_start(out=mu_sb[:, c:c + 1],
                                                            in_=dmur[c * 128:(c + 1) * 128, :]),
                            deps=[tmu1] if not isinstance(tmu1, list) else tmu1,
                            sem="dl", inc=16)
                tmu1 = p.op("y", lambda e, c=c: e.dma_start(out=mu_sb[:, 8 + c:9 + c],
                                                            in_=dmuk[c * 128:(c + 1) * 128, :]),
                            deps=[tmu1], sem="dl", inc=16)
            tzt = None
            for c in range(NCH):
                tzt = p.op("v", lambda e, c=c: e.memset(z_sb[c].ap[:, 0:NT + 1], 0.0),
                           deps=tmax(*a_done, ("t", p.n["t"])) if c == 0 else (), inc=1)
                z_sb[c].wrote(tzt)
            load_x_tile(0, xbuf[0], dx1, extra=tuple(a_done) + (tmu1, tzt))
            for ti in range(NTI):
                ffn_tile(ti, wB_tok)

        p.op("g", lambda e: e.nop(), deps=[("dga", p.n["dga"]), ("dgb", p.n["dgb"]),
                                           ("dg", p.n["dg"])])


        with nc.Block() as block:
            @block.tensor
            def _(eng):
                for f in p.q["t"]:
                    f(eng)

            @block.vector
            def _(eng):
                for f in p.q["v"]:
                    f(eng)

            @block.scalar
            def _(eng):
                for f in p.q["s"]:
                    f(eng)

            @block.gpsimd
            def _(eng):
                for f in p.q["g"]:
                    f(eng)

            @block.sync
            def _(eng):
                for f in p.q["y"]:
                    f(eng)

    return nc



_NC_CACHE = {}


def _get_nc():
    if "nc" not in _NC_CACHE:
        _NC_CACHE["nc"] = build_program()
    return _NC_CACHE["nc"]


def kernel(L0_attn_x, L0_attn_Wq, L0_attn_Wk, L0_attn_Wv, L0_attn_Wb, L0_attn_Wa,
           L0_attn_A_log, L0_attn_dt_bias, L0_attn_Wo, L0_attn_conv_q, L0_attn_conv_k,
           L0_attn_conv_v, L0_ffn_W_r, L0_ffn_W_k, L0_ffn_W_v, L0_ffn_mu_r, L0_ffn_mu_k):
    x = np.asarray(L0_attn_x, np.float32)
    B = x.shape[0]
    f = np.float32
    ct = np.ascontiguousarray
    wq = ct(np.asarray(L0_attn_Wq, f).T)
    wk = ct(np.asarray(L0_attn_Wk, f).T)
    wv = ct(np.asarray(L0_attn_Wv, f).T)
    wo = ct(np.asarray(L0_attn_Wo, f).T)
    wab_t = np.concatenate([np.asarray(L0_attn_Wa, f), np.asarray(L0_attn_Wb, f)], axis=0).T
    wab = ct(wab_t.reshape(8, 128, 32).transpose(1, 0, 2).reshape(128, 8 * 32))

    def _chunkify(w):
        w = np.asarray(w, f)
        k = w.shape[0] // 128
        return ct(w.reshape(k, 128, KW).transpose(1, 0, 2).reshape(128, k * KW))
    wr = ct(np.asarray(L0_ffn_W_r, f).T)
    wkf = ct(np.asarray(L0_ffn_W_k, f).T)
    wvf = ct(np.asarray(L0_ffn_W_v, f).T)
    shared = {
        "wq": wq, "wk": wk, "wv": wv, "wo": wo, "wab": wab,
        "convq": _chunkify(L0_attn_conv_q),
        "convk": _chunkify(L0_attn_conv_k),
        "convv": _chunkify(L0_attn_conv_v),
        "alog": ct(np.asarray(L0_attn_A_log, f).reshape(1, H)),
        "dtb": ct(np.asarray(L0_attn_dt_bias, f).reshape(1, H)),
        "wr": wr, "wkf": wkf, "wvf": wvf,
        "mur": ct(np.asarray(L0_ffn_mu_r, f).reshape(D, 1)),
        "muk": ct(np.asarray(L0_ffn_mu_k, f).reshape(D, 1)),
        "const": _CONSTS,
    }
    in_maps = []
    for b in range(B):
        m = dict(shared)
        m["x"] = ct(x[b].T)
        in_maps.append(m)
    nc = _get_nc()
    res = run_bass_kernel_spmd(nc, in_maps, core_ids=list(range(B)))
    out = np.stack([np.asarray(r["out"], np.float32).T for r in res.results], axis=0)
    return out


# version 02
# speedup vs baseline: 1.0346x; 1.0346x over previous
import contextlib
import numpy as np
import concourse.bass as bass
import concourse.mybir as mybir
from concourse.bass_utils import run_bass_kernel_spmd

KW_HOST = 4
F32 = mybir.dt.float32
F32R = mybir.dt.float32r
BF16 = mybir.dt.bfloat16
AF = mybir.ActivationFunctionType
ALU = mybir.AluOpType
AX = mybir.AxisListType

D = 1024
T = 8192
H = 16
DKH = 64
DVH = 128
KW = 4
NT = 256
CK = 128
NCH = D // 128
EPS = 1e-6
LN_EPS = 1e-5


def _consts_host():
    cols = []
    cmap = {}

    def add(name, tile, which):
        tile = np.asarray(tile, np.float32)
        t = np.zeros((128, tile.shape[1]), np.float32)
        t[: tile.shape[0]] = tile
        cmap[name] = (sum(c.shape[1] for c in cols), tile.shape[1], which)
        cols.append(t)

    i128 = np.arange(128)
    dw = lambda t: np.tile(t, (1, 2))
    add("ident", np.eye(128, dtype=np.float32), "b")
    add("ident_dw", dw(np.eye(128, dtype=np.float32)), "b")
    incl_u = (i128[:, None] <= i128[None, :]).astype(np.float32)
    strict_l = (i128[:, None] > i128[None, :]).astype(np.float32)
    add("incl_u_dw", dw(incl_u), "b")
    add("strict_l_dw", dw(strict_l), "b")
    blk8 = i128 // 8
    m8 = ((blk8[:, None] == blk8[None, :]) & (i128[:, None] < i128[None, :])).astype(np.float32)
    add("m8u_dw", dw(m8), "b")
    add("m8l_dw", dw(m8.T), "b")
    for m in (8, 16, 32, 64):
        b2 = i128 // (2 * m)
        half = (i128 // m) % 2
        r = ((b2[:, None] == b2[None, :]) & (half[:, None] == 0) & (half[None, :] == 1)).astype(np.float32)
        add(f"r{m}u_dw", dw(r), "b")
        add(f"r{m}l_dw", dw(r.T), "b")
    bd = np.zeros((128, 2), np.float32)
    bd[:64, 0] = 1.0
    bd[64:, 1] = 1.0
    add("bdones", bd, "b")
    add("ones_col", np.ones((128, 1), np.float32), "b")
    add("ones_row", np.ones((1, 128), np.float32), "b")
    add("incl_u_r", incl_u, "r")
    add("ones_row_r", np.ones((1, 128), np.float32), "r")
    oh = np.zeros((16, 16 * 128), np.float32)
    for h in range(16):
        oh[h, h * 128:(h + 1) * 128] = 1.0
    add("onehot", oh, "r")
    offs = {"b": 0, "r": 0}
    cmap2 = {}
    for name, (o, w, which) in cmap.items():
        cmap2[name] = (o, offs[which], w, which)
        offs[which] += w
    return (np.ascontiguousarray(np.concatenate(cols, axis=1)), cmap2,
            offs["b"], offs["r"])


_CONSTS, _CMAP, _CWB, _CWR = _consts_host()



class P:
    def __init__(self, nc):
        self.nc = nc
        self.q = {e: [] for e in ("t", "v", "s", "g", "y")}
        self.n = {k: 0 for k in ("t", "v", "s", "g", "y", "dl", "dg",
                                 "dxa", "dxb", "dca", "dcb", "dga", "dgb")}
        self.hw = {e: {} for e in self.q}
        self.sems = {}
        self.load = {"v": 0.0, "s": 0.0, "g": 0.0}

    def pick(self, costs):
        e = min(costs, key=lambda x: self.load[x] + costs[x])
        self.load[e] += costs[e]
        return e

    def _wait(self, e, tok):
        if tok is None:
            return
        s, c = tok
        if c <= 0 or c <= self.hw[e].get(s, 0):
            return
        self.hw[e][s] = c
        self.q[e].append(lambda eng, s=s, c=c: eng.wait_ge(self.sems[s], c))

    DMA_SEMS = ("dl", "dg", "dxa", "dxb", "dca", "dcb", "dga", "dgb")

    def op(self, e, emitfn, deps=(), sem=None, inc=0, pbase=0):
        sname = sem or e
        if e == "t":
            last = getattr(self, "pe_base", 0)
            if pbase != last:
                cnt = self.n["t"]
                self.q["t"].append(lambda eng, c=cnt: eng.wait_ge(self.sems["t"], c))
            self.pe_base = pbase
        if sname in self.DMA_SEMS:
            prev = getattr(self, "last_dma", {}).get(sname)
            if prev is not None:
                deps = list(deps) + [prev]
        for t in deps:
            self._wait(e, t)
        if inc:
            self.n[sname] += inc
            tok = (sname, self.n[sname])

            def run(eng, f=emitfn, sn=sname, k=inc):
                f(eng).then_inc(self.sems[sn], k)
            self.q[e].append(run)
            if sname in self.DMA_SEMS:
                if not hasattr(self, "last_dma"):
                    self.last_dma = {}
                self.last_dma[sname] = tok
            return tok
        self.q[e].append(lambda eng, f=emitfn: f(eng))
        return None


class Guard:
    def __init__(self, ap=None):
        self.ap = ap
        self.w = None
        self.rs = []

    def rd(self):
        return [self.w] if self.w else []

    def wr(self):
        return ([self.w] if self.w else []) + self.rs

    def wrote(self, tok):
        if tok is not None:
            self.w = tok
            self.rs = []

    def read(self, tok):
        if tok is not None:
            self.rs.append(tok)


def tmax(*toks):
    best = {}
    stack = list(toks)
    while stack:
        t = stack.pop()
        if t is None:
            continue
        if isinstance(t, list):
            stack.extend(t)
        elif t[1] > best.get(t[0], -1):
            best[t[0]] = t[1]
    return [(s, c) for s, c in best.items()]



def build_program(n_tiles=T // NT, debug=(), phases=("A", "B"), x1_out=False,
                  stop_after=None):
    nc = bass.Bass("TRN2", target_bir_lowering=False)
    NTI = n_tiles

    dx = nc.dram_tensor("x", [D, T], F32, kind="ExternalInput").ap()
    dwq = nc.dram_tensor("wq", [D, D], F32, kind="ExternalInput").ap()
    dwk = nc.dram_tensor("wk", [D, D], F32, kind="ExternalInput").ap()
    dwv = nc.dram_tensor("wv", [D, 2048], F32, kind="ExternalInput").ap()
    dwo = nc.dram_tensor("wo", [2048, D], F32, kind="ExternalInput").ap()
    dwab = nc.dram_tensor("wab", [128, NCH * 32], F32, kind="ExternalInput").ap()
    dcq = nc.dram_tensor("convq", [128, 8 * KW], F32, kind="ExternalInput").ap()
    dck = nc.dram_tensor("convk", [128, 8 * KW], F32, kind="ExternalInput").ap()
    dcv = nc.dram_tensor("convv", [128, 16 * KW], F32, kind="ExternalInput").ap()
    dalog = nc.dram_tensor("alog", [1, H], F32, kind="ExternalInput").ap()
    ddtb = nc.dram_tensor("dtb", [1, H], F32, kind="ExternalInput").ap()
    dwr = nc.dram_tensor("wr", [D, D], F32, kind="ExternalInput").ap()
    dwkf = nc.dram_tensor("wkf", [D, D], F32, kind="ExternalInput").ap()
    dwvf = nc.dram_tensor("wvf", [D, D], F32, kind="ExternalInput").ap()
    dmur = nc.dram_tensor("mur", [D, 1], F32, kind="ExternalInput").ap()
    dmuk = nc.dram_tensor("muk", [D, 1], F32, kind="ExternalInput").ap()
    dconst = nc.dram_tensor("const", list(_CONSTS.shape), F32, kind="ExternalInput").ap()
    dout = nc.dram_tensor("out", [D, T], F32, kind="ExternalOutput").ap()
    dx1 = nc.dram_tensor("x1s", [D, T], F32,
                         kind="ExternalOutput" if x1_out else "Internal").ap()
    dcdiag = nc.dram_tensor("cdiag", [128, 32 * KW * 128], BF16, kind="Internal").ap()

    ddbg = {}
    for name, shape in debug:
        ddbg[name] = nc.dram_tensor("dbg_" + name, list(shape), F32, kind="ExternalOutput").ap()

    p = P(nc)
    es = contextlib.ExitStack()

    def sb(name, shape, dt):
        return es.enter_context(nc.sbuf_tensor(name, list(shape), dt))

    with es:
        for s in ("t", "v", "s", "g", "y", "dl", "dg",
                  "dxa", "dxb", "dca", "dcb", "dga", "dgb"):
            p.sems[s] = es.enter_context(nc.semaphore("sem_" + s))

        cst_b = sb("cst_b", (128, _CWB), BF16)
        cst_r = sb("cst_r", (128, _CWR), F32R)
        epsc = sb("epsc", (128, 2), F32)
        p.op("v", lambda e: e.memset(epsc[:, 0:1], LN_EPS), inc=1)
        p.op("v", lambda e: e.memset(epsc[:, 1:2], EPS), inc=1)
        nc.const_aps.register(F32, LN_EPS, epsc[:, 0:1])
        nc.const_aps.register(F32, EPS, epsc[:, 1:2])

        def cb(nm):
            _, o, w, which = _CMAP[nm]
            assert which == "b"
            return cst_b[:, o:o + w]

        def cr(nm):
            _, o, w, which = _CMAP[nm]
            assert which == "r"
            return cst_r[:, o:o + w]

        bank = [Guard(es.enter_context(nc.psum_tensor(f"ps{i}", [128, 512], F32)))
                for i in range(5)]
        bank5 = Guard(es.enter_context(nc.psum_tensor("ps5", [128, 1024], BF16)))
        bank6 = Guard(es.enter_context(nc.psum_tensor("ps6", [128, 512], F32)))
        bank7 = Guard(es.enter_context(nc.psum_tensor("ps7", [128, 512], F32)))

        wpool = sb("wpool", (128, 48 * 1024), BF16)
        wq_sb = [wpool[:, (k) * 1024:(k + 1) * 1024] for k in range(8)]
        wk_sb = [wpool[:, (8 + k) * 1024:(9 + k) * 1024] for k in range(8)]
        wv_sb = [wpool[:, (16 + 2 * k) * 1024:(16 + 2 * k + 2) * 1024] for k in range(8)]
        wo_sb = [wpool[:, (32 + k) * 1024:(33 + k) * 1024] for k in range(16)]
        wr_sb = [wpool[:, (k) * 1024:(k + 1) * 1024] for k in range(8)]
        wkf_sb = [wpool[:, (8 + k) * 1024:(9 + k) * 1024] for k in range(8)]
        wvf_sb = [wpool[:, (16 + k) * 1024:(17 + k) * 1024] for k in range(8)]
        wg = Guard()

        wab_sb = sb("wab_sb", (128, NCH * 32), BF16)
        wbar = sb("wbar", (1, 2 * D + 32 + 2048), BF16)
        WBQ, WBK, WBAB, WBV = 0, D, 2 * D, 2 * D + 32
        convw = sb("convw", (128, 32 * KW), F32)
        negA = sb("negA", (128, H), F32)
        dtbb = sb("dtbb", (128, H), F32)
        alog_r = sb("alog_r", (1, 2 * H), F32R)


        xbuf = [[Guard(sb(f"x{b}_{c}", (128, NT), F32)) for c in range(NCH)] for b in range(2)]
        apool = sb("apool", (128, 12 * 1024), BF16)
        qkv = [Guard(apool[:, pc * NT:(pc + 1) * NT]) for pc in range(32)]
        on_fm = [Guard(apool[:, 8 * 1024 + hv * NT:8 * 1024 + (hv + 1) * NT]) for hv in range(16)]
        z_sb = [Guard(apool[:, c * (NT + 1):(c + 1) * (NT + 1)]) for c in range(8)]
        xr_sb = [Guard(apool[:, 2080 + c * NT:2080 + (c + 1) * NT]) for c in range(8)]
        xk_sb = [Guard(apool[:, 4128 + c * NT:4128 + (c + 1) * NT]) for c in range(8)]
        r_sb = [Guard(apool[:, 6176 + c * NT:6176 + (c + 1) * NT]) for c in range(8)]
        ka_sb = [Guard(apool[:, 8224 + c * NT:8224 + (c + 1) * NT]) for c in range(8)]

        ao_sb = [Guard(sb(f"ao{c}", (128, NT), BF16)) for c in range(NCH)]
        ybuf = [Guard(sb(f"y_{c}", (128, NT), BF16)) for c in range(NCH)]
        xwork = Guard(sb("xwork", (128, 2 * NT), BF16))
        stat = Guard(sb("stat", (1, 2 * NT), F32))
        mr_bf = Guard(sb("mr_bf", (1, NT), BF16))
        rstd_bf = Guard(sb("rstd_bf", (1, NT), BF16))
        pbuf = [Guard(sb("pb0", (128, NT + KW - 1), BF16))] * 2
        ptail = Guard(sb("ptail", (128, 32 * (KW - 1)), BF16))
        cdg = [Guard(sb(f"cdg{i}", (128, KW * 128), BF16)) for i in range(2)]

        ab_sb = Guard(sb("ab_sb", (128, 32), F32))
        loga = Guard(sb("loga", (128, H), F32R))
        beta_g = Guard(sb("beta", (128, H), F32))
        ell_tm = Guard(sb("ell_tm", (128, H), F32))
        ell_fm = Guard(sb("ell_fm", (16, CK), F32R))
        gam_tm = Guard(sb("gam_tm", (128, H), F32))
        rnorm = Guard(sb("rnorm", (128, 32), F32))
        scl = Guard(sb("scl", (128, 6 * H), F32))
        negbg = Guard(sb("negbg", (128, H), F32))

        S_f = [Guard(None) for _ in range(H)]
        S_b = [Guard(None) for _ in range(H)]
        S_sb = sb("S", (128, 8 * DVH), F32)
        S_bf = sb("S_bf", (128, 8 * DVH), BF16)
        for h in range(H):
            hb, hp_ = 64 * (h % 2), h // 2
            S_f[h].ap = S_sb[hb:hb + 64, hp_ * DVH:(hp_ + 1) * DVH]
            S_b[h].ap = S_bf[hb:hb + 64, hp_ * DVH:(hp_ + 1) * DVH]

        def gpair(name, cols, dt, npar=2):
            return [Guard(sb(f"{name}{i}", (128, cols), dt)) for i in range(npar)]

        EbatP = gpair("Ebat", 2 * CK, F32, 1)
        expEP = gpair("expE", 2 * CK, BF16)
        expE2P = gpair("expE2", 2 * CK, BF16, 1)
        dincP = gpair("dinc", 2 * CK, BF16)
        lfacP = gpair("lfac", 2 * CK, BF16)
        kqtmP = gpair("kqtm", 2 * CK, BF16, 1)
        khnP = gpair("khn", 2 * CK, BF16, 1)
        hnfP = gpair("hnf", 2 * CK, BF16, 1)
        vtmP = gpair("vtm", 2 * DVH, BF16, npar=1)
        khatP = gpair("khat", 2 * DKH, BF16, npar=1)
        LtP = gpair("Lt", 2 * CK, BF16)
        LTtP = gpair("LTt", 2 * CK, BF16)
        mmtP = gpair("mmt", 2 * CK, BF16, 1)
        RtmP = gpair("Rtm", 2 * CK, BF16, npar=1)
        rsc = Guard(sb("rsc", (128, 2 * CK), F32))
        Ngt = Guard(sb("Ngt", (128, 4 * CK), BF16))
        h2p = Guard(sb("h2p", (128, 4 * CK), BF16))
        h4p = Guard(sb("h4p", (128, 4 * CK), BF16))
        t0ap = Guard(sb("t0ap", (128, 4 * CK), BF16))
        Tgp = Guard(sb("Tgp", (128, 2 * CK), BF16))
        TTgp = Guard(sb("TTgp", (128, 2 * CK), BF16))
        Pgp = Guard(sb("Pgp", (128, 4 * CK), BF16))
        Utmp = Guard(sb("Utmp", (128, 2 * CK), BF16))
        oscp = Guard(sb("oscp", (128, 2 * CK), F32))
        osqp = Guard(sb("osqp", (128, 2 * CK), BF16))
        oms = Guard(sb("oms", (128, 4), F32))
        ontm = Guard(sb("ontm", (128, 2 * DVH), BF16))
        nsq_t = sb("nsq", (128, 2 * CK), BF16)
        nsqG = [Guard(nsq_t[:, 0:CK]), Guard(nsq_t[:, CK:2 * CK])]

        dkt = []
        for i in range(2):
            dkt.append(dict(
                ab=Guard(sb(f"ab_sb{i}", (128, 32), F32)),
                loga=Guard(sb(f"loga{i}", (128, H), F32R)),
                beta=Guard(sb(f"beta{i}", (128, H), F32)),
                ell_tm=Guard(sb(f"ell_tm{i}", (128, H), F32)),
                ell_fm=Guard(sb(f"ell_fm{i}", (16, CK), F32R)),
                gam=Guard(sb(f"gam{i}", (128, H), F32)),
                rnorm=Guard(sb(f"rnorm{i}", (128, 32), F32)),
                scl=Guard(sb(f"scl{i}", (128, 6 * H), F32)),
                negbg=Guard(sb(f"negbg{i}", (128, H), F32)),
            ))

        mu_sb = sb("mu_sb", (128, 16), F32)
        zt_g = Guard(None)
        fsc = Guard(sb("fsc", (128, NT), F32))

        cw_all = _CONSTS.shape[1]
        tkb = None
        td_prev = None
        for go in range(0, cw_all, NT):
            w = min(NT, cw_all - go)
            g = xbuf[0][(go // NT) % 4]
            td = p.op("y", lambda e, go=go, w=w, g=g:
                      e.dma_start(out=g.ap[:, :w], in_=dconst[:, go:go + w]),
                      deps=tmax(g.wr(), td_prev), sem="dl", inc=16)
            td_prev = td
            g.wrote(td)
            for nm, (osrc, odst, wd, which) in _CMAP.items():
                a, b = max(osrc, go), min(osrc + wd, go + w)
                if a >= b:
                    continue
                dstt = cst_b if which == "b" else cst_r
                tkb = p.op("v", lambda e, a=a, b=b, go=go, osrc=osrc, odst=odst, dstt=dstt, g=g:
                           e.tensor_copy(dstt[:, odst + a - osrc:odst + b - osrc],
                                         g.ap[:, a - go:b - go]),
                           deps=[td], inc=1)
                g.read(tkb)
        tkr = tkb

        _stage_sems = ("dl", "dxa", "dxb", "dca")

        def stage_load(src_view, dst_view, gidx, extra_deps=()):
            ncols = src_view.shape[1]
            dtoks = []
            pieces = []
            for o in range(0, ncols, NT):
                w = min(NT, ncols - o)
                j = (o // NT) % 4
                g = xbuf[0][j]
                td = p.op("y", lambda e, g=g, o=o, w=w: e.dma_start(
                    out=g.ap[:, :w], in_=src_view[:, o:o + w]),
                    deps=tmax(g.wr(), *extra_deps), sem=_stage_sems[j], inc=16)
                g.wrote(td)
                dtoks.append(td)
                pieces.append((g, o, w))
            gend = tmax(*dtoks)
            toks = []
            for (g, o, w) in pieces:
                tc = p.op("v", lambda e, g=g, o=o, w=w: e.tensor_copy(
                    dst_view[:, o:o + w], g.ap[:, :w]),
                    deps=tmax(gend, wg.w), inc=1)
                g.read(tc)
                toks.append(tc)
            return tmax(*toks)

        def load_phase_a_weights():
            toks = []
            for k in range(8):
                toks.append(stage_load(dwq[k * 128:(k + 1) * 128, :], wq_sb[k], k % 2))
                toks.append(stage_load(dwk[k * 128:(k + 1) * 128, :], wk_sb[k], k % 2))
                for hf in range(2):
                    toks.append(stage_load(dwv[k * 128:(k + 1) * 128, hf * 1024:(hf + 1) * 1024],
                                           wv_sb[k][:, hf * 1024:(hf + 1) * 1024], k % 2))
            for k in range(16):
                toks.append(stage_load(dwo[k * 128:(k + 1) * 128, :], wo_sb[k], k % 2))
            return toks

        wtoks = load_phase_a_weights()
        wg.wrote(wtoks[-1])
        wA_tok = tmax(*wtoks)

        tab_w = stage_load(dwab, wab_sb[:], 0)

        tcw = ("v", p.n["v"])
        tcw = tab_w
        for (srcd, nch_, off) in ((dcq, 8, 0), (dck, 8, 8), (dcv, 16, 16)):
            tcw = p.op("y", lambda e, srcd=srcd, nch_=nch_, off=off:
                       e.dma_start(out=convw[:, off * KW:(off + nch_) * KW], in_=srcd),
                       deps=tmax(tcw), sem="dl", inc=16)

        t_al = p.op("g", lambda e: e.dma_start(out=alog_r[:, :H], in_=dalog),
                    deps=tmax(tcw), sem="dg", inc=16)
        t_db = p.op("g", lambda e: e.dma_start(out=alog_r[:, H:], in_=ddtb),
                    deps=[t_al], sem="dg", inc=16)
        t_bc = p.op("t", lambda e: e.matmul(bank6.ap[:, :2 * H], cr("ones_row_r")[:1, :], alog_r[:],
                                            start=True, stop=True),
                    deps=tmax(t_al, t_db, tkr, bank6.wr()), inc=1)
        bank6.wrote(t_bc)
        tA1 = p.op("s", lambda e: e.activation(negA[:], bank6.ap[:, :H], AF.Exp),
                   deps=bank6.rd(), inc=1)
        t_negA = p.op("s", lambda e: e.mul(negA[:], negA[:], -1.0), deps=[tA1], inc=1)
        t_dtb = p.op("v", lambda e: e.tensor_copy(dtbb[:], bank6.ap[:, H:2 * H]),
                     deps=bank6.rd(), inc=1)
        bank6.read(t_negA)
        bank6.read(t_dtb)

        def build_wbar(dst_off, slabs, width, dep):
            for part in range(0, width, 512):
                w = min(512, width - part)
                tprev = None
                for ki, slab in enumerate(slabs):
                    tprev = p.op("t", lambda e, slab=slab, part=part, w=w, ki=ki:
                                 e.matmul(bank7.ap[:1, :w], cb("ones_col"),
                                          slab[:, part:part + w],
                                          start=(ki == 0), stop=(ki == len(slabs) - 1)),
                                 deps=tmax(tkb, dep, bank7.wr()) if ki == 0 else (), inc=1)
                bank7.wrote(tprev)
                te = p.op("v", lambda e, part=part, w=w, dst_off=dst_off:
                          e.tensor_scalar(wbar[:1, dst_off + part:dst_off + part + w],
                                          bank7.ap[:1, :w], -1.0, None, ALU.mult),
                          deps=bank7.rd(), inc=1)
                bank7.read(te)

        build_wbar(WBQ, wq_sb, D, wA_tok)
        build_wbar(WBK, wk_sb, D, wA_tok)
        build_wbar(WBV, wv_sb, 2048, wA_tok)
        build_wbar(WBAB, [wab_sb[:, k * 32:(k + 1) * 32] for k in range(NCH)], 32, [tab_w])
        wbar_tok = p.n["v"]

        cstg = Guard(apool[:, 0:2 * KW * 128])
        for c in range(32):
            half = (c % 2) * KW * 128
            tb = None
            for j in range(KW):
                tb = p.op("v", lambda e, c=c, j=j, half=half:
                          e.tensor_scalar(cstg.ap[:, half + j * 128:half + (j + 1) * 128],
                                          cb("ident"), convw[:, c * KW + j:c * KW + j + 1],
                                          None, ALU.mult),
                          deps=tmax(tcw, tkb, cstg.wr()) if j == 0 else (), inc=1)
            cstg.wrote(tb)
            ts_ = p.op("g", lambda e, c=c, half=half:
                       e.dma_start(out=dcdiag[:, c * KW * 128:(c + 1) * KW * 128],
                                   in_=cstg.ap[:, half:half + KW * 128]),
                       deps=tmax(cstg.rd(), t_db if c == 0 else []), sem="dg", inc=16)
            cstg.read(ts_)
        cdiag_tok = ("dg", p.n["dg"])
        for pc in range(8):
            qkv[pc].rs.append(cdiag_tok)

        def ln_stats(src, eps):
            t2 = None
            for c in range(NCH):
                tq = p.op("v", lambda e, c=c: e.tensor_tensor(
                    xwork.ap[:, NT:2 * NT], src[c].ap[:], src[c].ap[:], ALU.mult),
                    deps=tmax(src[c].w, xwork.wr()), inc=1)
                src[c].read(tq)
                tb_ = p.op("v", lambda e, c=c: e.tensor_copy(xwork.ap[:, 0:NT], src[c].ap[:]),
                           deps=[tq], inc=1)
                xwork.wrote(tb_)
                src[c].read(tb_)
                t2 = p.op("t", lambda e, c=c: e.matmul(bank7.ap[:1, 0:2 * NT], cb("ones_col"),
                                                  xwork.ap[:],
                                                  start=(c == 0), stop=(c == NCH - 1)),
                          deps=tmax(tb_, tkb, bank7.wr() if c == 0 else []), inc=1)
                xwork.read(t2)
            bank7.wrote(t2)
            tm = p.op("v", lambda e: e.tensor_scalar(stat.ap[:1, 0:NT], bank7.ap[:1, 0:NT],
                                                     1.0 / D, None, ALU.mult),
                      deps=tmax(t2, stat.wr()), inc=1)
            tmsq = p.op("v", lambda e: e.tensor_tensor(stat.ap[:1, NT:2 * NT], stat.ap[:1, 0:NT],
                                                       stat.ap[:1, 0:NT], ALU.mult),
                        deps=[tm], inc=1)
            tvar = p.op("v", lambda e: e.scalar_tensor_tensor(
                stat.ap[:1, NT:2 * NT], bank7.ap[:1, NT:2 * NT], 1.0 / D,
                stat.ap[:1, NT:2 * NT], ALU.mult, ALU.subtract), deps=[tmsq], inc=1)
            bank7.read(tvar)
            tl = p.op("s", lambda e: e.activation(stat.ap[:1, NT:2 * NT],
                                                  stat.ap[:1, NT:2 * NT], AF.Ln,
                                                  bias=float(eps)),
                      deps=[tvar], inc=1)
            tr = p.op("s", lambda e: e.activation(stat.ap[:1, NT:2 * NT],
                                                  stat.ap[:1, NT:2 * NT], AF.Exp, scale=-0.5),
                      deps=[tl], inc=1)
            stat.wrote(tr)
            trb = p.op("v", lambda e: e.tensor_copy(rstd_bf.ap[:], stat.ap[:1, NT:2 * NT]),
                       deps=tmax(tr, rstd_bf.wr()), inc=1)
            rstd_bf.wrote(trb)
            tmr2 = p.op("v", lambda e: e.tensor_tensor(stat.ap[:1, 0:NT], stat.ap[:1, 0:NT],
                                                       stat.ap[:1, NT:2 * NT], ALU.mult),
                        deps=[tr], inc=1)
            tmrb = p.op("v", lambda e: e.tensor_copy(mr_bf.ap[:], stat.ap[:1, 0:NT]),
                        deps=tmax(tmr2, mr_bf.wr()), inc=1)
            mr_bf.wrote(tmrb)
            trbc = p.op("t", lambda e: e.matmul(bank6.ap[:, 0:NT], cb("ones_row")[:1, :],
                                                rstd_bf.ap[:], start=True, stop=True),
                        deps=tmax(trb, bank6.wr()), inc=1)
            bank6.wrote(trbc)
            rstd_bf.read(trbc)
            return trb, tmrb, trbc

        def bcast2(rowg, width):
            pass

        def load_x_tile(ti, dst, src_dram, extra=()):
            sem = "dxa" if ti % 2 == 0 else "dxb"
            tk_ = None
            for c in range(NCH):
                g = dst[c]
                tk_ = p.op("y", lambda e, c=c, ti=ti, g=g, src_dram=src_dram:
                           e.dma_start(out=g.ap[:], in_=src_dram[c * 128:(c + 1) * 128,
                                                             ti * NT:(ti + 1) * NT]),
                           deps=tmax(g.wr(), *extra), sem=sem, inc=16)
            for c in range(NCH):
                dst[c].wrote(tk_)
            return [tk_]

        def wslab(pc):
            if pc < 8:
                return [wq_sb[k][:, pc * 128:(pc + 1) * 128] for k in range(8)], WBQ + pc * 128
            if pc < 16:
                o = pc - 8
                return [wk_sb[k][:, o * 128:(o + 1) * 128] for k in range(8)], WBK + o * 128
            o = pc - 16
            return [wv_sb[k][:, o * 128:(o + 1) * 128] for k in range(8)], WBV + o * 128

        def attn_tile(ti):
            xb = xbuf[ti % 2]
            if ti + 1 < NTI:
                load_x_tile(ti + 1, xbuf[(ti + 1) % 2], dx)

            trb, tmr, trbc = ln_stats(xb, LN_EPS)

            ytoks = []
            for c in range(NCH):
                ty = p.op("v", lambda e, c=c: e.tensor_tensor(
                    ybuf[c].ap[:], xb[c].ap[:], bank6.ap[:, 0:NT], ALU.mult),
                    deps=tmax(xb[c].w, bank6.w, ybuf[c].wr()), inc=1)
                ybuf[c].wrote(ty)
                bank6.read(ty)
                xb[c].read(ty)
                ytoks.append(ty)

            for pc in range(32):
                pb = bank[pc % 2]
                slabs, wb_off = wslab(pc)
                tmm = None
                for k in range(NCH):
                    tmm = p.op("t", lambda e, k=k, slabs=slabs, pb=pb: e.matmul(
                        pb.ap[:, 0:NT], slabs[k], ybuf[k].ap[:],
                        start=(k == 0), stop=False),
                        deps=tmax(ytoks[k], wA_tok, pb.wr()) if k == 0 else [ytoks[k]],
                        inc=1)
                tmm = p.op("t", lambda e, wb_off=wb_off, pb=pb: e.matmul(
                    pb.ap[:, 0:NT], wbar[:1, wb_off:wb_off + 128], mr_bf.ap[:],
                    start=False, stop=True),
                    deps=tmax(mr_bf.w, ("v", wbar_tok)), inc=1)
                pb.wrote(tmm)
                mr_bf.read(tmm)
                for k in range(NCH):
                    ybuf[k].read(tmm)
                cd = cdg[pc % 2]
                tcd = p.op("y", lambda e, pc=pc, cd=cd:
                           e.dma_start(out=cd.ap[:],
                                       in_=dcdiag[:, pc * KW * 128:(pc + 1) * KW * 128]),
                           deps=tmax(cd.wr(), cdiag_tok),
                           sem="dca" if pc % 2 == 0 else "dcb", inc=16)
                cd.wrote(tcd)
                pg = pbuf[pc % 2]
                tp0 = p.op("v", lambda e, pc=pc, pg=pg: e.tensor_copy(
                    pg.ap[:, 0:KW - 1], ptail.ap[:, pc * (KW - 1):(pc + 1) * (KW - 1)]),
                    deps=tmax(ptail.w, pg.wr()), inc=1)
                ptail.read(tp0)
                tp = p.op("s", lambda e, pg=pg, pb=pb: e.copy(pg.ap[:, KW - 1:], pb.ap[:, 0:NT]),
                          deps=tmax(tmm, tp0), inc=1)
                pb.read(tp)
                pg.wrote(tp)
                tt = p.op("v", lambda e, pc=pc, pg=pg: e.tensor_copy(
                    ptail.ap[:, pc * (KW - 1):(pc + 1) * (KW - 1)], pg.ap[:, NT:NT + KW - 1]),
                    deps=[tp], inc=1)
                ptail.wrote(tt)
                pg.read(tt)
                cbnk = bank[2 + (pc % 2)]
                tc_ = None
                for j in range(KW):
                    tc_ = p.op("t", lambda e, j=j, cd=cd, pg=pg, cbnk=cbnk: e.matmul(
                        cbnk.ap[:, 0:NT], cd.ap[:, j * 128:(j + 1) * 128], pg.ap[:, j:j + NT],
                        start=(j == 0), stop=(j == KW - 1)),
                        deps=tmax(tcd, tp, cbnk.wr()) if j == 0 else (), inc=1)
                cbnk.wrote(tc_)
                cd.read(tc_)
                pg.read(tc_)
                tsl = p.op("s", lambda e, pc=pc, cbnk=cbnk: e.activation(
                    qkv[pc].ap[:], cbnk.ap[:, 0:NT], AF.Silu),
                    deps=tmax(tc_, qkv[pc].wr()), inc=1)
                cbnk.read(tsl)
                qkv[pc].wrote(tsl)

            if stop_after == "proj":
                return []
            for cc in range(NT // CK):
                chunk_recurrence(ti, cc, ytoks)
                if isinstance(stop_after, str) and stop_after.startswith("cc0") and cc == 0:
                    break
            if stop_after is not None:
                return []

            for oc in range(NCH):
                pb = bank[oc % 2]
                tmm = None
                for hv in range(16):
                    tmm = p.op("t", lambda e, hv=hv, oc=oc, pb=pb: e.matmul(
                        pb.ap[:, 0:NT], wo_sb[hv][:, oc * 128:(oc + 1) * 128],
                        apool[:, 8 * 1024 + hv * NT:8 * 1024 + (hv + 1) * NT],
                        start=(hv == 0), stop=(hv == 15)),
                        deps=tmax(on_fm[hv].w, pb.wr()) if hv == 0 else [on_fm[hv].w],
                        inc=1)
                pb.wrote(tmm)
                for hv in range(16):
                    on_fm[hv].read(tmm)
                te = p.op("s", lambda e, oc=oc, pb=pb: e.copy(ao_sb[oc].ap[:], pb.ap[:, 0:NT]),
                          deps=tmax(tmm, ao_sb[oc].wr()), inc=1)
                pb.read(te)
                ao_sb[oc].wrote(te)

            trb2, tmr2, trbc2 = ln_stats(ao_sb, LN_EPS)
            tmrbc = p.op("t", lambda e: e.matmul(bank6.ap[:, NT:2 * NT], cb("ones_row")[:1, :],
                                                 mr_bf.ap[:], start=True, stop=True),
                         deps=tmax(tmr2, mr_bf.w), inc=1)
            bank6.wrote(tmrbc)
            mr_bf.read(tmrbc)
            st_toks = []
            for c in range(NCH):
                t1 = p.op("v", lambda e, c=c: e.tensor_tensor(
                    ao_sb[c].ap[:], ao_sb[c].ap[:], bank6.ap[:, 0:NT], ALU.mult),
                    deps=tmax(ao_sb[c].w, tmrbc), inc=1)
                t2 = p.op("v", lambda e, c=c: e.tensor_tensor(
                    ao_sb[c].ap[:], ao_sb[c].ap[:], bank6.ap[:, NT:2 * NT], ALU.subtract),
                    deps=[t1], inc=1)
                bank6.read(t2)
                t3 = p.op("v", lambda e, c=c: e.tensor_tensor(
                    xb[c].ap[:], xb[c].ap[:], ao_sb[c].ap[:], ALU.add),
                    deps=tmax(t2, xb[c].w, xb[c].rs), inc=1)
                ao_sb[c].wrote(t3)
                xb[c].wrote(t3)
                ts_ = p.op("g", lambda e, c=c, ti=ti: e.dma_start(
                    out=dx1[c * 128:(c + 1) * 128, ti * NT:(ti + 1) * NT], in_=xb[c].ap[:]),
                    deps=[t3], sem="dga" if ti % 2 == 0 else "dgb", inc=16)
                xb[c].read(ts_)
                st_toks.append(ts_)
            for c in range(NCH):
                xb[c].rs = [st_toks[-1]]
            return st_toks

        RB = [2, 3, 4]
        rb_state = [0]

        def nxt():
            b = bank[RB[rb_state[0] % 3]]
            rb_state[0] += 1
            return b

        import os as _os
        _FORCE = _os.environ.get("EW_FORCE") or None

        def _cost(kind, fd, bf, ps):
            c = {}
            if kind in ("tt", "stt"):
                accv = 2 if bf else 1
                c["v"] = ((120 if ps else 58) + fd / accv) / 0.96
                if not ps:
                    c["g"] = (180 + fd * 2.6) / 1.2
            elif kind in ("ts", "copy"):
                accv = (2 if ps else 4) if bf else 2
                c["v"] = ((120 if ps else 58) + fd / accv) / 0.96
                if not ps:
                    c["g"] = (180 + fd * 1.05) / 1.2
                c["s"] = ((172 if ps else 224) + fd / 2) / 1.2
            return c

        def ew_tt(out, a, b, op, deps, fd, bf=True, ps=False, eng=None):
            c = _cost("tt", fd, bf, ps)
            if _FORCE and _FORCE in c:
                eng = _FORCE
            if eng is None:
                eng = p.pick(c)
            else:
                p.load[eng] += c.get(eng, fd)
            return p.op(eng, lambda e, out=out, a=a, b=b, op=op: e.tensor_tensor(out, a, b, op),
                        deps=deps, inc=1)

        def ew_stt(out, a, s1, b, op0, op1, deps, fd, bf=True, ps=False):
            c = _cost("tt", fd, bf, ps)
            eng = _FORCE if (_FORCE and _FORCE in c) else p.pick(c)
            return p.op(eng, lambda e, out=out, a=a, s1=s1, b=b: e.scalar_tensor_tensor(
                out, a, s1, b, op0, op1), deps=deps, inc=1)

        def ew_ts(out, a, s1, op, deps, fd, s2=None, op2=None, bf=True, ps=False, eng=None):
            c = _cost("ts", fd, bf, ps)
            c.pop("s", None)
            if _FORCE and _FORCE in c:
                eng = _FORCE
            if eng is None:
                eng = p.pick(c)
            else:
                p.load[eng] += c.get(eng, fd)
            if eng == "s":
                return p.op("s", lambda e, out=out, a=a, s1=s1: e.activation(
                    out, a, AF.Copy, scale=s1), deps=deps, inc=1)
            if op2 is None:
                return p.op(eng, lambda e, out=out, a=a, s1=s1, op=op: e.tensor_scalar(
                    out, a, s1, None, op), deps=deps, inc=1)
            return p.op(eng, lambda e, out=out, a=a, s1=s1, s2=s2, op=op, op2=op2:
                        e.tensor_scalar(out, a, s1, s2, op, op2), deps=deps, inc=1)

        def ew_copy(out, a, deps, fd, bf=True, ps=False, eng=None):
            c = _cost("copy", fd, bf, ps)
            if _FORCE and _FORCE in c:
                eng = _FORCE
            if eng is None:
                eng = p.pick(c)
            else:
                p.load[eng] += c.get(eng, fd)
            if eng == "s":
                return p.op("s", lambda e, out=out, a=a: e.copy(out, a), deps=deps, inc=1)
            return p.op(eng, lambda e, out=out, a=a: e.tensor_copy(out, a), deps=deps, inc=1)

        def chunk_recurrence(ti, cc, ytoks):
            tof = cc * CK
            dk = dkt[cc % 2]
            ab_sb, loga, beta_g = dk["ab"], dk["loga"], dk["beta"]
            ell_tm, ell_fm, gam_tm = dk["ell_tm"], dk["ell_fm"], dk["gam"]
            rnorm, scl, negbg = dk["rnorm"], dk["scl"], dk["negbg"]

            tab = None
            for k in range(NCH):
                tab = p.op("t", lambda e, k=k: e.matmul(
                    bank7.ap[:, 0:32], ybuf[k].ap[:, tof:tof + CK],
                    wab_sb[:, k * 32:(k + 1) * 32],
                    start=(k == 0), stop=False),
                    deps=tmax(ytoks[k], tab_w, bank7.wr()) if k == 0 else [ytoks[k]], inc=1)
            tab = p.op("t", lambda e: e.matmul(
                bank7.ap[:, 0:32], mr_bf.ap[:1, tof:tof + CK], wbar[:1, WBAB:WBAB + 32],
                start=False, stop=True),
                deps=tmax(mr_bf.w, ("v", wbar_tok)), inc=1)
            bank7.wrote(tab)
            mr_bf.read(tab)
            for k in range(NCH):
                ybuf[k].read(tab)
            tabe = ew_copy(ab_sb.ap[:], bank7.ap[:, 0:32], tmax(tab, ab_sb.wr()), 32,
                           bf=False, ps=True)
            bank7.read(tabe)
            ab_sb.wrote(tabe)

            t1 = ew_tt(scl.ap[:, 0:H], ab_sb.ap[:, 0:H], dtbb[:], ALU.add,
                       tmax(tabe, t_dtb, scl.wr()), H, bf=False)
            t2 = p.op("s", lambda e: e.activation(scl.ap[:, H:2 * H], scl.ap[:, 0:H], AF.Abs),
                      deps=[t1], inc=1)
            t3 = p.op("s", lambda e: e.activation(scl.ap[:, 2 * H:3 * H], scl.ap[:, H:2 * H],
                                                  AF.Exp, scale=-1.0), deps=[t2], inc=1)
            t4 = p.op("s", lambda e: e.activation(scl.ap[:, 2 * H:3 * H], scl.ap[:, 2 * H:3 * H],
                                                  AF.Ln, bias=1.0), deps=[t3], inc=1)
            t5 = ew_ts(scl.ap[:, 3 * H:4 * H], scl.ap[:, 0:H], 0.0, ALU.max, [t1], H, bf=False)
            t6 = ew_tt(scl.ap[:, 2 * H:3 * H], scl.ap[:, 2 * H:3 * H], scl.ap[:, 3 * H:4 * H],
                       ALU.add, [t4, t5], H, bf=False, eng="v")
            tlg = ew_tt(loga.ap[:], scl.ap[:, 2 * H:3 * H], negA[:], ALU.mult,
                        tmax(t6, t_negA, loga.wr()), H, bf=False, eng="v")
            loga.wrote(tlg)
            scl.wrote(t6)
            tbt = p.op("s", lambda e: e.activation(beta_g.ap[:], ab_sb.ap[:, H:2 * H], AF.Sigmoid),
                       deps=tmax(tabe, beta_g.wr()), inc=1)
            tbt = p.op("s", lambda e: e.mul(beta_g.ap[:], beta_g.ap[:], 2.0), deps=[tbt], inc=1)
            beta_g.wrote(tbt)
            ab_sb.read(tbt)

            tc1 = p.op("t", lambda e: e.matmul(bank7.ap[:, 64:64 + H], cr("incl_u_r"), loga.ap[:],
                                               start=True, stop=True),
                       deps=tmax(tlg, tkr, bank7.wr()), inc=1)
            tc2 = p.op("t", lambda e: e.matmul(bank7.ap[:16, 128:128 + CK], loga.ap[:],
                                               cr("incl_u_r"), start=True, stop=True),
                       deps=tmax(tlg), inc=1)
            bank7.wrote(tc2)
            loga.read(tc2)
            te1 = ew_copy(ell_tm.ap[:], bank7.ap[:, 64:64 + H],
                          tmax(tc1, tc2, ell_tm.wr()), H, bf=False, ps=True, eng="v")
            ell_tm.wrote(te1)
            te2 = ew_copy(ell_fm.ap[:], bank7.ap[:16, 128:128 + CK],
                          tmax(tc1, tc2, ell_fm.wr()), CK, bf=False, ps=True, eng="v")
            ell_fm.wrote(te2)
            bank7.read(te1)
            bank7.read(te2)
            tg = p.op("s", lambda e: e.activation(gam_tm.ap[:], ell_tm.ap[:], AF.Exp),
                      deps=tmax(te1, gam_tm.wr()), inc=1)
            gam_tm.wrote(tg)
            tnb = ew_tt(negbg.ap[:], beta_g.ap[:], gam_tm.ap[:], ALU.mult,
                        tmax(tbt, tg, negbg.wr()), H, bf=False, eng="v")
            tnb = ew_ts(negbg.ap[:], negbg.ap[:], -1.0, ALU.mult, [tnb], H, bf=False)
            negbg.wrote(tnb)
            beta_g.read(tnb)
            gam_tm.read(tnb)

            ntk = None
            for qc in range(16):
                ng = nsqG[qc % 2]
                tq = ew_tt(ng.ap[:], qkv[qc].ap[:, tof:tof + CK],
                           qkv[qc].ap[:, tof:tof + CK], ALU.mult,
                           tmax(qkv[qc].w, ng.wr()), CK, eng="v")
                ng.wrote(tq)
                qkv[qc].read(tq)
                ntk = p.op("t", lambda e, qc=qc, ng=ng: e.matmul(
                    bank7.ap[:, 256 + qc * 2:256 + qc * 2 + 2], ng.ap[:],
                    cb("bdones"), start=True, stop=True),
                    deps=tmax(tq, tkb, bank7.wr() if qc == 0 else []), inc=1)
                ng.read(ntk)
            bank7.wrote(ntk)
            tn1 = p.op("s", lambda e: e.activation(rnorm.ap[:], bank7.ap[:, 256:288], AF.Ln),
                       deps=tmax(ntk, rnorm.wr()), inc=1)
            bank7.read(tn1)
            tn2 = p.op("s", lambda e: e.activation(rnorm.ap[:], rnorm.ap[:], AF.Exp, scale=-0.5),
                       deps=[tn1], inc=1)
            tn3 = ew_ts(rnorm.ap[:], rnorm.ap[:], 1.0e6, ALU.min, [tn2], 32, bf=False, eng="v")
            rnorm.wrote(tn3)

            if stop_after == "decay" or stop_after == "cc0decay":
                return
            nhp = 8
            if isinstance(stop_after, str) and "hp" in stop_after:
                import re as _re
                nhp = int(_re.search(r"hp(\d+)", stop_after).group(1))
            for hp in range(nhp):
                head_pair(ti, cc, tof, hp, dk, te1, te2)

        def head_pair(ti, cc, tof, hp, dk, te1, te2):
            h0 = 2 * hp
            par = hp % 2
            ab_sb, loga, beta_g = dk["ab"], dk["loga"], dk["beta"]
            ell_tm, ell_fm, gam_tm = dk["ell_tm"], dk["ell_fm"], dk["gam"]
            rnorm, scl, negbg = dk["rnorm"], dk["scl"], dk["negbg"]
            sel = lambda L: L[par % len(L)]
            EB, XE, XE2 = sel(EbatP), sel(expEP), sel(expE2P)
            DI, LF = sel(dincP), sel(lfacP)
            KQ, KH, HN, VT, KA = sel(kqtmP), sel(khnP), sel(hnfP), sel(vtmP), sel(khatP)
            LTP, LTTP, MMP, RTP = sel(LtP), sel(LTtP), sel(mmtP), sel(RtmP)

            tb_ = p.op("t", lambda e: e.matmul(bank6.ap[:, 0:CK],
                                               cr("onehot")[:16, h0 * 128:(h0 + 1) * 128],
                                               ell_fm.ap[:], start=True, stop=True),
                       deps=tmax(te2, tkr, bank6.wr()), inc=1)
            tb1 = p.op("t", lambda e: e.matmul(bank6.ap[:, CK:2 * CK],
                                               cr("onehot")[:16, (h0 + 1) * 128:(h0 + 2) * 128],
                                               ell_fm.ap[:], start=True, stop=True), inc=1)
            bank6.wrote(tb1)
            ell_fm.read(tb1)
            tE = None
            for i in range(2):
                tE = p.op("v", lambda e, i=i: e.tensor_scalar(
                    EB.ap[:, i * CK:(i + 1) * CK], bank6.ap[:, i * CK:(i + 1) * CK],
                    ell_tm.ap[:, h0 + i:h0 + i + 1], None, ALU.subtract),
                    deps=tmax(tb1, te1, EB.wr() if i == 0 else []), inc=1)
                ell_tm.read(tE)
            p.load["v"] += 400.0
            bank6.read(tE)
            EB.wrote(tE)
            tek = None
            for i in range(2):
                tek = p.op("s", lambda e, i=i: e.activation(
                    scl.ap[:, 4 * H + i:4 * H + i + 1],
                    EB.ap[:, i * CK + CK - 1:i * CK + CK], AF.Exp),
                    deps=tmax(tE, scl.wr() if i == 0 else []), inc=1)
                EB.read(tek)
            scl.wrote(tek)
            tm1 = ew_ts(rsc.ap[:, 0:2 * CK], EB.ap[:], 0.0, ALU.min,
                        tmax(tE, rsc.wr()), 2 * CK, bf=False)
            rsc.wrote(tm1)
            tx1 = p.op("s", lambda e: e.activation(XE.ap[:], rsc.ap[:, 0:2 * CK], AF.Exp),
                       deps=tmax(tm1, XE.wr()), inc=1)
            XE.wrote(tx1)
            rsc.read(tx1)
            tm2 = ew_ts(rsc.ap[:, 0:2 * CK], EB.ap[:], -1.0, ALU.mult,
                        tmax(tE, tx1), 2 * CK, s2=0.0, op2=ALU.min, bf=False)
            EB.read(tm2)
            rsc.wrote(tm2)
            tx2 = p.op("s", lambda e: e.activation(XE2.ap[:], rsc.ap[:, 0:2 * CK], AF.Exp),
                       deps=tmax(tm2, XE2.wr()), inc=1)
            XE2.wrote(tx2)
            rsc.read(tx2)
            td_ = ew_tt(DI.ap[:], XE.ap[:], cb("incl_u_dw"), ALU.mult,
                        tmax(tx1, tkb, DI.wr()), 2 * CK)
            DI.wrote(td_)
            XE.read(td_)
            tl_ = ew_tt(LF.ap[:], XE2.ap[:], cb("strict_l_dw"), ALU.mult,
                        tmax(tx2, tkb, LF.wr()), 2 * CK)
            LF.wrote(tl_)
            XE2.read(tl_)

            ttr = p.op("t", lambda e: e.transpose(
                bank5.ap[:, 0:CK], qkv[8 + hp].ap[:, tof:tof + CK], cb("ident")),
                deps=tmax(qkv[8 + hp].w, tkb, bank5.wr()), inc=1)
            qkv[8 + hp].read(ttr)
            ttr = p.op("t", lambda e: e.transpose(
                bank5.ap[:, CK:2 * CK], qkv[hp].ap[:, tof:tof + CK], cb("ident")), inc=1)
            qkv[hp].read(ttr)
            bank5.wrote(ttr)
            tkq = ew_copy(KQ.ap[:], bank5.ap[:, 0:256], tmax(ttr, KQ.wr()), 2 * CK, ps=True)
            bank5.read(tkq)
            KQ.wrote(tkq)
            kh_toks = []
            for i in range(2):
                h = h0 + i
                ta_ = ew_ts(KH.ap[:, i * 64:(i + 1) * 64], KQ.ap[:, i * 64:(i + 1) * 64],
                            rnorm.ap[:, 16 + h:17 + h], ALU.mult,
                            tmax(tkq, rnorm.w, KH.wr() if i == 0 else []), 64)
                tb2 = ew_ts(KH.ap[:, 128 + i * 64:128 + (i + 1) * 64],
                            KQ.ap[:, 128 + i * 64:128 + (i + 1) * 64],
                            rnorm.ap[:, h:h + 1], ALU.mult, tmax(tkq, rnorm.w), 64)
                rnorm.read(ta_)
                rnorm.read(tb2)
                KQ.read(ta_)
                KQ.read(tb2)
                kh_toks += [ta_, tb2]
            kh_done = tmax(*kh_toks)
            KH.w = None
            KH.rs = list(kh_done)
            ttr2 = None
            for i in range(2):
                for j in range(2):
                    ttr2 = p.op("t", lambda e, i=i, j=j: e.transpose(
                        bank5.ap[64 * i:64 * i + 64, 256 + j * 128:256 + (j + 1) * 128],
                        KH.ap[:, j * 128 + i * 64:j * 128 + (i + 1) * 64], cb("ident")),
                        deps=tmax(kh_done, bank5.wr()) if (i == 0 and j == 0) else (), inc=1)
            bank5.wrote(ttr2)
            thn = ew_copy(HN.ap[:], bank5.ap[:, 256:512], tmax(ttr2, HN.wr()), 2 * CK, ps=True)
            bank5.read(thn)
            HN.wrote(thn)
            ttv = None
            for i in range(2):
                h = h0 + i
                ttv = p.op("t", lambda e, i=i, h=h: e.transpose(
                    bank5.ap[:, 512 + i * 128:512 + (i + 1) * 128],
                    qkv[16 + h].ap[:, tof:tof + CK], cb("ident")),
                    deps=tmax(qkv[16 + h].w, bank5.wr() if i == 0 else []), inc=1)
                qkv[16 + h].read(ttv)
            bank5.wrote(ttv)
            tve = ew_copy(VT.ap[:], bank5.ap[:, 512:768], tmax(ttv, VT.wr()), 2 * CK, ps=True)
            bank5.read(tve)
            VT.wrote(tve)
            tkh = None
            for i in range(2):
                tkh = ew_ts(KA.ap[:, i * 64:(i + 1) * 64], KH.ap[:, i * 64:(i + 1) * 64],
                            scl.ap[:, 4 * H + i:4 * H + i + 1], ALU.mult,
                            tmax(kh_done, tek, KA.wr() if i == 0 else []), 64)
                scl.read(tkh)
                KH.read(tkh)
            KA.wrote(tkh)
            if stop_after is not None and "tp" in str(stop_after):
                return

            if stop_after is not None and "xpre" in str(stop_after):
                return
            kn = [HN.ap[64 * i:64 * i + 64, 0:CK] for i in range(2)]
            qn = [HN.ap[64 * i:64 * i + 64, CK:2 * CK] for i in range(2)]
            bgr = nxt()
            import os as _os2
            _gm = _os2.environ.get("GRAM_MODE", "")
            _irange = {"i0": (0,), "i1": (1,)}.get(_gm, (0, 1))
            tg2 = None
            for i in _irange:
                tg2 = p.op("t", lambda e, i=i, bgr=bgr: e.matmul(
                    bgr.ap[:, i * CK:(i + 1) * CK], kn[i], kn[i], start=True, stop=True),
                    deps=tmax(thn, bgr.wr()) if i == 0 else (), inc=1, pbase=64 * i)
                tg2 = p.op("t", lambda e, i=i, bgr=bgr: e.matmul(
                    bgr.ap[:, 2 * CK + i * CK:2 * CK + (i + 1) * CK], kn[i], qn[i],
                    start=True, stop=True), inc=1, pbase=64 * i)
            bgr.wrote(tg2)
            HN.read(tg2)
            if stop_after is not None and "xgr" in str(stop_after):
                return
            tL = ew_tt(LTP.ap[:], LF.ap[:], bgr.ap[:, 0:2 * CK], ALU.mult,
                       tmax(tl_, tg2, LTP.wr()), 2 * CK, ps=True, eng="v")
            LF.read(tL)
            bgr.read(tL)
            for i in range(2):
                h = h0 + i
                tL = ew_ts(LTP.ap[:, i * CK:(i + 1) * CK], LTP.ap[:, i * CK:(i + 1) * CK],
                           beta_g.ap[:, h:h + 1], ALU.mult, [tL], CK)
                beta_g.read(tL)
            LTP.wrote(tL)
            tMm = ew_tt(MMP.ap[:], DI.ap[:], bgr.ap[:, 2 * CK:4 * CK], ALU.mult,
                        tmax(td_, tg2, MMP.wr()), 2 * CK, ps=True, eng="v")
            MMP.wrote(tMm)
            DI.read(tMm)
            bgr.read(tMm)
            if stop_after is not None and "xlm" in str(stop_after):
                return
            tLT = None
            for i in range(2):
                tLT = p.op("t", lambda e, i=i: e.transpose(
                    bank5.ap[:, 768 + i * 128:768 + (i + 1) * 128],
                    LTP.ap[:, i * CK:(i + 1) * CK], cb("ident")),
                    deps=tmax(tL, tkb, bank5.wr()) if i == 0 else (), inc=1)
            bank5.wrote(tLT)
            LTP.read(tLT)
            tLTe = ew_copy(LTTP.ap[:], bank5.ap[:, 768:1024], tmax(tLT, LTTP.wr()), 2 * CK,
                           ps=True)
            bank5.read(tLTe)
            LTTP.wrote(tLTe)

            if stop_after is not None and "xlt" in str(stop_after):
                return
            tN = ew_tt(Ngt.ap[:, 0:2 * CK], LTTP.ap[:], cb("m8u_dw"), ALU.mult,
                       tmax(tLTe, Ngt.wr()), 2 * CK)
            LTTP.read(tN)
            tNT = ew_tt(Ngt.ap[:, 2 * CK:4 * CK], LTP.ap[:], cb("m8l_dw"), ALU.mult,
                        tmax(tL, Ngt.wr()), 2 * CK)
            LTP.read(tNT)
            ngt_done = tmax(tN, tNT)
            Ngt.w = None
            Ngt.rs = list(ngt_done)
            if stop_after is not None and "xnn" in str(stop_after):
                return
            bh = nxt()
            tmm2 = None
            for i in range(2):
                tmm2 = p.op("t", lambda e, i=i, bh=bh: e.matmul(
                    bh.ap[:, i * CK:(i + 1) * CK], Ngt.ap[:, 2 * CK + i * CK:2 * CK + (i + 1) * CK],
                    Ngt.ap[:, i * CK:(i + 1) * CK], start=True, stop=True),
                    deps=tmax(ngt_done, bh.wr()) if i == 0 else (), inc=1)
                tmm2 = p.op("t", lambda e, i=i, bh=bh: e.matmul(
                    bh.ap[:, 2 * CK + i * CK:2 * CK + (i + 1) * CK],
                    Ngt.ap[:, i * CK:(i + 1) * CK],
                    Ngt.ap[:, 2 * CK + i * CK:2 * CK + (i + 1) * CK],
                    start=True, stop=True), inc=1)
            bh.wrote(tmm2)
            Ngt.read(tmm2)
            th2 = ew_copy(h2p.ap[:], bh.ap[:, 0:4 * CK], tmax(tmm2, h2p.wr()), 4 * CK, ps=True)
            bh.read(th2)
            h2p.wrote(th2)
            bh = nxt()
            tmm4 = None
            for i in range(2):
                tmm4 = p.op("t", lambda e, i=i, bh=bh: e.matmul(
                    bh.ap[:, i * CK:(i + 1) * CK], h2p.ap[:, 2 * CK + i * CK:2 * CK + (i + 1) * CK],
                    h2p.ap[:, i * CK:(i + 1) * CK], start=True, stop=True),
                    deps=tmax(th2, bh.wr()) if i == 0 else (), inc=1)
                tmm4 = p.op("t", lambda e, i=i, bh=bh: e.matmul(
                    bh.ap[:, 2 * CK + i * CK:2 * CK + (i + 1) * CK],
                    h2p.ap[:, i * CK:(i + 1) * CK],
                    h2p.ap[:, 2 * CK + i * CK:2 * CK + (i + 1) * CK],
                    start=True, stop=True), inc=1)
            bh.wrote(tmm4)
            h2p.read(tmm4)
            th4 = ew_copy(h4p.ap[:], bh.ap[:, 0:4 * CK], tmax(tmm4, h4p.wr()), 4 * CK, ps=True)
            bh.read(th4)
            h4p.wrote(th4)
            bh = nxt()
            tmm24 = None
            for i in range(2):
                tmm24 = p.op("t", lambda e, i=i, bh=bh: e.matmul(
                    bh.ap[:, i * CK:(i + 1) * CK], h2p.ap[:, 2 * CK + i * CK:2 * CK + (i + 1) * CK],
                    h4p.ap[:, i * CK:(i + 1) * CK], start=True, stop=True),
                    deps=tmax(th4, bh.wr()) if i == 0 else (), inc=1)
                tmm24 = p.op("t", lambda e, i=i, bh=bh: e.matmul(
                    bh.ap[:, 2 * CK + i * CK:2 * CK + (i + 1) * CK],
                    h4p.ap[:, i * CK:(i + 1) * CK],
                    h2p.ap[:, 2 * CK + i * CK:2 * CK + (i + 1) * CK],
                    start=True, stop=True), inc=1)
            bh.wrote(tmm24)
            h2p.read(tmm24)
            h4p.read(tmm24)
            if stop_after is not None and "xh24" in str(stop_after):
                return
            ta1 = ew_tt(t0ap.ap[:], h2p.ap[:], h4p.ap[:], ALU.add,
                        tmax(th2, th4, t0ap.wr()), 4 * CK)
            ta2 = ew_tt(t0ap.ap[:], t0ap.ap[:], bh.ap[:, 0:4 * CK], ALU.add,
                        tmax(ta1, tmm24), 4 * CK, ps=True, eng="v")
            bh.read(ta2)
            ta3 = None
            for j in range(2):
                ta3 = ew_tt(t0ap.ap[:, j * 2 * CK:(j + 1) * 2 * CK],
                            t0ap.ap[:, j * 2 * CK:(j + 1) * 2 * CK],
                            cb("ident_dw"), ALU.add, [ta2] if j == 0 else [ta3], 2 * CK)
            t0ap.wrote(ta3)
            bh = nxt()
            tmc = None
            for i in range(2):
                tmc = p.op("t", lambda e, i=i, bh=bh: e.matmul(
                    bh.ap[:, i * CK:(i + 1) * CK],
                    Ngt.ap[:, 2 * CK + i * CK:2 * CK + (i + 1) * CK],
                    t0ap.ap[:, i * CK:(i + 1) * CK], start=True, stop=True),
                    deps=tmax(ta3, bh.wr()) if i == 0 else (), inc=1)
                tmc = p.op("t", lambda e, i=i, bh=bh: e.matmul(
                    bh.ap[:, 2 * CK + i * CK:2 * CK + (i + 1) * CK],
                    t0ap.ap[:, i * CK:(i + 1) * CK],
                    Ngt.ap[:, 2 * CK + i * CK:2 * CK + (i + 1) * CK],
                    start=True, stop=True), inc=1)
            bh.wrote(tmc)
            Ngt.read(tmc)
            t0ap.read(tmc)
            tT0 = ew_tt(Tgp.ap[:], t0ap.ap[:, 0:2 * CK], bh.ap[:, 0:2 * CK], ALU.subtract,
                        tmax(tmc, Tgp.wr()), 2 * CK, ps=True, eng="v")
            Tgp.wrote(tT0)
            tT0T = ew_tt(TTgp.ap[:], t0ap.ap[:, 2 * CK:4 * CK], bh.ap[:, 2 * CK:4 * CK],
                         ALU.subtract, tmax(tmc, TTgp.wr()), 2 * CK, ps=True, eng="v")
            TTgp.wrote(tT0T)
            bh.read(tT0)
            bh.read(tT0T)
            t0ap.read(tT0T)
            if stop_after is not None and "tbase" in str(stop_after):
                return

            for li, m in enumerate((8, 16, 32, 64)):
                last = (m == 64)
                tRk = ew_tt(Ngt.ap[:, 0:2 * CK], LTTP.ap[:], cb(f"r{m}u_dw"), ALU.mult,
                            tmax(tLTe, Ngt.wr()), 2 * CK)
                LTTP.read(tRk)
                tRkT = ew_tt(Ngt.ap[:, 2 * CK:4 * CK], LTP.ap[:], cb(f"r{m}l_dw"), ALU.mult,
                             tmax(tL, Ngt.wr()), 2 * CK)
                LTP.read(tRkT)
                rk_done = tmax(tRk, tRkT)
                Ngt.w = None
                Ngt.rs = list(rk_done)
                bh = nxt()
                tmP = None
                for i in range(2):
                    tmP = p.op("t", lambda e, i=i, bh=bh: e.matmul(
                        bh.ap[:, i * CK:(i + 1) * CK], TTgp.ap[:, i * CK:(i + 1) * CK],
                        Ngt.ap[:, i * CK:(i + 1) * CK], start=True, stop=True),
                        deps=tmax(rk_done, TTgp.w, bh.wr()) if i == 0 else (), inc=1)
                    tmP = p.op("t", lambda e, i=i, bh=bh: e.matmul(
                        bh.ap[:, 2 * CK + i * CK:2 * CK + (i + 1) * CK],
                        Ngt.ap[:, i * CK:(i + 1) * CK], TTgp.ap[:, i * CK:(i + 1) * CK],
                        start=True, stop=True), inc=1)
                bh.wrote(tmP)
                Ngt.read(tmP)
                TTgp.read(tmP)
                tPe = ew_copy(Pgp.ap[:], bh.ap[:, 0:4 * CK], tmax(tmP, Pgp.wr()), 4 * CK,
                              ps=True)
                bh.read(tPe)
                Pgp.wrote(tPe)
                bh = nxt()
                tmU = None
                for i in range(2):
                    tmU = p.op("t", lambda e, i=i, bh=bh: e.matmul(
                        bh.ap[:, i * CK:(i + 1) * CK],
                        Pgp.ap[:, 2 * CK + i * CK:2 * CK + (i + 1) * CK],
                        Tgp.ap[:, i * CK:(i + 1) * CK], start=True, stop=True),
                        deps=tmax(tPe, Tgp.w, bh.wr()) if i == 0 else (), inc=1)
                    if not last:
                        tmU = p.op("t", lambda e, i=i, bh=bh: e.matmul(
                            bh.ap[:, 2 * CK + i * CK:2 * CK + (i + 1) * CK],
                            Tgp.ap[:, i * CK:(i + 1) * CK],
                            Pgp.ap[:, 2 * CK + i * CK:2 * CK + (i + 1) * CK],
                            start=True, stop=True), inc=1)
                bh.wrote(tmU)
                Pgp.read(tmU)
                Tgp.read(tmU)
                tTn = ew_tt(Tgp.ap[:], Tgp.ap[:], bh.ap[:, 0:2 * CK], ALU.subtract,
                            tmax(tmU, Tgp.wr()), 2 * CK, ps=True, eng="v")
                Tgp.wrote(tTn)
                bh.read(tTn)
                if not last:
                    tTTn = ew_tt(TTgp.ap[:], TTgp.ap[:], bh.ap[:, 2 * CK:4 * CK], ALU.subtract,
                                 tmax(tmU, TTgp.wr()), 2 * CK, ps=True, eng="v")
                    TTgp.wrote(tTTn)
                    bh.read(tTTn)
            if stop_after is not None and "tlvl" in str(stop_after):
                return

            bS = nxt()
            tKS = None
            for i in range(2):
                h = h0 + i
                tKS = p.op("t", lambda e, i=i, h=h, bS=bS: e.matmul(
                    bS.ap[:, i * CK:(i + 1) * CK], kn[i], S_b[h].ap, start=True, stop=True),
                    deps=tmax(thn, S_b[h].w, bS.wr()) if i == 0 else tmax(S_b[h].w), inc=1,
                    pbase=64 * i)
                S_b[h].read(tKS)
                tKS = p.op("t", lambda e, i=i, h=h, bS=bS: e.matmul(
                    bS.ap[:, 2 * CK + i * CK:2 * CK + (i + 1) * CK], qn[i], S_b[h].ap,
                    start=True, stop=True), inc=1, pbase=64 * i)
                S_b[h].read(tKS)
            bS.wrote(tKS)
            HN.read(tKS)
            tR = None
            for i in range(2):
                h = h0 + i
                tR = ew_stt(RTP.ap[:, i * CK:(i + 1) * CK], bS.ap[:, i * CK:(i + 1) * CK],
                            negbg.ap[:, h:h + 1], VT.ap[:, i * CK:(i + 1) * CK],
                            ALU.mult, ALU.add,
                            tmax(tKS, tve, negbg.w, RTP.wr() if i == 0 else []), CK, ps=True)
                VT.read(tR)
                negbg.read(tR)
                bS.read(tR)
            RTP.wrote(tR)
            tsc = None
            for i in range(2):
                h = h0 + i
                tsc = p.op("v", lambda e, i=i, h=h, bS=bS: e.tensor_scalar(
                    oscp.ap[:, i * CK:(i + 1) * CK], bS.ap[:, 2 * CK + i * CK:2 * CK + (i + 1) * CK],
                    gam_tm.ap[:, h:h + 1], None, ALU.mult),
                    deps=tmax(tKS, gam_tm.w, oscp.wr() if i == 0 else []), inc=1)
                gam_tm.read(tsc)
                bS.read(tsc)
            p.load["v"] += 500.0
            oscp.wrote(tsc)
            bU = nxt()
            tU = None
            for i in range(2):
                tU = p.op("t", lambda e, i=i, bU=bU: e.matmul(
                    bU.ap[:, i * CK:(i + 1) * CK], Tgp.ap[:, i * CK:(i + 1) * CK],
                    RTP.ap[:, i * CK:(i + 1) * CK], start=True, stop=True),
                    deps=tmax(tR, Tgp.w, bU.wr()) if i == 0 else (), inc=1)
            bU.wrote(tU)
            Tgp.read(tU)
            RTP.read(tU)
            tUe = ew_copy(Utmp.ap[:], bU.ap[:, 0:2 * CK], tmax(tU, Utmp.wr()), 2 * CK, ps=True)
            bU.read(tUe)
            Utmp.wrote(tUe)
            bM = nxt()
            tO = None
            for i in range(2):
                tO = p.op("t", lambda e, i=i, bM=bM: e.matmul(
                    bM.ap[:, i * CK:(i + 1) * CK], MMP.ap[:, i * CK:(i + 1) * CK],
                    Utmp.ap[:, i * CK:(i + 1) * CK], start=True, stop=True),
                    deps=tmax(tUe, tMm, bM.wr()) if i == 0 else (), inc=1)
            bM.wrote(tO)
            MMP.read(tO)
            Utmp.read(tO)
            tOa = ew_tt(oscp.ap[:], oscp.ap[:], bM.ap[:, 0:2 * CK], ALU.add,
                        tmax(tO, tsc), 2 * CK, bf=False, ps=True, eng="v")
            bM.read(tOa)
            oscp.wrote(tOa)
            tq2 = p.op("s", lambda e: e.activation(osqp.ap[:], oscp.ap[:], AF.Square),
                       deps=tmax(tOa, osqp.wr()), inc=1)
            osqp.wrote(tq2)
            tms = None
            for i in range(2):
                tms = p.op("v", lambda e, i=i: e.reduce_sum(
                    oms.ap[:, i:i + 1], osqp.ap[:, i * CK:(i + 1) * CK], AX.X),
                    deps=tmax(tq2, oms.wr()) if i == 0 else (), inc=1)
            p.load["v"] += 400.0
            osqp.read(tms)
            tln = p.op("s", lambda e: e.activation(oms.ap[:, 2:4], oms.ap[:, 0:2], AF.Ln,
                                                   scale=1.0 / DVH, bias=EPS),
                       deps=[tms], inc=1)
            trs = p.op("s", lambda e: e.activation(oms.ap[:, 2:4], oms.ap[:, 2:4], AF.Exp,
                                                   scale=-0.5), deps=[tln], inc=1)
            oms.wrote(trs)
            tno = None
            for i in range(2):
                tno = ew_ts(ontm.ap[:, i * CK:(i + 1) * CK], oscp.ap[:, i * CK:(i + 1) * CK],
                            oms.ap[:, 2 + i:3 + i], ALU.mult,
                            tmax(trs, tOa, ontm.wr() if i == 0 else []), CK, bf=False)
                oms.read(tno)
                oscp.read(tno)
            ontm.wrote(tno)
            bSt = nxt()
            tS = None
            for i in range(2):
                tS = p.op("t", lambda e, i=i, bSt=bSt: e.matmul(
                    bSt.ap[64 * i:64 * i + 64, i * CK:(i + 1) * CK], KA.ap[:, i * 64:(i + 1) * 64],
                    Utmp.ap[:, i * CK:(i + 1) * CK], start=True, stop=True),
                    deps=tmax(tkh, tUe, bSt.wr()) if i == 0 else (), inc=1)
            bSt.wrote(tS)
            KA.read(tS)
            Utmp.read(tS)
            for i in range(2):
                h = h0 + i
                hb = 64 * (h % 2)
                tgc = ew_tt(scl.ap[:, 4 * H + 2 + i:4 * H + 3 + i],
                            scl.ap[:, 4 * H + i:4 * H + i + 1],
                            gam_tm.ap[:, h:h + 1], ALU.mult,
                            tmax(tek, gam_tm.w, scl.wr() if i == 0 else []), 1, bf=False,
                            eng="v")
                gam_tm.read(tgc)
                scl.wrote(tgc)
                tSu = p.op("v", lambda e, i=i, h=h, hb=hb, bSt=bSt: e.scalar_tensor_tensor(
                    S_f[h].ap, S_f[h].ap, scl.ap[hb:hb + 64, 4 * H + 2 + i:4 * H + 3 + i],
                    bSt.ap[hb:hb + 64, i * CK:(i + 1) * CK], ALU.mult, ALU.add),
                    deps=tmax(tS, tgc, S_f[h].wr()), inc=1)
                p.load["v"] += 300.0
                scl.read(tSu)
                bSt.read(tSu)
                S_f[h].wrote(tSu)
            tSb = ew_copy(S_bf[:, hp * DVH:(hp + 1) * DVH], S_sb[:, hp * DVH:(hp + 1) * DVH],
                          tmax(S_f[h0].w, S_f[h0 + 1].w,
                               S_b[h0].wr(), S_b[h0 + 1].wr()), DVH)
            S_f[h0].read(tSb)
            S_f[h0 + 1].read(tSb)
            S_b[h0].wrote(tSb)
            S_b[h0 + 1].wrote(tSb)
            ttn = None
            for i in range(2):
                ttn = p.op("t", lambda e, i=i: e.transpose(
                    bank5.ap[:, i * 128:(i + 1) * 128],
                    ontm.ap[:, i * DVH:(i + 1) * DVH], cb("ident")),
                    deps=tmax(tno, tkb, bank5.wr() if i == 0 else []), inc=1)
            bank5.wrote(ttn)
            ontm.read(ttn)
            toe = None
            for i in range(2):
                h = h0 + i
                toe = ew_copy(on_fm[h].ap[:, tof:tof + CK],
                              bank5.ap[:, i * 128:(i + 1) * 128],
                              tmax(ttn, on_fm[h].wr()), CK, ps=True)
                on_fm[h].wrote(toe)
            bank5.read(toe)

        def load_phase_b_weights(dep):
            toks = []
            for k in range(8):
                toks.append(stage_load(dwr[k * 128:(k + 1) * 128, :], wr_sb[k], k % 2, (dep,)))
                toks.append(stage_load(dwkf[k * 128:(k + 1) * 128, :], wkf_sb[k], k % 2))
                toks.append(stage_load(dwvf[k * 128:(k + 1) * 128, :], wvf_sb[k], k % 2))
            return tmax(*toks)

        def ffn_tile(ti, wB_tok):
            xb = xbuf[ti % 2]
            if ti + 1 < NTI:
                load_x_tile(ti + 1, xbuf[(ti + 1) % 2], dx1)

            trb, tmr, trbc = ln_stats(xb, LN_EPS)
            tmrbc = p.op("t", lambda e: e.matmul(bank6.ap[:, NT:2 * NT], cb("ones_row")[:1, :],
                                                 mr_bf.ap[:], start=True, stop=True),
                         deps=tmax(tmr, mr_bf.w), inc=1)
            bank6.wrote(tmrbc)
            mr_bf.read(tmrbc)
            ztoks = []
            for c in range(NCH):
                zt = p.op("v", lambda e, c=c: e.tensor_copy(
                    z_sb[c].ap[:, 0:1], z_sb[c].ap[:, NT:NT + 1]),
                    deps=tmax(z_sb[c].w, z_sb[c].rs), inc=1)
                t1 = p.op("v", lambda e, c=c: e.tensor_tensor(
                    ao_sb[c].ap[:], xb[c].ap[:], bank6.ap[:, 0:NT], ALU.mult),
                    deps=tmax(xb[c].w, tmrbc, ao_sb[c].wr()), inc=1)
                xb[c].read(t1)
                t2 = p.op("v", lambda e, c=c: e.tensor_tensor(
                    z_sb[c].ap[:, 1:NT + 1], ao_sb[c].ap[:], bank6.ap[:, NT:2 * NT],
                    ALU.subtract),
                    deps=tmax(t1, zt), inc=1)
                ao_sb[c].wrote(t1)
                ao_sb[c].read(t2)
                bank6.read(t2)
                z_sb[c].wrote(t2)
                ztoks.append(t2)
            mixtoks = []
            for c in range(NCH):
                td_ = p.op("v", lambda e, c=c: e.tensor_tensor(
                    fsc.ap[:], z_sb[c].ap[:, 1:NT + 1], z_sb[c].ap[:, 0:NT], ALU.subtract),
                    deps=tmax(ztoks[c], fsc.wr()), inc=1)
                fsc.wrote(td_)
                z_sb[c].read(td_)
                t1 = p.op("v", lambda e, c=c: e.scalar_tensor_tensor(
                    xr_sb[c].ap[:], fsc.ap[:], mu_sb[:, c:c + 1], z_sb[c].ap[:, 0:NT],
                    ALU.mult, ALU.add),
                    deps=tmax(td_, xr_sb[c].wr()), inc=1)
                xr_sb[c].wrote(t1)
                t2 = p.op("v", lambda e, c=c: e.scalar_tensor_tensor(
                    xk_sb[c].ap[:], fsc.ap[:], mu_sb[:, 8 + c:9 + c], z_sb[c].ap[:, 0:NT],
                    ALU.mult, ALU.add),
                    deps=tmax(td_, xk_sb[c].wr()), inc=1)
                xk_sb[c].wrote(t2)
                fsc.read(t2)
                z_sb[c].read(t2)
                mixtoks.append((t1, t2))
            for oc in range(NCH):
                pb = bank[oc % 2]
                tmm = None
                for k in range(NCH):
                    tmm = p.op("t", lambda e, k=k, oc=oc, pb=pb: e.matmul(
                        pb.ap[:, 0:NT], wr_sb[k][:, oc * 128:(oc + 1) * 128], xr_sb[k].ap[:],
                        start=(k == 0), stop=(k == NCH - 1)),
                        deps=tmax(mixtoks[k][0], wB_tok, pb.wr()) if k == 0 else [mixtoks[k][0]],
                        inc=1)
                pb.wrote(tmm)
                for k in range(NCH):
                    xr_sb[k].read(tmm)
                tr_ = p.op("s", lambda e, oc=oc, pb=pb: e.activation(
                    r_sb[oc].ap[:], pb.ap[:, 0:NT], AF.Sigmoid),
                    deps=tmax(tmm, r_sb[oc].wr()), inc=1)
                pb.read(tr_)
                r_sb[oc].wrote(tr_)
            for oc in range(NCH):
                pb = bank[2 + oc % 2]
                tmm = None
                for k in range(NCH):
                    tmm = p.op("t", lambda e, k=k, oc=oc, pb=pb: e.matmul(
                        pb.ap[:, 0:NT], wkf_sb[k][:, oc * 128:(oc + 1) * 128], xk_sb[k].ap[:],
                        start=(k == 0), stop=(k == NCH - 1)),
                        deps=tmax(mixtoks[k][1], wB_tok, pb.wr()) if k == 0 else [mixtoks[k][1]],
                        inc=1)
                pb.wrote(tmm)
                for k in range(NCH):
                    xk_sb[k].read(tmm)
                trl = p.op("s", lambda e, pb=pb: e.activation(fsc.ap[:], pb.ap[:, 0:NT], AF.Relu),
                           deps=tmax(tmm, fsc.wr()), inc=1)
                pb.read(trl)
                fsc.wrote(trl)
                tsq = p.op("v", lambda e, oc=oc: e.tensor_tensor(
                    ka_sb[oc].ap[:], fsc.ap[:], fsc.ap[:], ALU.mult),
                    deps=tmax(trl, ka_sb[oc].wr()), inc=1)
                fsc.read(tsq)
                ka_sb[oc].wrote(tsq)
            for oc in range(NCH):
                pb = bank[4] if oc % 2 == 0 else bank[2 + (oc + 1) % 2]
                pb = bank[4 - (oc % 2)]
                tmm = None
                for k in range(NCH):
                    tmm = p.op("t", lambda e, k=k, oc=oc, pb=pb: e.matmul(
                        pb.ap[:, 0:NT], wvf_sb[k][:, oc * 128:(oc + 1) * 128], ka_sb[k].ap[:],
                        start=(k == 0), stop=(k == NCH - 1)),
                        deps=tmax(ka_sb[k].w, pb.wr()) if k == 0 else [ka_sb[k].w], inc=1)
                pb.wrote(tmm)
                for k in range(NCH):
                    ka_sb[k].read(tmm)
                tev = p.op("v", lambda e, oc=oc, pb=pb: e.tensor_tensor(
                    ao_sb[oc].ap[:], pb.ap[:, 0:NT], r_sb[oc].ap[:], ALU.mult),
                    deps=tmax(tmm, r_sb[oc].w, ao_sb[oc].wr()), inc=1)
                pb.read(tev)
                r_sb[oc].read(tev)
                ao_sb[oc].wrote(tev)
            trb4, tmr4, trbc4 = ln_stats(ao_sb, LN_EPS)
            tmrbc4 = p.op("t", lambda e: e.matmul(bank6.ap[:, NT:2 * NT], cb("ones_row")[:1, :],
                                                  mr_bf.ap[:], start=True, stop=True),
                          deps=tmax(tmr4, mr_bf.w), inc=1)
            bank6.wrote(tmrbc4)
            mr_bf.read(tmrbc4)
            st_toks = []
            for c in range(NCH):
                t1 = p.op("v", lambda e, c=c: e.tensor_tensor(
                    ao_sb[c].ap[:], ao_sb[c].ap[:], bank6.ap[:, 0:NT], ALU.mult),
                    deps=tmax(ao_sb[c].w, tmrbc4), inc=1)
                t2 = p.op("v", lambda e, c=c: e.tensor_tensor(
                    ao_sb[c].ap[:], ao_sb[c].ap[:], bank6.ap[:, NT:2 * NT], ALU.subtract),
                    deps=[t1], inc=1)
                bank6.read(t2)
                t3 = p.op("v", lambda e, c=c: e.tensor_tensor(
                    xb[c].ap[:], xb[c].ap[:], ao_sb[c].ap[:], ALU.add),
                    deps=tmax(t2, xb[c].w, xb[c].rs), inc=1)
                ao_sb[c].wrote(t3)
                xb[c].wrote(t3)
                ts_ = p.op("g", lambda e, c=c, ti=ti: e.dma_start(
                    out=dout[c * 128:(c + 1) * 128, ti * NT:(ti + 1) * NT], in_=xb[c].ap[:]),
                    deps=[t3], sem="dga" if ti % 2 == 0 else "dgb", inc=16)
                xb[c].read(ts_)
                st_toks.append(ts_)
            for c in range(NCH):
                xb[c].rs = [st_toks[-1]]
            return st_toks

        if "A" in phases:
            tms_ = p.op("v", lambda e: e.memset(ptail.ap[:], 0.0), inc=1)
            ptail.wrote(tms_)
            tms2 = p.op("v", lambda e: e.memset(S_sb[:], 0.0), inc=1)
            tms3 = p.op("v", lambda e: e.memset(S_bf[:], 0.0), inc=1)
            for h in range(H):
                S_f[h].wrote(tms2)
                S_b[h].wrote(tms3)
            load_x_tile(0, xbuf[0], dx)
            for ti in range(NTI):
                attn_tile(ti)
            a_done = [("dga", p.n["dga"]), ("dgb", p.n["dgb"]), ("dg", p.n["dg"])]
        else:
            a_done = [("dga", p.n["dga"]), ("dgb", p.n["dgb"]), ("dg", p.n["dg"])]

        if "B" in phases:
            wB_tok = load_phase_b_weights(("t", p.n["t"]))
            tmu1 = wB_tok[0] if isinstance(wB_tok, list) else wB_tok
            for c in range(NCH):
                tmu1 = p.op("y", lambda e, c=c: e.dma_start(out=mu_sb[:, c:c + 1],
                                                            in_=dmur[c * 128:(c + 1) * 128, :]),
                            deps=[tmu1] if not isinstance(tmu1, list) else tmu1,
                            sem="dl", inc=16)
                tmu1 = p.op("y", lambda e, c=c: e.dma_start(out=mu_sb[:, 8 + c:9 + c],
                                                            in_=dmuk[c * 128:(c + 1) * 128, :]),
                            deps=[tmu1], sem="dl", inc=16)
            tzt = None
            for c in range(NCH):
                tzt = p.op("v", lambda e, c=c: e.memset(z_sb[c].ap[:, 0:NT + 1], 0.0),
                           deps=tmax(*a_done, ("t", p.n["t"])) if c == 0 else (), inc=1)
                z_sb[c].wrote(tzt)
            load_x_tile(0, xbuf[0], dx1, extra=tuple(a_done) + (tmu1, tzt))
            for ti in range(NTI):
                ffn_tile(ti, wB_tok)

        p.op("g", lambda e: e.nop(), deps=[("dga", p.n["dga"]), ("dgb", p.n["dgb"]),
                                           ("dg", p.n["dg"])])


        with nc.Block() as block:
            @block.tensor
            def _(eng):
                for f in p.q["t"]:
                    f(eng)

            @block.vector
            def _(eng):
                for f in p.q["v"]:
                    f(eng)

            @block.scalar
            def _(eng):
                for f in p.q["s"]:
                    f(eng)

            @block.gpsimd
            def _(eng):
                for f in p.q["g"]:
                    f(eng)

            @block.sync
            def _(eng):
                for f in p.q["y"]:
                    f(eng)

    return nc



_NC_CACHE = {}


def _get_nc():
    if "nc" not in _NC_CACHE:
        _NC_CACHE["nc"] = build_program()
    return _NC_CACHE["nc"]


def kernel(L0_attn_x, L0_attn_Wq, L0_attn_Wk, L0_attn_Wv, L0_attn_Wb, L0_attn_Wa,
           L0_attn_A_log, L0_attn_dt_bias, L0_attn_Wo, L0_attn_conv_q, L0_attn_conv_k,
           L0_attn_conv_v, L0_ffn_W_r, L0_ffn_W_k, L0_ffn_W_v, L0_ffn_mu_r, L0_ffn_mu_k):
    x = np.asarray(L0_attn_x, np.float32)
    B = x.shape[0]
    f = np.float32
    ct = np.ascontiguousarray
    wq = ct(np.asarray(L0_attn_Wq, f).T)
    wk = ct(np.asarray(L0_attn_Wk, f).T)
    wv = ct(np.asarray(L0_attn_Wv, f).T)
    wo = ct(np.asarray(L0_attn_Wo, f).T)
    wab_t = np.concatenate([np.asarray(L0_attn_Wa, f), np.asarray(L0_attn_Wb, f)], axis=0).T
    wab = ct(wab_t.reshape(8, 128, 32).transpose(1, 0, 2).reshape(128, 8 * 32))

    def _chunkify(w):
        w = np.asarray(w, f)
        k = w.shape[0] // 128
        return ct(w.reshape(k, 128, KW).transpose(1, 0, 2).reshape(128, k * KW))
    wr = ct(np.asarray(L0_ffn_W_r, f).T)
    wkf = ct(np.asarray(L0_ffn_W_k, f).T)
    wvf = ct(np.asarray(L0_ffn_W_v, f).T)
    shared = {
        "wq": wq, "wk": wk, "wv": wv, "wo": wo, "wab": wab,
        "convq": _chunkify(L0_attn_conv_q),
        "convk": _chunkify(L0_attn_conv_k),
        "convv": _chunkify(L0_attn_conv_v),
        "alog": ct(np.asarray(L0_attn_A_log, f).reshape(1, H)),
        "dtb": ct(np.asarray(L0_attn_dt_bias, f).reshape(1, H)),
        "wr": wr, "wkf": wkf, "wvf": wvf,
        "mur": ct(np.asarray(L0_ffn_mu_r, f).reshape(D, 1)),
        "muk": ct(np.asarray(L0_ffn_mu_k, f).reshape(D, 1)),
        "const": _CONSTS,
    }
    in_maps = []
    for b in range(B):
        m = dict(shared)
        m["x"] = ct(x[b].T)
        in_maps.append(m)
    nc = _get_nc()
    res = run_bass_kernel_spmd(nc, in_maps, core_ids=list(range(B)))
    out = np.stack([np.asarray(r["out"], np.float32).T for r in res.results], axis=0)
    return out


# version 04
# speedup vs baseline: 1.2559x; 1.2139x over previous
import contextlib
import numpy as np
import concourse.bass as bass
import concourse.mybir as mybir
from concourse.bass_utils import run_bass_kernel_spmd

KW_HOST = 4
F32 = mybir.dt.float32
F32R = mybir.dt.float32r
BF16 = mybir.dt.bfloat16
AF = mybir.ActivationFunctionType
ALU = mybir.AluOpType
AX = mybir.AxisListType

D = 1024
T = 8192
H = 16
DKH = 64
DVH = 128
KW = 4
NT = 256
CK = 128
NCH = D // 128
EPS = 1e-6
LN_EPS = 1e-5


def _consts_host():
    cols = []
    cmap = {}

    def add(name, tile, which):
        tile = np.asarray(tile, np.float32)
        t = np.zeros((128, tile.shape[1]), np.float32)
        t[: tile.shape[0]] = tile
        cmap[name] = (sum(c.shape[1] for c in cols), tile.shape[1], which)
        cols.append(t)

    i128 = np.arange(128)
    dw = lambda t: np.tile(t, (1, 2))
    add("ident", np.eye(128, dtype=np.float32), "b")
    add("ident_dw", dw(np.eye(128, dtype=np.float32)), "b")
    incl_u = (i128[:, None] <= i128[None, :]).astype(np.float32)
    strict_l = (i128[:, None] > i128[None, :]).astype(np.float32)
    add("incl_u_dw", dw(incl_u), "b")
    add("strict_l_dw", dw(strict_l), "b")
    blk8 = i128 // 8
    m8 = ((blk8[:, None] == blk8[None, :]) & (i128[:, None] < i128[None, :])).astype(np.float32)
    add("m8u_dw", dw(m8), "b")
    add("m8l_dw", dw(m8.T), "b")
    for m in (8, 16, 32, 64):
        b2 = i128 // (2 * m)
        half = (i128 // m) % 2
        r = ((b2[:, None] == b2[None, :]) & (half[:, None] == 0) & (half[None, :] == 1)).astype(np.float32)
        add(f"r{m}u_dw", dw(r), "b")
        add(f"r{m}l_dw", dw(r.T), "b")
    bd = np.zeros((128, 2), np.float32)
    bd[:64, 0] = 1.0
    bd[64:, 1] = 1.0
    add("bdones", bd, "b")
    add("ones_col", np.ones((128, 1), np.float32), "b")
    add("ones_row", np.ones((1, 128), np.float32), "b")
    add("incl_u_r", incl_u, "r")
    add("ones_row_r", np.ones((1, 128), np.float32), "r")
    oh = np.zeros((16, 16 * 128), np.float32)
    for h in range(16):
        oh[h, h * 128:(h + 1) * 128] = 1.0
    add("onehot", oh, "r")
    offs = {"b": 0, "r": 0}
    cmap2 = {}
    for name, (o, w, which) in cmap.items():
        cmap2[name] = (o, offs[which], w, which)
        offs[which] += w
    return (np.ascontiguousarray(np.concatenate(cols, axis=1)), cmap2,
            offs["b"], offs["r"])


_CONSTS, _CMAP, _CWB, _CWR = _consts_host()



class P:
    def __init__(self, nc):
        self.nc = nc
        self.q = {e: [] for e in ("t", "v", "s", "g", "y")}
        self.n = {k: 0 for k in ("t", "v", "s", "g", "y", "dl", "dg",
                                 "dxa", "dxb", "dca", "dcb", "dga", "dgb")}
        self.hw = {e: {} for e in self.q}
        self.sems = {}
        self.load = {"v": 0.0, "s": 0.0, "g": 0.0}

    def pick(self, costs):
        e = min(costs, key=lambda x: self.load[x] + costs[x])
        self.load[e] += costs[e]
        return e

    def _wait(self, e, tok):
        if tok is None:
            return
        s, c = tok
        if c <= 0 or c <= self.hw[e].get(s, 0):
            return
        self.hw[e][s] = c
        self.q[e].append(lambda eng, s=s, c=c: eng.wait_ge(self.sems[s], c))

    DMA_SEMS = ("dl", "dg", "dxa", "dxb", "dca", "dcb", "dga", "dgb")

    def op(self, e, emitfn, deps=(), sem=None, inc=0, pbase=0):
        sname = sem or e
        if e == "t":
            last = getattr(self, "pe_base", 0)
            if pbase != last:
                cnt = self.n["t"]
                self.q["t"].append(lambda eng, c=cnt: eng.wait_ge(self.sems["t"], c))
            self.pe_base = pbase
        if sname in self.DMA_SEMS:
            prev = getattr(self, "last_dma", {}).get(sname)
            if prev is not None:
                deps = list(deps) + [prev]
        for t in deps:
            self._wait(e, t)
        if inc:
            self.n[sname] += inc
            tok = (sname, self.n[sname])

            def run(eng, f=emitfn, sn=sname, k=inc):
                f(eng).then_inc(self.sems[sn], k)
            self.q[e].append(run)
            if sname in self.DMA_SEMS:
                if not hasattr(self, "last_dma"):
                    self.last_dma = {}
                self.last_dma[sname] = tok
            return tok
        self.q[e].append(lambda eng, f=emitfn: f(eng))
        return None


class Guard:
    def __init__(self, ap=None):
        self.ap = ap
        self.w = None
        self.rs = []

    def rd(self):
        return [self.w] if self.w else []

    def wr(self):
        return ([self.w] if self.w else []) + self.rs

    def wrote(self, tok):
        if tok is not None:
            self.w = tok
            self.rs = []

    def read(self, tok):
        if tok is not None:
            self.rs.append(tok)


def tmax(*toks):
    best = {}
    stack = list(toks)
    while stack:
        t = stack.pop()
        if t is None:
            continue
        if isinstance(t, list):
            stack.extend(t)
        elif t[1] > best.get(t[0], -1):
            best[t[0]] = t[1]
    return [(s, c) for s, c in best.items()]



def build_program(n_tiles=T // NT, debug=(), phases=("A", "B"), x1_out=False,
                  stop_after=None):
    nc = bass.Bass("TRN2", target_bir_lowering=False)
    NTI = n_tiles

    dx = nc.dram_tensor("x", [D, T], F32, kind="ExternalInput").ap()
    dwq = nc.dram_tensor("wq", [D, D], F32, kind="ExternalInput").ap()
    dwk = nc.dram_tensor("wk", [D, D], F32, kind="ExternalInput").ap()
    dwv = nc.dram_tensor("wv", [D, 2048], F32, kind="ExternalInput").ap()
    dwo = nc.dram_tensor("wo", [2048, D], F32, kind="ExternalInput").ap()
    dwab = nc.dram_tensor("wab", [128, NCH * 32], F32, kind="ExternalInput").ap()
    dcq = nc.dram_tensor("convq", [128, 8 * KW], F32, kind="ExternalInput").ap()
    dck = nc.dram_tensor("convk", [128, 8 * KW], F32, kind="ExternalInput").ap()
    dcv = nc.dram_tensor("convv", [128, 16 * KW], F32, kind="ExternalInput").ap()
    dalog = nc.dram_tensor("alog", [1, H], F32, kind="ExternalInput").ap()
    ddtb = nc.dram_tensor("dtb", [1, H], F32, kind="ExternalInput").ap()
    dwr = nc.dram_tensor("wr", [D, D], F32, kind="ExternalInput").ap()
    dwkf = nc.dram_tensor("wkf", [D, D], F32, kind="ExternalInput").ap()
    dwvf = nc.dram_tensor("wvf", [D, D], F32, kind="ExternalInput").ap()
    dmur = nc.dram_tensor("mur", [D, 1], F32, kind="ExternalInput").ap()
    dmuk = nc.dram_tensor("muk", [D, 1], F32, kind="ExternalInput").ap()
    dconst = nc.dram_tensor("const", list(_CONSTS.shape), F32, kind="ExternalInput").ap()
    dout = nc.dram_tensor("out", [D, T], F32, kind="ExternalOutput").ap()
    dx1 = nc.dram_tensor("x1s", [D, T], F32,
                         kind="ExternalOutput" if x1_out else "Internal").ap()
    dcdiag = nc.dram_tensor("cdiag", [128, 32 * KW * 128], BF16, kind="Internal").ap()

    ddbg = {}
    for name, shape in debug:
        ddbg[name] = nc.dram_tensor("dbg_" + name, list(shape), F32, kind="ExternalOutput").ap()

    p = P(nc)
    es = contextlib.ExitStack()

    def sb(name, shape, dt):
        return es.enter_context(nc.sbuf_tensor(name, list(shape), dt))

    with es:
        for s in ("t", "v", "s", "g", "y", "dl", "dg",
                  "dxa", "dxb", "dca", "dcb", "dga", "dgb"):
            p.sems[s] = es.enter_context(nc.semaphore("sem_" + s))

        cst_b = sb("cst_b", (128, _CWB), BF16)
        cst_r = sb("cst_r", (128, _CWR), F32R)
        epsc = sb("epsc", (128, 2), F32)
        p.op("v", lambda e: e.memset(epsc[:, 0:1], LN_EPS), inc=1)
        p.op("v", lambda e: e.memset(epsc[:, 1:2], EPS), inc=1)
        nc.const_aps.register(F32, LN_EPS, epsc[:, 0:1])
        nc.const_aps.register(F32, EPS, epsc[:, 1:2])

        def cb(nm):
            _, o, w, which = _CMAP[nm]
            assert which == "b"
            return cst_b[:, o:o + w]

        def cr(nm):
            _, o, w, which = _CMAP[nm]
            assert which == "r"
            return cst_r[:, o:o + w]

        bank = [Guard(es.enter_context(nc.psum_tensor(f"ps{i}", [128, 512], F32)))
                for i in range(5)]
        bank5 = Guard(es.enter_context(nc.psum_tensor("ps5", [128, 1024], BF16)))
        bank6 = Guard(es.enter_context(nc.psum_tensor("ps6", [128, 512], F32)))
        bank7 = Guard(es.enter_context(nc.psum_tensor("ps7", [128, 512], F32)))

        wpool = sb("wpool", (128, 48 * 1024), BF16)
        wq_sb = [wpool[:, (k) * 1024:(k + 1) * 1024] for k in range(8)]
        wk_sb = [wpool[:, (8 + k) * 1024:(9 + k) * 1024] for k in range(8)]
        wv_sb = [wpool[:, (16 + 2 * k) * 1024:(16 + 2 * k + 2) * 1024] for k in range(8)]
        wo_sb = [wpool[:, (32 + k) * 1024:(33 + k) * 1024] for k in range(16)]
        wr_sb = [wpool[:, (k) * 1024:(k + 1) * 1024] for k in range(8)]
        wkf_sb = [wpool[:, (8 + k) * 1024:(9 + k) * 1024] for k in range(8)]
        wvf_sb = [wpool[:, (16 + k) * 1024:(17 + k) * 1024] for k in range(8)]
        wg = Guard()

        wab_sb = sb("wab_sb", (128, NCH * 32), BF16)
        convw = sb("convw", (128, 32 * KW), F32)
        negA = sb("negA", (128, H), F32)
        dtbb = sb("dtbb", (128, H), F32)
        alog_r = sb("alog_r", (1, 2 * H), F32R)


        xbuf = [[Guard(sb(f"x0_{c}", (128, NT), F32)) for c in range(NCH)]]
        apool = sb("apool", (128, 12 * 1024), BF16)
        qkv = [Guard(apool[:, pc * NT:(pc + 1) * NT]) for pc in range(32)]
        on_fm = [Guard(apool[:, 8 * 1024 + hv * NT:8 * 1024 + (hv + 1) * NT]) for hv in range(16)]
        z_sb = [Guard(apool[:, c * (NT + 1):(c + 1) * (NT + 1)]) for c in range(8)]
        xr_sb = [Guard(apool[:, 2080 + c * NT:2080 + (c + 1) * NT]) for c in range(8)]
        xk_sb = [Guard(apool[:, 4128 + c * NT:4128 + (c + 1) * NT]) for c in range(8)]
        r_sb = [Guard(apool[:, 6176 + c * NT:6176 + (c + 1) * NT]) for c in range(8)]
        ka_sb = [Guard(apool[:, 8224 + c * NT:8224 + (c + 1) * NT]) for c in range(8)]

        ao_sb = [Guard(sb(f"ao{c}", (128, NT), BF16)) for c in range(NCH)]
        ybuf = [Guard(sb(f"y_{c}", (128, NT), BF16)) for c in range(NCH)]
        xwork = Guard(sb("xwork", (128, 2 * NT), BF16))
        stat = Guard(sb("stat", (1, 2 * NT), F32))
        mr_bf = Guard(sb("mr_bf", (1, NT), BF16))
        rstd_bf = Guard(sb("rstd_bf", (1, NT), BF16))
        pbuf = [Guard(sb("pb0", (128, NT + KW - 1), BF16))] * 2
        ptail = Guard(sb("ptail", (128, 32 * (KW - 1)), BF16))
        cdg = [Guard(sb(f"cdg{i}", (128, KW * 128), BF16)) for i in range(2)]

        ab_sb = Guard(sb("ab_sb", (128, 32), F32))
        loga = Guard(sb("loga", (128, H), F32R))
        beta_g = Guard(sb("beta", (128, H), F32))
        ell_tm = Guard(sb("ell_tm", (128, H), F32))
        ell_fm = Guard(sb("ell_fm", (16, CK), F32R))
        gam_tm = Guard(sb("gam_tm", (128, H), F32))
        rnorm = Guard(sb("rnorm", (128, 32), F32))
        scl = Guard(sb("scl", (128, 6 * H), F32))
        negbg = Guard(sb("negbg", (128, H), F32))

        S_f = [Guard(None) for _ in range(H)]
        S_b = [Guard(None) for _ in range(H)]
        S_sb = sb("S", (128, 8 * DVH), F32)
        S_bf = sb("S_bf", (128, 8 * DVH), BF16)
        for h in range(H):
            hb, hp_ = 64 * (h % 2), h // 2
            S_f[h].ap = S_sb[hb:hb + 64, hp_ * DVH:(hp_ + 1) * DVH]
            S_b[h].ap = S_bf[hb:hb + 64, hp_ * DVH:(hp_ + 1) * DVH]

        def gpair(name, cols, dt, npar=2):
            return [Guard(sb(f"{name}{i}", (128, cols), dt)) for i in range(npar)]

        EbatP = gpair("Ebat", 2 * CK, F32)
        expEP = gpair("expE", 2 * CK, BF16)
        expE2P = gpair("expE2", 2 * CK, BF16)
        dincP = gpair("dinc", 2 * CK, BF16)
        lfacP = gpair("lfac", 2 * CK, BF16)
        kqtmP = gpair("kqtm", 2 * CK, BF16)
        khnP = gpair("khn", 2 * CK, BF16)
        hnfP = gpair("hnf", 2 * CK, BF16)
        vtmP = gpair("vtm", 2 * DVH, BF16)
        khatP = gpair("khat", 2 * DKH, BF16)
        LtP = gpair("Lt", 2 * CK, BF16)
        LTtP = gpair("LTt", 2 * CK, BF16)
        mmtP = gpair("mmt", 2 * CK, BF16)
        RtmP = gpair("Rtm", 2 * CK, BF16)
        rscP = gpair("rsc", 2 * CK, F32)
        NgtP = gpair("Ngt", 4 * CK, BF16)
        h2pP = gpair("h2p", 4 * CK, BF16)
        h4pP = gpair("h4p", 4 * CK, BF16)
        t0apP = gpair("t0ap", 4 * CK, BF16)
        TgpP = gpair("Tgp", 2 * CK, BF16)
        TTgpP = gpair("TTgp", 2 * CK, BF16)
        PgpP = gpair("Pgp", 4 * CK, BF16)
        UtmpP = gpair("Utmp", 2 * CK, BF16)
        oscpP = gpair("oscp", 2 * CK, F32)
        osqpP = gpair("osqp", 2 * CK, BF16)
        omsP = gpair("oms", 4, F32)
        ontmP = gpair("ontm", 2 * DVH, BF16)
        nsq_t = sb("nsq", (128, 2 * CK), BF16)
        nsqG = [Guard(nsq_t[:, 0:CK]), Guard(nsq_t[:, CK:2 * CK])]

        dkt = []
        for i in range(2):
            dkt.append(dict(
                ab=Guard(sb(f"ab_sb{i}", (128, 32), F32)),
                loga=Guard(sb(f"loga{i}", (128, H), F32R)),
                beta=Guard(sb(f"beta{i}", (128, H), F32)),
                ell_tm=Guard(sb(f"ell_tm{i}", (128, H), F32)),
                ell_fm=Guard(sb(f"ell_fm{i}", (16, CK), F32R)),
                gam=Guard(sb(f"gam{i}", (128, H), F32)),
                rnorm=Guard(sb(f"rnorm{i}", (128, 32), F32)),
                scl=Guard(sb(f"scl{i}", (128, 6 * H), F32)),
                negbg=Guard(sb(f"negbg{i}", (128, H), F32)),
            ))

        mu_sb = sb("mu_sb", (128, 16), F32)
        zt_g = Guard(None)
        fsc = Guard(sb("fsc", (128, NT), F32))

        cw_all = _CONSTS.shape[1]
        tkb = None
        td_prev = None
        for go in range(0, cw_all, NT):
            w = min(NT, cw_all - go)
            g = xbuf[0][(go // NT) % 4]
            td = p.op("y", lambda e, go=go, w=w, g=g:
                      e.dma_start(out=g.ap[:, :w], in_=dconst[:, go:go + w]),
                      deps=tmax(g.wr(), td_prev), sem="dl", inc=16)
            td_prev = td
            g.wrote(td)
            for nm, (osrc, odst, wd, which) in _CMAP.items():
                a, b = max(osrc, go), min(osrc + wd, go + w)
                if a >= b:
                    continue
                dstt = cst_b if which == "b" else cst_r
                tkb = p.op("v", lambda e, a=a, b=b, go=go, osrc=osrc, odst=odst, dstt=dstt, g=g:
                           e.tensor_copy(dstt[:, odst + a - osrc:odst + b - osrc],
                                         g.ap[:, a - go:b - go]),
                           deps=[td], inc=1)
                g.read(tkb)
        tkr = tkb

        _stage_sems = ("dl", "dxa", "dxb", "dca")

        def stage_load(src_view, dst_view, gidx, extra_deps=()):
            ncols = src_view.shape[1]
            dtoks = []
            pieces = []
            for o in range(0, ncols, NT):
                w = min(NT, ncols - o)
                j = (o // NT) % 4
                g = xbuf[0][j]
                td = p.op("y", lambda e, g=g, o=o, w=w: e.dma_start(
                    out=g.ap[:, :w], in_=src_view[:, o:o + w]),
                    deps=tmax(g.wr(), *extra_deps), sem=_stage_sems[j], inc=16)
                g.wrote(td)
                dtoks.append(td)
                pieces.append((g, o, w))
            gend = tmax(*dtoks)
            toks = []
            for (g, o, w) in pieces:
                tc = p.op("v", lambda e, g=g, o=o, w=w: e.tensor_copy(
                    dst_view[:, o:o + w], g.ap[:, :w]),
                    deps=tmax(gend, wg.w), inc=1)
                g.read(tc)
                toks.append(tc)
            return tmax(*toks)

        def load_phase_a_weights():
            toks = []
            for k in range(8):
                toks.append(stage_load(dwq[k * 128:(k + 1) * 128, :], wq_sb[k], k % 2))
                toks.append(stage_load(dwk[k * 128:(k + 1) * 128, :], wk_sb[k], k % 2))
                for hf in range(2):
                    toks.append(stage_load(dwv[k * 128:(k + 1) * 128, hf * 1024:(hf + 1) * 1024],
                                           wv_sb[k][:, hf * 1024:(hf + 1) * 1024], k % 2))
            for k in range(16):
                toks.append(stage_load(dwo[k * 128:(k + 1) * 128, :], wo_sb[k], k % 2))
            return toks

        wtoks = load_phase_a_weights()
        wg.wrote(wtoks[-1])
        wA_tok = tmax(*wtoks)

        tab_w = stage_load(dwab, wab_sb[:], 0)

        tcw = ("v", p.n["v"])
        tcw = tab_w
        for (srcd, nch_, off) in ((dcq, 8, 0), (dck, 8, 8), (dcv, 16, 16)):
            tcw = p.op("y", lambda e, srcd=srcd, nch_=nch_, off=off:
                       e.dma_start(out=convw[:, off * KW:(off + nch_) * KW], in_=srcd),
                       deps=tmax(tcw), sem="dl", inc=16)

        t_al = p.op("g", lambda e: e.dma_start(out=alog_r[:, :H], in_=dalog),
                    deps=tmax(tcw), sem="dg", inc=16)
        t_db = p.op("g", lambda e: e.dma_start(out=alog_r[:, H:], in_=ddtb),
                    deps=[t_al], sem="dg", inc=16)
        t_bc = p.op("t", lambda e: e.matmul(bank6.ap[:, :2 * H], cr("ones_row_r")[:1, :], alog_r[:],
                                            start=True, stop=True),
                    deps=tmax(t_al, t_db, tkr, bank6.wr()), inc=1)
        bank6.wrote(t_bc)
        tA1 = p.op("s", lambda e: e.activation(negA[:], bank6.ap[:, :H], AF.Exp),
                   deps=bank6.rd(), inc=1)
        t_negA = p.op("s", lambda e: e.mul(negA[:], negA[:], -1.0), deps=[tA1], inc=1)
        t_dtb = p.op("v", lambda e: e.tensor_copy(dtbb[:], bank6.ap[:, H:2 * H]),
                     deps=bank6.rd(), inc=1)
        bank6.read(t_negA)
        bank6.read(t_dtb)

        cstg = Guard(apool[:, 0:2 * KW * 128])
        for c in range(32):
            half = (c % 2) * KW * 128
            tb = None
            for j in range(KW):
                tb = p.op("v", lambda e, c=c, j=j, half=half:
                          e.tensor_scalar(cstg.ap[:, half + j * 128:half + (j + 1) * 128],
                                          cb("ident"), convw[:, c * KW + j:c * KW + j + 1],
                                          None, ALU.mult),
                          deps=tmax(tcw, tkb, cstg.wr()) if j == 0 else (), inc=1)
            cstg.wrote(tb)
            ts_ = p.op("g", lambda e, c=c, half=half:
                       e.dma_start(out=dcdiag[:, c * KW * 128:(c + 1) * KW * 128],
                                   in_=cstg.ap[:, half:half + KW * 128]),
                       deps=tmax(cstg.rd(), t_db if c == 0 else []), sem="dg", inc=16)
            cstg.read(ts_)
        cdiag_tok = ("dg", p.n["dg"])
        for pc in range(8):
            qkv[pc].rs.append(cdiag_tok)

        def ln_stats(src, eps):
            t2 = None
            for c in range(NCH):
                tq = p.op("v", lambda e, c=c: e.tensor_tensor(
                    xwork.ap[:, NT:2 * NT], src[c].ap[:], src[c].ap[:], ALU.mult),
                    deps=tmax(src[c].w, xwork.wr()), inc=1)
                src[c].read(tq)
                tb_ = p.op("v", lambda e, c=c: e.tensor_copy(xwork.ap[:, 0:NT], src[c].ap[:]),
                           deps=[tq], inc=1)
                xwork.wrote(tb_)
                src[c].read(tb_)
                t2 = p.op("t", lambda e, c=c: e.matmul(bank7.ap[:1, 0:2 * NT], cb("ones_col"),
                                                  xwork.ap[:],
                                                  start=(c == 0), stop=(c == NCH - 1)),
                          deps=tmax(tb_, tkb, bank7.wr() if c == 0 else []), inc=1)
                xwork.read(t2)
            bank7.wrote(t2)
            tm = p.op("v", lambda e: e.tensor_scalar(stat.ap[:1, 0:NT], bank7.ap[:1, 0:NT],
                                                     1.0 / D, None, ALU.mult),
                      deps=tmax(t2, stat.wr()), inc=1)
            tmsq = p.op("v", lambda e: e.tensor_tensor(stat.ap[:1, NT:2 * NT], stat.ap[:1, 0:NT],
                                                       stat.ap[:1, 0:NT], ALU.mult),
                        deps=[tm], inc=1)
            tvar = p.op("v", lambda e: e.scalar_tensor_tensor(
                stat.ap[:1, NT:2 * NT], bank7.ap[:1, NT:2 * NT], 1.0 / D,
                stat.ap[:1, NT:2 * NT], ALU.mult, ALU.subtract), deps=[tmsq], inc=1)
            bank7.read(tvar)
            tl = p.op("s", lambda e: e.activation(stat.ap[:1, NT:2 * NT],
                                                  stat.ap[:1, NT:2 * NT], AF.Ln,
                                                  bias=float(eps)),
                      deps=[tvar], inc=1)
            tr = p.op("s", lambda e: e.activation(stat.ap[:1, NT:2 * NT],
                                                  stat.ap[:1, NT:2 * NT], AF.Exp, scale=-0.5),
                      deps=[tl], inc=1)
            stat.wrote(tr)
            trb = p.op("v", lambda e: e.tensor_copy(rstd_bf.ap[:], stat.ap[:1, NT:2 * NT]),
                       deps=tmax(tr, rstd_bf.wr()), inc=1)
            rstd_bf.wrote(trb)
            tmr2 = p.op("v", lambda e: e.tensor_tensor(stat.ap[:1, 0:NT], stat.ap[:1, 0:NT],
                                                       stat.ap[:1, NT:2 * NT], ALU.mult),
                        deps=[tr], inc=1)
            tmrb = p.op("v", lambda e: e.tensor_copy(mr_bf.ap[:], stat.ap[:1, 0:NT]),
                        deps=tmax(tmr2, mr_bf.wr()), inc=1)
            mr_bf.wrote(tmrb)
            trbc = p.op("t", lambda e: e.matmul(bank6.ap[:, 0:NT], cb("ones_row")[:1, :],
                                                rstd_bf.ap[:], start=True, stop=True),
                        deps=tmax(trb, bank6.wr()), inc=1)
            bank6.wrote(trbc)
            rstd_bf.read(trbc)
            return trb, tmrb, trbc

        def bcast2(rowg, width):
            pass

        def load_x_tile(ti, dst, src_dram, extra=()):
            sem = "dxa" if ti % 2 == 0 else "dxb"
            tk_ = None
            for c in range(NCH):
                g = dst[c]
                tk_ = p.op("y", lambda e, c=c, ti=ti, g=g, src_dram=src_dram:
                           e.dma_start(out=g.ap[:], in_=src_dram[c * 128:(c + 1) * 128,
                                                             ti * NT:(ti + 1) * NT]),
                           deps=tmax(g.wr(), *extra), sem=sem, inc=16)
            for c in range(NCH):
                dst[c].wrote(tk_)
            return [tk_]

        def wslab(pc):
            if pc < 8:
                return [wq_sb[k][:, pc * 128:(pc + 1) * 128] for k in range(8)]
            if pc < 16:
                o = pc - 8
                return [wk_sb[k][:, o * 128:(o + 1) * 128] for k in range(8)]
            o = pc - 16
            return [wv_sb[k][:, o * 128:(o + 1) * 128] for k in range(8)]

        def attn_tile(ti):
            xb = xbuf[0]
            load_x_tile(ti, xb, dx)

            trb, tmr, trbc = ln_stats(xb, LN_EPS)

            tmrbc0 = p.op("t", lambda e: e.matmul(bank6.ap[:, NT:2 * NT], cb("ones_row")[:1, :],
                                                  mr_bf.ap[:], start=True, stop=True),
                          deps=tmax(tmr, mr_bf.w), inc=1)
            bank6.wrote(tmrbc0)
            mr_bf.read(tmrbc0)
            ytoks = []
            for c in range(NCH):
                ty = p.op("v", lambda e, c=c: e.tensor_tensor(
                    ybuf[c].ap[:], xb[c].ap[:], bank6.ap[:, 0:NT], ALU.mult),
                    deps=tmax(xb[c].w, tmrbc0, ybuf[c].wr()), inc=1)
                xb[c].read(ty)
                ty = p.op("v", lambda e, c=c: e.tensor_tensor(
                    ybuf[c].ap[:], ybuf[c].ap[:], bank6.ap[:, NT:2 * NT], ALU.subtract),
                    deps=[ty], inc=1)
                p.load["v"] += 700.0
                ybuf[c].wrote(ty)
                bank6.read(ty)
                ytoks.append(ty)

            for pc in range(32):
                pb = bank[pc % 2]
                slabs = wslab(pc)
                tmm = None
                for k in range(NCH):
                    tmm = p.op("t", lambda e, k=k, slabs=slabs, pb=pb: e.matmul(
                        pb.ap[:, 0:NT], slabs[k], ybuf[k].ap[:],
                        start=(k == 0), stop=(k == NCH - 1)),
                        deps=tmax(ytoks[k], wA_tok, pb.wr()) if k == 0 else [ytoks[k]],
                        inc=1)
                pb.wrote(tmm)
                for k in range(NCH):
                    ybuf[k].read(tmm)
                cd = cdg[pc % 2]
                tcd = p.op("y", lambda e, pc=pc, cd=cd:
                           e.dma_start(out=cd.ap[:],
                                       in_=dcdiag[:, pc * KW * 128:(pc + 1) * KW * 128]),
                           deps=tmax(cd.wr(), cdiag_tok),
                           sem="dca" if pc % 2 == 0 else "dcb", inc=16)
                cd.wrote(tcd)
                pg = pbuf[pc % 2]
                tp0 = p.op("v", lambda e, pc=pc, pg=pg: e.tensor_copy(
                    pg.ap[:, 0:KW - 1], ptail.ap[:, pc * (KW - 1):(pc + 1) * (KW - 1)]),
                    deps=tmax(ptail.w, pg.wr()), inc=1)
                ptail.read(tp0)
                tp = p.op("s", lambda e, pg=pg, pb=pb: e.copy(pg.ap[:, KW - 1:], pb.ap[:, 0:NT]),
                          deps=tmax(tmm, tp0), inc=1)
                pb.read(tp)
                pg.wrote(tp)
                tt = p.op("v", lambda e, pc=pc, pg=pg: e.tensor_copy(
                    ptail.ap[:, pc * (KW - 1):(pc + 1) * (KW - 1)], pg.ap[:, NT:NT + KW - 1]),
                    deps=[tp], inc=1)
                ptail.wrote(tt)
                pg.read(tt)
                cbnk = bank[2 + (pc % 2)]
                tc_ = None
                for j in range(KW):
                    tc_ = p.op("t", lambda e, j=j, cd=cd, pg=pg, cbnk=cbnk: e.matmul(
                        cbnk.ap[:, 0:NT], cd.ap[:, j * 128:(j + 1) * 128], pg.ap[:, j:j + NT],
                        start=(j == 0), stop=(j == KW - 1)),
                        deps=tmax(tcd, tp, cbnk.wr()) if j == 0 else (), inc=1)
                cbnk.wrote(tc_)
                cd.read(tc_)
                pg.read(tc_)
                tsl = p.op("s", lambda e, pc=pc, cbnk=cbnk: e.activation(
                    qkv[pc].ap[:], cbnk.ap[:, 0:NT], AF.Silu),
                    deps=tmax(tc_, qkv[pc].wr()), inc=1)
                cbnk.read(tsl)
                qkv[pc].wrote(tsl)

            if stop_after == "proj":
                return []
            for cc in range(NT // CK):
                chunk_recurrence(ti, cc, ytoks)
                if isinstance(stop_after, str) and stop_after.startswith("cc0") and cc == 0:
                    break
            if stop_after is not None:
                return []

            for oc in range(NCH):
                pb = bank[oc % 2]
                tmm = None
                for hv in range(16):
                    tmm = p.op("t", lambda e, hv=hv, oc=oc, pb=pb: e.matmul(
                        pb.ap[:, 0:NT], wo_sb[hv][:, oc * 128:(oc + 1) * 128],
                        apool[:, 8 * 1024 + hv * NT:8 * 1024 + (hv + 1) * NT],
                        start=(hv == 0), stop=(hv == 15)),
                        deps=tmax(on_fm[hv].w, pb.wr()) if hv == 0 else [on_fm[hv].w],
                        inc=1)
                pb.wrote(tmm)
                for hv in range(16):
                    on_fm[hv].read(tmm)
                te = p.op("s", lambda e, oc=oc, pb=pb: e.copy(ao_sb[oc].ap[:], pb.ap[:, 0:NT]),
                          deps=tmax(tmm, ao_sb[oc].wr()), inc=1)
                pb.read(te)
                ao_sb[oc].wrote(te)

            trb2, tmr2, trbc2 = ln_stats(ao_sb, LN_EPS)
            tmrbc = p.op("t", lambda e: e.matmul(bank6.ap[:, NT:2 * NT], cb("ones_row")[:1, :],
                                                 mr_bf.ap[:], start=True, stop=True),
                         deps=tmax(tmr2, mr_bf.w), inc=1)
            bank6.wrote(tmrbc)
            mr_bf.read(tmrbc)
            st_toks = []
            for c in range(NCH):
                t1 = p.op("v", lambda e, c=c: e.tensor_tensor(
                    ao_sb[c].ap[:], ao_sb[c].ap[:], bank6.ap[:, 0:NT], ALU.mult),
                    deps=tmax(ao_sb[c].w, tmrbc), inc=1)
                t2 = p.op("v", lambda e, c=c: e.tensor_tensor(
                    ao_sb[c].ap[:], ao_sb[c].ap[:], bank6.ap[:, NT:2 * NT], ALU.subtract),
                    deps=[t1], inc=1)
                bank6.read(t2)
                t3 = p.op("v", lambda e, c=c: e.tensor_tensor(
                    xb[c].ap[:], xb[c].ap[:], ao_sb[c].ap[:], ALU.add),
                    deps=tmax(t2, xb[c].w, xb[c].rs), inc=1)
                ao_sb[c].wrote(t3)
                xb[c].wrote(t3)
                ts_ = p.op("g", lambda e, c=c, ti=ti: e.dma_start(
                    out=dx1[c * 128:(c + 1) * 128, ti * NT:(ti + 1) * NT], in_=xb[c].ap[:]),
                    deps=[t3], sem="dga" if ti % 2 == 0 else "dgb", inc=16)
                xb[c].read(ts_)
                st_toks.append(ts_)
            for c in range(NCH):
                xb[c].rs = [st_toks[-1]]
            return st_toks

        RB = [0, 1, 2, 3, 4]
        rb_state = [0]

        def nxt():
            b = bank[RB[rb_state[0] % len(RB)]]
            rb_state[0] += 1
            return b

        import os as _os
        _FORCE = _os.environ.get("EW_FORCE") or None

        def _cost(kind, fd, bf, ps):
            c = {}
            if kind in ("tt", "stt"):
                accv = 2 if bf else 1
                c["v"] = ((120 if ps else 58) + fd / accv) / 0.96
                if not ps:
                    c["g"] = (300 + fd * 5.2) / 1.2
            elif kind in ("ts", "copy"):
                accv = (2 if ps else 4) if bf else 2
                c["v"] = ((120 if ps else 58) + fd / accv) / 0.96
                if not ps:
                    c["g"] = (300 + fd * 2.1) / 1.2
                c["s"] = ((172 if ps else 224) + fd / 2) / 1.2
            return c

        def ew_tt(out, a, b, op, deps, fd, bf=True, ps=False, eng=None):
            c = _cost("tt", fd, bf, ps)
            if _FORCE and _FORCE in c:
                eng = _FORCE
            if eng is None:
                eng = p.pick(c)
            else:
                p.load[eng] += c.get(eng, fd)
            return p.op(eng, lambda e, out=out, a=a, b=b, op=op: e.tensor_tensor(out, a, b, op),
                        deps=deps, inc=1)

        def ew_stt(out, a, s1, b, op0, op1, deps, fd, bf=True, ps=False):
            c = _cost("tt", fd, bf, ps)
            eng = _FORCE if (_FORCE and _FORCE in c) else p.pick(c)
            return p.op(eng, lambda e, out=out, a=a, s1=s1, b=b: e.scalar_tensor_tensor(
                out, a, s1, b, op0, op1), deps=deps, inc=1)

        def ew_ts(out, a, s1, op, deps, fd, s2=None, op2=None, bf=True, ps=False, eng=None):
            c = _cost("ts", fd, bf, ps)
            c.pop("s", None)
            if _FORCE and _FORCE in c:
                eng = _FORCE
            if eng is None:
                eng = p.pick(c)
            else:
                p.load[eng] += c.get(eng, fd)
            if eng == "s":
                return p.op("s", lambda e, out=out, a=a, s1=s1: e.activation(
                    out, a, AF.Copy, scale=s1), deps=deps, inc=1)
            if op2 is None:
                return p.op(eng, lambda e, out=out, a=a, s1=s1, op=op: e.tensor_scalar(
                    out, a, s1, None, op), deps=deps, inc=1)
            return p.op(eng, lambda e, out=out, a=a, s1=s1, s2=s2, op=op, op2=op2:
                        e.tensor_scalar(out, a, s1, s2, op, op2), deps=deps, inc=1)

        def ew_copy(out, a, deps, fd, bf=True, ps=False, eng=None):
            c = _cost("copy", fd, bf, ps)
            if _FORCE and _FORCE in c:
                eng = _FORCE
            if eng is None:
                eng = p.pick(c)
            else:
                p.load[eng] += c.get(eng, fd)
            if eng == "s":
                return p.op("s", lambda e, out=out, a=a: e.copy(out, a), deps=deps, inc=1)
            return p.op(eng, lambda e, out=out, a=a: e.tensor_copy(out, a), deps=deps, inc=1)

        def chunk_recurrence(ti, cc, ytoks):
            tof = cc * CK
            dk = dkt[cc % 2]
            ab_sb, loga, beta_g = dk["ab"], dk["loga"], dk["beta"]
            ell_tm, ell_fm, gam_tm = dk["ell_tm"], dk["ell_fm"], dk["gam"]
            rnorm, scl, negbg = dk["rnorm"], dk["scl"], dk["negbg"]

            tab = None
            for k in range(NCH):
                tab = p.op("t", lambda e, k=k: e.matmul(
                    bank7.ap[:, 0:32], ybuf[k].ap[:, tof:tof + CK],
                    wab_sb[:, k * 32:(k + 1) * 32],
                    start=(k == 0), stop=(k == NCH - 1)),
                    deps=tmax(ytoks[k], tab_w, bank7.wr()) if k == 0 else [ytoks[k]], inc=1)
            bank7.wrote(tab)
            for k in range(NCH):
                ybuf[k].read(tab)
            tabe = ew_copy(ab_sb.ap[:], bank7.ap[:, 0:32], tmax(tab, ab_sb.wr()), 32,
                           bf=False, ps=True)
            bank7.read(tabe)
            ab_sb.wrote(tabe)

            t1 = ew_tt(scl.ap[:, 0:H], ab_sb.ap[:, 0:H], dtbb[:], ALU.add,
                       tmax(tabe, t_dtb, scl.wr()), H, bf=False)
            t2 = p.op("s", lambda e: e.activation(scl.ap[:, H:2 * H], scl.ap[:, 0:H], AF.Abs),
                      deps=[t1], inc=1)
            t3 = p.op("s", lambda e: e.activation(scl.ap[:, 2 * H:3 * H], scl.ap[:, H:2 * H],
                                                  AF.Exp, scale=-1.0), deps=[t2], inc=1)
            t4 = p.op("s", lambda e: e.activation(scl.ap[:, 2 * H:3 * H], scl.ap[:, 2 * H:3 * H],
                                                  AF.Ln, bias=1.0), deps=[t3], inc=1)
            t5 = ew_ts(scl.ap[:, 3 * H:4 * H], scl.ap[:, 0:H], 0.0, ALU.max, [t1], H, bf=False)
            t6 = ew_tt(scl.ap[:, 2 * H:3 * H], scl.ap[:, 2 * H:3 * H], scl.ap[:, 3 * H:4 * H],
                       ALU.add, [t4, t5], H, bf=False, eng="v")
            tlg = ew_tt(loga.ap[:], scl.ap[:, 2 * H:3 * H], negA[:], ALU.mult,
                        tmax(t6, t_negA, loga.wr()), H, bf=False, eng="v")
            loga.wrote(tlg)
            scl.wrote(t6)
            tbt = p.op("s", lambda e: e.activation(beta_g.ap[:], ab_sb.ap[:, H:2 * H], AF.Sigmoid),
                       deps=tmax(tabe, beta_g.wr()), inc=1)
            tbt = p.op("s", lambda e: e.mul(beta_g.ap[:], beta_g.ap[:], 2.0), deps=[tbt], inc=1)
            beta_g.wrote(tbt)
            ab_sb.read(tbt)

            tc1 = p.op("t", lambda e: e.matmul(bank7.ap[:, 64:64 + H], cr("incl_u_r"), loga.ap[:],
                                               start=True, stop=True),
                       deps=tmax(tlg, tkr, bank7.wr()), inc=1)
            tc2 = p.op("t", lambda e: e.matmul(bank7.ap[:16, 128:128 + CK], loga.ap[:],
                                               cr("incl_u_r"), start=True, stop=True),
                       deps=tmax(tlg), inc=1)
            bank7.wrote(tc2)
            loga.read(tc2)
            te1 = ew_copy(ell_tm.ap[:], bank7.ap[:, 64:64 + H],
                          tmax(tc1, tc2, ell_tm.wr()), H, bf=False, ps=True, eng="v")
            ell_tm.wrote(te1)
            te2 = ew_copy(ell_fm.ap[:], bank7.ap[:16, 128:128 + CK],
                          tmax(tc1, tc2, ell_fm.wr()), CK, bf=False, ps=True, eng="v")
            ell_fm.wrote(te2)
            bank7.read(te1)
            bank7.read(te2)
            tg = p.op("s", lambda e: e.activation(gam_tm.ap[:], ell_tm.ap[:], AF.Exp),
                      deps=tmax(te1, gam_tm.wr()), inc=1)
            gam_tm.wrote(tg)
            tnb = ew_tt(negbg.ap[:], beta_g.ap[:], gam_tm.ap[:], ALU.mult,
                        tmax(tbt, tg, negbg.wr()), H, bf=False, eng="v")
            tnb = ew_ts(negbg.ap[:], negbg.ap[:], -1.0, ALU.mult, [tnb], H, bf=False)
            negbg.wrote(tnb)
            beta_g.read(tnb)
            gam_tm.read(tnb)

            ntk = None
            for qc in range(16):
                ng = nsqG[qc % 2]
                tq = ew_tt(ng.ap[:], qkv[qc].ap[:, tof:tof + CK],
                           qkv[qc].ap[:, tof:tof + CK], ALU.mult,
                           tmax(qkv[qc].w, ng.wr()), CK, eng="v")
                ng.wrote(tq)
                qkv[qc].read(tq)
                ntk = p.op("t", lambda e, qc=qc, ng=ng: e.matmul(
                    bank7.ap[:, 256 + qc * 2:256 + qc * 2 + 2], ng.ap[:],
                    cb("bdones"), start=True, stop=True),
                    deps=tmax(tq, tkb, bank7.wr() if qc == 0 else []), inc=1)
                ng.read(ntk)
            bank7.wrote(ntk)
            tn1 = p.op("s", lambda e: e.activation(rnorm.ap[:], bank7.ap[:, 256:288], AF.Ln),
                       deps=tmax(ntk, rnorm.wr()), inc=1)
            bank7.read(tn1)
            tn2 = p.op("s", lambda e: e.activation(rnorm.ap[:], rnorm.ap[:], AF.Exp, scale=-0.5),
                       deps=[tn1], inc=1)
            tn3 = ew_ts(rnorm.ap[:], rnorm.ap[:], 1.0e6, ALU.min, [tn2], 32, bf=False, eng="v")
            rnorm.wrote(tn3)

            if stop_after == "decay" or stop_after == "cc0decay":
                return
            nhp = 8
            if isinstance(stop_after, str) and "hp" in stop_after:
                import re as _re
                nhp = int(_re.search(r"hp(\d+)", stop_after).group(1))
            from collections import deque
            win = deque()
            nxt_hp = 0
            while win or nxt_hp < nhp:
                while len(win) < 2 and nxt_hp < nhp:
                    win.append(head_pair(ti, cc, tof, nxt_hp, dk, te1, te2))
                    nxt_hp += 1
                for ggen in list(win):
                    try:
                        next(ggen)
                    except StopIteration:
                        win.remove(ggen)

        def head_pair(ti, cc, tof, hp, dk, te1, te2):
            h0 = 2 * hp
            par = hp % 2
            ab_sb, loga, beta_g = dk["ab"], dk["loga"], dk["beta"]
            ell_tm, ell_fm, gam_tm = dk["ell_tm"], dk["ell_fm"], dk["gam"]
            rnorm, scl, negbg = dk["rnorm"], dk["scl"], dk["negbg"]
            sel = lambda L: L[par % len(L)]
            EB, XE, XE2 = sel(EbatP), sel(expEP), sel(expE2P)
            DI, LF = sel(dincP), sel(lfacP)
            KQ, KH, HN, VT, KA = sel(kqtmP), sel(khnP), sel(hnfP), sel(vtmP), sel(khatP)
            LTP, LTTP, MMP, RTP = sel(LtP), sel(LTtP), sel(mmtP), sel(RtmP)
            rsc, Ngt, h2p, h4p = sel(rscP), sel(NgtP), sel(h2pP), sel(h4pP)
            t0ap, Tgp, TTgp, Pgp = sel(t0apP), sel(TgpP), sel(TTgpP), sel(PgpP)
            Utmp, oscp, osqp, oms, ontm = sel(UtmpP), sel(oscpP), sel(osqpP), sel(omsP), sel(ontmP)

            tb_ = p.op("t", lambda e: e.matmul(bank6.ap[:, 0:CK],
                                               cr("onehot")[:16, h0 * 128:(h0 + 1) * 128],
                                               ell_fm.ap[:], start=True, stop=True),
                       deps=tmax(te2, tkr, bank6.wr()), inc=1)
            tb1 = p.op("t", lambda e: e.matmul(bank6.ap[:, CK:2 * CK],
                                               cr("onehot")[:16, (h0 + 1) * 128:(h0 + 2) * 128],
                                               ell_fm.ap[:], start=True, stop=True), inc=1)
            bank6.wrote(tb1)
            ell_fm.read(tb1)
            tE = None
            for i in range(2):
                tE = p.op("v", lambda e, i=i: e.tensor_scalar(
                    EB.ap[:, i * CK:(i + 1) * CK], bank6.ap[:, i * CK:(i + 1) * CK],
                    ell_tm.ap[:, h0 + i:h0 + i + 1], None, ALU.subtract),
                    deps=tmax(tb1, te1, EB.wr() if i == 0 else []), inc=1)
                ell_tm.read(tE)
            p.load["v"] += 400.0
            bank6.read(tE)
            EB.wrote(tE)
            yield
            tek = None
            for i in range(2):
                tek = p.op("s", lambda e, i=i: e.activation(
                    scl.ap[:, 4 * H + 4 * par + i:4 * H + 4 * par + i + 1],
                    EB.ap[:, i * CK + CK - 1:i * CK + CK], AF.Exp),
                    deps=tmax(tE, scl.wr() if i == 0 else []), inc=1)
                EB.read(tek)
            scl.wrote(tek)
            tm1 = ew_ts(rsc.ap[:, 0:2 * CK], EB.ap[:], 0.0, ALU.min,
                        tmax(tE, rsc.wr()), 2 * CK, bf=False)
            rsc.wrote(tm1)
            tx1 = p.op("s", lambda e: e.activation(XE.ap[:], rsc.ap[:, 0:2 * CK], AF.Exp),
                       deps=tmax(tm1, XE.wr()), inc=1)
            XE.wrote(tx1)
            rsc.read(tx1)
            tm2 = ew_ts(rsc.ap[:, 0:2 * CK], EB.ap[:], -1.0, ALU.mult,
                        tmax(tE, tx1), 2 * CK, s2=0.0, op2=ALU.min, bf=False)
            EB.read(tm2)
            rsc.wrote(tm2)
            tx2 = p.op("s", lambda e: e.activation(XE2.ap[:], rsc.ap[:, 0:2 * CK], AF.Exp),
                       deps=tmax(tm2, XE2.wr()), inc=1)
            XE2.wrote(tx2)
            rsc.read(tx2)
            yield
            td_ = ew_tt(DI.ap[:], XE.ap[:], cb("incl_u_dw"), ALU.mult,
                        tmax(tx1, tkb, DI.wr()), 2 * CK)
            DI.wrote(td_)
            XE.read(td_)
            tl_ = ew_tt(LF.ap[:], XE2.ap[:], cb("strict_l_dw"), ALU.mult,
                        tmax(tx2, tkb, LF.wr()), 2 * CK)
            LF.wrote(tl_)
            XE2.read(tl_)

            yield
            ttr = p.op("t", lambda e: e.transpose(
                bank5.ap[:, 0:CK], qkv[8 + hp].ap[:, tof:tof + CK], cb("ident")),
                deps=tmax(qkv[8 + hp].w, tkb, bank5.wr()), inc=1)
            qkv[8 + hp].read(ttr)
            ttr = p.op("t", lambda e: e.transpose(
                bank5.ap[:, CK:2 * CK], qkv[hp].ap[:, tof:tof + CK], cb("ident")), inc=1)
            qkv[hp].read(ttr)
            bank5.wrote(ttr)
            tkq = ew_copy(KQ.ap[:], bank5.ap[:, 0:256], tmax(ttr, KQ.wr()), 2 * CK, ps=True)
            bank5.read(tkq)
            KQ.wrote(tkq)
            yield
            kh_toks = []
            for i in range(2):
                h = h0 + i
                ta_ = ew_ts(KH.ap[:, i * 64:(i + 1) * 64], KQ.ap[:, i * 64:(i + 1) * 64],
                            rnorm.ap[:, 16 + h:17 + h], ALU.mult,
                            tmax(tkq, rnorm.w, KH.wr() if i == 0 else []), 64)
                tb2 = ew_ts(KH.ap[:, 128 + i * 64:128 + (i + 1) * 64],
                            KQ.ap[:, 128 + i * 64:128 + (i + 1) * 64],
                            rnorm.ap[:, h:h + 1], ALU.mult, tmax(tkq, rnorm.w), 64)
                rnorm.read(ta_)
                rnorm.read(tb2)
                KQ.read(ta_)
                KQ.read(tb2)
                kh_toks += [ta_, tb2]
            kh_done = tmax(*kh_toks)
            KH.w = None
            KH.rs = list(kh_done)
            yield
            ttr2 = None
            for i in range(2):
                for j in range(2):
                    ttr2 = p.op("t", lambda e, i=i, j=j: e.transpose(
                        bank5.ap[64 * i:64 * i + 64, 256 + j * 128:256 + (j + 1) * 128],
                        KH.ap[:, j * 128 + i * 64:j * 128 + (i + 1) * 64], cb("ident")),
                        deps=tmax(kh_done, bank5.wr()) if (i == 0 and j == 0) else (), inc=1)
            bank5.wrote(ttr2)
            thn = ew_copy(HN.ap[:], bank5.ap[:, 256:512], tmax(ttr2, HN.wr()), 2 * CK, ps=True)
            bank5.read(thn)
            HN.wrote(thn)
            yield
            ttv = None
            for i in range(2):
                h = h0 + i
                ttv = p.op("t", lambda e, i=i, h=h: e.transpose(
                    bank5.ap[:, 512 + i * 128:512 + (i + 1) * 128],
                    qkv[16 + h].ap[:, tof:tof + CK], cb("ident")),
                    deps=tmax(qkv[16 + h].w, bank5.wr() if i == 0 else []), inc=1)
                qkv[16 + h].read(ttv)
            bank5.wrote(ttv)
            tve = ew_copy(VT.ap[:], bank5.ap[:, 512:768], tmax(ttv, VT.wr()), 2 * CK, ps=True)
            bank5.read(tve)
            VT.wrote(tve)
            yield
            tkh = None
            for i in range(2):
                tkh = ew_ts(KA.ap[:, i * 64:(i + 1) * 64], KH.ap[:, i * 64:(i + 1) * 64],
                            scl.ap[:, 4 * H + 4 * par + i:4 * H + 4 * par + i + 1], ALU.mult,
                            tmax(kh_done, tek, KA.wr() if i == 0 else []), 64)
                scl.read(tkh)
                KH.read(tkh)
            KA.wrote(tkh)
            if stop_after is not None and "tp" in str(stop_after):
                return

            if stop_after is not None and "xpre" in str(stop_after):
                return
            yield
            kn = [HN.ap[64 * i:64 * i + 64, 0:CK] for i in range(2)]
            qn = [HN.ap[64 * i:64 * i + 64, CK:2 * CK] for i in range(2)]
            bgr = nxt()
            import os as _os2
            _gm = _os2.environ.get("GRAM_MODE", "")
            _irange = {"i0": (0,), "i1": (1,)}.get(_gm, (0, 1))
            tg2 = None
            for i in _irange:
                tg2 = p.op("t", lambda e, i=i, bgr=bgr: e.matmul(
                    bgr.ap[:, i * CK:(i + 1) * CK], kn[i], kn[i], start=True, stop=True),
                    deps=tmax(thn, bgr.wr()) if i == 0 else (), inc=1, pbase=64 * i)
                tg2 = p.op("t", lambda e, i=i, bgr=bgr: e.matmul(
                    bgr.ap[:, 2 * CK + i * CK:2 * CK + (i + 1) * CK], kn[i], qn[i],
                    start=True, stop=True), inc=1, pbase=64 * i)
            bgr.wrote(tg2)
            HN.read(tg2)
            if stop_after is not None and "xgr" in str(stop_after):
                return
            yield
            tL = ew_tt(LTP.ap[:], LF.ap[:], bgr.ap[:, 0:2 * CK], ALU.mult,
                       tmax(tl_, tg2, LTP.wr()), 2 * CK, ps=True, eng="v")
            LF.read(tL)
            bgr.read(tL)
            for i in range(2):
                h = h0 + i
                tL = ew_ts(LTP.ap[:, i * CK:(i + 1) * CK], LTP.ap[:, i * CK:(i + 1) * CK],
                           beta_g.ap[:, h:h + 1], ALU.mult, [tL], CK)
                beta_g.read(tL)
            LTP.wrote(tL)
            tMm = ew_tt(MMP.ap[:], DI.ap[:], bgr.ap[:, 2 * CK:4 * CK], ALU.mult,
                        tmax(td_, tg2, MMP.wr()), 2 * CK, ps=True, eng="v")
            MMP.wrote(tMm)
            DI.read(tMm)
            bgr.read(tMm)
            if stop_after is not None and "xlm" in str(stop_after):
                return
            yield
            tLT = None
            for i in range(2):
                tLT = p.op("t", lambda e, i=i: e.transpose(
                    bank5.ap[:, 768 + i * 128:768 + (i + 1) * 128],
                    LTP.ap[:, i * CK:(i + 1) * CK], cb("ident")),
                    deps=tmax(tL, tkb, bank5.wr()) if i == 0 else (), inc=1)
            bank5.wrote(tLT)
            LTP.read(tLT)
            tLTe = ew_copy(LTTP.ap[:], bank5.ap[:, 768:1024], tmax(tLT, LTTP.wr()), 2 * CK,
                           ps=True)
            bank5.read(tLTe)
            LTTP.wrote(tLTe)

            if stop_after is not None and "xlt" in str(stop_after):
                return
            yield
            tN = ew_tt(Ngt.ap[:, 0:2 * CK], LTTP.ap[:], cb("m8u_dw"), ALU.mult,
                       tmax(tLTe, Ngt.wr()), 2 * CK)
            LTTP.read(tN)
            tNT = ew_tt(Ngt.ap[:, 2 * CK:4 * CK], LTP.ap[:], cb("m8l_dw"), ALU.mult,
                        tmax(tL, Ngt.wr()), 2 * CK)
            LTP.read(tNT)
            ngt_done = tmax(tN, tNT)
            Ngt.w = None
            Ngt.rs = list(ngt_done)
            if stop_after is not None and "xnn" in str(stop_after):
                return
            yield
            bh = nxt()
            tmm2 = None
            for i in range(2):
                tmm2 = p.op("t", lambda e, i=i, bh=bh: e.matmul(
                    bh.ap[:, i * CK:(i + 1) * CK], Ngt.ap[:, 2 * CK + i * CK:2 * CK + (i + 1) * CK],
                    Ngt.ap[:, i * CK:(i + 1) * CK], start=True, stop=True),
                    deps=tmax(ngt_done, bh.wr()) if i == 0 else (), inc=1)
                tmm2 = p.op("t", lambda e, i=i, bh=bh: e.matmul(
                    bh.ap[:, 2 * CK + i * CK:2 * CK + (i + 1) * CK],
                    Ngt.ap[:, i * CK:(i + 1) * CK],
                    Ngt.ap[:, 2 * CK + i * CK:2 * CK + (i + 1) * CK],
                    start=True, stop=True), inc=1)
            bh.wrote(tmm2)
            Ngt.read(tmm2)
            th2 = ew_copy(h2p.ap[:], bh.ap[:, 0:4 * CK], tmax(tmm2, h2p.wr()), 4 * CK, ps=True)
            bh.read(th2)
            h2p.wrote(th2)
            yield
            bh = nxt()
            tmm4 = None
            for i in range(2):
                tmm4 = p.op("t", lambda e, i=i, bh=bh: e.matmul(
                    bh.ap[:, i * CK:(i + 1) * CK], h2p.ap[:, 2 * CK + i * CK:2 * CK + (i + 1) * CK],
                    h2p.ap[:, i * CK:(i + 1) * CK], start=True, stop=True),
                    deps=tmax(th2, bh.wr()) if i == 0 else (), inc=1)
                tmm4 = p.op("t", lambda e, i=i, bh=bh: e.matmul(
                    bh.ap[:, 2 * CK + i * CK:2 * CK + (i + 1) * CK],
                    h2p.ap[:, i * CK:(i + 1) * CK],
                    h2p.ap[:, 2 * CK + i * CK:2 * CK + (i + 1) * CK],
                    start=True, stop=True), inc=1)
            bh.wrote(tmm4)
            h2p.read(tmm4)
            th4 = ew_copy(h4p.ap[:], bh.ap[:, 0:4 * CK], tmax(tmm4, h4p.wr()), 4 * CK, ps=True)
            bh.read(th4)
            h4p.wrote(th4)
            yield
            bh = nxt()
            tmm24 = None
            for i in range(2):
                tmm24 = p.op("t", lambda e, i=i, bh=bh: e.matmul(
                    bh.ap[:, i * CK:(i + 1) * CK], h2p.ap[:, 2 * CK + i * CK:2 * CK + (i + 1) * CK],
                    h4p.ap[:, i * CK:(i + 1) * CK], start=True, stop=True),
                    deps=tmax(th4, bh.wr()) if i == 0 else (), inc=1)
                tmm24 = p.op("t", lambda e, i=i, bh=bh: e.matmul(
                    bh.ap[:, 2 * CK + i * CK:2 * CK + (i + 1) * CK],
                    h4p.ap[:, i * CK:(i + 1) * CK],
                    h2p.ap[:, 2 * CK + i * CK:2 * CK + (i + 1) * CK],
                    start=True, stop=True), inc=1)
            bh.wrote(tmm24)
            h2p.read(tmm24)
            h4p.read(tmm24)
            if stop_after is not None and "xh24" in str(stop_after):
                return
            yield
            ta1 = ew_tt(t0ap.ap[:], h2p.ap[:], h4p.ap[:], ALU.add,
                        tmax(th2, th4, t0ap.wr()), 4 * CK)
            ta2 = ew_tt(t0ap.ap[:], t0ap.ap[:], bh.ap[:, 0:4 * CK], ALU.add,
                        tmax(ta1, tmm24), 4 * CK, ps=True, eng="v")
            bh.read(ta2)
            ta3 = None
            for j in range(2):
                ta3 = ew_tt(t0ap.ap[:, j * 2 * CK:(j + 1) * 2 * CK],
                            t0ap.ap[:, j * 2 * CK:(j + 1) * 2 * CK],
                            cb("ident_dw"), ALU.add, [ta2] if j == 0 else [ta3], 2 * CK)
            t0ap.wrote(ta3)
            yield
            bh = nxt()
            tmc = None
            for i in range(2):
                tmc = p.op("t", lambda e, i=i, bh=bh: e.matmul(
                    bh.ap[:, i * CK:(i + 1) * CK],
                    Ngt.ap[:, 2 * CK + i * CK:2 * CK + (i + 1) * CK],
                    t0ap.ap[:, i * CK:(i + 1) * CK], start=True, stop=True),
                    deps=tmax(ta3, bh.wr()) if i == 0 else (), inc=1)
                tmc = p.op("t", lambda e, i=i, bh=bh: e.matmul(
                    bh.ap[:, 2 * CK + i * CK:2 * CK + (i + 1) * CK],
                    t0ap.ap[:, i * CK:(i + 1) * CK],
                    Ngt.ap[:, 2 * CK + i * CK:2 * CK + (i + 1) * CK],
                    start=True, stop=True), inc=1)
            bh.wrote(tmc)
            Ngt.read(tmc)
            t0ap.read(tmc)
            tT0 = ew_tt(Tgp.ap[:], t0ap.ap[:, 0:2 * CK], bh.ap[:, 0:2 * CK], ALU.subtract,
                        tmax(tmc, Tgp.wr()), 2 * CK, ps=True, eng="v")
            Tgp.wrote(tT0)
            tT0T = ew_tt(TTgp.ap[:], t0ap.ap[:, 2 * CK:4 * CK], bh.ap[:, 2 * CK:4 * CK],
                         ALU.subtract, tmax(tmc, TTgp.wr()), 2 * CK, ps=True, eng="v")
            TTgp.wrote(tT0T)
            bh.read(tT0)
            bh.read(tT0T)
            t0ap.read(tT0T)
            if stop_after is not None and "tbase" in str(stop_after):
                return

            yield
            for li, m in enumerate((8, 16, 32, 64)):
                yield
                last = (m == 64)
                tRk = ew_tt(Ngt.ap[:, 0:2 * CK], LTTP.ap[:], cb(f"r{m}u_dw"), ALU.mult,
                            tmax(tLTe, Ngt.wr()), 2 * CK)
                LTTP.read(tRk)
                tRkT = ew_tt(Ngt.ap[:, 2 * CK:4 * CK], LTP.ap[:], cb(f"r{m}l_dw"), ALU.mult,
                             tmax(tL, Ngt.wr()), 2 * CK)
                LTP.read(tRkT)
                rk_done = tmax(tRk, tRkT)
                Ngt.w = None
                Ngt.rs = list(rk_done)
                yield
                bh = nxt()
                tmP = None
                for i in range(2):
                    tmP = p.op("t", lambda e, i=i, bh=bh: e.matmul(
                        bh.ap[:, i * CK:(i + 1) * CK], TTgp.ap[:, i * CK:(i + 1) * CK],
                        Ngt.ap[:, i * CK:(i + 1) * CK], start=True, stop=True),
                        deps=tmax(rk_done, TTgp.w, bh.wr()) if i == 0 else (), inc=1)
                    tmP = p.op("t", lambda e, i=i, bh=bh: e.matmul(
                        bh.ap[:, 2 * CK + i * CK:2 * CK + (i + 1) * CK],
                        Ngt.ap[:, i * CK:(i + 1) * CK], TTgp.ap[:, i * CK:(i + 1) * CK],
                        start=True, stop=True), inc=1)
                bh.wrote(tmP)
                Ngt.read(tmP)
                TTgp.read(tmP)
                tPe = ew_copy(Pgp.ap[:], bh.ap[:, 0:4 * CK], tmax(tmP, Pgp.wr()), 4 * CK,
                              ps=True)
                bh.read(tPe)
                Pgp.wrote(tPe)
                yield
                bh = nxt()
                tmU = None
                for i in range(2):
                    tmU = p.op("t", lambda e, i=i, bh=bh: e.matmul(
                        bh.ap[:, i * CK:(i + 1) * CK],
                        Pgp.ap[:, 2 * CK + i * CK:2 * CK + (i + 1) * CK],
                        Tgp.ap[:, i * CK:(i + 1) * CK], start=True, stop=True),
                        deps=tmax(tPe, Tgp.w, bh.wr()) if i == 0 else (), inc=1)
                    if not last:
                        tmU = p.op("t", lambda e, i=i, bh=bh: e.matmul(
                            bh.ap[:, 2 * CK + i * CK:2 * CK + (i + 1) * CK],
                            Tgp.ap[:, i * CK:(i + 1) * CK],
                            Pgp.ap[:, 2 * CK + i * CK:2 * CK + (i + 1) * CK],
                            start=True, stop=True), inc=1)
                bh.wrote(tmU)
                Pgp.read(tmU)
                Tgp.read(tmU)
                tTn = ew_tt(Tgp.ap[:], Tgp.ap[:], bh.ap[:, 0:2 * CK], ALU.subtract,
                            tmax(tmU, Tgp.wr()), 2 * CK, ps=True, eng="v")
                Tgp.wrote(tTn)
                bh.read(tTn)
                if not last:
                    tTTn = ew_tt(TTgp.ap[:], TTgp.ap[:], bh.ap[:, 2 * CK:4 * CK], ALU.subtract,
                                 tmax(tmU, TTgp.wr()), 2 * CK, ps=True, eng="v")
                    TTgp.wrote(tTTn)
                    bh.read(tTTn)
            if stop_after is not None and "tlvl" in str(stop_after):
                return

            yield
            bS = nxt()
            tKS = None
            for i in range(2):
                h = h0 + i
                tKS = p.op("t", lambda e, i=i, h=h, bS=bS: e.matmul(
                    bS.ap[:, i * CK:(i + 1) * CK], kn[i], S_b[h].ap, start=True, stop=True),
                    deps=tmax(thn, S_b[h].w, bS.wr()) if i == 0 else tmax(S_b[h].w), inc=1,
                    pbase=64 * i)
                S_b[h].read(tKS)
                tKS = p.op("t", lambda e, i=i, h=h, bS=bS: e.matmul(
                    bS.ap[:, 2 * CK + i * CK:2 * CK + (i + 1) * CK], qn[i], S_b[h].ap,
                    start=True, stop=True), inc=1, pbase=64 * i)
                S_b[h].read(tKS)
            bS.wrote(tKS)
            HN.read(tKS)
            yield
            tR = None
            for i in range(2):
                h = h0 + i
                tR = ew_stt(RTP.ap[:, i * CK:(i + 1) * CK], bS.ap[:, i * CK:(i + 1) * CK],
                            negbg.ap[:, h:h + 1], VT.ap[:, i * CK:(i + 1) * CK],
                            ALU.mult, ALU.add,
                            tmax(tKS, tve, negbg.w, RTP.wr() if i == 0 else []), CK, ps=True)
                VT.read(tR)
                negbg.read(tR)
                bS.read(tR)
            RTP.wrote(tR)
            yield
            tsc = None
            for i in range(2):
                h = h0 + i
                tsc = p.op("v", lambda e, i=i, h=h, bS=bS: e.tensor_scalar(
                    oscp.ap[:, i * CK:(i + 1) * CK], bS.ap[:, 2 * CK + i * CK:2 * CK + (i + 1) * CK],
                    gam_tm.ap[:, h:h + 1], None, ALU.mult),
                    deps=tmax(tKS, gam_tm.w, oscp.wr() if i == 0 else []), inc=1)
                gam_tm.read(tsc)
                bS.read(tsc)
            p.load["v"] += 500.0
            oscp.wrote(tsc)
            yield
            bU = nxt()
            tU = None
            for i in range(2):
                tU = p.op("t", lambda e, i=i, bU=bU: e.matmul(
                    bU.ap[:, i * CK:(i + 1) * CK], Tgp.ap[:, i * CK:(i + 1) * CK],
                    RTP.ap[:, i * CK:(i + 1) * CK], start=True, stop=True),
                    deps=tmax(tR, Tgp.w, bU.wr()) if i == 0 else (), inc=1)
            bU.wrote(tU)
            Tgp.read(tU)
            RTP.read(tU)
            tUe = ew_copy(Utmp.ap[:], bU.ap[:, 0:2 * CK], tmax(tU, Utmp.wr()), 2 * CK, ps=True)
            bU.read(tUe)
            Utmp.wrote(tUe)
            yield
            bM = nxt()
            tO = None
            for i in range(2):
                tO = p.op("t", lambda e, i=i, bM=bM: e.matmul(
                    bM.ap[:, i * CK:(i + 1) * CK], MMP.ap[:, i * CK:(i + 1) * CK],
                    Utmp.ap[:, i * CK:(i + 1) * CK], start=True, stop=True),
                    deps=tmax(tUe, tMm, bM.wr()) if i == 0 else (), inc=1)
            bM.wrote(tO)
            MMP.read(tO)
            Utmp.read(tO)
            tOa = ew_tt(oscp.ap[:], oscp.ap[:], bM.ap[:, 0:2 * CK], ALU.add,
                        tmax(tO, tsc), 2 * CK, bf=False, ps=True, eng="v")
            bM.read(tOa)
            oscp.wrote(tOa)
            yield
            tq2 = p.op("s", lambda e: e.activation(osqp.ap[:], oscp.ap[:], AF.Square),
                       deps=tmax(tOa, osqp.wr()), inc=1)
            osqp.wrote(tq2)
            tms = None
            for i in range(2):
                tms = p.op("v", lambda e, i=i: e.reduce_sum(
                    oms.ap[:, i:i + 1], osqp.ap[:, i * CK:(i + 1) * CK], AX.X),
                    deps=tmax(tq2, oms.wr()) if i == 0 else (), inc=1)
            p.load["v"] += 400.0
            osqp.read(tms)
            tln = p.op("s", lambda e: e.activation(oms.ap[:, 2:4], oms.ap[:, 0:2], AF.Ln,
                                                   scale=1.0 / DVH, bias=EPS),
                       deps=[tms], inc=1)
            trs = p.op("s", lambda e: e.activation(oms.ap[:, 2:4], oms.ap[:, 2:4], AF.Exp,
                                                   scale=-0.5), deps=[tln], inc=1)
            oms.wrote(trs)
            tno = None
            for i in range(2):
                tno = ew_ts(ontm.ap[:, i * CK:(i + 1) * CK], oscp.ap[:, i * CK:(i + 1) * CK],
                            oms.ap[:, 2 + i:3 + i], ALU.mult,
                            tmax(trs, tOa, ontm.wr() if i == 0 else []), CK, bf=False)
                oms.read(tno)
                oscp.read(tno)
            ontm.wrote(tno)
            yield
            bSt = nxt()
            tS = None
            for i in range(2):
                tS = p.op("t", lambda e, i=i, bSt=bSt: e.matmul(
                    bSt.ap[64 * i:64 * i + 64, i * CK:(i + 1) * CK], KA.ap[:, i * 64:(i + 1) * 64],
                    Utmp.ap[:, i * CK:(i + 1) * CK], start=True, stop=True),
                    deps=tmax(tkh, tUe, bSt.wr()) if i == 0 else (), inc=1)
            bSt.wrote(tS)
            KA.read(tS)
            Utmp.read(tS)
            for i in range(2):
                h = h0 + i
                hb = 64 * (h % 2)
                tgc = ew_tt(scl.ap[:, 4 * H + 4 * par + 2 + i:4 * H + 4 * par + 3 + i],
                            scl.ap[:, 4 * H + 4 * par + i:4 * H + 4 * par + i + 1],
                            gam_tm.ap[:, h:h + 1], ALU.mult,
                            tmax(tek, gam_tm.w, scl.wr() if i == 0 else []), 1, bf=False,
                            eng="v")
                gam_tm.read(tgc)
                scl.wrote(tgc)
                tSu = p.op("v", lambda e, i=i, h=h, hb=hb, bSt=bSt, par=par: e.scalar_tensor_tensor(
                    S_f[h].ap, S_f[h].ap,
                    scl.ap[hb:hb + 64, 4 * H + 4 * par + 2 + i:4 * H + 4 * par + 3 + i],
                    bSt.ap[hb:hb + 64, i * CK:(i + 1) * CK], ALU.mult, ALU.add),
                    deps=tmax(tS, tgc, S_f[h].wr()), inc=1)
                p.load["v"] += 300.0
                scl.read(tSu)
                bSt.read(tSu)
                S_f[h].wrote(tSu)
            tSb = ew_copy(S_bf[:, hp * DVH:(hp + 1) * DVH], S_sb[:, hp * DVH:(hp + 1) * DVH],
                          tmax(S_f[h0].w, S_f[h0 + 1].w,
                               S_b[h0].wr(), S_b[h0 + 1].wr()), DVH)
            S_f[h0].read(tSb)
            S_f[h0 + 1].read(tSb)
            S_b[h0].wrote(tSb)
            S_b[h0 + 1].wrote(tSb)
            yield
            ttn = None
            for i in range(2):
                ttn = p.op("t", lambda e, i=i: e.transpose(
                    bank5.ap[:, i * 128:(i + 1) * 128],
                    ontm.ap[:, i * DVH:(i + 1) * DVH], cb("ident")),
                    deps=tmax(tno, tkb, bank5.wr() if i == 0 else []), inc=1)
            bank5.wrote(ttn)
            ontm.read(ttn)
            toe = None
            for i in range(2):
                h = h0 + i
                toe = ew_copy(on_fm[h].ap[:, tof:tof + CK],
                              bank5.ap[:, i * 128:(i + 1) * 128],
                              tmax(ttn, on_fm[h].wr()), CK, ps=True)
                on_fm[h].wrote(toe)
            bank5.read(toe)

        def load_phase_b_weights(dep):
            toks = []
            for k in range(8):
                toks.append(stage_load(dwr[k * 128:(k + 1) * 128, :], wr_sb[k], k % 2, (dep,)))
                toks.append(stage_load(dwkf[k * 128:(k + 1) * 128, :], wkf_sb[k], k % 2))
                toks.append(stage_load(dwvf[k * 128:(k + 1) * 128, :], wvf_sb[k], k % 2))
            return tmax(*toks)

        def ffn_tile(ti, wB_tok):
            xb = xbuf[0]
            if ti > 0:
                load_x_tile(ti, xb, dx1)

            trb, tmr, trbc = ln_stats(xb, LN_EPS)
            tmrbc = p.op("t", lambda e: e.matmul(bank6.ap[:, NT:2 * NT], cb("ones_row")[:1, :],
                                                 mr_bf.ap[:], start=True, stop=True),
                         deps=tmax(tmr, mr_bf.w), inc=1)
            bank6.wrote(tmrbc)
            mr_bf.read(tmrbc)
            ztoks = []
            for c in range(NCH):
                zt = p.op("v", lambda e, c=c: e.tensor_copy(
                    z_sb[c].ap[:, 0:1], z_sb[c].ap[:, NT:NT + 1]),
                    deps=tmax(z_sb[c].w, z_sb[c].rs), inc=1)
                t1 = p.op("v", lambda e, c=c: e.tensor_tensor(
                    ao_sb[c].ap[:], xb[c].ap[:], bank6.ap[:, 0:NT], ALU.mult),
                    deps=tmax(xb[c].w, tmrbc, ao_sb[c].wr()), inc=1)
                xb[c].read(t1)
                t2 = p.op("v", lambda e, c=c: e.tensor_tensor(
                    z_sb[c].ap[:, 1:NT + 1], ao_sb[c].ap[:], bank6.ap[:, NT:2 * NT],
                    ALU.subtract),
                    deps=tmax(t1, zt), inc=1)
                ao_sb[c].wrote(t1)
                ao_sb[c].read(t2)
                bank6.read(t2)
                z_sb[c].wrote(t2)
                ztoks.append(t2)
            mixtoks = []
            for c in range(NCH):
                td_ = p.op("v", lambda e, c=c: e.tensor_tensor(
                    fsc.ap[:], z_sb[c].ap[:, 1:NT + 1], z_sb[c].ap[:, 0:NT], ALU.subtract),
                    deps=tmax(ztoks[c], fsc.wr()), inc=1)
                fsc.wrote(td_)
                z_sb[c].read(td_)
                t1 = p.op("v", lambda e, c=c: e.scalar_tensor_tensor(
                    xr_sb[c].ap[:], fsc.ap[:], mu_sb[:, c:c + 1], z_sb[c].ap[:, 0:NT],
                    ALU.mult, ALU.add),
                    deps=tmax(td_, xr_sb[c].wr()), inc=1)
                xr_sb[c].wrote(t1)
                t2 = p.op("v", lambda e, c=c: e.scalar_tensor_tensor(
                    xk_sb[c].ap[:], fsc.ap[:], mu_sb[:, 8 + c:9 + c], z_sb[c].ap[:, 0:NT],
                    ALU.mult, ALU.add),
                    deps=tmax(td_, xk_sb[c].wr()), inc=1)
                xk_sb[c].wrote(t2)
                fsc.read(t2)
                z_sb[c].read(t2)
                mixtoks.append((t1, t2))
            for oc in range(NCH):
                pb = bank[oc % 2]
                tmm = None
                for k in range(NCH):
                    tmm = p.op("t", lambda e, k=k, oc=oc, pb=pb: e.matmul(
                        pb.ap[:, 0:NT], wr_sb[k][:, oc * 128:(oc + 1) * 128], xr_sb[k].ap[:],
                        start=(k == 0), stop=(k == NCH - 1)),
                        deps=tmax(mixtoks[k][0], wB_tok, pb.wr()) if k == 0 else [mixtoks[k][0]],
                        inc=1)
                pb.wrote(tmm)
                for k in range(NCH):
                    xr_sb[k].read(tmm)
                tr_ = p.op("s", lambda e, oc=oc, pb=pb: e.activation(
                    r_sb[oc].ap[:], pb.ap[:, 0:NT], AF.Sigmoid),
                    deps=tmax(tmm, r_sb[oc].wr()), inc=1)
                pb.read(tr_)
                r_sb[oc].wrote(tr_)
            for oc in range(NCH):
                pb = bank[2 + oc % 2]
                tmm = None
                for k in range(NCH):
                    tmm = p.op("t", lambda e, k=k, oc=oc, pb=pb: e.matmul(
                        pb.ap[:, 0:NT], wkf_sb[k][:, oc * 128:(oc + 1) * 128], xk_sb[k].ap[:],
                        start=(k == 0), stop=(k == NCH - 1)),
                        deps=tmax(mixtoks[k][1], wB_tok, pb.wr()) if k == 0 else [mixtoks[k][1]],
                        inc=1)
                pb.wrote(tmm)
                for k in range(NCH):
                    xk_sb[k].read(tmm)
                trl = p.op("s", lambda e, pb=pb: e.activation(fsc.ap[:], pb.ap[:, 0:NT], AF.Relu),
                           deps=tmax(tmm, fsc.wr()), inc=1)
                pb.read(trl)
                fsc.wrote(trl)
                tsq = p.op("v", lambda e, oc=oc: e.tensor_tensor(
                    ka_sb[oc].ap[:], fsc.ap[:], fsc.ap[:], ALU.mult),
                    deps=tmax(trl, ka_sb[oc].wr()), inc=1)
                fsc.read(tsq)
                ka_sb[oc].wrote(tsq)
            for oc in range(NCH):
                pb = bank[4] if oc % 2 == 0 else bank[2 + (oc + 1) % 2]
                pb = bank[4 - (oc % 2)]
                tmm = None
                for k in range(NCH):
                    tmm = p.op("t", lambda e, k=k, oc=oc, pb=pb: e.matmul(
                        pb.ap[:, 0:NT], wvf_sb[k][:, oc * 128:(oc + 1) * 128], ka_sb[k].ap[:],
                        start=(k == 0), stop=(k == NCH - 1)),
                        deps=tmax(ka_sb[k].w, pb.wr()) if k == 0 else [ka_sb[k].w], inc=1)
                pb.wrote(tmm)
                for k in range(NCH):
                    ka_sb[k].read(tmm)
                tev = p.op("v", lambda e, oc=oc, pb=pb: e.tensor_tensor(
                    ao_sb[oc].ap[:], pb.ap[:, 0:NT], r_sb[oc].ap[:], ALU.mult),
                    deps=tmax(tmm, r_sb[oc].w, ao_sb[oc].wr()), inc=1)
                pb.read(tev)
                r_sb[oc].read(tev)
                ao_sb[oc].wrote(tev)
            trb4, tmr4, trbc4 = ln_stats(ao_sb, LN_EPS)
            tmrbc4 = p.op("t", lambda e: e.matmul(bank6.ap[:, NT:2 * NT], cb("ones_row")[:1, :],
                                                  mr_bf.ap[:], start=True, stop=True),
                          deps=tmax(tmr4, mr_bf.w), inc=1)
            bank6.wrote(tmrbc4)
            mr_bf.read(tmrbc4)
            st_toks = []
            for c in range(NCH):
                t1 = p.op("v", lambda e, c=c: e.tensor_tensor(
                    ao_sb[c].ap[:], ao_sb[c].ap[:], bank6.ap[:, 0:NT], ALU.mult),
                    deps=tmax(ao_sb[c].w, tmrbc4), inc=1)
                t2 = p.op("v", lambda e, c=c: e.tensor_tensor(
                    ao_sb[c].ap[:], ao_sb[c].ap[:], bank6.ap[:, NT:2 * NT], ALU.subtract),
                    deps=[t1], inc=1)
                bank6.read(t2)
                t3 = p.op("v", lambda e, c=c: e.tensor_tensor(
                    xb[c].ap[:], xb[c].ap[:], ao_sb[c].ap[:], ALU.add),
                    deps=tmax(t2, xb[c].w, xb[c].rs), inc=1)
                ao_sb[c].wrote(t3)
                xb[c].wrote(t3)
                ts_ = p.op("g", lambda e, c=c, ti=ti: e.dma_start(
                    out=dout[c * 128:(c + 1) * 128, ti * NT:(ti + 1) * NT], in_=xb[c].ap[:]),
                    deps=[t3], sem="dga" if ti % 2 == 0 else "dgb", inc=16)
                xb[c].read(ts_)
                st_toks.append(ts_)
            for c in range(NCH):
                xb[c].rs = [st_toks[-1]]
            return st_toks

        if "A" in phases:
            tms_ = p.op("v", lambda e: e.memset(ptail.ap[:], 0.0), inc=1)
            ptail.wrote(tms_)
            tms2 = p.op("v", lambda e: e.memset(S_sb[:], 0.0), inc=1)
            tms3 = p.op("v", lambda e: e.memset(S_bf[:], 0.0), inc=1)
            for h in range(H):
                S_f[h].wrote(tms2)
                S_b[h].wrote(tms3)
            for ti in range(NTI):
                attn_tile(ti)
            a_done = [("dga", p.n["dga"]), ("dgb", p.n["dgb"]), ("dg", p.n["dg"])]
        else:
            a_done = [("dga", p.n["dga"]), ("dgb", p.n["dgb"]), ("dg", p.n["dg"])]

        if "B" in phases:
            wB_tok = load_phase_b_weights(("t", p.n["t"]))
            tmu1 = wB_tok[0] if isinstance(wB_tok, list) else wB_tok
            for c in range(NCH):
                tmu1 = p.op("y", lambda e, c=c: e.dma_start(out=mu_sb[:, c:c + 1],
                                                            in_=dmur[c * 128:(c + 1) * 128, :]),
                            deps=[tmu1] if not isinstance(tmu1, list) else tmu1,
                            sem="dl", inc=16)
                tmu1 = p.op("y", lambda e, c=c: e.dma_start(out=mu_sb[:, 8 + c:9 + c],
                                                            in_=dmuk[c * 128:(c + 1) * 128, :]),
                            deps=[tmu1], sem="dl", inc=16)
            tzt = None
            for c in range(NCH):
                tzt = p.op("v", lambda e, c=c: e.memset(z_sb[c].ap[:, 0:NT + 1], 0.0),
                           deps=tmax(*a_done, ("t", p.n["t"])) if c == 0 else (), inc=1)
                z_sb[c].wrote(tzt)
            load_x_tile(0, xbuf[0], dx1, extra=tuple(a_done) + (tmu1, tzt))
            for ti in range(NTI):
                ffn_tile(ti, wB_tok)

        p.op("g", lambda e: e.nop(), deps=[("dga", p.n["dga"]), ("dgb", p.n["dgb"]),
                                           ("dg", p.n["dg"])])


        with nc.Block() as block:
            @block.tensor
            def _(eng):
                for f in p.q["t"]:
                    f(eng)

            @block.vector
            def _(eng):
                for f in p.q["v"]:
                    f(eng)

            @block.scalar
            def _(eng):
                for f in p.q["s"]:
                    f(eng)

            @block.gpsimd
            def _(eng):
                for f in p.q["g"]:
                    f(eng)

            @block.sync
            def _(eng):
                for f in p.q["y"]:
                    f(eng)

    return nc



_NC_CACHE = {}


def _get_nc():
    if "nc" not in _NC_CACHE:
        _NC_CACHE["nc"] = build_program()
    return _NC_CACHE["nc"]


def kernel(L0_attn_x, L0_attn_Wq, L0_attn_Wk, L0_attn_Wv, L0_attn_Wb, L0_attn_Wa,
           L0_attn_A_log, L0_attn_dt_bias, L0_attn_Wo, L0_attn_conv_q, L0_attn_conv_k,
           L0_attn_conv_v, L0_ffn_W_r, L0_ffn_W_k, L0_ffn_W_v, L0_ffn_mu_r, L0_ffn_mu_k):
    x = np.asarray(L0_attn_x, np.float32)
    B = x.shape[0]
    f = np.float32
    ct = np.ascontiguousarray
    wq = ct(np.asarray(L0_attn_Wq, f).T)
    wk = ct(np.asarray(L0_attn_Wk, f).T)
    wv = ct(np.asarray(L0_attn_Wv, f).T)
    wo = ct(np.asarray(L0_attn_Wo, f).T)
    wab_t = np.concatenate([np.asarray(L0_attn_Wa, f), np.asarray(L0_attn_Wb, f)], axis=0).T
    wab = ct(wab_t.reshape(8, 128, 32).transpose(1, 0, 2).reshape(128, 8 * 32))

    def _chunkify(w):
        w = np.asarray(w, f)
        k = w.shape[0] // 128
        return ct(w.reshape(k, 128, KW).transpose(1, 0, 2).reshape(128, k * KW))
    wr = ct(np.asarray(L0_ffn_W_r, f).T)
    wkf = ct(np.asarray(L0_ffn_W_k, f).T)
    wvf = ct(np.asarray(L0_ffn_W_v, f).T)
    shared = {
        "wq": wq, "wk": wk, "wv": wv, "wo": wo, "wab": wab,
        "convq": _chunkify(L0_attn_conv_q),
        "convk": _chunkify(L0_attn_conv_k),
        "convv": _chunkify(L0_attn_conv_v),
        "alog": ct(np.asarray(L0_attn_A_log, f).reshape(1, H)),
        "dtb": ct(np.asarray(L0_attn_dt_bias, f).reshape(1, H)),
        "wr": wr, "wkf": wkf, "wvf": wvf,
        "mur": ct(np.asarray(L0_ffn_mu_r, f).reshape(D, 1)),
        "muk": ct(np.asarray(L0_ffn_mu_k, f).reshape(D, 1)),
        "const": _CONSTS,
    }
    in_maps = []
    for b in range(B):
        m = dict(shared)
        m["x"] = ct(x[b].T)
        in_maps.append(m)
    nc = _get_nc()
    res = run_bass_kernel_spmd(nc, in_maps, core_ids=list(range(B)))
    out = np.stack([np.asarray(r["out"], np.float32).T for r in res.results], axis=0)
    return out


# version 05
# speedup vs baseline: 1.2823x; 1.0210x over previous
import contextlib
import numpy as np
import concourse.bass as bass
import concourse.mybir as mybir
from concourse.bass_utils import run_bass_kernel_spmd

KW_HOST = 4
F32 = mybir.dt.float32
F32R = mybir.dt.float32r
BF16 = mybir.dt.bfloat16
AF = mybir.ActivationFunctionType
ALU = mybir.AluOpType
AX = mybir.AxisListType

D = 1024
T = 8192
H = 16
DKH = 64
DVH = 128
KW = 4
NT = 256
CK = 128
NCH = D // 128
EPS = 1e-6
LN_EPS = 1e-5


def _consts_host():
    cols = []
    cmap = {}

    def add(name, tile, which):
        tile = np.asarray(tile, np.float32)
        t = np.zeros((128, tile.shape[1]), np.float32)
        t[: tile.shape[0]] = tile
        cmap[name] = (sum(c.shape[1] for c in cols), tile.shape[1], which)
        cols.append(t)

    i128 = np.arange(128)
    dw = lambda t: np.tile(t, (1, 2))
    add("ident", np.eye(128, dtype=np.float32), "b")
    add("ident_dw", dw(np.eye(128, dtype=np.float32)), "b")
    incl_u = (i128[:, None] <= i128[None, :]).astype(np.float32)
    strict_l = (i128[:, None] > i128[None, :]).astype(np.float32)
    add("incl_u_dw", dw(incl_u), "b")
    add("strict_l_dw", dw(strict_l), "b")
    blk8 = i128 // 8
    m8 = ((blk8[:, None] == blk8[None, :]) & (i128[:, None] < i128[None, :])).astype(np.float32)
    add("m8u_dw", dw(m8), "b")
    add("m8l_dw", dw(m8.T), "b")
    for m in (8, 16, 32, 64):
        b2 = i128 // (2 * m)
        half = (i128 // m) % 2
        r = ((b2[:, None] == b2[None, :]) & (half[:, None] == 0) & (half[None, :] == 1)).astype(np.float32)
        add(f"r{m}u_dw", dw(r), "b")
        add(f"r{m}l_dw", dw(r.T), "b")
    bd = np.zeros((128, 2), np.float32)
    bd[:64, 0] = 1.0
    bd[64:, 1] = 1.0
    add("bdones", bd, "b")
    add("ones_col", np.ones((128, 1), np.float32), "b")
    add("ones_row", np.ones((1, 128), np.float32), "b")
    add("incl_u_r", incl_u, "r")
    add("ones_row_r", np.ones((1, 128), np.float32), "r")
    oh = np.zeros((16, 16 * 128), np.float32)
    for h in range(16):
        oh[h, h * 128:(h + 1) * 128] = 1.0
    add("onehot", oh, "r")
    offs = {"b": 0, "r": 0}
    cmap2 = {}
    for name, (o, w, which) in cmap.items():
        cmap2[name] = (o, offs[which], w, which)
        offs[which] += w
    return (np.ascontiguousarray(np.concatenate(cols, axis=1)), cmap2,
            offs["b"], offs["r"])


_CONSTS, _CMAP, _CWB, _CWR = _consts_host()



class P:
    def __init__(self, nc):
        self.nc = nc
        self.q = {e: [] for e in ("t", "v", "s", "g", "y")}
        self.n = {k: 0 for k in ("t", "v", "s", "g", "y", "dl", "dg",
                                 "dxa", "dxb", "dca", "dcb", "dga", "dgb")}
        self.hw = {e: {} for e in self.q}
        self.sems = {}
        self.load = {"v": 0.0, "s": 0.0, "g": 0.0}

    def pick(self, costs):
        e = min(costs, key=lambda x: self.load[x] + costs[x])
        self.load[e] += costs[e]
        return e

    def _wait(self, e, tok):
        if tok is None:
            return
        s, c = tok
        if c <= 0 or c <= self.hw[e].get(s, 0):
            return
        self.hw[e][s] = c
        self.q[e].append(lambda eng, s=s, c=c: eng.wait_ge(self.sems[s], c))

    DMA_SEMS = ("dl", "dg", "dxa", "dxb", "dca", "dcb", "dga", "dgb")

    def op(self, e, emitfn, deps=(), sem=None, inc=0, pbase=0):
        sname = sem or e
        if e == "t":
            last = getattr(self, "pe_base", 0)
            if pbase != last:
                cnt = self.n["t"]
                self.q["t"].append(lambda eng, c=cnt: eng.wait_ge(self.sems["t"], c))
            self.pe_base = pbase
        if sname in self.DMA_SEMS:
            prev = getattr(self, "last_dma", {}).get(sname)
            if prev is not None:
                deps = list(deps) + [prev]
        for t in deps:
            self._wait(e, t)
        if inc:
            self.n[sname] += inc
            tok = (sname, self.n[sname])

            def run(eng, f=emitfn, sn=sname, k=inc):
                f(eng).then_inc(self.sems[sn], k)
            self.q[e].append(run)
            if sname in self.DMA_SEMS:
                if not hasattr(self, "last_dma"):
                    self.last_dma = {}
                self.last_dma[sname] = tok
            return tok
        self.q[e].append(lambda eng, f=emitfn: f(eng))
        return None


class Guard:
    def __init__(self, ap=None):
        self.ap = ap
        self.w = None
        self.rs = []

    def rd(self):
        return [self.w] if self.w else []

    def wr(self):
        return ([self.w] if self.w else []) + self.rs

    def wrote(self, tok):
        if tok is not None:
            self.w = tok
            self.rs = []

    def read(self, tok):
        if tok is not None:
            self.rs.append(tok)


def tmax(*toks):
    best = {}
    stack = list(toks)
    while stack:
        t = stack.pop()
        if t is None:
            continue
        if isinstance(t, list):
            stack.extend(t)
        elif t[1] > best.get(t[0], -1):
            best[t[0]] = t[1]
    return [(s, c) for s, c in best.items()]



def build_program(n_tiles=T // NT, debug=(), phases=("A", "B"), x1_out=False,
                  stop_after=None):
    nc = bass.Bass("TRN2", target_bir_lowering=False)
    NTI = n_tiles

    dx = nc.dram_tensor("x", [D, T], F32, kind="ExternalInput").ap()
    dwq = nc.dram_tensor("wq", [D, D], F32, kind="ExternalInput").ap()
    dwk = nc.dram_tensor("wk", [D, D], F32, kind="ExternalInput").ap()
    dwv = nc.dram_tensor("wv", [D, 2048], F32, kind="ExternalInput").ap()
    dwo = nc.dram_tensor("wo", [2048, D], F32, kind="ExternalInput").ap()
    dwab = nc.dram_tensor("wab", [128, NCH * 32], F32, kind="ExternalInput").ap()
    dcq = nc.dram_tensor("convq", [128, 8 * KW], F32, kind="ExternalInput").ap()
    dck = nc.dram_tensor("convk", [128, 8 * KW], F32, kind="ExternalInput").ap()
    dcv = nc.dram_tensor("convv", [128, 16 * KW], F32, kind="ExternalInput").ap()
    dalog = nc.dram_tensor("alog", [1, H], F32, kind="ExternalInput").ap()
    ddtb = nc.dram_tensor("dtb", [1, H], F32, kind="ExternalInput").ap()
    dwr = nc.dram_tensor("wr", [D, D], F32, kind="ExternalInput").ap()
    dwkf = nc.dram_tensor("wkf", [D, D], F32, kind="ExternalInput").ap()
    dwvf = nc.dram_tensor("wvf", [D, D], F32, kind="ExternalInput").ap()
    dmur = nc.dram_tensor("mur", [D, 1], F32, kind="ExternalInput").ap()
    dmuk = nc.dram_tensor("muk", [D, 1], F32, kind="ExternalInput").ap()
    dconst = nc.dram_tensor("const", list(_CONSTS.shape), F32, kind="ExternalInput").ap()
    dout = nc.dram_tensor("out", [D, T], F32, kind="ExternalOutput").ap()
    dx1 = nc.dram_tensor("x1s", [D, T], F32,
                         kind="ExternalOutput" if x1_out else "Internal").ap()
    dcdiag = nc.dram_tensor("cdiag", [128, 32 * KW * 128], BF16, kind="Internal").ap()

    ddbg = {}
    for name, shape in debug:
        ddbg[name] = nc.dram_tensor("dbg_" + name, list(shape), F32, kind="ExternalOutput").ap()

    p = P(nc)
    es = contextlib.ExitStack()

    def sb(name, shape, dt):
        return es.enter_context(nc.sbuf_tensor(name, list(shape), dt))

    with es:
        for s in ("t", "v", "s", "g", "y", "dl", "dg",
                  "dxa", "dxb", "dca", "dcb", "dga", "dgb"):
            p.sems[s] = es.enter_context(nc.semaphore("sem_" + s))

        cst_b = sb("cst_b", (128, _CWB), BF16)
        cst_r = sb("cst_r", (128, _CWR), F32R)
        epsc = sb("epsc", (128, 2), F32)
        p.op("v", lambda e: e.memset(epsc[:, 0:1], LN_EPS), inc=1)
        p.op("v", lambda e: e.memset(epsc[:, 1:2], EPS), inc=1)
        nc.const_aps.register(F32, LN_EPS, epsc[:, 0:1])
        nc.const_aps.register(F32, EPS, epsc[:, 1:2])

        def cb(nm):
            _, o, w, which = _CMAP[nm]
            assert which == "b"
            return cst_b[:, o:o + w]

        def cr(nm):
            _, o, w, which = _CMAP[nm]
            assert which == "r"
            return cst_r[:, o:o + w]

        bank = [Guard(es.enter_context(nc.psum_tensor(f"ps{i}", [128, 512], F32)))
                for i in range(5)]
        bank5 = Guard(es.enter_context(nc.psum_tensor("ps5", [128, 1024], BF16)))
        bank6 = Guard(es.enter_context(nc.psum_tensor("ps6", [128, 512], F32)))
        bank7 = Guard(es.enter_context(nc.psum_tensor("ps7", [128, 512], F32)))

        wpool = sb("wpool", (128, 48 * 1024), BF16)
        wq_sb = [wpool[:, (k) * 1024:(k + 1) * 1024] for k in range(8)]
        wk_sb = [wpool[:, (8 + k) * 1024:(9 + k) * 1024] for k in range(8)]
        wv_sb = [wpool[:, (16 + 2 * k) * 1024:(16 + 2 * k + 2) * 1024] for k in range(8)]
        wo_sb = [wpool[:, (32 + k) * 1024:(33 + k) * 1024] for k in range(16)]
        wr_sb = [wpool[:, (k) * 1024:(k + 1) * 1024] for k in range(8)]
        wkf_sb = [wpool[:, (8 + k) * 1024:(9 + k) * 1024] for k in range(8)]
        wvf_sb = [wpool[:, (16 + k) * 1024:(17 + k) * 1024] for k in range(8)]
        wg = Guard()

        wab_sb = sb("wab_sb", (128, NCH * 32), BF16)
        convw = sb("convw", (128, 32 * KW), F32)
        negA = sb("negA", (128, H), F32)
        dtbb = sb("dtbb", (128, H), F32)
        alog_r = sb("alog_r", (1, 2 * H), F32R)


        xbuf = [[Guard(sb(f"x0_{c}", (128, NT), F32)) for c in range(NCH)]]
        apool = sb("apool", (128, 12 * 1024), BF16)
        qkv = [Guard(apool[:, pc * NT:(pc + 1) * NT]) for pc in range(32)]
        on_fm = [Guard(apool[:, 8 * 1024 + hv * NT:8 * 1024 + (hv + 1) * NT]) for hv in range(16)]
        z_sb = [Guard(apool[:, c * (NT + 1):(c + 1) * (NT + 1)]) for c in range(8)]
        xr_sb = [Guard(apool[:, 2080 + c * NT:2080 + (c + 1) * NT]) for c in range(8)]
        xk_sb = [Guard(apool[:, 4128 + c * NT:4128 + (c + 1) * NT]) for c in range(8)]
        r_sb = [Guard(apool[:, 6176 + c * NT:6176 + (c + 1) * NT]) for c in range(8)]
        ka_sb = [Guard(apool[:, 8224 + c * NT:8224 + (c + 1) * NT]) for c in range(8)]

        ao_sb = [Guard(sb(f"ao{c}", (128, NT), BF16)) for c in range(NCH)]
        ybuf = [Guard(sb(f"y_{c}", (128, NT), BF16)) for c in range(NCH)]
        xwork = Guard(sb("xwork", (128, 2 * NT), BF16))
        stat = Guard(sb("stat", (1, 2 * NT), F32))
        mr_bf = Guard(sb("mr_bf", (1, NT), BF16))
        rstd_bf = Guard(sb("rstd_bf", (1, NT), BF16))
        pbuf = [Guard(sb("pb0", (128, NT + KW - 1), BF16))] * 2
        ptail = Guard(sb("ptail", (128, 32 * (KW - 1)), BF16))
        cdg = [Guard(sb(f"cdg{i}", (128, KW * 128), BF16)) for i in range(2)]

        ab_sb = Guard(sb("ab_sb", (128, 32), F32))
        loga = Guard(sb("loga", (128, H), F32R))
        beta_g = Guard(sb("beta", (128, H), F32))
        ell_tm = Guard(sb("ell_tm", (128, H), F32))
        ell_fm = Guard(sb("ell_fm", (16, CK), F32R))
        gam_tm = Guard(sb("gam_tm", (128, H), F32))
        rnorm = Guard(sb("rnorm", (128, 32), F32))
        scl = Guard(sb("scl", (128, 6 * H), F32))
        negbg = Guard(sb("negbg", (128, H), F32))

        S_f = [Guard(None) for _ in range(H)]
        S_b = [Guard(None) for _ in range(H)]
        S_sb = sb("S", (128, 8 * DVH), F32)
        S_bf = sb("S_bf", (128, 8 * DVH), BF16)
        for h in range(H):
            hb, hp_ = 64 * (h % 2), h // 2
            S_f[h].ap = S_sb[hb:hb + 64, hp_ * DVH:(hp_ + 1) * DVH]
            S_b[h].ap = S_bf[hb:hb + 64, hp_ * DVH:(hp_ + 1) * DVH]

        def gpair(name, cols, dt, npar=2):
            return [Guard(sb(f"{name}{i}", (128, cols), dt)) for i in range(npar)]

        EbatP = gpair("Ebat", 2 * CK, F32)
        expEP = gpair("expE", 2 * CK, BF16)
        expE2P = gpair("expE2", 2 * CK, BF16)
        dincP = gpair("dinc", 2 * CK, BF16)
        lfacP = gpair("lfac", 2 * CK, BF16)
        kqtmP = gpair("kqtm", 2 * CK, BF16)
        khnP = gpair("khn", 2 * CK, BF16)
        hnfP = gpair("hnf", 2 * CK, BF16)
        vtmP = gpair("vtm", 2 * DVH, BF16)
        khatP = gpair("khat", 2 * DKH, BF16)
        LtP = gpair("Lt", 2 * CK, BF16)
        LTtP = gpair("LTt", 2 * CK, BF16)
        mmtP = gpair("mmt", 2 * CK, BF16)
        RtmP = gpair("Rtm", 2 * CK, BF16)
        rscP = gpair("rsc", 2 * CK, F32)
        NgtP = gpair("Ngt", 4 * CK, BF16)
        h2pP = gpair("h2p", 4 * CK, BF16)
        h4pP = gpair("h4p", 4 * CK, BF16)
        t0apP = gpair("t0ap", 4 * CK, BF16)
        TgpP = gpair("Tgp", 2 * CK, BF16)
        TTgpP = gpair("TTgp", 2 * CK, BF16)
        PgpP = gpair("Pgp", 4 * CK, BF16)
        UtmpP = gpair("Utmp", 2 * CK, BF16)
        oscpP = gpair("oscp", 2 * CK, F32)
        osqpP = gpair("osqp", 2 * CK, BF16)
        omsP = gpair("oms", 4, F32)
        ontmP = gpair("ontm", 2 * DVH, BF16)
        nsq_t = sb("nsq", (128, 2 * CK), BF16)
        nsqG = [Guard(nsq_t[:, 0:CK]), Guard(nsq_t[:, CK:2 * CK])]

        dkt = []
        for i in range(2):
            dkt.append(dict(
                ab=Guard(sb(f"ab_sb{i}", (128, 32), F32)),
                loga=Guard(sb(f"loga{i}", (128, H), F32R)),
                beta=Guard(sb(f"beta{i}", (128, H), F32)),
                ell_tm=Guard(sb(f"ell_tm{i}", (128, H), F32)),
                ell_fm=Guard(sb(f"ell_fm{i}", (16, CK), F32R)),
                gam=Guard(sb(f"gam{i}", (128, H), F32)),
                rnorm=Guard(sb(f"rnorm{i}", (128, 32), F32)),
                scl=Guard(sb(f"scl{i}", (128, 6 * H), F32)),
                negbg=Guard(sb(f"negbg{i}", (128, H), F32)),
            ))

        mu_sb = sb("mu_sb", (128, 16), F32)
        zt_g = Guard(None)
        fsc = Guard(sb("fsc", (128, NT), F32))

        cw_all = _CONSTS.shape[1]
        tkb = None
        td_prev = None
        for go in range(0, cw_all, NT):
            w = min(NT, cw_all - go)
            g = xbuf[0][(go // NT) % 4]
            td = p.op("y", lambda e, go=go, w=w, g=g:
                      e.dma_start(out=g.ap[:, :w], in_=dconst[:, go:go + w]),
                      deps=tmax(g.wr(), td_prev), sem="dl", inc=16)
            td_prev = td
            g.wrote(td)
            for nm, (osrc, odst, wd, which) in _CMAP.items():
                a, b = max(osrc, go), min(osrc + wd, go + w)
                if a >= b:
                    continue
                dstt = cst_b if which == "b" else cst_r
                tkb = p.op("v", lambda e, a=a, b=b, go=go, osrc=osrc, odst=odst, dstt=dstt, g=g:
                           e.tensor_copy(dstt[:, odst + a - osrc:odst + b - osrc],
                                         g.ap[:, a - go:b - go]),
                           deps=[td], inc=1)
                g.read(tkb)
        tkr = tkb

        _stage_sems = ("dl", "dxa", "dxb", "dca")

        def stage_load(src_view, dst_view, gidx, extra_deps=()):
            ncols = src_view.shape[1]
            dtoks = []
            pieces = []
            for o in range(0, ncols, NT):
                w = min(NT, ncols - o)
                j = (o // NT) % 4
                g = xbuf[0][j]
                td = p.op("y", lambda e, g=g, o=o, w=w: e.dma_start(
                    out=g.ap[:, :w], in_=src_view[:, o:o + w]),
                    deps=tmax(g.wr(), *extra_deps), sem=_stage_sems[j], inc=16)
                g.wrote(td)
                dtoks.append(td)
                pieces.append((g, o, w))
            gend = tmax(*dtoks)
            toks = []
            for (g, o, w) in pieces:
                tc = p.op("v", lambda e, g=g, o=o, w=w: e.tensor_copy(
                    dst_view[:, o:o + w], g.ap[:, :w]),
                    deps=tmax(gend, wg.w), inc=1)
                g.read(tc)
                toks.append(tc)
            return tmax(*toks)

        def load_phase_a_weights():
            toks = []
            for k in range(8):
                toks.append(stage_load(dwq[k * 128:(k + 1) * 128, :], wq_sb[k], k % 2))
                toks.append(stage_load(dwk[k * 128:(k + 1) * 128, :], wk_sb[k], k % 2))
                for hf in range(2):
                    toks.append(stage_load(dwv[k * 128:(k + 1) * 128, hf * 1024:(hf + 1) * 1024],
                                           wv_sb[k][:, hf * 1024:(hf + 1) * 1024], k % 2))
            for k in range(16):
                toks.append(stage_load(dwo[k * 128:(k + 1) * 128, :], wo_sb[k], k % 2))
            return toks

        wtoks = load_phase_a_weights()
        wg.wrote(wtoks[-1])
        wA_tok = tmax(*wtoks)

        tab_w = stage_load(dwab, wab_sb[:], 0)

        tcw = ("v", p.n["v"])
        tcw = tab_w
        for (srcd, nch_, off) in ((dcq, 8, 0), (dck, 8, 8), (dcv, 16, 16)):
            tcw = p.op("y", lambda e, srcd=srcd, nch_=nch_, off=off:
                       e.dma_start(out=convw[:, off * KW:(off + nch_) * KW], in_=srcd),
                       deps=tmax(tcw), sem="dl", inc=16)

        t_al = p.op("g", lambda e: e.dma_start(out=alog_r[:, :H], in_=dalog),
                    deps=tmax(tcw), sem="dg", inc=16)
        t_db = p.op("g", lambda e: e.dma_start(out=alog_r[:, H:], in_=ddtb),
                    deps=[t_al], sem="dg", inc=16)
        t_bc = p.op("t", lambda e: e.matmul(bank6.ap[:, :2 * H], cr("ones_row_r")[:1, :], alog_r[:],
                                            start=True, stop=True),
                    deps=tmax(t_al, t_db, tkr, bank6.wr()), inc=1)
        bank6.wrote(t_bc)
        tA1 = p.op("s", lambda e: e.activation(negA[:], bank6.ap[:, :H], AF.Exp),
                   deps=bank6.rd(), inc=1)
        t_negA = p.op("s", lambda e: e.mul(negA[:], negA[:], -1.0), deps=[tA1], inc=1)
        t_dtb = p.op("v", lambda e: e.tensor_copy(dtbb[:], bank6.ap[:, H:2 * H]),
                     deps=bank6.rd(), inc=1)
        bank6.read(t_negA)
        bank6.read(t_dtb)

        cstg = Guard(apool[:, 0:2 * KW * 128])
        for c in range(32):
            half = (c % 2) * KW * 128
            tb = None
            for j in range(KW):
                tb = p.op("v", lambda e, c=c, j=j, half=half:
                          e.tensor_scalar(cstg.ap[:, half + j * 128:half + (j + 1) * 128],
                                          cb("ident"), convw[:, c * KW + j:c * KW + j + 1],
                                          None, ALU.mult),
                          deps=tmax(tcw, tkb, cstg.wr()) if j == 0 else (), inc=1)
            cstg.wrote(tb)
            ts_ = p.op("g", lambda e, c=c, half=half:
                       e.dma_start(out=dcdiag[:, c * KW * 128:(c + 1) * KW * 128],
                                   in_=cstg.ap[:, half:half + KW * 128]),
                       deps=tmax(cstg.rd(), t_db if c == 0 else []), sem="dg", inc=16)
            cstg.read(ts_)
        cdiag_tok = ("dg", p.n["dg"])
        for pc in range(8):
            qkv[pc].rs.append(cdiag_tok)

        def ln_stats(src, eps):
            t2 = None
            for c in range(NCH):
                tq = p.op("v", lambda e, c=c: e.tensor_tensor(
                    xwork.ap[:, NT:2 * NT], src[c].ap[:], src[c].ap[:], ALU.mult),
                    deps=tmax(src[c].w, xwork.wr()), inc=1)
                src[c].read(tq)
                tb_ = p.op("v", lambda e, c=c: e.tensor_copy(xwork.ap[:, 0:NT], src[c].ap[:]),
                           deps=[tq], inc=1)
                xwork.wrote(tb_)
                src[c].read(tb_)
                t2 = p.op("t", lambda e, c=c: e.matmul(bank7.ap[:1, 0:2 * NT], cb("ones_col"),
                                                  xwork.ap[:],
                                                  start=(c == 0), stop=(c == NCH - 1)),
                          deps=tmax(tb_, tkb, bank7.wr() if c == 0 else []), inc=1)
                xwork.read(t2)
            bank7.wrote(t2)
            tm = p.op("v", lambda e: e.tensor_scalar(stat.ap[:1, 0:NT], bank7.ap[:1, 0:NT],
                                                     1.0 / D, None, ALU.mult),
                      deps=tmax(t2, stat.wr()), inc=1)
            tmsq = p.op("v", lambda e: e.tensor_tensor(stat.ap[:1, NT:2 * NT], stat.ap[:1, 0:NT],
                                                       stat.ap[:1, 0:NT], ALU.mult),
                        deps=[tm], inc=1)
            tvar = p.op("v", lambda e: e.scalar_tensor_tensor(
                stat.ap[:1, NT:2 * NT], bank7.ap[:1, NT:2 * NT], 1.0 / D,
                stat.ap[:1, NT:2 * NT], ALU.mult, ALU.subtract), deps=[tmsq], inc=1)
            bank7.read(tvar)
            tl = p.op("s", lambda e: e.activation(stat.ap[:1, NT:2 * NT],
                                                  stat.ap[:1, NT:2 * NT], AF.Ln,
                                                  bias=float(eps)),
                      deps=[tvar], inc=1)
            tr = p.op("s", lambda e: e.activation(stat.ap[:1, NT:2 * NT],
                                                  stat.ap[:1, NT:2 * NT], AF.Exp, scale=-0.5),
                      deps=[tl], inc=1)
            stat.wrote(tr)
            trb = p.op("v", lambda e: e.tensor_copy(rstd_bf.ap[:], stat.ap[:1, NT:2 * NT]),
                       deps=tmax(tr, rstd_bf.wr()), inc=1)
            rstd_bf.wrote(trb)
            tmr2 = p.op("v", lambda e: e.tensor_tensor(stat.ap[:1, 0:NT], stat.ap[:1, 0:NT],
                                                       stat.ap[:1, NT:2 * NT], ALU.mult),
                        deps=[tr], inc=1)
            tmrb = p.op("v", lambda e: e.tensor_copy(mr_bf.ap[:], stat.ap[:1, 0:NT]),
                        deps=tmax(tmr2, mr_bf.wr()), inc=1)
            mr_bf.wrote(tmrb)
            trbc = p.op("t", lambda e: e.matmul(bank6.ap[:, 0:NT], cb("ones_row")[:1, :],
                                                rstd_bf.ap[:], start=True, stop=True),
                        deps=tmax(trb, bank6.wr()), inc=1)
            bank6.wrote(trbc)
            rstd_bf.read(trbc)
            return trb, tmrb, trbc

        def bcast2(rowg, width):
            pass

        def load_x_tile(ti, dst, src_dram, extra=()):
            sem = "dxa" if ti % 2 == 0 else "dxb"
            tk_ = None
            for c in range(NCH):
                g = dst[c]
                tk_ = p.op("y", lambda e, c=c, ti=ti, g=g, src_dram=src_dram:
                           e.dma_start(out=g.ap[:], in_=src_dram[c * 128:(c + 1) * 128,
                                                             ti * NT:(ti + 1) * NT]),
                           deps=tmax(g.wr(), *extra), sem=sem, inc=16)
            for c in range(NCH):
                dst[c].wrote(tk_)
            return [tk_]

        def wslab(pc):
            if pc < 8:
                return [wq_sb[k][:, pc * 128:(pc + 1) * 128] for k in range(8)]
            if pc < 16:
                o = pc - 8
                return [wk_sb[k][:, o * 128:(o + 1) * 128] for k in range(8)]
            o = pc - 16
            return [wv_sb[k][:, o * 128:(o + 1) * 128] for k in range(8)]

        def attn_tile(ti):
            xb = xbuf[0]
            load_x_tile(ti, xb, dx)

            trb, tmr, trbc = ln_stats(xb, LN_EPS)

            tmrbc0 = p.op("t", lambda e: e.matmul(bank6.ap[:, NT:2 * NT], cb("ones_row")[:1, :],
                                                  mr_bf.ap[:], start=True, stop=True),
                          deps=tmax(tmr, mr_bf.w), inc=1)
            bank6.wrote(tmrbc0)
            mr_bf.read(tmrbc0)
            ytoks = []
            for c in range(NCH):
                ty = p.op("v", lambda e, c=c: e.tensor_tensor(
                    ybuf[c].ap[:], xb[c].ap[:], bank6.ap[:, 0:NT], ALU.mult),
                    deps=tmax(xb[c].w, tmrbc0, ybuf[c].wr()), inc=1)
                xb[c].read(ty)
                ty = p.op("v", lambda e, c=c: e.tensor_tensor(
                    ybuf[c].ap[:], ybuf[c].ap[:], bank6.ap[:, NT:2 * NT], ALU.subtract),
                    deps=[ty], inc=1)
                p.load["v"] += 700.0
                ybuf[c].wrote(ty)
                bank6.read(ty)
                ytoks.append(ty)

            for pc in range(32):
                pb = bank[pc % 2]
                slabs = wslab(pc)
                tmm = None
                for k in range(NCH):
                    tmm = p.op("t", lambda e, k=k, slabs=slabs, pb=pb: e.matmul(
                        pb.ap[:, 0:NT], slabs[k], ybuf[k].ap[:],
                        start=(k == 0), stop=(k == NCH - 1)),
                        deps=tmax(ytoks[k], wA_tok, pb.wr()) if k == 0 else [ytoks[k]],
                        inc=1)
                pb.wrote(tmm)
                for k in range(NCH):
                    ybuf[k].read(tmm)
                cd = cdg[pc % 2]
                tcd = p.op("y", lambda e, pc=pc, cd=cd:
                           e.dma_start(out=cd.ap[:],
                                       in_=dcdiag[:, pc * KW * 128:(pc + 1) * KW * 128]),
                           deps=tmax(cd.wr(), cdiag_tok),
                           sem="dca" if pc % 2 == 0 else "dcb", inc=16)
                cd.wrote(tcd)
                pg = pbuf[pc % 2]
                tp0 = p.op("v", lambda e, pc=pc, pg=pg: e.tensor_copy(
                    pg.ap[:, 0:KW - 1], ptail.ap[:, pc * (KW - 1):(pc + 1) * (KW - 1)]),
                    deps=tmax(ptail.w, pg.wr()), inc=1)
                ptail.read(tp0)
                tp = p.op("s", lambda e, pg=pg, pb=pb: e.copy(pg.ap[:, KW - 1:], pb.ap[:, 0:NT]),
                          deps=tmax(tmm, tp0), inc=1)
                pb.read(tp)
                pg.wrote(tp)
                tt = p.op("v", lambda e, pc=pc, pg=pg: e.tensor_copy(
                    ptail.ap[:, pc * (KW - 1):(pc + 1) * (KW - 1)], pg.ap[:, NT:NT + KW - 1]),
                    deps=[tp], inc=1)
                ptail.wrote(tt)
                pg.read(tt)
                cbnk = bank[2 + (pc % 2)]
                tc_ = None
                for j in range(KW):
                    tc_ = p.op("t", lambda e, j=j, cd=cd, pg=pg, cbnk=cbnk: e.matmul(
                        cbnk.ap[:, 0:NT], cd.ap[:, j * 128:(j + 1) * 128], pg.ap[:, j:j + NT],
                        start=(j == 0), stop=(j == KW - 1)),
                        deps=tmax(tcd, tp, cbnk.wr()) if j == 0 else (), inc=1)
                cbnk.wrote(tc_)
                cd.read(tc_)
                pg.read(tc_)
                tsl = p.op("s", lambda e, pc=pc, cbnk=cbnk: e.activation(
                    qkv[pc].ap[:], cbnk.ap[:, 0:NT], AF.Silu),
                    deps=tmax(tc_, qkv[pc].wr()), inc=1)
                cbnk.read(tsl)
                qkv[pc].wrote(tsl)

            if stop_after == "proj":
                return []
            for cc in range(NT // CK):
                chunk_recurrence(ti, cc, ytoks)
                if isinstance(stop_after, str) and stop_after.startswith("cc0") and cc == 0:
                    break
            if stop_after is not None:
                return []

            for oc in range(NCH):
                pb = bank[oc % 2]
                tmm = None
                for hv in range(16):
                    tmm = p.op("t", lambda e, hv=hv, oc=oc, pb=pb: e.matmul(
                        pb.ap[:, 0:NT], wo_sb[hv][:, oc * 128:(oc + 1) * 128],
                        apool[:, 8 * 1024 + hv * NT:8 * 1024 + (hv + 1) * NT],
                        start=(hv == 0), stop=(hv == 15)),
                        deps=tmax(on_fm[hv].w, pb.wr()) if hv == 0 else [on_fm[hv].w],
                        inc=1)
                pb.wrote(tmm)
                for hv in range(16):
                    on_fm[hv].read(tmm)
                te = p.op("s", lambda e, oc=oc, pb=pb: e.copy(ao_sb[oc].ap[:], pb.ap[:, 0:NT]),
                          deps=tmax(tmm, ao_sb[oc].wr()), inc=1)
                pb.read(te)
                ao_sb[oc].wrote(te)

            trb2, tmr2, trbc2 = ln_stats(ao_sb, LN_EPS)
            tmrbc = p.op("t", lambda e: e.matmul(bank6.ap[:, NT:2 * NT], cb("ones_row")[:1, :],
                                                 mr_bf.ap[:], start=True, stop=True),
                         deps=tmax(tmr2, mr_bf.w), inc=1)
            bank6.wrote(tmrbc)
            mr_bf.read(tmrbc)
            st_toks = []
            for c in range(NCH):
                t1 = p.op("v", lambda e, c=c: e.tensor_tensor(
                    ao_sb[c].ap[:], ao_sb[c].ap[:], bank6.ap[:, 0:NT], ALU.mult),
                    deps=tmax(ao_sb[c].w, tmrbc), inc=1)
                t2 = p.op("v", lambda e, c=c: e.tensor_tensor(
                    ao_sb[c].ap[:], ao_sb[c].ap[:], bank6.ap[:, NT:2 * NT], ALU.subtract),
                    deps=[t1], inc=1)
                bank6.read(t2)
                t3 = p.op("v", lambda e, c=c: e.tensor_tensor(
                    xb[c].ap[:], xb[c].ap[:], ao_sb[c].ap[:], ALU.add),
                    deps=tmax(t2, xb[c].w, xb[c].rs), inc=1)
                ao_sb[c].wrote(t3)
                xb[c].wrote(t3)
                ts_ = p.op("g", lambda e, c=c, ti=ti: e.dma_start(
                    out=dx1[c * 128:(c + 1) * 128, ti * NT:(ti + 1) * NT], in_=xb[c].ap[:]),
                    deps=[t3], sem="dga" if ti % 2 == 0 else "dgb", inc=16)
                xb[c].read(ts_)
                st_toks.append(ts_)
            for c in range(NCH):
                xb[c].rs = [st_toks[-1]]
            return st_toks

        RB = [0, 1, 2, 3, 4]
        rb_state = [0]

        def nxt():
            b = bank[RB[rb_state[0] % len(RB)]]
            rb_state[0] += 1
            return b

        import os as _os
        _FORCE = _os.environ.get("EW_FORCE") or None

        def _cost(kind, fd, bf, ps):
            c = {}
            if kind in ("tt", "stt"):
                accv = 2 if bf else 1
                c["v"] = ((120 if ps else 58) + fd / accv) / 0.96
                if not ps:
                    c["g"] = (300 + fd * 5.2) / 1.2
            elif kind in ("ts", "copy"):
                accv = (2 if ps else 4) if bf else 2
                c["v"] = ((120 if ps else 58) + fd / accv) / 0.96
                if not ps:
                    c["g"] = (300 + fd * 2.1) / 1.2
                c["s"] = ((172 if ps else 224) + fd / 2) / 1.2
            return c

        def ew_tt(out, a, b, op, deps, fd, bf=True, ps=False, eng=None):
            c = _cost("tt", fd, bf, ps)
            if _FORCE and _FORCE in c:
                eng = _FORCE
            if eng is None:
                eng = p.pick(c)
            else:
                p.load[eng] += c.get(eng, fd)
            return p.op(eng, lambda e, out=out, a=a, b=b, op=op: e.tensor_tensor(out, a, b, op),
                        deps=deps, inc=1)

        def ew_stt(out, a, s1, b, op0, op1, deps, fd, bf=True, ps=False):
            c = _cost("tt", fd, bf, ps)
            eng = _FORCE if (_FORCE and _FORCE in c) else p.pick(c)
            return p.op(eng, lambda e, out=out, a=a, s1=s1, b=b: e.scalar_tensor_tensor(
                out, a, s1, b, op0, op1), deps=deps, inc=1)

        def ew_ts(out, a, s1, op, deps, fd, s2=None, op2=None, bf=True, ps=False, eng=None):
            c = _cost("ts", fd, bf, ps)
            c.pop("s", None)
            if _FORCE and _FORCE in c:
                eng = _FORCE
            if eng is None:
                eng = p.pick(c)
            else:
                p.load[eng] += c.get(eng, fd)
            if eng == "s":
                return p.op("s", lambda e, out=out, a=a, s1=s1: e.activation(
                    out, a, AF.Copy, scale=s1), deps=deps, inc=1)
            if op2 is None:
                return p.op(eng, lambda e, out=out, a=a, s1=s1, op=op: e.tensor_scalar(
                    out, a, s1, None, op), deps=deps, inc=1)
            return p.op(eng, lambda e, out=out, a=a, s1=s1, s2=s2, op=op, op2=op2:
                        e.tensor_scalar(out, a, s1, s2, op, op2), deps=deps, inc=1)

        def ew_copy(out, a, deps, fd, bf=True, ps=False, eng=None):
            c = _cost("copy", fd, bf, ps)
            if _FORCE and _FORCE in c:
                eng = _FORCE
            if eng is None:
                eng = p.pick(c)
            else:
                p.load[eng] += c.get(eng, fd)
            if eng == "s":
                return p.op("s", lambda e, out=out, a=a: e.copy(out, a), deps=deps, inc=1)
            return p.op(eng, lambda e, out=out, a=a: e.tensor_copy(out, a), deps=deps, inc=1)

        def chunk_recurrence(ti, cc, ytoks):
            tof = cc * CK
            dk = dkt[cc % 2]
            ab_sb, loga, beta_g = dk["ab"], dk["loga"], dk["beta"]
            ell_tm, ell_fm, gam_tm = dk["ell_tm"], dk["ell_fm"], dk["gam"]
            rnorm, scl, negbg = dk["rnorm"], dk["scl"], dk["negbg"]

            tab = None
            for k in range(NCH):
                tab = p.op("t", lambda e, k=k: e.matmul(
                    bank7.ap[:, 0:32], ybuf[k].ap[:, tof:tof + CK],
                    wab_sb[:, k * 32:(k + 1) * 32],
                    start=(k == 0), stop=(k == NCH - 1)),
                    deps=tmax(ytoks[k], tab_w, bank7.wr()) if k == 0 else [ytoks[k]], inc=1)
            bank7.wrote(tab)
            for k in range(NCH):
                ybuf[k].read(tab)
            tabe = ew_copy(ab_sb.ap[:], bank7.ap[:, 0:32], tmax(tab, ab_sb.wr()), 32,
                           bf=False, ps=True)
            bank7.read(tabe)
            ab_sb.wrote(tabe)

            t1 = ew_tt(scl.ap[:, 0:H], ab_sb.ap[:, 0:H], dtbb[:], ALU.add,
                       tmax(tabe, t_dtb, scl.wr()), H, bf=False)
            t2 = p.op("s", lambda e: e.activation(scl.ap[:, H:2 * H], scl.ap[:, 0:H], AF.Abs),
                      deps=[t1], inc=1)
            t3 = p.op("s", lambda e: e.activation(scl.ap[:, 2 * H:3 * H], scl.ap[:, H:2 * H],
                                                  AF.Exp, scale=-1.0), deps=[t2], inc=1)
            t4 = p.op("s", lambda e: e.activation(scl.ap[:, 2 * H:3 * H], scl.ap[:, 2 * H:3 * H],
                                                  AF.Ln, bias=1.0), deps=[t3], inc=1)
            t5 = ew_ts(scl.ap[:, 3 * H:4 * H], scl.ap[:, 0:H], 0.0, ALU.max, [t1], H, bf=False)
            t6 = ew_tt(scl.ap[:, 2 * H:3 * H], scl.ap[:, 2 * H:3 * H], scl.ap[:, 3 * H:4 * H],
                       ALU.add, [t4, t5], H, bf=False, eng="v")
            tlg = ew_tt(loga.ap[:], scl.ap[:, 2 * H:3 * H], negA[:], ALU.mult,
                        tmax(t6, t_negA, loga.wr()), H, bf=False, eng="v")
            loga.wrote(tlg)
            scl.wrote(t6)
            tbt = p.op("s", lambda e: e.activation(beta_g.ap[:], ab_sb.ap[:, H:2 * H], AF.Sigmoid),
                       deps=tmax(tabe, beta_g.wr()), inc=1)
            tbt = p.op("s", lambda e: e.mul(beta_g.ap[:], beta_g.ap[:], 2.0), deps=[tbt], inc=1)
            beta_g.wrote(tbt)
            ab_sb.read(tbt)

            tc1 = p.op("t", lambda e: e.matmul(bank7.ap[:, 64:64 + H], cr("incl_u_r"), loga.ap[:],
                                               start=True, stop=True),
                       deps=tmax(tlg, tkr, bank7.wr()), inc=1)
            tc2 = p.op("t", lambda e: e.matmul(bank7.ap[:16, 128:128 + CK], loga.ap[:],
                                               cr("incl_u_r"), start=True, stop=True),
                       deps=tmax(tlg), inc=1)
            bank7.wrote(tc2)
            loga.read(tc2)
            te1 = ew_copy(ell_tm.ap[:], bank7.ap[:, 64:64 + H],
                          tmax(tc1, tc2, ell_tm.wr()), H, bf=False, ps=True, eng="v")
            ell_tm.wrote(te1)
            te2 = ew_copy(ell_fm.ap[:], bank7.ap[:16, 128:128 + CK],
                          tmax(tc1, tc2, ell_fm.wr()), CK, bf=False, ps=True, eng="v")
            ell_fm.wrote(te2)
            bank7.read(te1)
            bank7.read(te2)
            tg = p.op("s", lambda e: e.activation(gam_tm.ap[:], ell_tm.ap[:], AF.Exp),
                      deps=tmax(te1, gam_tm.wr()), inc=1)
            gam_tm.wrote(tg)
            tnb = ew_tt(negbg.ap[:], beta_g.ap[:], gam_tm.ap[:], ALU.mult,
                        tmax(tbt, tg, negbg.wr()), H, bf=False, eng="v")
            tnb = ew_ts(negbg.ap[:], negbg.ap[:], -1.0, ALU.mult, [tnb], H, bf=False)
            negbg.wrote(tnb)
            beta_g.read(tnb)
            gam_tm.read(tnb)

            ntk = None
            for qc in range(16):
                ng = nsqG[qc % 2]
                tq = ew_tt(ng.ap[:], qkv[qc].ap[:, tof:tof + CK],
                           qkv[qc].ap[:, tof:tof + CK], ALU.mult,
                           tmax(qkv[qc].w, ng.wr()), CK, eng="v")
                ng.wrote(tq)
                qkv[qc].read(tq)
                ntk = p.op("t", lambda e, qc=qc, ng=ng: e.matmul(
                    bank7.ap[:, 256 + qc * 2:256 + qc * 2 + 2], ng.ap[:],
                    cb("bdones"), start=True, stop=True),
                    deps=tmax(tq, tkb, bank7.wr() if qc == 0 else []), inc=1)
                ng.read(ntk)
            bank7.wrote(ntk)
            tn1 = p.op("s", lambda e: e.activation(rnorm.ap[:], bank7.ap[:, 256:288], AF.Ln),
                       deps=tmax(ntk, rnorm.wr()), inc=1)
            bank7.read(tn1)
            tn2 = p.op("s", lambda e: e.activation(rnorm.ap[:], rnorm.ap[:], AF.Exp, scale=-0.5),
                       deps=[tn1], inc=1)
            tn3 = ew_ts(rnorm.ap[:], rnorm.ap[:], 1.0e6, ALU.min, [tn2], 32, bf=False, eng="v")
            rnorm.wrote(tn3)

            if stop_after == "decay" or stop_after == "cc0decay":
                return
            nhp = 8
            if isinstance(stop_after, str) and "hp" in stop_after:
                import re as _re
                nhp = int(_re.search(r"hp(\d+)", stop_after).group(1))
            from collections import deque
            win = deque()
            nxt_hp = 0
            while win or nxt_hp < nhp:
                while len(win) < 2 and nxt_hp < nhp:
                    win.append(head_pair(ti, cc, tof, nxt_hp, dk, te1, te2))
                    nxt_hp += 1
                for ggen in list(win):
                    try:
                        next(ggen)
                    except StopIteration:
                        win.remove(ggen)

        def head_pair(ti, cc, tof, hp, dk, te1, te2):
            h0 = 2 * hp
            par = hp % 2
            ab_sb, loga, beta_g = dk["ab"], dk["loga"], dk["beta"]
            ell_tm, ell_fm, gam_tm = dk["ell_tm"], dk["ell_fm"], dk["gam"]
            rnorm, scl, negbg = dk["rnorm"], dk["scl"], dk["negbg"]
            sel = lambda L: L[par % len(L)]
            EB, XE, XE2 = sel(EbatP), sel(expEP), sel(expE2P)
            DI, LF = sel(dincP), sel(lfacP)
            KQ, KH, HN, VT, KA = sel(kqtmP), sel(khnP), sel(hnfP), sel(vtmP), sel(khatP)
            LTP, LTTP, MMP, RTP = sel(LtP), sel(LTtP), sel(mmtP), sel(RtmP)
            rsc, Ngt, h2p, h4p = sel(rscP), sel(NgtP), sel(h2pP), sel(h4pP)
            t0ap, Tgp, TTgp, Pgp = sel(t0apP), sel(TgpP), sel(TTgpP), sel(PgpP)
            Utmp, oscp, osqp, oms, ontm = sel(UtmpP), sel(oscpP), sel(osqpP), sel(omsP), sel(ontmP)

            tb_ = p.op("t", lambda e: e.matmul(bank6.ap[:, 0:CK],
                                               cr("onehot")[:16, h0 * 128:(h0 + 1) * 128],
                                               ell_fm.ap[:], start=True, stop=True),
                       deps=tmax(te2, tkr, bank6.wr()), inc=1)
            tb1 = p.op("t", lambda e: e.matmul(bank6.ap[:, CK:2 * CK],
                                               cr("onehot")[:16, (h0 + 1) * 128:(h0 + 2) * 128],
                                               ell_fm.ap[:], start=True, stop=True), inc=1)
            bank6.wrote(tb1)
            ell_fm.read(tb1)
            tE = None
            for i in range(2):
                tE = p.op("v", lambda e, i=i: e.tensor_scalar(
                    EB.ap[:, i * CK:(i + 1) * CK], bank6.ap[:, i * CK:(i + 1) * CK],
                    ell_tm.ap[:, h0 + i:h0 + i + 1], None, ALU.subtract),
                    deps=tmax(tb1, te1, EB.wr() if i == 0 else []), inc=1)
                ell_tm.read(tE)
            p.load["v"] += 400.0
            bank6.read(tE)
            EB.wrote(tE)
            yield
            tek = None
            for i in range(2):
                tek = p.op("s", lambda e, i=i: e.activation(
                    scl.ap[:, 4 * H + 4 * par + i:4 * H + 4 * par + i + 1],
                    EB.ap[:, i * CK + CK - 1:i * CK + CK], AF.Exp),
                    deps=tmax(tE, scl.wr() if i == 0 else []), inc=1)
                EB.read(tek)
            scl.wrote(tek)
            tm1 = ew_ts(rsc.ap[:, 0:2 * CK], EB.ap[:], 0.0, ALU.min,
                        tmax(tE, rsc.wr()), 2 * CK, bf=False)
            rsc.wrote(tm1)
            tx1 = p.op("s", lambda e: e.activation(XE.ap[:], rsc.ap[:, 0:2 * CK], AF.Exp),
                       deps=tmax(tm1, XE.wr()), inc=1)
            XE.wrote(tx1)
            rsc.read(tx1)
            tm2 = ew_ts(rsc.ap[:, 0:2 * CK], EB.ap[:], -1.0, ALU.mult,
                        tmax(tE, tx1), 2 * CK, s2=0.0, op2=ALU.min, bf=False)
            EB.read(tm2)
            rsc.wrote(tm2)
            tx2 = p.op("s", lambda e: e.activation(XE2.ap[:], rsc.ap[:, 0:2 * CK], AF.Exp),
                       deps=tmax(tm2, XE2.wr()), inc=1)
            XE2.wrote(tx2)
            rsc.read(tx2)
            yield
            td_ = ew_tt(DI.ap[:], XE.ap[:], cb("incl_u_dw"), ALU.mult,
                        tmax(tx1, tkb, DI.wr()), 2 * CK)
            DI.wrote(td_)
            XE.read(td_)
            tl_ = ew_tt(LF.ap[:], XE2.ap[:], cb("strict_l_dw"), ALU.mult,
                        tmax(tx2, tkb, LF.wr()), 2 * CK)
            LF.wrote(tl_)
            XE2.read(tl_)

            yield
            ttr = p.op("t", lambda e: e.transpose(
                bank5.ap[:, 0:CK], qkv[8 + hp].ap[:, tof:tof + CK], cb("ident")),
                deps=tmax(qkv[8 + hp].w, tkb, bank5.wr()), inc=1)
            qkv[8 + hp].read(ttr)
            ttr = p.op("t", lambda e: e.transpose(
                bank5.ap[:, CK:2 * CK], qkv[hp].ap[:, tof:tof + CK], cb("ident")), inc=1)
            qkv[hp].read(ttr)
            bank5.wrote(ttr)
            tkq = ew_copy(KQ.ap[:], bank5.ap[:, 0:256], tmax(ttr, KQ.wr()), 2 * CK, ps=True)
            bank5.read(tkq)
            KQ.wrote(tkq)
            yield
            kh_toks = []
            for i in range(2):
                h = h0 + i
                ta_ = ew_ts(KH.ap[:, i * 64:(i + 1) * 64], KQ.ap[:, i * 64:(i + 1) * 64],
                            rnorm.ap[:, 16 + h:17 + h], ALU.mult,
                            tmax(tkq, rnorm.w, KH.wr() if i == 0 else []), 64)
                tb2 = ew_ts(KH.ap[:, 128 + i * 64:128 + (i + 1) * 64],
                            KQ.ap[:, 128 + i * 64:128 + (i + 1) * 64],
                            rnorm.ap[:, h:h + 1], ALU.mult, tmax(tkq, rnorm.w), 64)
                rnorm.read(ta_)
                rnorm.read(tb2)
                KQ.read(ta_)
                KQ.read(tb2)
                kh_toks += [ta_, tb2]
            kh_done = tmax(*kh_toks)
            KH.w = None
            KH.rs = list(kh_done)
            yield
            ttr2 = None
            for i in range(2):
                for j in range(2):
                    ttr2 = p.op("t", lambda e, i=i, j=j: e.transpose(
                        bank5.ap[64 * i:64 * i + 64, 256 + j * 128:256 + (j + 1) * 128],
                        KH.ap[:, j * 128 + i * 64:j * 128 + (i + 1) * 64], cb("ident")),
                        deps=tmax(kh_done, bank5.wr()) if (i == 0 and j == 0) else (), inc=1)
            bank5.wrote(ttr2)
            thn = ew_copy(HN.ap[:], bank5.ap[:, 256:512], tmax(ttr2, HN.wr()), 2 * CK, ps=True)
            bank5.read(thn)
            HN.wrote(thn)
            yield
            ttv = None
            for i in range(2):
                h = h0 + i
                ttv = p.op("t", lambda e, i=i, h=h: e.transpose(
                    bank5.ap[:, 512 + i * 128:512 + (i + 1) * 128],
                    qkv[16 + h].ap[:, tof:tof + CK], cb("ident")),
                    deps=tmax(qkv[16 + h].w, bank5.wr() if i == 0 else []), inc=1)
                qkv[16 + h].read(ttv)
            bank5.wrote(ttv)
            tve = ew_copy(VT.ap[:], bank5.ap[:, 512:768], tmax(ttv, VT.wr()), 2 * CK, ps=True)
            bank5.read(tve)
            VT.wrote(tve)
            yield
            tkh = None
            for i in range(2):
                tkh = ew_ts(KA.ap[:, i * 64:(i + 1) * 64], KH.ap[:, i * 64:(i + 1) * 64],
                            scl.ap[:, 4 * H + 4 * par + i:4 * H + 4 * par + i + 1], ALU.mult,
                            tmax(kh_done, tek, KA.wr() if i == 0 else []), 64)
                scl.read(tkh)
                KH.read(tkh)
            KA.wrote(tkh)
            if stop_after is not None and "tp" in str(stop_after):
                return

            if stop_after is not None and "xpre" in str(stop_after):
                return
            yield
            kn = [HN.ap[64 * i:64 * i + 64, 0:CK] for i in range(2)]
            qn = [HN.ap[64 * i:64 * i + 64, CK:2 * CK] for i in range(2)]
            bgr = nxt()
            import os as _os2
            _gm = _os2.environ.get("GRAM_MODE", "")
            _irange = {"i0": (0,), "i1": (1,)}.get(_gm, (0, 1))
            tg2 = None
            for i in _irange:
                tg2 = p.op("t", lambda e, i=i, bgr=bgr: e.matmul(
                    bgr.ap[:, i * CK:(i + 1) * CK], kn[i], kn[i], start=True, stop=True),
                    deps=tmax(thn, bgr.wr()) if i == 0 else (), inc=1, pbase=64 * i)
                tg2 = p.op("t", lambda e, i=i, bgr=bgr: e.matmul(
                    bgr.ap[:, 2 * CK + i * CK:2 * CK + (i + 1) * CK], kn[i], qn[i],
                    start=True, stop=True), inc=1, pbase=64 * i)
            bgr.wrote(tg2)
            HN.read(tg2)
            if stop_after is not None and "xgr" in str(stop_after):
                return
            yield
            tL = ew_tt(LTP.ap[:], LF.ap[:], bgr.ap[:, 0:2 * CK], ALU.mult,
                       tmax(tl_, tg2, LTP.wr()), 2 * CK, ps=True, eng="v")
            LF.read(tL)
            bgr.read(tL)
            for i in range(2):
                h = h0 + i
                tL = ew_ts(LTP.ap[:, i * CK:(i + 1) * CK], LTP.ap[:, i * CK:(i + 1) * CK],
                           beta_g.ap[:, h:h + 1], ALU.mult, [tL], CK)
                beta_g.read(tL)
            LTP.wrote(tL)
            tMm = ew_tt(MMP.ap[:], DI.ap[:], bgr.ap[:, 2 * CK:4 * CK], ALU.mult,
                        tmax(td_, tg2, MMP.wr()), 2 * CK, ps=True, eng="v")
            MMP.wrote(tMm)
            DI.read(tMm)
            bgr.read(tMm)
            if stop_after is not None and "xlm" in str(stop_after):
                return
            yield
            tLT = None
            for i in range(2):
                tLT = p.op("t", lambda e, i=i: e.transpose(
                    bank5.ap[:, 768 + i * 128:768 + (i + 1) * 128],
                    LTP.ap[:, i * CK:(i + 1) * CK], cb("ident")),
                    deps=tmax(tL, tkb, bank5.wr()) if i == 0 else (), inc=1)
            bank5.wrote(tLT)
            LTP.read(tLT)
            tLTe = ew_copy(LTTP.ap[:], bank5.ap[:, 768:1024], tmax(tLT, LTTP.wr()), 2 * CK,
                           ps=True)
            bank5.read(tLTe)
            LTTP.wrote(tLTe)

            if stop_after is not None and "xlt" in str(stop_after):
                return
            yield
            tN = ew_tt(Ngt.ap[:, 0:2 * CK], LTTP.ap[:], cb("m8u_dw"), ALU.mult,
                       tmax(tLTe, Ngt.wr()), 2 * CK)
            LTTP.read(tN)
            tNT = ew_tt(Ngt.ap[:, 2 * CK:4 * CK], LTP.ap[:], cb("m8l_dw"), ALU.mult,
                        tmax(tL, Ngt.wr()), 2 * CK)
            LTP.read(tNT)
            ngt_done = tmax(tN, tNT)
            Ngt.w = None
            Ngt.rs = list(ngt_done)
            if stop_after is not None and "xnn" in str(stop_after):
                return
            yield
            bh = nxt()
            tmm2 = None
            for i in range(2):
                tmm2 = p.op("t", lambda e, i=i, bh=bh: e.matmul(
                    bh.ap[:, i * CK:(i + 1) * CK], Ngt.ap[:, 2 * CK + i * CK:2 * CK + (i + 1) * CK],
                    Ngt.ap[:, i * CK:(i + 1) * CK], start=True, stop=True),
                    deps=tmax(ngt_done, bh.wr()) if i == 0 else (), inc=1)
                tmm2 = p.op("t", lambda e, i=i, bh=bh: e.matmul(
                    bh.ap[:, 2 * CK + i * CK:2 * CK + (i + 1) * CK],
                    Ngt.ap[:, i * CK:(i + 1) * CK],
                    Ngt.ap[:, 2 * CK + i * CK:2 * CK + (i + 1) * CK],
                    start=True, stop=True), inc=1)
            bh.wrote(tmm2)
            Ngt.read(tmm2)
            th2 = ew_copy(h2p.ap[:], bh.ap[:, 0:4 * CK], tmax(tmm2, h2p.wr()), 4 * CK, ps=True)
            bh.read(th2)
            h2p.wrote(th2)
            yield
            bh = nxt()
            tmm4 = None
            for i in range(2):
                tmm4 = p.op("t", lambda e, i=i, bh=bh: e.matmul(
                    bh.ap[:, i * CK:(i + 1) * CK], h2p.ap[:, 2 * CK + i * CK:2 * CK + (i + 1) * CK],
                    h2p.ap[:, i * CK:(i + 1) * CK], start=True, stop=True),
                    deps=tmax(th2, bh.wr()) if i == 0 else (), inc=1)
                tmm4 = p.op("t", lambda e, i=i, bh=bh: e.matmul(
                    bh.ap[:, 2 * CK + i * CK:2 * CK + (i + 1) * CK],
                    h2p.ap[:, i * CK:(i + 1) * CK],
                    h2p.ap[:, 2 * CK + i * CK:2 * CK + (i + 1) * CK],
                    start=True, stop=True), inc=1)
            bh.wrote(tmm4)
            h2p.read(tmm4)
            th4 = ew_copy(h4p.ap[:], bh.ap[:, 0:4 * CK], tmax(tmm4, h4p.wr()), 4 * CK, ps=True)
            bh.read(th4)
            h4p.wrote(th4)
            yield
            bh = nxt()
            tmm24 = None
            for i in range(2):
                tmm24 = p.op("t", lambda e, i=i, bh=bh: e.matmul(
                    bh.ap[:, i * CK:(i + 1) * CK], h2p.ap[:, 2 * CK + i * CK:2 * CK + (i + 1) * CK],
                    h4p.ap[:, i * CK:(i + 1) * CK], start=True, stop=True),
                    deps=tmax(th4, bh.wr()) if i == 0 else (), inc=1)
                tmm24 = p.op("t", lambda e, i=i, bh=bh: e.matmul(
                    bh.ap[:, 2 * CK + i * CK:2 * CK + (i + 1) * CK],
                    h4p.ap[:, i * CK:(i + 1) * CK],
                    h2p.ap[:, 2 * CK + i * CK:2 * CK + (i + 1) * CK],
                    start=True, stop=True), inc=1)
            bh.wrote(tmm24)
            h2p.read(tmm24)
            h4p.read(tmm24)
            if stop_after is not None and "xh24" in str(stop_after):
                return
            yield
            ta1 = ew_tt(t0ap.ap[:], h2p.ap[:], h4p.ap[:], ALU.add,
                        tmax(th2, th4, t0ap.wr()), 4 * CK)
            ta2 = ew_tt(t0ap.ap[:], t0ap.ap[:], bh.ap[:, 0:4 * CK], ALU.add,
                        tmax(ta1, tmm24), 4 * CK, ps=True, eng="v")
            bh.read(ta2)
            ta3 = None
            for j in range(2):
                ta3 = ew_tt(t0ap.ap[:, j * 2 * CK:(j + 1) * 2 * CK],
                            t0ap.ap[:, j * 2 * CK:(j + 1) * 2 * CK],
                            cb("ident_dw"), ALU.add, [ta2] if j == 0 else [ta3], 2 * CK)
            t0ap.wrote(ta3)
            yield
            bh = nxt()
            tmc = None
            for i in range(2):
                tmc = p.op("t", lambda e, i=i, bh=bh: e.matmul(
                    bh.ap[:, i * CK:(i + 1) * CK],
                    Ngt.ap[:, 2 * CK + i * CK:2 * CK + (i + 1) * CK],
                    t0ap.ap[:, i * CK:(i + 1) * CK], start=True, stop=True),
                    deps=tmax(ta3, bh.wr()) if i == 0 else (), inc=1)
                tmc = p.op("t", lambda e, i=i, bh=bh: e.matmul(
                    bh.ap[:, 2 * CK + i * CK:2 * CK + (i + 1) * CK],
                    t0ap.ap[:, i * CK:(i + 1) * CK],
                    Ngt.ap[:, 2 * CK + i * CK:2 * CK + (i + 1) * CK],
                    start=True, stop=True), inc=1)
            bh.wrote(tmc)
            Ngt.read(tmc)
            t0ap.read(tmc)
            tT0 = ew_tt(Tgp.ap[:], t0ap.ap[:, 0:2 * CK], bh.ap[:, 0:2 * CK], ALU.subtract,
                        tmax(tmc, Tgp.wr()), 2 * CK, ps=True, eng="v")
            Tgp.wrote(tT0)
            tT0T = ew_tt(TTgp.ap[:], t0ap.ap[:, 2 * CK:4 * CK], bh.ap[:, 2 * CK:4 * CK],
                         ALU.subtract, tmax(tmc, TTgp.wr()), 2 * CK, ps=True, eng="v")
            TTgp.wrote(tT0T)
            bh.read(tT0)
            bh.read(tT0T)
            t0ap.read(tT0T)
            if stop_after is not None and "tbase" in str(stop_after):
                return

            yield
            for li, m in enumerate((8, 16, 32, 64)):
                yield
                last = (m == 64)
                tRk = ew_tt(Ngt.ap[:, 0:2 * CK], LTTP.ap[:], cb(f"r{m}u_dw"), ALU.mult,
                            tmax(tLTe, Ngt.wr()), 2 * CK)
                LTTP.read(tRk)
                rk_done = tmax(tRk)
                Ngt.w = None
                Ngt.rs = list(rk_done)
                yield
                bh = nxt()
                tmP = None
                for i in range(2):
                    tmP = p.op("t", lambda e, i=i, bh=bh: e.matmul(
                        bh.ap[:, 2 * CK + i * CK:2 * CK + (i + 1) * CK],
                        Ngt.ap[:, i * CK:(i + 1) * CK], TTgp.ap[:, i * CK:(i + 1) * CK],
                        start=True, stop=True),
                        deps=tmax(rk_done, TTgp.w, bh.wr()) if i == 0 else (), inc=1)
                bh.wrote(tmP)
                Ngt.read(tmP)
                TTgp.read(tmP)
                tPe = ew_copy(Pgp.ap[:, 2 * CK:4 * CK], bh.ap[:, 2 * CK:4 * CK],
                              tmax(tmP, Pgp.wr()), 2 * CK, ps=True)
                bh.read(tPe)
                Pgp.wrote(tPe)
                yield
                bh = nxt()
                tmU = None
                for i in range(2):
                    tmU = p.op("t", lambda e, i=i, bh=bh: e.matmul(
                        bh.ap[:, i * CK:(i + 1) * CK],
                        Pgp.ap[:, 2 * CK + i * CK:2 * CK + (i + 1) * CK],
                        Tgp.ap[:, i * CK:(i + 1) * CK], start=True, stop=True),
                        deps=tmax(tPe, Tgp.w, bh.wr()) if i == 0 else (), inc=1)
                    if not last:
                        tmU = p.op("t", lambda e, i=i, bh=bh: e.matmul(
                            bh.ap[:, 2 * CK + i * CK:2 * CK + (i + 1) * CK],
                            Tgp.ap[:, i * CK:(i + 1) * CK],
                            Pgp.ap[:, 2 * CK + i * CK:2 * CK + (i + 1) * CK],
                            start=True, stop=True), inc=1)
                bh.wrote(tmU)
                Pgp.read(tmU)
                Tgp.read(tmU)
                tTn = ew_tt(Tgp.ap[:], Tgp.ap[:], bh.ap[:, 0:2 * CK], ALU.subtract,
                            tmax(tmU, Tgp.wr()), 2 * CK, ps=True, eng="v")
                Tgp.wrote(tTn)
                bh.read(tTn)
                if not last:
                    tTTn = ew_tt(TTgp.ap[:], TTgp.ap[:], bh.ap[:, 2 * CK:4 * CK], ALU.subtract,
                                 tmax(tmU, TTgp.wr()), 2 * CK, ps=True, eng="v")
                    TTgp.wrote(tTTn)
                    bh.read(tTTn)
            if stop_after is not None and "tlvl" in str(stop_after):
                return

            yield
            bS = nxt()
            tKS = None
            for i in range(2):
                h = h0 + i
                tKS = p.op("t", lambda e, i=i, h=h, bS=bS: e.matmul(
                    bS.ap[:, i * CK:(i + 1) * CK], kn[i], S_b[h].ap, start=True, stop=True),
                    deps=tmax(thn, S_b[h].w, bS.wr()) if i == 0 else tmax(S_b[h].w), inc=1,
                    pbase=64 * i)
                S_b[h].read(tKS)
                tKS = p.op("t", lambda e, i=i, h=h, bS=bS: e.matmul(
                    bS.ap[:, 2 * CK + i * CK:2 * CK + (i + 1) * CK], qn[i], S_b[h].ap,
                    start=True, stop=True), inc=1, pbase=64 * i)
                S_b[h].read(tKS)
            bS.wrote(tKS)
            HN.read(tKS)
            yield
            tR = None
            for i in range(2):
                h = h0 + i
                tR = ew_stt(RTP.ap[:, i * CK:(i + 1) * CK], bS.ap[:, i * CK:(i + 1) * CK],
                            negbg.ap[:, h:h + 1], VT.ap[:, i * CK:(i + 1) * CK],
                            ALU.mult, ALU.add,
                            tmax(tKS, tve, negbg.w, RTP.wr() if i == 0 else []), CK, ps=True)
                VT.read(tR)
                negbg.read(tR)
                bS.read(tR)
            RTP.wrote(tR)
            yield
            tsc = None
            for i in range(2):
                h = h0 + i
                tsc = p.op("v", lambda e, i=i, h=h, bS=bS: e.tensor_scalar(
                    oscp.ap[:, i * CK:(i + 1) * CK], bS.ap[:, 2 * CK + i * CK:2 * CK + (i + 1) * CK],
                    gam_tm.ap[:, h:h + 1], None, ALU.mult),
                    deps=tmax(tKS, gam_tm.w, oscp.wr() if i == 0 else []), inc=1)
                gam_tm.read(tsc)
                bS.read(tsc)
            p.load["v"] += 500.0
            oscp.wrote(tsc)
            yield
            bU = nxt()
            tU = None
            for i in range(2):
                tU = p.op("t", lambda e, i=i, bU=bU: e.matmul(
                    bU.ap[:, i * CK:(i + 1) * CK], Tgp.ap[:, i * CK:(i + 1) * CK],
                    RTP.ap[:, i * CK:(i + 1) * CK], start=True, stop=True),
                    deps=tmax(tR, Tgp.w, bU.wr()) if i == 0 else (), inc=1)
            bU.wrote(tU)
            Tgp.read(tU)
            RTP.read(tU)
            tUe = ew_copy(Utmp.ap[:], bU.ap[:, 0:2 * CK], tmax(tU, Utmp.wr()), 2 * CK, ps=True)
            bU.read(tUe)
            Utmp.wrote(tUe)
            yield
            bM = nxt()
            tO = None
            for i in range(2):
                tO = p.op("t", lambda e, i=i, bM=bM: e.matmul(
                    bM.ap[:, i * CK:(i + 1) * CK], MMP.ap[:, i * CK:(i + 1) * CK],
                    Utmp.ap[:, i * CK:(i + 1) * CK], start=True, stop=True),
                    deps=tmax(tUe, tMm, bM.wr()) if i == 0 else (), inc=1)
            bM.wrote(tO)
            MMP.read(tO)
            Utmp.read(tO)
            tOa = ew_tt(oscp.ap[:], oscp.ap[:], bM.ap[:, 0:2 * CK], ALU.add,
                        tmax(tO, tsc), 2 * CK, bf=False, ps=True, eng="v")
            bM.read(tOa)
            oscp.wrote(tOa)
            yield
            tq2 = p.op("s", lambda e: e.activation(osqp.ap[:], oscp.ap[:], AF.Square),
                       deps=tmax(tOa, osqp.wr()), inc=1)
            osqp.wrote(tq2)
            tms = None
            for i in range(2):
                tms = p.op("v", lambda e, i=i: e.reduce_sum(
                    oms.ap[:, i:i + 1], osqp.ap[:, i * CK:(i + 1) * CK], AX.X),
                    deps=tmax(tq2, oms.wr()) if i == 0 else (), inc=1)
            p.load["v"] += 400.0
            osqp.read(tms)
            tln = p.op("s", lambda e: e.activation(oms.ap[:, 2:4], oms.ap[:, 0:2], AF.Ln,
                                                   scale=1.0 / DVH, bias=EPS),
                       deps=[tms], inc=1)
            trs = p.op("s", lambda e: e.activation(oms.ap[:, 2:4], oms.ap[:, 2:4], AF.Exp,
                                                   scale=-0.5), deps=[tln], inc=1)
            oms.wrote(trs)
            tno = None
            for i in range(2):
                tno = ew_ts(ontm.ap[:, i * CK:(i + 1) * CK], oscp.ap[:, i * CK:(i + 1) * CK],
                            oms.ap[:, 2 + i:3 + i], ALU.mult,
                            tmax(trs, tOa, ontm.wr() if i == 0 else []), CK, bf=False)
                oms.read(tno)
                oscp.read(tno)
            ontm.wrote(tno)
            yield
            bSt = nxt()
            tS = None
            for i in range(2):
                tS = p.op("t", lambda e, i=i, bSt=bSt: e.matmul(
                    bSt.ap[64 * i:64 * i + 64, i * CK:(i + 1) * CK], KA.ap[:, i * 64:(i + 1) * 64],
                    Utmp.ap[:, i * CK:(i + 1) * CK], start=True, stop=True),
                    deps=tmax(tkh, tUe, bSt.wr()) if i == 0 else (), inc=1)
            bSt.wrote(tS)
            KA.read(tS)
            Utmp.read(tS)
            for i in range(2):
                h = h0 + i
                hb = 64 * (h % 2)
                tgc = ew_tt(scl.ap[:, 4 * H + 4 * par + 2 + i:4 * H + 4 * par + 3 + i],
                            scl.ap[:, 4 * H + 4 * par + i:4 * H + 4 * par + i + 1],
                            gam_tm.ap[:, h:h + 1], ALU.mult,
                            tmax(tek, gam_tm.w, scl.wr() if i == 0 else []), 1, bf=False,
                            eng="v")
                gam_tm.read(tgc)
                scl.wrote(tgc)
                tSu = p.op("v", lambda e, i=i, h=h, hb=hb, bSt=bSt, par=par: e.scalar_tensor_tensor(
                    S_f[h].ap, S_f[h].ap,
                    scl.ap[hb:hb + 64, 4 * H + 4 * par + 2 + i:4 * H + 4 * par + 3 + i],
                    bSt.ap[hb:hb + 64, i * CK:(i + 1) * CK], ALU.mult, ALU.add),
                    deps=tmax(tS, tgc, S_f[h].wr()), inc=1)
                p.load["v"] += 300.0
                scl.read(tSu)
                bSt.read(tSu)
                S_f[h].wrote(tSu)
            tSb = ew_copy(S_bf[:, hp * DVH:(hp + 1) * DVH], S_sb[:, hp * DVH:(hp + 1) * DVH],
                          tmax(S_f[h0].w, S_f[h0 + 1].w,
                               S_b[h0].wr(), S_b[h0 + 1].wr()), DVH)
            S_f[h0].read(tSb)
            S_f[h0 + 1].read(tSb)
            S_b[h0].wrote(tSb)
            S_b[h0 + 1].wrote(tSb)
            yield
            ttn = None
            for i in range(2):
                ttn = p.op("t", lambda e, i=i: e.transpose(
                    bank5.ap[:, i * 128:(i + 1) * 128],
                    ontm.ap[:, i * DVH:(i + 1) * DVH], cb("ident")),
                    deps=tmax(tno, tkb, bank5.wr() if i == 0 else []), inc=1)
            bank5.wrote(ttn)
            ontm.read(ttn)
            toe = None
            for i in range(2):
                h = h0 + i
                toe = ew_copy(on_fm[h].ap[:, tof:tof + CK],
                              bank5.ap[:, i * 128:(i + 1) * 128],
                              tmax(ttn, on_fm[h].wr()), CK, ps=True)
                on_fm[h].wrote(toe)
            bank5.read(toe)

        def load_phase_b_weights(dep):
            toks = []
            for k in range(8):
                toks.append(stage_load(dwr[k * 128:(k + 1) * 128, :], wr_sb[k], k % 2, (dep,)))
                toks.append(stage_load(dwkf[k * 128:(k + 1) * 128, :], wkf_sb[k], k % 2))
                toks.append(stage_load(dwvf[k * 128:(k + 1) * 128, :], wvf_sb[k], k % 2))
            return tmax(*toks)

        def ffn_tile(ti, wB_tok):
            xb = xbuf[0]
            if ti > 0:
                load_x_tile(ti, xb, dx1)

            trb, tmr, trbc = ln_stats(xb, LN_EPS)
            tmrbc = p.op("t", lambda e: e.matmul(bank6.ap[:, NT:2 * NT], cb("ones_row")[:1, :],
                                                 mr_bf.ap[:], start=True, stop=True),
                         deps=tmax(tmr, mr_bf.w), inc=1)
            bank6.wrote(tmrbc)
            mr_bf.read(tmrbc)
            ztoks = []
            for c in range(NCH):
                zt = p.op("v", lambda e, c=c: e.tensor_copy(
                    z_sb[c].ap[:, 0:1], z_sb[c].ap[:, NT:NT + 1]),
                    deps=tmax(z_sb[c].w, z_sb[c].rs), inc=1)
                t1 = p.op("v", lambda e, c=c: e.tensor_tensor(
                    ao_sb[c].ap[:], xb[c].ap[:], bank6.ap[:, 0:NT], ALU.mult),
                    deps=tmax(xb[c].w, tmrbc, ao_sb[c].wr()), inc=1)
                xb[c].read(t1)
                t2 = p.op("v", lambda e, c=c: e.tensor_tensor(
                    z_sb[c].ap[:, 1:NT + 1], ao_sb[c].ap[:], bank6.ap[:, NT:2 * NT],
                    ALU.subtract),
                    deps=tmax(t1, zt), inc=1)
                ao_sb[c].wrote(t1)
                ao_sb[c].read(t2)
                bank6.read(t2)
                z_sb[c].wrote(t2)
                ztoks.append(t2)
            mixtoks = []
            for c in range(NCH):
                td_ = p.op("v", lambda e, c=c: e.tensor_tensor(
                    fsc.ap[:], z_sb[c].ap[:, 1:NT + 1], z_sb[c].ap[:, 0:NT], ALU.subtract),
                    deps=tmax(ztoks[c], fsc.wr()), inc=1)
                fsc.wrote(td_)
                z_sb[c].read(td_)
                t1 = p.op("v", lambda e, c=c: e.scalar_tensor_tensor(
                    xr_sb[c].ap[:], fsc.ap[:], mu_sb[:, c:c + 1], z_sb[c].ap[:, 0:NT],
                    ALU.mult, ALU.add),
                    deps=tmax(td_, xr_sb[c].wr()), inc=1)
                xr_sb[c].wrote(t1)
                t2 = p.op("v", lambda e, c=c: e.scalar_tensor_tensor(
                    xk_sb[c].ap[:], fsc.ap[:], mu_sb[:, 8 + c:9 + c], z_sb[c].ap[:, 0:NT],
                    ALU.mult, ALU.add),
                    deps=tmax(td_, xk_sb[c].wr()), inc=1)
                xk_sb[c].wrote(t2)
                fsc.read(t2)
                z_sb[c].read(t2)
                mixtoks.append((t1, t2))
            for oc in range(NCH):
                pb = bank[oc % 2]
                tmm = None
                for k in range(NCH):
                    tmm = p.op("t", lambda e, k=k, oc=oc, pb=pb: e.matmul(
                        pb.ap[:, 0:NT], wr_sb[k][:, oc * 128:(oc + 1) * 128], xr_sb[k].ap[:],
                        start=(k == 0), stop=(k == NCH - 1)),
                        deps=tmax(mixtoks[k][0], wB_tok, pb.wr()) if k == 0 else [mixtoks[k][0]],
                        inc=1)
                pb.wrote(tmm)
                for k in range(NCH):
                    xr_sb[k].read(tmm)
                tr_ = p.op("s", lambda e, oc=oc, pb=pb: e.activation(
                    r_sb[oc].ap[:], pb.ap[:, 0:NT], AF.Sigmoid),
                    deps=tmax(tmm, r_sb[oc].wr()), inc=1)
                pb.read(tr_)
                r_sb[oc].wrote(tr_)
            for oc in range(NCH):
                pb = bank[2 + oc % 2]
                tmm = None
                for k in range(NCH):
                    tmm = p.op("t", lambda e, k=k, oc=oc, pb=pb: e.matmul(
                        pb.ap[:, 0:NT], wkf_sb[k][:, oc * 128:(oc + 1) * 128], xk_sb[k].ap[:],
                        start=(k == 0), stop=(k == NCH - 1)),
                        deps=tmax(mixtoks[k][1], wB_tok, pb.wr()) if k == 0 else [mixtoks[k][1]],
                        inc=1)
                pb.wrote(tmm)
                for k in range(NCH):
                    xk_sb[k].read(tmm)
                trl = p.op("s", lambda e, pb=pb: e.activation(fsc.ap[:], pb.ap[:, 0:NT], AF.Relu),
                           deps=tmax(tmm, fsc.wr()), inc=1)
                pb.read(trl)
                fsc.wrote(trl)
                tsq = p.op("v", lambda e, oc=oc: e.tensor_tensor(
                    ka_sb[oc].ap[:], fsc.ap[:], fsc.ap[:], ALU.mult),
                    deps=tmax(trl, ka_sb[oc].wr()), inc=1)
                fsc.read(tsq)
                ka_sb[oc].wrote(tsq)
            for oc in range(NCH):
                pb = bank[4] if oc % 2 == 0 else bank[2 + (oc + 1) % 2]
                pb = bank[4 - (oc % 2)]
                tmm = None
                for k in range(NCH):
                    tmm = p.op("t", lambda e, k=k, oc=oc, pb=pb: e.matmul(
                        pb.ap[:, 0:NT], wvf_sb[k][:, oc * 128:(oc + 1) * 128], ka_sb[k].ap[:],
                        start=(k == 0), stop=(k == NCH - 1)),
                        deps=tmax(ka_sb[k].w, pb.wr()) if k == 0 else [ka_sb[k].w], inc=1)
                pb.wrote(tmm)
                for k in range(NCH):
                    ka_sb[k].read(tmm)
                tev = p.op("v", lambda e, oc=oc, pb=pb: e.tensor_tensor(
                    ao_sb[oc].ap[:], pb.ap[:, 0:NT], r_sb[oc].ap[:], ALU.mult),
                    deps=tmax(tmm, r_sb[oc].w, ao_sb[oc].wr()), inc=1)
                pb.read(tev)
                r_sb[oc].read(tev)
                ao_sb[oc].wrote(tev)
            trb4, tmr4, trbc4 = ln_stats(ao_sb, LN_EPS)
            tmrbc4 = p.op("t", lambda e: e.matmul(bank6.ap[:, NT:2 * NT], cb("ones_row")[:1, :],
                                                  mr_bf.ap[:], start=True, stop=True),
                          deps=tmax(tmr4, mr_bf.w), inc=1)
            bank6.wrote(tmrbc4)
            mr_bf.read(tmrbc4)
            st_toks = []
            for c in range(NCH):
                t1 = p.op("v", lambda e, c=c: e.tensor_tensor(
                    ao_sb[c].ap[:], ao_sb[c].ap[:], bank6.ap[:, 0:NT], ALU.mult),
                    deps=tmax(ao_sb[c].w, tmrbc4), inc=1)
                t2 = p.op("v", lambda e, c=c: e.tensor_tensor(
                    ao_sb[c].ap[:], ao_sb[c].ap[:], bank6.ap[:, NT:2 * NT], ALU.subtract),
                    deps=[t1], inc=1)
                bank6.read(t2)
                t3 = p.op("v", lambda e, c=c: e.tensor_tensor(
                    xb[c].ap[:], xb[c].ap[:], ao_sb[c].ap[:], ALU.add),
                    deps=tmax(t2, xb[c].w, xb[c].rs), inc=1)
                ao_sb[c].wrote(t3)
                xb[c].wrote(t3)
                ts_ = p.op("g", lambda e, c=c, ti=ti: e.dma_start(
                    out=dout[c * 128:(c + 1) * 128, ti * NT:(ti + 1) * NT], in_=xb[c].ap[:]),
                    deps=[t3], sem="dga" if ti % 2 == 0 else "dgb", inc=16)
                xb[c].read(ts_)
                st_toks.append(ts_)
            for c in range(NCH):
                xb[c].rs = [st_toks[-1]]
            return st_toks

        if "A" in phases:
            tms_ = p.op("v", lambda e: e.memset(ptail.ap[:], 0.0), inc=1)
            ptail.wrote(tms_)
            tms2 = p.op("v", lambda e: e.memset(S_sb[:], 0.0), inc=1)
            tms3 = p.op("v", lambda e: e.memset(S_bf[:], 0.0), inc=1)
            for h in range(H):
                S_f[h].wrote(tms2)
                S_b[h].wrote(tms3)
            for ti in range(NTI):
                attn_tile(ti)
            a_done = [("dga", p.n["dga"]), ("dgb", p.n["dgb"]), ("dg", p.n["dg"])]
        else:
            a_done = [("dga", p.n["dga"]), ("dgb", p.n["dgb"]), ("dg", p.n["dg"])]

        if "B" in phases:
            wB_tok = load_phase_b_weights(("t", p.n["t"]))
            tmu1 = wB_tok[0] if isinstance(wB_tok, list) else wB_tok
            for c in range(NCH):
                tmu1 = p.op("y", lambda e, c=c: e.dma_start(out=mu_sb[:, c:c + 1],
                                                            in_=dmur[c * 128:(c + 1) * 128, :]),
                            deps=[tmu1] if not isinstance(tmu1, list) else tmu1,
                            sem="dl", inc=16)
                tmu1 = p.op("y", lambda e, c=c: e.dma_start(out=mu_sb[:, 8 + c:9 + c],
                                                            in_=dmuk[c * 128:(c + 1) * 128, :]),
                            deps=[tmu1], sem="dl", inc=16)
            tzt = None
            for c in range(NCH):
                tzt = p.op("v", lambda e, c=c: e.memset(z_sb[c].ap[:, 0:NT + 1], 0.0),
                           deps=tmax(*a_done, ("t", p.n["t"])) if c == 0 else (), inc=1)
                z_sb[c].wrote(tzt)
            load_x_tile(0, xbuf[0], dx1, extra=tuple(a_done) + (tmu1, tzt))
            for ti in range(NTI):
                ffn_tile(ti, wB_tok)

        p.op("g", lambda e: e.nop(), deps=[("dga", p.n["dga"]), ("dgb", p.n["dgb"]),
                                           ("dg", p.n["dg"])])


        with nc.Block() as block:
            @block.tensor
            def _(eng):
                for f in p.q["t"]:
                    f(eng)

            @block.vector
            def _(eng):
                for f in p.q["v"]:
                    f(eng)

            @block.scalar
            def _(eng):
                for f in p.q["s"]:
                    f(eng)

            @block.gpsimd
            def _(eng):
                for f in p.q["g"]:
                    f(eng)

            @block.sync
            def _(eng):
                for f in p.q["y"]:
                    f(eng)

    return nc



_NC_CACHE = {}


def _get_nc():
    if "nc" not in _NC_CACHE:
        _NC_CACHE["nc"] = build_program()
    return _NC_CACHE["nc"]


def kernel(L0_attn_x, L0_attn_Wq, L0_attn_Wk, L0_attn_Wv, L0_attn_Wb, L0_attn_Wa,
           L0_attn_A_log, L0_attn_dt_bias, L0_attn_Wo, L0_attn_conv_q, L0_attn_conv_k,
           L0_attn_conv_v, L0_ffn_W_r, L0_ffn_W_k, L0_ffn_W_v, L0_ffn_mu_r, L0_ffn_mu_k):
    x = np.asarray(L0_attn_x, np.float32)
    B = x.shape[0]
    f = np.float32
    ct = np.ascontiguousarray
    wq = ct(np.asarray(L0_attn_Wq, f).T)
    wk = ct(np.asarray(L0_attn_Wk, f).T)
    wv = ct(np.asarray(L0_attn_Wv, f).T)
    wo = ct(np.asarray(L0_attn_Wo, f).T)
    wab_t = np.concatenate([np.asarray(L0_attn_Wa, f), np.asarray(L0_attn_Wb, f)], axis=0).T
    wab = ct(wab_t.reshape(8, 128, 32).transpose(1, 0, 2).reshape(128, 8 * 32))

    def _chunkify(w):
        w = np.asarray(w, f)
        k = w.shape[0] // 128
        return ct(w.reshape(k, 128, KW).transpose(1, 0, 2).reshape(128, k * KW))
    wr = ct(np.asarray(L0_ffn_W_r, f).T)
    wkf = ct(np.asarray(L0_ffn_W_k, f).T)
    wvf = ct(np.asarray(L0_ffn_W_v, f).T)
    shared = {
        "wq": wq, "wk": wk, "wv": wv, "wo": wo, "wab": wab,
        "convq": _chunkify(L0_attn_conv_q),
        "convk": _chunkify(L0_attn_conv_k),
        "convv": _chunkify(L0_attn_conv_v),
        "alog": ct(np.asarray(L0_attn_A_log, f).reshape(1, H)),
        "dtb": ct(np.asarray(L0_attn_dt_bias, f).reshape(1, H)),
        "wr": wr, "wkf": wkf, "wvf": wvf,
        "mur": ct(np.asarray(L0_ffn_mu_r, f).reshape(D, 1)),
        "muk": ct(np.asarray(L0_ffn_mu_k, f).reshape(D, 1)),
        "const": _CONSTS,
    }
    in_maps = []
    for b in range(B):
        m = dict(shared)
        m["x"] = ct(x[b].T)
        in_maps.append(m)
    nc = _get_nc()
    res = run_bass_kernel_spmd(nc, in_maps, core_ids=list(range(B)))
    out = np.stack([np.asarray(r["out"], np.float32).T for r in res.results], axis=0)
    return out


# version 06
# speedup vs baseline: 1.4808x; 1.1548x over previous
import contextlib
import os
import numpy as np
import concourse.bass as bass
import concourse.mybir as mybir
from concourse.bass_utils import run_bass_kernel_spmd

KW_HOST = 4
F32 = mybir.dt.float32
F32R = mybir.dt.float32r
BF16 = mybir.dt.bfloat16
AF = mybir.ActivationFunctionType
ALU = mybir.AluOpType
AX = mybir.AxisListType

D = 1024
T = 8192
H = 16
DKH = 64
DVH = 128
KW = 4
NT = 256
CK = 128
NCH = D // 128
EPS = 1e-6
LN_EPS = 1e-5


def _consts_host():
    cols = []
    cmap = {}

    def add(name, tile, which):
        tile = np.asarray(tile, np.float32)
        t = np.zeros((128, tile.shape[1]), np.float32)
        t[: tile.shape[0]] = tile
        cmap[name] = (sum(c.shape[1] for c in cols), tile.shape[1], which)
        cols.append(t)

    i128 = np.arange(128)
    dw = lambda t: np.tile(t, (1, 2))
    add("ident", np.eye(128, dtype=np.float32), "b")
    add("ident_dw", dw(np.eye(128, dtype=np.float32)), "b")
    incl_u = (i128[:, None] <= i128[None, :]).astype(np.float32)
    strict_l = (i128[:, None] > i128[None, :]).astype(np.float32)
    add("incl_u_dw", dw(incl_u), "b")
    add("strict_l_dw", dw(strict_l), "b")
    blk8 = i128 // 8
    m8 = ((blk8[:, None] == blk8[None, :]) & (i128[:, None] < i128[None, :])).astype(np.float32)
    add("m8u_dw", dw(m8), "b")
    add("m8l_dw", dw(m8.T), "b")
    for m in (8, 16, 32, 64):
        b2 = i128 // (2 * m)
        half = (i128 // m) % 2
        r = ((b2[:, None] == b2[None, :]) & (half[:, None] == 0) & (half[None, :] == 1)).astype(np.float32)
        add(f"r{m}u_dw", dw(r), "b")
        add(f"r{m}l_dw", dw(r.T), "b")
    bd = np.zeros((128, 2), np.float32)
    bd[:64, 0] = 1.0
    bd[64:, 1] = 1.0
    add("bdones", bd, "b")
    add("ones_col", np.ones((128, 1), np.float32), "b")
    add("ones_row", np.ones((1, 128), np.float32), "b")
    add("incl_u_r", incl_u, "r")
    add("ones_row_r", np.ones((1, 128), np.float32), "r")
    oh = np.zeros((16, 16 * 128), np.float32)
    for h in range(16):
        oh[h, h * 128:(h + 1) * 128] = 1.0
    add("onehot", oh, "r")
    offs = {"b": 0, "r": 0}
    cmap2 = {}
    for name, (o, w, which) in cmap.items():
        cmap2[name] = (o, offs[which], w, which)
        offs[which] += w
    return (np.ascontiguousarray(np.concatenate(cols, axis=1)), cmap2,
            offs["b"], offs["r"])


_CONSTS, _CMAP, _CWB, _CWR = _consts_host()
_LAST_P = None



class P:
    def __init__(self, nc):
        self.nc = nc
        self.q = {e: [] for e in ("t", "v", "s", "g", "y")}
        self.n = {k: 0 for k in ("t", "v", "s", "g", "y", "dl", "dg",
                                 "dxa", "dxb", "dca", "dcb", "dga", "dgb")}
        self.hw = {e: {} for e in self.q}
        self.sems = {}
        self.load = {"v": 0.0, "s": 0.0, "g": 0.0}
        self.ctx = None
        self.dbg = {}

    def pick(self, costs):
        e = min(costs, key=lambda x: self.load[x] + costs[x])
        self.load[e] += costs[e]
        return e

    def _wait(self, e, tok):
        if tok is None:
            return
        s, c = tok
        if c <= 0 or c <= self.hw[e].get(s, 0):
            return
        self.hw[e][s] = c
        self.q[e].append(lambda eng, s=s, c=c: eng.wait_ge(self.sems[s], c))

    DMA_SEMS = ("dl", "dg", "dxa", "dxb", "dca", "dcb", "dga", "dgb")

    def op(self, e, emitfn, deps=(), sem=None, inc=0, pbase=0):
        sname = sem or e
        if e == "t":
            last = getattr(self, "pe_base", 0)
            if pbase != last:
                cnt = self.n["t"]
                self.q["t"].append(lambda eng, c=cnt: eng.wait_ge(self.sems["t"], c))
            self.pe_base = pbase
        if sname in self.DMA_SEMS:
            prev = getattr(self, "last_dma", {}).get(sname)
            if prev is not None:
                deps = list(deps) + [prev]
        for t in deps:
            self._wait(e, t)
        if inc:
            self.n[sname] += inc
            tok = (sname, self.n[sname])
            if os.environ.get("PDBG"):
                self.dbg[tok] = self.ctx

            def run(eng, f=emitfn, sn=sname, k=inc):
                f(eng).then_inc(self.sems[sn], k)
            self.q[e].append(run)
            if sname in self.DMA_SEMS:
                if not hasattr(self, "last_dma"):
                    self.last_dma = {}
                self.last_dma[sname] = tok
            return tok
        self.q[e].append(lambda eng, f=emitfn: f(eng))
        return None


_GDBG = []


class Guard:
    def __init__(self, ap=None, name=""):
        self.ap = ap
        self.name = name
        self.w = None
        self.rs = []

    def rd(self):
        if not self.w:
            return []
        return self.w if isinstance(self.w, list) else [self.w]

    def wr(self):
        return self.rd() + self.rs

    def wrote(self, tok):
        if tok is not None and tok != []:
            self.w = tok
            self.rs = []
            if _LAST_P is not None and _LAST_P.ctx and os.environ.get("PDBG"):
                _GDBG.append((self.name, "w", _LAST_P.ctx))

    def read(self, tok):
        if tok is not None:
            self.rs.append(tok)
            if _LAST_P is not None and _LAST_P.ctx and os.environ.get("PDBG"):
                _GDBG.append((self.name, "r", _LAST_P.ctx))


def tmax(*toks):
    best = {}
    stack = list(toks)
    while stack:
        t = stack.pop()
        if t is None:
            continue
        if isinstance(t, list):
            stack.extend(t)
        elif t[1] > best.get(t[0], -1):
            best[t[0]] = t[1]
    return [(s, c) for s, c in best.items()]



def build_program(n_tiles=T // NT, debug=(), phases=("A", "B"), x1_out=False,
                  stop_after=None):
    nc = bass.Bass("TRN2", target_bir_lowering=False)
    NTI = n_tiles
    global _LAST_P

    dx = nc.dram_tensor("x", [D, T], F32, kind="ExternalInput").ap()
    dwq = nc.dram_tensor("wq", [D, D], F32, kind="ExternalInput").ap()
    dwk = nc.dram_tensor("wk", [D, D], F32, kind="ExternalInput").ap()
    dwv = nc.dram_tensor("wv", [D, 2048], F32, kind="ExternalInput").ap()
    dwo = nc.dram_tensor("wo", [2048, D], F32, kind="ExternalInput").ap()
    dwab = nc.dram_tensor("wab", [128, NCH * 32], F32, kind="ExternalInput").ap()
    dcq = nc.dram_tensor("convq", [128, 8 * KW], F32, kind="ExternalInput").ap()
    dck = nc.dram_tensor("convk", [128, 8 * KW], F32, kind="ExternalInput").ap()
    dcv = nc.dram_tensor("convv", [128, 16 * KW], F32, kind="ExternalInput").ap()
    dalog = nc.dram_tensor("alog", [1, H], F32, kind="ExternalInput").ap()
    ddtb = nc.dram_tensor("dtb", [1, H], F32, kind="ExternalInput").ap()
    dwr = nc.dram_tensor("wr", [D, D], F32, kind="ExternalInput").ap()
    dwkf = nc.dram_tensor("wkf", [D, D], F32, kind="ExternalInput").ap()
    dwvf = nc.dram_tensor("wvf", [D, D], F32, kind="ExternalInput").ap()
    dmur = nc.dram_tensor("mur", [D, 1], F32, kind="ExternalInput").ap()
    dmuk = nc.dram_tensor("muk", [D, 1], F32, kind="ExternalInput").ap()
    dconst = nc.dram_tensor("const", list(_CONSTS.shape), F32, kind="ExternalInput").ap()
    dout = nc.dram_tensor("out", [D, T], F32, kind="ExternalOutput").ap()
    dx1 = nc.dram_tensor("x1s", [D, T], F32,
                         kind="ExternalOutput" if x1_out else "Internal").ap()
    dcdiag = nc.dram_tensor("cdiag", [128, 32 * KW * 128], BF16, kind="Internal").ap()

    ddbg = {}
    for name, shape in debug:
        ddbg[name] = nc.dram_tensor("dbg_" + name, list(shape), F32, kind="ExternalOutput").ap()

    p = P(nc)

    _LAST_P = p
    es = contextlib.ExitStack()

    def sb(name, shape, dt):
        return es.enter_context(nc.sbuf_tensor(name, list(shape), dt))

    with es:
        for s in ("t", "v", "s", "g", "y", "dl", "dg",
                  "dxa", "dxb", "dca", "dcb", "dga", "dgb"):
            p.sems[s] = es.enter_context(nc.semaphore("sem_" + s))

        cst_b = sb("cst_b", (128, _CWB), BF16)
        cst_r = sb("cst_r", (128, _CWR), F32R)
        epsc = sb("epsc", (128, 2), F32)
        p.op("v", lambda e: e.memset(epsc[:, 0:1], LN_EPS), inc=1)
        p.op("v", lambda e: e.memset(epsc[:, 1:2], EPS), inc=1)
        nc.const_aps.register(F32, LN_EPS, epsc[:, 0:1])
        nc.const_aps.register(F32, EPS, epsc[:, 1:2])

        def cb(nm):
            _, o, w, which = _CMAP[nm]
            assert which == "b"
            return cst_b[:, o:o + w]

        def cr(nm):
            _, o, w, which = _CMAP[nm]
            assert which == "r"
            return cst_r[:, o:o + w]

        bank = [Guard(es.enter_context(nc.psum_tensor(f"ps{i}", [128, 512], F32)))
                for i in range(5)]
        bank5 = Guard(es.enter_context(nc.psum_tensor("ps5", [128, 1024], BF16)))
        bank6 = Guard(es.enter_context(nc.psum_tensor("ps6", [128, 512], F32)))
        bank7 = Guard(es.enter_context(nc.psum_tensor("ps7", [128, 512], F32)))

        wpool = sb("wpool", (128, 48 * 1024), BF16)
        wq_sb = [wpool[:, (k) * 1024:(k + 1) * 1024] for k in range(8)]
        wk_sb = [wpool[:, (8 + k) * 1024:(9 + k) * 1024] for k in range(8)]
        wv_sb = [wpool[:, (16 + 2 * k) * 1024:(16 + 2 * k + 2) * 1024] for k in range(8)]
        wo_sb = [wpool[:, (32 + k) * 1024:(33 + k) * 1024] for k in range(16)]
        wr_sb = [wpool[:, (k) * 1024:(k + 1) * 1024] for k in range(8)]
        wkf_sb = [wpool[:, (8 + k) * 1024:(9 + k) * 1024] for k in range(8)]
        wvf_sb = [wpool[:, (16 + k) * 1024:(17 + k) * 1024] for k in range(8)]
        wg = Guard()

        wab_sb = sb("wab_sb", (128, NCH * 32), BF16)
        convw = sb("convw", (128, 32 * KW), F32)
        negA = sb("negA", (128, H), F32)
        dtbb = sb("dtbb", (128, H), F32)
        alog_r = sb("alog_r", (1, 2 * H), F32R)


        xbuf = [[Guard(sb(f"x0_{c}", (128, NT), F32)) for c in range(NCH)]]
        apool = sb("apool", (128, 12 * 1024), BF16)
        qkv = [Guard(apool[:, pc * NT:(pc + 1) * NT]) for pc in range(32)]
        on_fm = [Guard(apool[:, 8 * 1024 + hv * NT:8 * 1024 + (hv + 1) * NT]) for hv in range(16)]
        z_sb = [Guard(apool[:, c * (NT + 1):(c + 1) * (NT + 1)]) for c in range(8)]
        xr_sb = [Guard(apool[:, 2080 + c * NT:2080 + (c + 1) * NT]) for c in range(8)]
        xk_sb = [Guard(apool[:, 4128 + c * NT:4128 + (c + 1) * NT]) for c in range(8)]
        r_sb = [Guard(apool[:, 6176 + c * NT:6176 + (c + 1) * NT]) for c in range(8)]
        ka_sb = [Guard(apool[:, 8224 + c * NT:8224 + (c + 1) * NT]) for c in range(8)]

        ao_sb = [Guard(sb(f"ao{c}", (128, NT), BF16)) for c in range(NCH)]
        ybuf = [Guard(sb(f"y_{c}", (128, NT), BF16)) for c in range(NCH)]
        xwork = Guard(sb("xwork", (128, 2 * NT), BF16))
        stat = Guard(sb("stat", (1, 2 * NT), F32))
        mr_bf = Guard(sb("mr_bf", (1, NT), BF16))
        rstd_bf = Guard(sb("rstd_bf", (1, NT), BF16))
        pbuf = [Guard(sb("pb0", (128, NT + KW - 1), BF16))] * 2
        ptail = Guard(sb("ptail", (128, 32 * (KW - 1)), BF16))
        cdg = [Guard(sb(f"cdg{i}", (128, KW * 128), BF16)) for i in range(2)]

        ab_sb = Guard(sb("ab_sb", (128, 32), F32))
        loga = Guard(sb("loga", (128, H), F32R))
        beta_g = Guard(sb("beta", (128, H), F32))
        ell_tm = Guard(sb("ell_tm", (128, H), F32))
        ell_fm = Guard(sb("ell_fm", (16, CK), F32R))
        gam_tm = Guard(sb("gam_tm", (128, H), F32))
        rnorm = Guard(sb("rnorm", (128, 32), F32))
        scl = Guard(sb("scl", (128, 6 * H), F32))
        negbg = Guard(sb("negbg", (128, H), F32))

        S_f = [Guard(None) for _ in range(H)]
        S_b = [Guard(None) for _ in range(H)]
        S_sb = sb("S", (128, 8 * DVH), F32)
        S_bf = sb("S_bf", (128, 8 * DVH), BF16)
        for h in range(H):
            hb, hp_ = 64 * (h % 2), h // 2
            S_f[h].ap = S_sb[hb:hb + 64, hp_ * DVH:(hp_ + 1) * DVH]
            S_b[h].ap = S_bf[hb:hb + 64, hp_ * DVH:(hp_ + 1) * DVH]

        def gpair(name, cols, dt, npar=2):
            return [Guard(sb(f"{name}{i}", (128, cols), dt), name=name)
                    for i in range(npar)]

        EbatP = gpair("Ebat", 2 * CK, F32)
        expEP = gpair("expE", 2 * CK, BF16)
        expE2P = gpair("expE2", 2 * CK, BF16)
        dincP = gpair("dinc", 2 * CK, BF16)
        lfacP = gpair("lfac", 2 * CK, BF16)
        kqtmP = gpair("kqtm", 2 * CK, BF16)
        khnP = gpair("khn", 2 * CK, BF16)
        hnfP = gpair("hnf", 2 * CK, BF16, npar=3)
        vtmP = gpair("vtm", 2 * DVH, BF16, npar=3)
        khatP = gpair("khat", 2 * DKH, BF16, npar=3)
        LtP = gpair("Lt", 2 * CK, BF16)
        LTtP = gpair("LTt", 2 * CK, BF16)
        mmtP = gpair("mmt", 2 * CK, BF16, npar=3)
        RtmP = gpair("Rtm", 2 * CK, BF16)
        rscP = gpair("rsc", 2 * CK, F32, npar=1)
        NgtP = gpair("Ngt", 4 * CK, BF16)
        h2pP = gpair("h2p", 4 * CK, BF16)
        h4pP = gpair("h4p", 4 * CK, BF16)
        t0apP = gpair("t0ap", 4 * CK, BF16)
        TgpP = gpair("Tgp", 2 * CK, BF16)
        TTgpP = gpair("TTgp", 2 * CK, BF16)
        PgpP = gpair("Pgp", 2 * CK, BF16)
        UtmpP = gpair("Utmp", 2 * CK, BF16)
        oscpP = gpair("oscp", 2 * CK, F32)
        osqpP = gpair("osqp", 2 * CK, BF16, npar=1)
        omsP = gpair("oms", 4, F32, npar=1)
        ontmP = gpair("ontm", 2 * DVH, BF16)
        nsq_t = sb("nsq", (128, 2 * CK), BF16)
        nsqG = [Guard(nsq_t[:, 0:CK]), Guard(nsq_t[:, CK:2 * CK])]

        dkt = []
        for i in range(2):
            dkt.append(dict(
                ab=Guard(sb(f"ab_sb{i}", (128, 32), F32)),
                loga=Guard(sb(f"loga{i}", (128, H), F32R)),
                beta=Guard(sb(f"beta{i}", (128, H), F32)),
                ell_tm=Guard(sb(f"ell_tm{i}", (128, H), F32)),
                ell_fm=Guard(sb(f"ell_fm{i}", (16, CK), F32R)),
                gam=Guard(sb(f"gam{i}", (128, H), F32)),
                rnorm=Guard(sb(f"rnorm{i}", (128, 32), F32)),
                scl=Guard(sb(f"scl{i}", (128, 6 * H), F32)),
                negbg=Guard(sb(f"negbg{i}", (128, H), F32)),
            ))

        mu_sb = sb("mu_sb", (128, 16), F32)
        zt_g = Guard(None)
        fsc = Guard(sb("fsc", (128, NT), F32))

        cw_all = _CONSTS.shape[1]
        tkb = None
        td_prev = None
        for go in range(0, cw_all, NT):
            w = min(NT, cw_all - go)
            g = xbuf[0][(go // NT) % 4]
            td = p.op("y", lambda e, go=go, w=w, g=g:
                      e.dma_start(out=g.ap[:, :w], in_=dconst[:, go:go + w]),
                      deps=tmax(g.wr(), td_prev), sem="dl", inc=16)
            td_prev = td
            g.wrote(td)
            for nm, (osrc, odst, wd, which) in _CMAP.items():
                a, b = max(osrc, go), min(osrc + wd, go + w)
                if a >= b:
                    continue
                dstt = cst_b if which == "b" else cst_r
                tkb = p.op("v", lambda e, a=a, b=b, go=go, osrc=osrc, odst=odst, dstt=dstt, g=g:
                           e.tensor_copy(dstt[:, odst + a - osrc:odst + b - osrc],
                                         g.ap[:, a - go:b - go]),
                           deps=[td], inc=1)
                g.read(tkb)
        tkr = tkb

        _stage_sems = ("dl", "dxa", "dxb", "dca")

        def stage_load(src_view, dst_view, gidx, extra_deps=()):
            ncols = src_view.shape[1]
            dtoks = []
            pieces = []
            for o in range(0, ncols, NT):
                w = min(NT, ncols - o)
                j = (o // NT) % 4
                g = xbuf[0][j]
                td = p.op("y", lambda e, g=g, o=o, w=w: e.dma_start(
                    out=g.ap[:, :w], in_=src_view[:, o:o + w]),
                    deps=tmax(g.wr(), *extra_deps), sem=_stage_sems[j], inc=16)
                g.wrote(td)
                dtoks.append(td)
                pieces.append((g, o, w))
            gend = tmax(*dtoks)
            toks = []
            for (g, o, w) in pieces:
                tc = p.op("v", lambda e, g=g, o=o, w=w: e.tensor_copy(
                    dst_view[:, o:o + w], g.ap[:, :w]),
                    deps=tmax(gend, wg.w), inc=1)
                g.read(tc)
                toks.append(tc)
            return tmax(*toks)

        def load_phase_a_weights():
            toks = []
            for k in range(8):
                toks.append(stage_load(dwq[k * 128:(k + 1) * 128, :], wq_sb[k], k % 2))
                toks.append(stage_load(dwk[k * 128:(k + 1) * 128, :], wk_sb[k], k % 2))
                for hf in range(2):
                    toks.append(stage_load(dwv[k * 128:(k + 1) * 128, hf * 1024:(hf + 1) * 1024],
                                           wv_sb[k][:, hf * 1024:(hf + 1) * 1024], k % 2))
            for k in range(16):
                toks.append(stage_load(dwo[k * 128:(k + 1) * 128, :], wo_sb[k], k % 2))
            return toks

        wtoks = load_phase_a_weights()
        wg.wrote(wtoks[-1])
        wA_tok = tmax(*wtoks)

        tab_w = stage_load(dwab, wab_sb[:], 0)

        tcw = ("v", p.n["v"])
        tcw = tab_w
        for (srcd, nch_, off) in ((dcq, 8, 0), (dck, 8, 8), (dcv, 16, 16)):
            tcw = p.op("y", lambda e, srcd=srcd, nch_=nch_, off=off:
                       e.dma_start(out=convw[:, off * KW:(off + nch_) * KW], in_=srcd),
                       deps=tmax(tcw), sem="dl", inc=16)

        t_al = p.op("g", lambda e: e.dma_start(out=alog_r[:, :H], in_=dalog),
                    deps=tmax(tcw), sem="dg", inc=16)
        t_db = p.op("g", lambda e: e.dma_start(out=alog_r[:, H:], in_=ddtb),
                    deps=[t_al], sem="dg", inc=16)
        t_bc = p.op("t", lambda e: e.matmul(bank6.ap[:, :2 * H], cr("ones_row_r")[:1, :], alog_r[:],
                                            start=True, stop=True),
                    deps=tmax(t_al, t_db, tkr, bank6.wr()), inc=1)
        bank6.wrote(t_bc)
        tA1 = p.op("s", lambda e: e.activation(negA[:], bank6.ap[:, :H], AF.Exp),
                   deps=bank6.rd(), inc=1)
        t_negA = p.op("s", lambda e: e.mul(negA[:], negA[:], -1.0), deps=[tA1], inc=1)
        t_dtb = p.op("v", lambda e: e.tensor_copy(dtbb[:], bank6.ap[:, H:2 * H]),
                     deps=bank6.rd(), inc=1)
        bank6.read(t_negA)
        bank6.read(t_dtb)

        cstg = Guard(apool[:, 0:2 * KW * 128])
        for c in range(32):
            half = (c % 2) * KW * 128
            tb = None
            for j in range(KW):
                tb = p.op("v", lambda e, c=c, j=j, half=half:
                          e.tensor_scalar(cstg.ap[:, half + j * 128:half + (j + 1) * 128],
                                          cb("ident"), convw[:, c * KW + j:c * KW + j + 1],
                                          None, ALU.mult),
                          deps=tmax(tcw, tkb, cstg.wr()) if j == 0 else (), inc=1)
            cstg.wrote(tb)
            ts_ = p.op("g", lambda e, c=c, half=half:
                       e.dma_start(out=dcdiag[:, c * KW * 128:(c + 1) * KW * 128],
                                   in_=cstg.ap[:, half:half + KW * 128]),
                       deps=tmax(cstg.rd(), t_db if c == 0 else []), sem="dg", inc=16)
            cstg.read(ts_)
        cdiag_tok = ("dg", p.n["dg"])
        for pc in range(8):
            qkv[pc].rs.append(cdiag_tok)

        def ln_stats(src, eps):
            t2 = None
            for c in range(NCH):
                tq = p.op("v", lambda e, c=c: e.tensor_tensor(
                    xwork.ap[:, NT:2 * NT], src[c].ap[:], src[c].ap[:], ALU.mult),
                    deps=tmax(src[c].w, xwork.wr()), inc=1)
                src[c].read(tq)
                tb_ = p.op("v", lambda e, c=c: e.tensor_copy(xwork.ap[:, 0:NT], src[c].ap[:]),
                           deps=[tq], inc=1)
                xwork.wrote(tb_)
                src[c].read(tb_)
                t2 = p.op("t", lambda e, c=c: e.matmul(bank7.ap[:1, 0:2 * NT], cb("ones_col"),
                                                  xwork.ap[:],
                                                  start=(c == 0), stop=(c == NCH - 1)),
                          deps=tmax(tb_, tkb, bank7.wr() if c == 0 else []), inc=1)
                xwork.read(t2)
            bank7.wrote(t2)
            tm = p.op("v", lambda e: e.tensor_scalar(stat.ap[:1, 0:NT], bank7.ap[:1, 0:NT],
                                                     1.0 / D, None, ALU.mult),
                      deps=tmax(t2, stat.wr()), inc=1)
            tmsq = p.op("v", lambda e: e.tensor_tensor(stat.ap[:1, NT:2 * NT], stat.ap[:1, 0:NT],
                                                       stat.ap[:1, 0:NT], ALU.mult),
                        deps=[tm], inc=1)
            tvar = p.op("v", lambda e: e.scalar_tensor_tensor(
                stat.ap[:1, NT:2 * NT], bank7.ap[:1, NT:2 * NT], 1.0 / D,
                stat.ap[:1, NT:2 * NT], ALU.mult, ALU.subtract), deps=[tmsq], inc=1)
            bank7.read(tvar)
            tl = p.op("s", lambda e: e.activation(stat.ap[:1, NT:2 * NT],
                                                  stat.ap[:1, NT:2 * NT], AF.Ln,
                                                  bias=float(eps)),
                      deps=[tvar], inc=1)
            tr = p.op("s", lambda e: e.activation(stat.ap[:1, NT:2 * NT],
                                                  stat.ap[:1, NT:2 * NT], AF.Exp, scale=-0.5),
                      deps=[tl], inc=1)
            stat.wrote(tr)
            trb = p.op("v", lambda e: e.tensor_copy(rstd_bf.ap[:], stat.ap[:1, NT:2 * NT]),
                       deps=tmax(tr, rstd_bf.wr()), inc=1)
            rstd_bf.wrote(trb)
            tmr2 = p.op("v", lambda e: e.tensor_tensor(stat.ap[:1, 0:NT], stat.ap[:1, 0:NT],
                                                       stat.ap[:1, NT:2 * NT], ALU.mult),
                        deps=[tr], inc=1)
            tmrb = p.op("v", lambda e: e.tensor_copy(mr_bf.ap[:], stat.ap[:1, 0:NT]),
                        deps=tmax(tmr2, mr_bf.wr()), inc=1)
            mr_bf.wrote(tmrb)
            trbc = p.op("t", lambda e: e.matmul(bank6.ap[:, 0:NT], cb("ones_row")[:1, :],
                                                rstd_bf.ap[:], start=True, stop=True),
                        deps=tmax(trb, bank6.wr()), inc=1)
            bank6.wrote(trbc)
            rstd_bf.read(trbc)
            return trb, tmrb, trbc

        def bcast2(rowg, width):
            pass

        def load_x_tile(ti, dst, src_dram, extra=()):
            sem = "dxa" if ti % 2 == 0 else "dxb"
            tk_ = None
            for c in range(NCH):
                g = dst[c]
                tk_ = p.op("y", lambda e, c=c, ti=ti, g=g, src_dram=src_dram:
                           e.dma_start(out=g.ap[:], in_=src_dram[c * 128:(c + 1) * 128,
                                                             ti * NT:(ti + 1) * NT]),
                           deps=tmax(g.wr(), *extra), sem=sem, inc=16)
            for c in range(NCH):
                dst[c].wrote(tk_)
            return [tk_]

        def wslab(pc):
            if pc < 8:
                return [wq_sb[k][:, pc * 128:(pc + 1) * 128] for k in range(8)]
            if pc < 16:
                o = pc - 8
                return [wk_sb[k][:, o * 128:(o + 1) * 128] for k in range(8)]
            o = pc - 16
            return [wv_sb[k][:, o * 128:(o + 1) * 128] for k in range(8)]

        def attn_tile(ti):
            xb = xbuf[0]
            load_x_tile(ti, xb, dx)

            trb, tmr, trbc = ln_stats(xb, LN_EPS)

            tmrbc0 = p.op("t", lambda e: e.matmul(bank6.ap[:, NT:2 * NT], cb("ones_row")[:1, :],
                                                  mr_bf.ap[:], start=True, stop=True),
                          deps=tmax(tmr, mr_bf.w), inc=1)
            bank6.wrote(tmrbc0)
            mr_bf.read(tmrbc0)
            ytoks = []
            for c in range(NCH):
                ty = p.op("v", lambda e, c=c: e.tensor_tensor(
                    ybuf[c].ap[:], xb[c].ap[:], bank6.ap[:, 0:NT], ALU.mult),
                    deps=tmax(xb[c].w, tmrbc0, ybuf[c].wr()), inc=1)
                xb[c].read(ty)
                ty = p.op("v", lambda e, c=c: e.tensor_tensor(
                    ybuf[c].ap[:], ybuf[c].ap[:], bank6.ap[:, NT:2 * NT], ALU.subtract),
                    deps=[ty], inc=1)
                p.load["v"] += 700.0
                ybuf[c].wrote(ty)
                bank6.read(ty)
                ytoks.append(ty)

            for pc in range(32):
                pb = bank[pc % 2]
                slabs = wslab(pc)
                tmm = None
                for k in range(NCH):
                    tmm = p.op("t", lambda e, k=k, slabs=slabs, pb=pb: e.matmul(
                        pb.ap[:, 0:NT], slabs[k], ybuf[k].ap[:],
                        start=(k == 0), stop=(k == NCH - 1)),
                        deps=tmax(ytoks[k], wA_tok, pb.wr()) if k == 0 else [ytoks[k]],
                        inc=1)
                pb.wrote(tmm)
                for k in range(NCH):
                    ybuf[k].read(tmm)
                cd = cdg[pc % 2]
                tcd = p.op("y", lambda e, pc=pc, cd=cd:
                           e.dma_start(out=cd.ap[:],
                                       in_=dcdiag[:, pc * KW * 128:(pc + 1) * KW * 128]),
                           deps=tmax(cd.wr(), cdiag_tok),
                           sem="dca" if pc % 2 == 0 else "dcb", inc=16)
                cd.wrote(tcd)
                pg = pbuf[pc % 2]
                tp0 = p.op("v", lambda e, pc=pc, pg=pg: e.tensor_copy(
                    pg.ap[:, 0:KW - 1], ptail.ap[:, pc * (KW - 1):(pc + 1) * (KW - 1)]),
                    deps=tmax(ptail.w, pg.wr()), inc=1)
                ptail.read(tp0)
                tp = p.op("s", lambda e, pg=pg, pb=pb: e.copy(pg.ap[:, KW - 1:], pb.ap[:, 0:NT]),
                          deps=tmax(tmm, tp0), inc=1)
                pb.read(tp)
                pg.wrote(tp)
                tt = p.op("v", lambda e, pc=pc, pg=pg: e.tensor_copy(
                    ptail.ap[:, pc * (KW - 1):(pc + 1) * (KW - 1)], pg.ap[:, NT:NT + KW - 1]),
                    deps=[tp], inc=1)
                ptail.wrote(tt)
                pg.read(tt)
                cbnk = bank[2 + (pc % 2)]
                tc_ = None
                for j in range(KW):
                    tc_ = p.op("t", lambda e, j=j, cd=cd, pg=pg, cbnk=cbnk: e.matmul(
                        cbnk.ap[:, 0:NT], cd.ap[:, j * 128:(j + 1) * 128], pg.ap[:, j:j + NT],
                        start=(j == 0), stop=(j == KW - 1)),
                        deps=tmax(tcd, tp, cbnk.wr()) if j == 0 else (), inc=1)
                cbnk.wrote(tc_)
                cd.read(tc_)
                pg.read(tc_)
                tsl = p.op("s", lambda e, pc=pc, cbnk=cbnk: e.activation(
                    qkv[pc].ap[:], cbnk.ap[:, 0:NT], AF.Silu),
                    deps=tmax(tc_, qkv[pc].wr()), inc=1)
                cbnk.read(tsl)
                qkv[pc].wrote(tsl)

            if stop_after == "proj":
                return []
            for cc in range(NT // CK):
                chunk_recurrence(ti, cc, ytoks)
                if isinstance(stop_after, str) and stop_after.startswith("cc0") and cc == 0:
                    break
            if stop_after is not None:
                return []

            for oc in range(NCH):
                pb = bank[oc % 2]
                tmm = None
                for hv in range(16):
                    tmm = p.op("t", lambda e, hv=hv, oc=oc, pb=pb: e.matmul(
                        pb.ap[:, 0:NT], wo_sb[hv][:, oc * 128:(oc + 1) * 128],
                        apool[:, 8 * 1024 + hv * NT:8 * 1024 + (hv + 1) * NT],
                        start=(hv == 0), stop=(hv == 15)),
                        deps=tmax(on_fm[hv].w, pb.wr()) if hv == 0 else [on_fm[hv].w],
                        inc=1)
                pb.wrote(tmm)
                for hv in range(16):
                    on_fm[hv].read(tmm)
                te = p.op("s", lambda e, oc=oc, pb=pb: e.copy(ao_sb[oc].ap[:], pb.ap[:, 0:NT]),
                          deps=tmax(tmm, ao_sb[oc].wr()), inc=1)
                pb.read(te)
                ao_sb[oc].wrote(te)

            trb2, tmr2, trbc2 = ln_stats(ao_sb, LN_EPS)
            tmrbc = p.op("t", lambda e: e.matmul(bank6.ap[:, NT:2 * NT], cb("ones_row")[:1, :],
                                                 mr_bf.ap[:], start=True, stop=True),
                         deps=tmax(tmr2, mr_bf.w), inc=1)
            bank6.wrote(tmrbc)
            mr_bf.read(tmrbc)
            st_toks = []
            for c in range(NCH):
                t1 = p.op("v", lambda e, c=c: e.tensor_tensor(
                    ao_sb[c].ap[:], ao_sb[c].ap[:], bank6.ap[:, 0:NT], ALU.mult),
                    deps=tmax(ao_sb[c].w, tmrbc), inc=1)
                t2 = p.op("v", lambda e, c=c: e.tensor_tensor(
                    ao_sb[c].ap[:], ao_sb[c].ap[:], bank6.ap[:, NT:2 * NT], ALU.subtract),
                    deps=[t1], inc=1)
                bank6.read(t2)
                t3 = p.op("v", lambda e, c=c: e.tensor_tensor(
                    xb[c].ap[:], xb[c].ap[:], ao_sb[c].ap[:], ALU.add),
                    deps=tmax(t2, xb[c].w, xb[c].rs), inc=1)
                ao_sb[c].wrote(t3)
                xb[c].wrote(t3)
                ts_ = p.op("g", lambda e, c=c, ti=ti: e.dma_start(
                    out=dx1[c * 128:(c + 1) * 128, ti * NT:(ti + 1) * NT], in_=xb[c].ap[:]),
                    deps=[t3], sem="dga" if ti % 2 == 0 else "dgb", inc=16)
                xb[c].read(ts_)
                st_toks.append(ts_)
            for c in range(NCH):
                xb[c].rs = [st_toks[-1]]
            return st_toks

        RB = [0, 1, 2, 3, 4]
        rb_state = [0]

        def nxt():
            b = bank[RB[rb_state[0] % len(RB)]]
            rb_state[0] += 1
            return b

        import os as _os
        _FORCE = _os.environ.get("EW_FORCE") or None

        def _cost(kind, fd, bf, ps):
            c = {}
            if kind in ("tt", "stt"):
                accv = 2 if bf else 1
                c["v"] = ((120 if ps else 58) + fd / accv) / 0.96
                if not ps:
                    c["g"] = (300 + fd * 5.2) / 1.2
            elif kind in ("ts", "copy"):
                accv = (2 if ps else 4) if bf else 2
                c["v"] = ((120 if ps else 58) + fd / accv) / 0.96
                if not ps:
                    c["g"] = (300 + fd * 2.1) / 1.2
                c["s"] = ((172 if ps else 224) + fd / 2) / 1.2
            return c

        def ew_tt(out, a, b, op, deps, fd, bf=True, ps=False, eng=None):
            c = _cost("tt", fd, bf, ps)
            if _FORCE and _FORCE in c:
                eng = _FORCE
            if eng is None:
                eng = p.pick(c)
            else:
                p.load[eng] += c.get(eng, fd)
            return p.op(eng, lambda e, out=out, a=a, b=b, op=op: e.tensor_tensor(out, a, b, op),
                        deps=deps, inc=1)

        def ew_stt(out, a, s1, b, op0, op1, deps, fd, bf=True, ps=False):
            c = _cost("tt", fd, bf, ps)
            eng = _FORCE if (_FORCE and _FORCE in c) else p.pick(c)
            return p.op(eng, lambda e, out=out, a=a, s1=s1, b=b: e.scalar_tensor_tensor(
                out, a, s1, b, op0, op1), deps=deps, inc=1)

        def ew_ts(out, a, s1, op, deps, fd, s2=None, op2=None, bf=True, ps=False, eng=None):
            c = _cost("ts", fd, bf, ps)
            c.pop("s", None)
            if _FORCE and _FORCE in c:
                eng = _FORCE
            if eng is None:
                eng = p.pick(c)
            else:
                p.load[eng] += c.get(eng, fd)
            if eng == "s":
                return p.op("s", lambda e, out=out, a=a, s1=s1: e.activation(
                    out, a, AF.Copy, scale=s1), deps=deps, inc=1)
            if op2 is None:
                return p.op(eng, lambda e, out=out, a=a, s1=s1, op=op: e.tensor_scalar(
                    out, a, s1, None, op), deps=deps, inc=1)
            return p.op(eng, lambda e, out=out, a=a, s1=s1, s2=s2, op=op, op2=op2:
                        e.tensor_scalar(out, a, s1, s2, op, op2), deps=deps, inc=1)

        def ew_copy(out, a, deps, fd, bf=True, ps=False, eng=None):
            c = _cost("copy", fd, bf, ps)
            if _FORCE and _FORCE in c:
                eng = _FORCE
            if eng is None:
                eng = p.pick(c)
            else:
                p.load[eng] += c.get(eng, fd)
            if eng == "s":
                return p.op("s", lambda e, out=out, a=a: e.copy(out, a), deps=deps, inc=1)
            return p.op(eng, lambda e, out=out, a=a: e.tensor_copy(out, a), deps=deps, inc=1)

        def chunk_recurrence(ti, cc, ytoks):
            tof = cc * CK
            dk = dkt[cc % 2]
            ab_sb, loga, beta_g = dk["ab"], dk["loga"], dk["beta"]
            ell_tm, ell_fm, gam_tm = dk["ell_tm"], dk["ell_fm"], dk["gam"]
            rnorm, scl, negbg = dk["rnorm"], dk["scl"], dk["negbg"]

            tab = None
            for k in range(NCH):
                tab = p.op("t", lambda e, k=k: e.matmul(
                    bank7.ap[:, 0:32], ybuf[k].ap[:, tof:tof + CK],
                    wab_sb[:, k * 32:(k + 1) * 32],
                    start=(k == 0), stop=(k == NCH - 1)),
                    deps=tmax(ytoks[k], tab_w, bank7.wr()) if k == 0 else [ytoks[k]], inc=1)
            bank7.wrote(tab)
            for k in range(NCH):
                ybuf[k].read(tab)
            tabe = ew_copy(ab_sb.ap[:], bank7.ap[:, 0:32], tmax(tab, ab_sb.wr()), 32,
                           bf=False, ps=True)
            bank7.read(tabe)
            ab_sb.wrote(tabe)

            t1 = ew_tt(scl.ap[:, 0:H], ab_sb.ap[:, 0:H], dtbb[:], ALU.add,
                       tmax(tabe, t_dtb, scl.wr()), H, bf=False)
            t2 = p.op("s", lambda e: e.activation(scl.ap[:, H:2 * H], scl.ap[:, 0:H], AF.Abs),
                      deps=[t1], inc=1)
            t3 = p.op("s", lambda e: e.activation(scl.ap[:, 2 * H:3 * H], scl.ap[:, H:2 * H],
                                                  AF.Exp, scale=-1.0), deps=[t2], inc=1)
            t4 = p.op("s", lambda e: e.activation(scl.ap[:, 2 * H:3 * H], scl.ap[:, 2 * H:3 * H],
                                                  AF.Ln, bias=1.0), deps=[t3], inc=1)
            t5 = ew_ts(scl.ap[:, 3 * H:4 * H], scl.ap[:, 0:H], 0.0, ALU.max, [t1], H, bf=False)
            t6 = ew_tt(scl.ap[:, 2 * H:3 * H], scl.ap[:, 2 * H:3 * H], scl.ap[:, 3 * H:4 * H],
                       ALU.add, [t4, t5], H, bf=False, eng="v")
            tlg = ew_tt(loga.ap[:], scl.ap[:, 2 * H:3 * H], negA[:], ALU.mult,
                        tmax(t6, t_negA, loga.wr()), H, bf=False, eng="v")
            loga.wrote(tlg)
            scl.wrote(t6)
            tbt = p.op("s", lambda e: e.activation(beta_g.ap[:], ab_sb.ap[:, H:2 * H], AF.Sigmoid),
                       deps=tmax(tabe, beta_g.wr()), inc=1)
            tbt = p.op("s", lambda e: e.mul(beta_g.ap[:], beta_g.ap[:], 2.0), deps=[tbt], inc=1)
            beta_g.wrote(tbt)
            ab_sb.read(tbt)

            tc1 = p.op("t", lambda e: e.matmul(bank7.ap[:, 64:64 + H], cr("incl_u_r"), loga.ap[:],
                                               start=True, stop=True),
                       deps=tmax(tlg, tkr, bank7.wr()), inc=1)
            tc2 = p.op("t", lambda e: e.matmul(bank7.ap[:16, 128:128 + CK], loga.ap[:],
                                               cr("incl_u_r"), start=True, stop=True),
                       deps=tmax(tlg), inc=1)
            bank7.wrote(tc2)
            loga.read(tc2)
            te1 = ew_copy(ell_tm.ap[:], bank7.ap[:, 64:64 + H],
                          tmax(tc1, tc2, ell_tm.wr()), H, bf=False, ps=True, eng="v")
            ell_tm.wrote(te1)
            te2 = ew_copy(ell_fm.ap[:], bank7.ap[:16, 128:128 + CK],
                          tmax(tc1, tc2, ell_fm.wr()), CK, bf=False, ps=True, eng="v")
            ell_fm.wrote(te2)
            bank7.read(te1)
            bank7.read(te2)
            tg = p.op("s", lambda e: e.activation(gam_tm.ap[:], ell_tm.ap[:], AF.Exp),
                      deps=tmax(te1, gam_tm.wr()), inc=1)
            gam_tm.wrote(tg)
            tnb = ew_tt(negbg.ap[:], beta_g.ap[:], gam_tm.ap[:], ALU.mult,
                        tmax(tbt, tg, negbg.wr()), H, bf=False, eng="v")
            tnb = ew_ts(negbg.ap[:], negbg.ap[:], -1.0, ALU.mult, [tnb], H, bf=False)
            negbg.wrote(tnb)
            beta_g.read(tnb)
            gam_tm.read(tnb)

            ntk = None
            for qc in range(16):
                ng = nsqG[qc % 2]
                tq = ew_tt(ng.ap[:], qkv[qc].ap[:, tof:tof + CK],
                           qkv[qc].ap[:, tof:tof + CK], ALU.mult,
                           tmax(qkv[qc].w, ng.wr()), CK, eng="v")
                ng.wrote(tq)
                qkv[qc].read(tq)
                ntk = p.op("t", lambda e, qc=qc, ng=ng: e.matmul(
                    bank7.ap[:, 256 + qc * 2:256 + qc * 2 + 2], ng.ap[:],
                    cb("bdones"), start=True, stop=True),
                    deps=tmax(tq, tkb, bank7.wr() if qc == 0 else []), inc=1)
                ng.read(ntk)
            bank7.wrote(ntk)
            tn1 = p.op("s", lambda e: e.activation(rnorm.ap[:], bank7.ap[:, 256:288], AF.Ln),
                       deps=tmax(ntk, rnorm.wr()), inc=1)
            bank7.read(tn1)
            tn2 = p.op("s", lambda e: e.activation(rnorm.ap[:], rnorm.ap[:], AF.Exp, scale=-0.5),
                       deps=[tn1], inc=1)
            tn3 = ew_ts(rnorm.ap[:], rnorm.ap[:], 1.0e6, ALU.min, [tn2], 32, bf=False, eng="v")
            rnorm.wrote(tn3)

            if stop_after == "decay" or stop_after == "cc0decay":
                return
            nhp = 8
            if isinstance(stop_after, str) and "hp" in stop_after:
                import re as _re
                nhp = int(_re.search(r"hp(\d+)", stop_after).group(1))
            from collections import deque
            SSP = int(os.environ.get("SSP", "10"))
            win = deque()
            nxt_hp = 0
            it = 0
            while win or nxt_hp < nhp:
                while nxt_hp < nhp and it >= nxt_hp * SSP:
                    win.append(head_pair(ti, cc, tof, nxt_hp, dk, te1, te2))
                    nxt_hp += 1
                for ggen in list(win):
                    try:
                        next(ggen)
                    except StopIteration:
                        win.remove(ggen)
                it += 1

        def head_pair(ti, cc, tof, hp, dk, te1, te2):
            gen = _head_pair(ti, cc, tof, hp, dk, te1, te2)
            st = 0
            while True:
                p.ctx = ("hp", ti, cc, hp, st)
                try:
                    next(gen)
                except StopIteration:
                    return
                st += 1
                yield

        def _head_pair(ti, cc, tof, hp, dk, te1, te2):
            h0 = 2 * hp
            par = hp % 3
            ab_sb, loga, beta_g = dk["ab"], dk["loga"], dk["beta"]
            ell_tm, ell_fm, gam_tm = dk["ell_tm"], dk["ell_fm"], dk["gam"]
            rnorm, scl, negbg = dk["rnorm"], dk["scl"], dk["negbg"]
            sel = lambda L: L[hp % len(L)]
            EB, XE, XE2 = sel(EbatP), sel(expEP), sel(expE2P)
            DI, LF = sel(dincP), sel(lfacP)
            KQ, KH, HN, VT, KA = sel(kqtmP), sel(khnP), sel(hnfP), sel(vtmP), sel(khatP)
            LTP, LTTP, MMP, RTP = sel(LtP), sel(LTtP), sel(mmtP), sel(RtmP)
            rsc, Ngt, h2p, h4p = sel(rscP), sel(NgtP), sel(h2pP), sel(h4pP)
            t0ap, Tgp, TTgp, Pgp = sel(t0apP), sel(TgpP), sel(TTgpP), sel(PgpP)
            Utmp, oscp, osqp, oms, ontm = sel(UtmpP), sel(oscpP), sel(osqpP), sel(omsP), sel(ontmP)

            tb_ = p.op("t", lambda e: e.matmul(bank6.ap[:, 0:CK],
                                               cr("onehot")[:16, h0 * 128:(h0 + 1) * 128],
                                               ell_fm.ap[:], start=True, stop=True),
                       deps=tmax(te2, tkr, bank6.wr()), inc=1)
            tb1 = p.op("t", lambda e: e.matmul(bank6.ap[:, CK:2 * CK],
                                               cr("onehot")[:16, (h0 + 1) * 128:(h0 + 2) * 128],
                                               ell_fm.ap[:], start=True, stop=True), inc=1)
            bank6.wrote(tb1)
            ell_fm.read(tb1)
            tE = None
            for i in range(2):
                tE = p.op("v", lambda e, i=i: e.tensor_scalar(
                    EB.ap[:, i * CK:(i + 1) * CK], bank6.ap[:, i * CK:(i + 1) * CK],
                    ell_tm.ap[:, h0 + i:h0 + i + 1], None, ALU.subtract),
                    deps=tmax(tb1, te1, EB.wr() if i == 0 else []), inc=1)
                ell_tm.read(tE)
            p.load["v"] += 400.0
            bank6.read(tE)
            EB.wrote(tE)
            yield
            tek = None
            for i in range(2):
                tek = p.op("s", lambda e, i=i: e.activation(
                    scl.ap[:, 4 * H + 4 * hp + i:4 * H + 4 * hp + i + 1],
                    EB.ap[:, i * CK + CK - 1:i * CK + CK], AF.Exp),
                    deps=tmax(tE, scl.wr() if i == 0 else []), inc=1)
                EB.read(tek)
            scl.wrote(tek)
            tm1 = ew_ts(rsc.ap[:, 0:2 * CK], EB.ap[:], 0.0, ALU.min,
                        tmax(tE, rsc.wr()), 2 * CK, bf=False)
            rsc.wrote(tm1)
            tx1 = p.op("s", lambda e: e.activation(XE.ap[:], rsc.ap[:, 0:2 * CK], AF.Exp),
                       deps=tmax(tm1, XE.wr()), inc=1)
            XE.wrote(tx1)
            rsc.read(tx1)
            tm2 = ew_ts(rsc.ap[:, 0:2 * CK], EB.ap[:], -1.0, ALU.mult,
                        tmax(tE, tx1), 2 * CK, s2=0.0, op2=ALU.min, bf=False)
            EB.read(tm2)
            rsc.wrote(tm2)
            tx2 = p.op("s", lambda e: e.activation(XE2.ap[:], rsc.ap[:, 0:2 * CK], AF.Exp),
                       deps=tmax(tm2, XE2.wr()), inc=1)
            XE2.wrote(tx2)
            rsc.read(tx2)
            yield
            td_ = ew_tt(DI.ap[:], XE.ap[:], cb("incl_u_dw"), ALU.mult,
                        tmax(tx1, tkb, DI.wr()), 2 * CK)
            DI.wrote(td_)
            XE.read(td_)
            tl_ = ew_tt(LF.ap[:], XE2.ap[:], cb("strict_l_dw"), ALU.mult,
                        tmax(tx2, tkb, LF.wr()), 2 * CK)
            LF.wrote(tl_)
            XE2.read(tl_)

            yield
            ttr = p.op("t", lambda e: e.transpose(
                bank5.ap[:, 0:CK], qkv[8 + hp].ap[:, tof:tof + CK], cb("ident")),
                deps=tmax(qkv[8 + hp].w, tkb, bank5.wr()), inc=1)
            qkv[8 + hp].read(ttr)
            ttr = p.op("t", lambda e: e.transpose(
                bank5.ap[:, CK:2 * CK], qkv[hp].ap[:, tof:tof + CK], cb("ident")), inc=1)
            qkv[hp].read(ttr)
            bank5.wrote(ttr)
            tkq = ew_copy(KQ.ap[:], bank5.ap[:, 0:256], tmax(ttr, KQ.wr()), 2 * CK, ps=True)
            bank5.read(tkq)
            KQ.wrote(tkq)
            yield
            kh_toks = []
            for i in range(2):
                h = h0 + i
                ta_ = ew_ts(KH.ap[:, i * 64:(i + 1) * 64], KQ.ap[:, i * 64:(i + 1) * 64],
                            rnorm.ap[:, 16 + h:17 + h], ALU.mult,
                            tmax(tkq, rnorm.w, KH.wr() if i == 0 else []), 64)
                tb2 = ew_ts(KH.ap[:, 128 + i * 64:128 + (i + 1) * 64],
                            KQ.ap[:, 128 + i * 64:128 + (i + 1) * 64],
                            rnorm.ap[:, h:h + 1], ALU.mult, tmax(tkq, rnorm.w), 64)
                rnorm.read(ta_)
                rnorm.read(tb2)
                KQ.read(ta_)
                KQ.read(tb2)
                kh_toks += [ta_, tb2]
            kh_done = tmax(*kh_toks)
            KH.w = None
            KH.rs = list(kh_done)
            yield
            ttr2 = None
            for i in range(2):
                for j in range(2):
                    ttr2 = p.op("t", lambda e, i=i, j=j: e.transpose(
                        bank5.ap[64 * i:64 * i + 64, 256 + j * 128:256 + (j + 1) * 128],
                        KH.ap[:, j * 128 + i * 64:j * 128 + (i + 1) * 64], cb("ident")),
                        deps=tmax(kh_done, bank5.wr()) if (i == 0 and j == 0) else (), inc=1)
            bank5.wrote(ttr2)
            thn = ew_copy(HN.ap[:], bank5.ap[:, 256:512], tmax(ttr2, HN.wr()), 2 * CK, ps=True)
            bank5.read(thn)
            HN.wrote(thn)
            yield
            ttv = None
            for i in range(2):
                h = h0 + i
                ttv = p.op("t", lambda e, i=i, h=h: e.transpose(
                    bank5.ap[:, 512 + i * 128:512 + (i + 1) * 128],
                    qkv[16 + h].ap[:, tof:tof + CK], cb("ident")),
                    deps=tmax(qkv[16 + h].w, bank5.wr() if i == 0 else []), inc=1)
                qkv[16 + h].read(ttv)
            bank5.wrote(ttv)
            tve = ew_copy(VT.ap[:], bank5.ap[:, 512:768], tmax(ttv, VT.wr()), 2 * CK, ps=True)
            bank5.read(tve)
            VT.wrote(tve)
            yield
            tkh_p = []
            for i in range(2):
                t_ = ew_ts(KA.ap[:, i * 64:(i + 1) * 64], KH.ap[:, i * 64:(i + 1) * 64],
                           scl.ap[:, 4 * H + 4 * hp + i:4 * H + 4 * hp + i + 1], ALU.mult,
                           tmax(kh_done, tek, KA.wr()), 64)
                scl.read(t_)
                KH.read(t_)
                tkh_p.append(t_)
            tkh = tmax(tkh_p)
            KA.wrote(tkh)
            if stop_after is not None and "tp" in str(stop_after):
                return

            if stop_after is not None and "xpre" in str(stop_after):
                return
            yield
            kn = [HN.ap[64 * i:64 * i + 64, 0:CK] for i in range(2)]
            qn = [HN.ap[64 * i:64 * i + 64, CK:2 * CK] for i in range(2)]
            bgr = nxt()
            tg2 = None
            for i in range(2):
                tg2 = p.op("t", lambda e, i=i, bgr=bgr: e.matmul(
                    bgr.ap[:, i * CK:(i + 1) * CK], kn[i], kn[i], start=True, stop=True),
                    deps=tmax(thn, bgr.wr()) if i == 0 else (), inc=1, pbase=64 * i)
                tg2 = p.op("t", lambda e, i=i, bgr=bgr: e.matmul(
                    bgr.ap[:, 2 * CK + i * CK:2 * CK + (i + 1) * CK], kn[i], qn[i],
                    start=True, stop=True), inc=1, pbase=64 * i)
            bgr.wrote(tg2)
            HN.read(tg2)
            if stop_after is not None and "xgr" in str(stop_after):
                return
            yield
            tL = ew_tt(LTP.ap[:], LF.ap[:], bgr.ap[:, 0:2 * CK], ALU.mult,
                       tmax(tl_, tg2, LTP.wr()), 2 * CK, ps=True, eng="v")
            LF.read(tL)
            bgr.read(tL)
            for i in range(2):
                h = h0 + i
                tL = ew_ts(LTP.ap[:, i * CK:(i + 1) * CK], LTP.ap[:, i * CK:(i + 1) * CK],
                           beta_g.ap[:, h:h + 1], ALU.mult, [tL], CK)
                beta_g.read(tL)
            LTP.wrote(tL)
            tMm = ew_tt(MMP.ap[:], DI.ap[:], bgr.ap[:, 2 * CK:4 * CK], ALU.mult,
                        tmax(td_, tg2, MMP.wr()), 2 * CK, ps=True, eng="v")
            MMP.wrote(tMm)
            DI.read(tMm)
            bgr.read(tMm)
            if stop_after is not None and "xlm" in str(stop_after):
                return
            yield
            tLT = None
            for i in range(2):
                tLT = p.op("t", lambda e, i=i: e.transpose(
                    bank5.ap[:, 768 + i * 128:768 + (i + 1) * 128],
                    LTP.ap[:, i * CK:(i + 1) * CK], cb("ident")),
                    deps=tmax(tL, tkb, bank5.wr()) if i == 0 else (), inc=1)
            bank5.wrote(tLT)
            LTP.read(tLT)
            tLTe = ew_copy(LTTP.ap[:], bank5.ap[:, 768:1024], tmax(tLT, LTTP.wr()), 2 * CK,
                           ps=True)
            bank5.read(tLTe)
            LTTP.wrote(tLTe)

            if stop_after is not None and "xlt" in str(stop_after):
                return
            yield
            tN = ew_tt(Ngt.ap[:, 0:2 * CK], LTTP.ap[:], cb("m8u_dw"), ALU.mult,
                       tmax(tLTe, Ngt.wr()), 2 * CK)
            LTTP.read(tN)
            tNT = ew_tt(Ngt.ap[:, 2 * CK:4 * CK], LTP.ap[:], cb("m8l_dw"), ALU.mult,
                        tmax(tL, Ngt.wr()), 2 * CK)
            LTP.read(tNT)
            ngt_done = tmax(tN, tNT)
            Ngt.w = None
            Ngt.rs = list(ngt_done)
            if stop_after is not None and "xnn" in str(stop_after):
                return
            yield
            bh = nxt()
            tmm2 = None
            for i in range(2):
                tmm2 = p.op("t", lambda e, i=i, bh=bh: e.matmul(
                    bh.ap[:, i * CK:(i + 1) * CK], Ngt.ap[:, 2 * CK + i * CK:2 * CK + (i + 1) * CK],
                    Ngt.ap[:, i * CK:(i + 1) * CK], start=True, stop=True),
                    deps=tmax(ngt_done, bh.wr()) if i == 0 else (), inc=1)
                tmm2 = p.op("t", lambda e, i=i, bh=bh: e.matmul(
                    bh.ap[:, 2 * CK + i * CK:2 * CK + (i + 1) * CK],
                    Ngt.ap[:, i * CK:(i + 1) * CK],
                    Ngt.ap[:, 2 * CK + i * CK:2 * CK + (i + 1) * CK],
                    start=True, stop=True), inc=1)
            bh.wrote(tmm2)
            Ngt.read(tmm2)
            th2 = ew_copy(h2p.ap[:], bh.ap[:, 0:4 * CK], tmax(tmm2, h2p.wr()), 4 * CK, ps=True)
            bh.read(th2)
            h2p.wrote(th2)
            yield
            bh = nxt()
            tmm4 = None
            for i in range(2):
                tmm4 = p.op("t", lambda e, i=i, bh=bh: e.matmul(
                    bh.ap[:, i * CK:(i + 1) * CK], h2p.ap[:, 2 * CK + i * CK:2 * CK + (i + 1) * CK],
                    h2p.ap[:, i * CK:(i + 1) * CK], start=True, stop=True),
                    deps=tmax(th2, bh.wr()) if i == 0 else (), inc=1)
                tmm4 = p.op("t", lambda e, i=i, bh=bh: e.matmul(
                    bh.ap[:, 2 * CK + i * CK:2 * CK + (i + 1) * CK],
                    h2p.ap[:, i * CK:(i + 1) * CK],
                    h2p.ap[:, 2 * CK + i * CK:2 * CK + (i + 1) * CK],
                    start=True, stop=True), inc=1)
            bh.wrote(tmm4)
            h2p.read(tmm4)
            th4 = ew_copy(h4p.ap[:], bh.ap[:, 0:4 * CK], tmax(tmm4, h4p.wr()), 4 * CK, ps=True)
            bh.read(th4)
            h4p.wrote(th4)
            yield
            bh = nxt()
            tmm24 = None
            for i in range(2):
                tmm24 = p.op("t", lambda e, i=i, bh=bh: e.matmul(
                    bh.ap[:, i * CK:(i + 1) * CK], h2p.ap[:, 2 * CK + i * CK:2 * CK + (i + 1) * CK],
                    h4p.ap[:, i * CK:(i + 1) * CK], start=True, stop=True),
                    deps=tmax(th4, bh.wr()) if i == 0 else (), inc=1)
                tmm24 = p.op("t", lambda e, i=i, bh=bh: e.matmul(
                    bh.ap[:, 2 * CK + i * CK:2 * CK + (i + 1) * CK],
                    h4p.ap[:, i * CK:(i + 1) * CK],
                    h2p.ap[:, 2 * CK + i * CK:2 * CK + (i + 1) * CK],
                    start=True, stop=True), inc=1)
            bh.wrote(tmm24)
            h2p.read(tmm24)
            h4p.read(tmm24)
            if stop_after is not None and "xh24" in str(stop_after):
                return
            yield
            ta1 = ew_tt(t0ap.ap[:], h2p.ap[:], h4p.ap[:], ALU.add,
                        tmax(th2, th4, t0ap.wr()), 4 * CK)
            ta2 = ew_tt(t0ap.ap[:], t0ap.ap[:], bh.ap[:, 0:4 * CK], ALU.add,
                        tmax(ta1, tmm24), 4 * CK, ps=True, eng="v")
            bh.read(ta2)
            ta3 = None
            for j in range(2):
                ta3 = ew_tt(t0ap.ap[:, j * 2 * CK:(j + 1) * 2 * CK],
                            t0ap.ap[:, j * 2 * CK:(j + 1) * 2 * CK],
                            cb("ident_dw"), ALU.add, [ta2] if j == 0 else [ta3], 2 * CK)
            t0ap.wrote(ta3)
            yield
            bh = nxt()
            tmc = None
            for i in range(2):
                tmc = p.op("t", lambda e, i=i, bh=bh: e.matmul(
                    bh.ap[:, i * CK:(i + 1) * CK],
                    Ngt.ap[:, 2 * CK + i * CK:2 * CK + (i + 1) * CK],
                    t0ap.ap[:, i * CK:(i + 1) * CK], start=True, stop=True),
                    deps=tmax(ta3, bh.wr()) if i == 0 else (), inc=1)
                tmc = p.op("t", lambda e, i=i, bh=bh: e.matmul(
                    bh.ap[:, 2 * CK + i * CK:2 * CK + (i + 1) * CK],
                    t0ap.ap[:, i * CK:(i + 1) * CK],
                    Ngt.ap[:, 2 * CK + i * CK:2 * CK + (i + 1) * CK],
                    start=True, stop=True), inc=1)
            bh.wrote(tmc)
            Ngt.read(tmc)
            t0ap.read(tmc)
            tT0 = ew_tt(Tgp.ap[:], t0ap.ap[:, 0:2 * CK], bh.ap[:, 0:2 * CK], ALU.subtract,
                        tmax(tmc, Tgp.wr()), 2 * CK, ps=True, eng="v")
            Tgp.wrote(tT0)
            tT0T = ew_tt(TTgp.ap[:], t0ap.ap[:, 2 * CK:4 * CK], bh.ap[:, 2 * CK:4 * CK],
                         ALU.subtract, tmax(tmc, TTgp.wr()), 2 * CK, ps=True, eng="v")
            TTgp.wrote(tT0T)
            bh.read(tT0)
            bh.read(tT0T)
            t0ap.read(tT0T)
            if stop_after is not None and "tbase" in str(stop_after):
                return

            yield
            for li, m in enumerate((8, 16, 32, 64)):
                yield
                last = (m == 64)
                tRk = ew_tt(Ngt.ap[:, 0:2 * CK], LTTP.ap[:], cb(f"r{m}u_dw"), ALU.mult,
                            tmax(tLTe, Ngt.wr()), 2 * CK)
                LTTP.read(tRk)
                rk_done = tmax(tRk)
                Ngt.w = None
                Ngt.rs = list(rk_done)
                yield
                bh = nxt()
                tmP = None
                for i in range(2):
                    tmP = p.op("t", lambda e, i=i, bh=bh: e.matmul(
                        bh.ap[:, 2 * CK + i * CK:2 * CK + (i + 1) * CK],
                        Ngt.ap[:, i * CK:(i + 1) * CK], TTgp.ap[:, i * CK:(i + 1) * CK],
                        start=True, stop=True),
                        deps=tmax(rk_done, TTgp.w, bh.wr()) if i == 0 else (), inc=1)
                bh.wrote(tmP)
                Ngt.read(tmP)
                TTgp.read(tmP)
                tPe = ew_copy(Pgp.ap[:, 0:2 * CK], bh.ap[:, 2 * CK:4 * CK],
                              tmax(tmP, Pgp.wr()), 2 * CK, ps=True)
                bh.read(tPe)
                Pgp.wrote(tPe)
                yield
                bh = nxt()
                tmU = None
                for i in range(2):
                    tmU = p.op("t", lambda e, i=i, bh=bh: e.matmul(
                        bh.ap[:, i * CK:(i + 1) * CK],
                        Pgp.ap[:, i * CK:(i + 1) * CK],
                        Tgp.ap[:, i * CK:(i + 1) * CK], start=True, stop=True),
                        deps=tmax(tPe, Tgp.w, bh.wr()) if i == 0 else (), inc=1)
                    if not last:
                        tmU = p.op("t", lambda e, i=i, bh=bh: e.matmul(
                            bh.ap[:, 2 * CK + i * CK:2 * CK + (i + 1) * CK],
                            Tgp.ap[:, i * CK:(i + 1) * CK],
                            Pgp.ap[:, i * CK:(i + 1) * CK],
                            start=True, stop=True), inc=1)
                bh.wrote(tmU)
                Pgp.read(tmU)
                Tgp.read(tmU)
                tTn = ew_tt(Tgp.ap[:], Tgp.ap[:], bh.ap[:, 0:2 * CK], ALU.subtract,
                            tmax(tmU, Tgp.wr()), 2 * CK, ps=True, eng="v")
                Tgp.wrote(tTn)
                bh.read(tTn)
                if not last:
                    tTTn = ew_tt(TTgp.ap[:], TTgp.ap[:], bh.ap[:, 2 * CK:4 * CK], ALU.subtract,
                                 tmax(tmU, TTgp.wr()), 2 * CK, ps=True, eng="v")
                    TTgp.wrote(tTTn)
                    bh.read(tTTn)
            if stop_after is not None and "tlvl" in str(stop_after):
                return

            yield
            bS = nxt()
            tKS = None
            for i in range(2):
                h = h0 + i
                tKS = p.op("t", lambda e, i=i, h=h, bS=bS: e.matmul(
                    bS.ap[:, i * CK:(i + 1) * CK], kn[i], S_b[h].ap, start=True, stop=True),
                    deps=tmax(thn, S_b[h].w, bS.wr()) if i == 0 else tmax(S_b[h].w), inc=1,
                    pbase=64 * i)
                S_b[h].read(tKS)
                tKS = p.op("t", lambda e, i=i, h=h, bS=bS: e.matmul(
                    bS.ap[:, 2 * CK + i * CK:2 * CK + (i + 1) * CK], qn[i], S_b[h].ap,
                    start=True, stop=True), inc=1, pbase=64 * i)
                S_b[h].read(tKS)
            bS.wrote(tKS)
            HN.read(tKS)
            yield
            tR_p = []
            for i in range(2):
                h = h0 + i
                t_ = ew_stt(RTP.ap[:, i * CK:(i + 1) * CK], bS.ap[:, i * CK:(i + 1) * CK],
                            negbg.ap[:, h:h + 1], VT.ap[:, i * CK:(i + 1) * CK],
                            ALU.mult, ALU.add,
                            tmax(tKS, tve, negbg.w, RTP.wr()), CK, ps=True)
                VT.read(t_)
                negbg.read(t_)
                bS.read(t_)
                tR_p.append(t_)
            tR = tmax(tR_p)
            RTP.wrote(tR)
            yield
            tsc = None
            for i in range(2):
                h = h0 + i
                tsc = p.op("v", lambda e, i=i, h=h, bS=bS: e.tensor_scalar(
                    oscp.ap[:, i * CK:(i + 1) * CK], bS.ap[:, 2 * CK + i * CK:2 * CK + (i + 1) * CK],
                    gam_tm.ap[:, h:h + 1], None, ALU.mult),
                    deps=tmax(tKS, gam_tm.w, oscp.wr() if i == 0 else []), inc=1)
                gam_tm.read(tsc)
                bS.read(tsc)
            p.load["v"] += 500.0
            oscp.wrote(tsc)
            yield
            bU = nxt()
            tU = None
            for i in range(2):
                tU = p.op("t", lambda e, i=i, bU=bU: e.matmul(
                    bU.ap[:, i * CK:(i + 1) * CK], Tgp.ap[:, i * CK:(i + 1) * CK],
                    RTP.ap[:, i * CK:(i + 1) * CK], start=True, stop=True),
                    deps=tmax(tR, Tgp.w, bU.wr()) if i == 0 else (), inc=1)
            bU.wrote(tU)
            Tgp.read(tU)
            RTP.read(tU)
            tUe = ew_copy(Utmp.ap[:], bU.ap[:, 0:2 * CK], tmax(tU, Utmp.wr()), 2 * CK, ps=True)
            bU.read(tUe)
            Utmp.wrote(tUe)
            yield
            bM = nxt()
            tO = None
            for i in range(2):
                tO = p.op("t", lambda e, i=i, bM=bM: e.matmul(
                    bM.ap[:, i * CK:(i + 1) * CK], MMP.ap[:, i * CK:(i + 1) * CK],
                    Utmp.ap[:, i * CK:(i + 1) * CK], start=True, stop=True),
                    deps=tmax(tUe, tMm, bM.wr()) if i == 0 else (), inc=1)
            bM.wrote(tO)
            MMP.read(tO)
            Utmp.read(tO)
            tOa = ew_tt(oscp.ap[:], oscp.ap[:], bM.ap[:, 0:2 * CK], ALU.add,
                        tmax(tO, tsc), 2 * CK, bf=False, ps=True, eng="v")
            bM.read(tOa)
            oscp.wrote(tOa)
            yield
            tq2 = p.op("s", lambda e: e.activation(osqp.ap[:], oscp.ap[:], AF.Square),
                       deps=tmax(tOa, osqp.wr()), inc=1)
            osqp.wrote(tq2)
            tms = None
            for i in range(2):
                tms = p.op("v", lambda e, i=i: e.reduce_sum(
                    oms.ap[:, i:i + 1], osqp.ap[:, i * CK:(i + 1) * CK], AX.X),
                    deps=tmax(tq2, oms.wr()) if i == 0 else (), inc=1)
            p.load["v"] += 400.0
            osqp.read(tms)
            tln = p.op("s", lambda e: e.activation(oms.ap[:, 2:4], oms.ap[:, 0:2], AF.Ln,
                                                   scale=1.0 / DVH, bias=EPS),
                       deps=[tms], inc=1)
            trs = p.op("s", lambda e: e.activation(oms.ap[:, 2:4], oms.ap[:, 2:4], AF.Exp,
                                                   scale=-0.5), deps=[tln], inc=1)
            oms.wrote(trs)
            tno_p = []
            for i in range(2):
                t_ = ew_ts(ontm.ap[:, i * CK:(i + 1) * CK], oscp.ap[:, i * CK:(i + 1) * CK],
                           oms.ap[:, 2 + i:3 + i], ALU.mult,
                           tmax(trs, tOa, ontm.wr()), CK, bf=False)
                oms.read(t_)
                oscp.read(t_)
                tno_p.append(t_)
            tno = tmax(tno_p)
            ontm.wrote(tno)
            yield
            bSt = nxt()
            tS = None
            for i in range(2):
                tS = p.op("t", lambda e, i=i, bSt=bSt: e.matmul(
                    bSt.ap[64 * i:64 * i + 64, i * CK:(i + 1) * CK], KA.ap[:, i * 64:(i + 1) * 64],
                    Utmp.ap[:, i * CK:(i + 1) * CK], start=True, stop=True),
                    deps=tmax(tkh, tUe, bSt.wr()) if i == 0 else (), inc=1)
            bSt.wrote(tS)
            KA.read(tS)
            Utmp.read(tS)
            for i in range(2):
                h = h0 + i
                hb = 64 * (h % 2)
                tgc = ew_tt(scl.ap[:, 4 * H + 4 * hp + 2 + i:4 * H + 4 * hp + 3 + i],
                            scl.ap[:, 4 * H + 4 * hp + i:4 * H + 4 * hp + i + 1],
                            gam_tm.ap[:, h:h + 1], ALU.mult,
                            tmax(tek, gam_tm.w, scl.wr() if i == 0 else []), 1, bf=False,
                            eng="v")
                gam_tm.read(tgc)
                scl.wrote(tgc)
                tSu = p.op("v", lambda e, i=i, h=h, hb=hb, bSt=bSt, hp=hp: e.scalar_tensor_tensor(
                    S_f[h].ap, S_f[h].ap,
                    scl.ap[hb:hb + 64, 4 * H + 4 * hp + 2 + i:4 * H + 4 * hp + 3 + i],
                    bSt.ap[hb:hb + 64, i * CK:(i + 1) * CK], ALU.mult, ALU.add),
                    deps=tmax(tS, tgc, S_f[h].wr()), inc=1)
                p.load["v"] += 300.0
                scl.read(tSu)
                bSt.read(tSu)
                S_f[h].wrote(tSu)
            tSb = ew_copy(S_bf[:, hp * DVH:(hp + 1) * DVH], S_sb[:, hp * DVH:(hp + 1) * DVH],
                          tmax(S_f[h0].w, S_f[h0 + 1].w,
                               S_b[h0].wr(), S_b[h0 + 1].wr()), DVH)
            S_f[h0].read(tSb)
            S_f[h0 + 1].read(tSb)
            S_b[h0].wrote(tSb)
            S_b[h0 + 1].wrote(tSb)
            yield
            ttn = None
            for i in range(2):
                ttn = p.op("t", lambda e, i=i: e.transpose(
                    bank5.ap[:, i * 128:(i + 1) * 128],
                    ontm.ap[:, i * DVH:(i + 1) * DVH], cb("ident")),
                    deps=tmax(tno, tkb, bank5.wr() if i == 0 else []), inc=1)
            bank5.wrote(ttn)
            ontm.read(ttn)
            toe = None
            for i in range(2):
                h = h0 + i
                toe = ew_copy(on_fm[h].ap[:, tof:tof + CK],
                              bank5.ap[:, i * 128:(i + 1) * 128],
                              tmax(ttn, on_fm[h].wr()), CK, ps=True)
                on_fm[h].wrote(toe)
            bank5.read(toe)

        def load_phase_b_weights(dep):
            toks = []
            for k in range(8):
                toks.append(stage_load(dwr[k * 128:(k + 1) * 128, :], wr_sb[k], k % 2, (dep,)))
                toks.append(stage_load(dwkf[k * 128:(k + 1) * 128, :], wkf_sb[k], k % 2))
                toks.append(stage_load(dwvf[k * 128:(k + 1) * 128, :], wvf_sb[k], k % 2))
            return tmax(*toks)

        def ffn_tile(ti, wB_tok):
            xb = xbuf[0]
            if ti > 0:
                load_x_tile(ti, xb, dx1)

            trb, tmr, trbc = ln_stats(xb, LN_EPS)
            tmrbc = p.op("t", lambda e: e.matmul(bank6.ap[:, NT:2 * NT], cb("ones_row")[:1, :],
                                                 mr_bf.ap[:], start=True, stop=True),
                         deps=tmax(tmr, mr_bf.w), inc=1)
            bank6.wrote(tmrbc)
            mr_bf.read(tmrbc)
            ztoks = []
            for c in range(NCH):
                zt = p.op("v", lambda e, c=c: e.tensor_copy(
                    z_sb[c].ap[:, 0:1], z_sb[c].ap[:, NT:NT + 1]),
                    deps=tmax(z_sb[c].w, z_sb[c].rs), inc=1)
                t1 = p.op("v", lambda e, c=c: e.tensor_tensor(
                    ao_sb[c].ap[:], xb[c].ap[:], bank6.ap[:, 0:NT], ALU.mult),
                    deps=tmax(xb[c].w, tmrbc, ao_sb[c].wr()), inc=1)
                xb[c].read(t1)
                t2 = p.op("v", lambda e, c=c: e.tensor_tensor(
                    z_sb[c].ap[:, 1:NT + 1], ao_sb[c].ap[:], bank6.ap[:, NT:2 * NT],
                    ALU.subtract),
                    deps=tmax(t1, zt), inc=1)
                ao_sb[c].wrote(t1)
                ao_sb[c].read(t2)
                bank6.read(t2)
                z_sb[c].wrote(t2)
                ztoks.append(t2)
            mixtoks = []
            for c in range(NCH):
                td_ = p.op("v", lambda e, c=c: e.tensor_tensor(
                    fsc.ap[:], z_sb[c].ap[:, 1:NT + 1], z_sb[c].ap[:, 0:NT], ALU.subtract),
                    deps=tmax(ztoks[c], fsc.wr()), inc=1)
                fsc.wrote(td_)
                z_sb[c].read(td_)
                t1 = p.op("v", lambda e, c=c: e.scalar_tensor_tensor(
                    xr_sb[c].ap[:], fsc.ap[:], mu_sb[:, c:c + 1], z_sb[c].ap[:, 0:NT],
                    ALU.mult, ALU.add),
                    deps=tmax(td_, xr_sb[c].wr()), inc=1)
                xr_sb[c].wrote(t1)
                t2 = p.op("v", lambda e, c=c: e.scalar_tensor_tensor(
                    xk_sb[c].ap[:], fsc.ap[:], mu_sb[:, 8 + c:9 + c], z_sb[c].ap[:, 0:NT],
                    ALU.mult, ALU.add),
                    deps=tmax(td_, xk_sb[c].wr()), inc=1)
                xk_sb[c].wrote(t2)
                fsc.read(t2)
                z_sb[c].read(t2)
                mixtoks.append((t1, t2))
            for oc in range(NCH):
                pb = bank[oc % 2]
                tmm = None
                for k in range(NCH):
                    tmm = p.op("t", lambda e, k=k, oc=oc, pb=pb: e.matmul(
                        pb.ap[:, 0:NT], wr_sb[k][:, oc * 128:(oc + 1) * 128], xr_sb[k].ap[:],
                        start=(k == 0), stop=(k == NCH - 1)),
                        deps=tmax(mixtoks[k][0], wB_tok, pb.wr()) if k == 0 else [mixtoks[k][0]],
                        inc=1)
                pb.wrote(tmm)
                for k in range(NCH):
                    xr_sb[k].read(tmm)
                tr_ = p.op("s", lambda e, oc=oc, pb=pb: e.activation(
                    r_sb[oc].ap[:], pb.ap[:, 0:NT], AF.Sigmoid),
                    deps=tmax(tmm, r_sb[oc].wr()), inc=1)
                pb.read(tr_)
                r_sb[oc].wrote(tr_)
            for oc in range(NCH):
                pb = bank[2 + oc % 2]
                tmm = None
                for k in range(NCH):
                    tmm = p.op("t", lambda e, k=k, oc=oc, pb=pb: e.matmul(
                        pb.ap[:, 0:NT], wkf_sb[k][:, oc * 128:(oc + 1) * 128], xk_sb[k].ap[:],
                        start=(k == 0), stop=(k == NCH - 1)),
                        deps=tmax(mixtoks[k][1], wB_tok, pb.wr()) if k == 0 else [mixtoks[k][1]],
                        inc=1)
                pb.wrote(tmm)
                for k in range(NCH):
                    xk_sb[k].read(tmm)
                trl = p.op("s", lambda e, pb=pb: e.activation(fsc.ap[:], pb.ap[:, 0:NT], AF.Relu),
                           deps=tmax(tmm, fsc.wr()), inc=1)
                pb.read(trl)
                fsc.wrote(trl)
                tsq = p.op("v", lambda e, oc=oc: e.tensor_tensor(
                    ka_sb[oc].ap[:], fsc.ap[:], fsc.ap[:], ALU.mult),
                    deps=tmax(trl, ka_sb[oc].wr()), inc=1)
                fsc.read(tsq)
                ka_sb[oc].wrote(tsq)
            for oc in range(NCH):
                pb = bank[4] if oc % 2 == 0 else bank[2 + (oc + 1) % 2]
                pb = bank[4 - (oc % 2)]
                tmm = None
                for k in range(NCH):
                    tmm = p.op("t", lambda e, k=k, oc=oc, pb=pb: e.matmul(
                        pb.ap[:, 0:NT], wvf_sb[k][:, oc * 128:(oc + 1) * 128], ka_sb[k].ap[:],
                        start=(k == 0), stop=(k == NCH - 1)),
                        deps=tmax(ka_sb[k].w, pb.wr()) if k == 0 else [ka_sb[k].w], inc=1)
                pb.wrote(tmm)
                for k in range(NCH):
                    ka_sb[k].read(tmm)
                tev = p.op("v", lambda e, oc=oc, pb=pb: e.tensor_tensor(
                    ao_sb[oc].ap[:], pb.ap[:, 0:NT], r_sb[oc].ap[:], ALU.mult),
                    deps=tmax(tmm, r_sb[oc].w, ao_sb[oc].wr()), inc=1)
                pb.read(tev)
                r_sb[oc].read(tev)
                ao_sb[oc].wrote(tev)
            trb4, tmr4, trbc4 = ln_stats(ao_sb, LN_EPS)
            tmrbc4 = p.op("t", lambda e: e.matmul(bank6.ap[:, NT:2 * NT], cb("ones_row")[:1, :],
                                                  mr_bf.ap[:], start=True, stop=True),
                          deps=tmax(tmr4, mr_bf.w), inc=1)
            bank6.wrote(tmrbc4)
            mr_bf.read(tmrbc4)
            st_toks = []
            for c in range(NCH):
                t1 = p.op("v", lambda e, c=c: e.tensor_tensor(
                    ao_sb[c].ap[:], ao_sb[c].ap[:], bank6.ap[:, 0:NT], ALU.mult),
                    deps=tmax(ao_sb[c].w, tmrbc4), inc=1)
                t2 = p.op("v", lambda e, c=c: e.tensor_tensor(
                    ao_sb[c].ap[:], ao_sb[c].ap[:], bank6.ap[:, NT:2 * NT], ALU.subtract),
                    deps=[t1], inc=1)
                bank6.read(t2)
                t3 = p.op("v", lambda e, c=c: e.tensor_tensor(
                    xb[c].ap[:], xb[c].ap[:], ao_sb[c].ap[:], ALU.add),
                    deps=tmax(t2, xb[c].w, xb[c].rs), inc=1)
                ao_sb[c].wrote(t3)
                xb[c].wrote(t3)
                ts_ = p.op("g", lambda e, c=c, ti=ti: e.dma_start(
                    out=dout[c * 128:(c + 1) * 128, ti * NT:(ti + 1) * NT], in_=xb[c].ap[:]),
                    deps=[t3], sem="dga" if ti % 2 == 0 else "dgb", inc=16)
                xb[c].read(ts_)
                st_toks.append(ts_)
            for c in range(NCH):
                xb[c].rs = [st_toks[-1]]
            return st_toks

        if "A" in phases:
            tms_ = p.op("v", lambda e: e.memset(ptail.ap[:], 0.0), inc=1)
            ptail.wrote(tms_)
            tms2 = p.op("v", lambda e: e.memset(S_sb[:], 0.0), inc=1)
            tms3 = p.op("v", lambda e: e.memset(S_bf[:], 0.0), inc=1)
            for h in range(H):
                S_f[h].wrote(tms2)
                S_b[h].wrote(tms3)
            for ti in range(NTI):
                attn_tile(ti)
            a_done = [("dga", p.n["dga"]), ("dgb", p.n["dgb"]), ("dg", p.n["dg"])]
        else:
            a_done = [("dga", p.n["dga"]), ("dgb", p.n["dgb"]), ("dg", p.n["dg"])]

        if "B" in phases:
            wB_tok = load_phase_b_weights(("t", p.n["t"]))
            tmu1 = wB_tok[0] if isinstance(wB_tok, list) else wB_tok
            for c in range(NCH):
                tmu1 = p.op("y", lambda e, c=c: e.dma_start(out=mu_sb[:, c:c + 1],
                                                            in_=dmur[c * 128:(c + 1) * 128, :]),
                            deps=[tmu1] if not isinstance(tmu1, list) else tmu1,
                            sem="dl", inc=16)
                tmu1 = p.op("y", lambda e, c=c: e.dma_start(out=mu_sb[:, 8 + c:9 + c],
                                                            in_=dmuk[c * 128:(c + 1) * 128, :]),
                            deps=[tmu1], sem="dl", inc=16)
            tzt = None
            for c in range(NCH):
                tzt = p.op("v", lambda e, c=c: e.memset(z_sb[c].ap[:, 0:NT + 1], 0.0),
                           deps=tmax(*a_done, ("t", p.n["t"])) if c == 0 else (), inc=1)
                z_sb[c].wrote(tzt)
            load_x_tile(0, xbuf[0], dx1, extra=tuple(a_done) + (tmu1, tzt))
            for ti in range(NTI):
                ffn_tile(ti, wB_tok)

        p.op("g", lambda e: e.nop(), deps=[("dga", p.n["dga"]), ("dgb", p.n["dgb"]),
                                           ("dg", p.n["dg"])])


        with nc.Block() as block:
            @block.tensor
            def _(eng):
                for f in p.q["t"]:
                    f(eng)

            @block.vector
            def _(eng):
                for f in p.q["v"]:
                    f(eng)

            @block.scalar
            def _(eng):
                for f in p.q["s"]:
                    f(eng)

            @block.gpsimd
            def _(eng):
                for f in p.q["g"]:
                    f(eng)

            @block.sync
            def _(eng):
                for f in p.q["y"]:
                    f(eng)

    return nc



_NC_CACHE = {}


def _get_nc():
    if "nc" not in _NC_CACHE:
        _NC_CACHE["nc"] = build_program()
    return _NC_CACHE["nc"]


def kernel(L0_attn_x, L0_attn_Wq, L0_attn_Wk, L0_attn_Wv, L0_attn_Wb, L0_attn_Wa,
           L0_attn_A_log, L0_attn_dt_bias, L0_attn_Wo, L0_attn_conv_q, L0_attn_conv_k,
           L0_attn_conv_v, L0_ffn_W_r, L0_ffn_W_k, L0_ffn_W_v, L0_ffn_mu_r, L0_ffn_mu_k):
    x = np.asarray(L0_attn_x, np.float32)
    B = x.shape[0]
    f = np.float32
    ct = np.ascontiguousarray
    wq = ct(np.asarray(L0_attn_Wq, f).T)
    wk = ct(np.asarray(L0_attn_Wk, f).T)
    wv = ct(np.asarray(L0_attn_Wv, f).T)
    wo = ct(np.asarray(L0_attn_Wo, f).T)
    wab_t = np.concatenate([np.asarray(L0_attn_Wa, f), np.asarray(L0_attn_Wb, f)], axis=0).T
    wab = ct(wab_t.reshape(8, 128, 32).transpose(1, 0, 2).reshape(128, 8 * 32))

    def _chunkify(w):
        w = np.asarray(w, f)
        k = w.shape[0] // 128
        return ct(w.reshape(k, 128, KW).transpose(1, 0, 2).reshape(128, k * KW))
    wr = ct(np.asarray(L0_ffn_W_r, f).T)
    wkf = ct(np.asarray(L0_ffn_W_k, f).T)
    wvf = ct(np.asarray(L0_ffn_W_v, f).T)
    shared = {
        "wq": wq, "wk": wk, "wv": wv, "wo": wo, "wab": wab,
        "convq": _chunkify(L0_attn_conv_q),
        "convk": _chunkify(L0_attn_conv_k),
        "convv": _chunkify(L0_attn_conv_v),
        "alog": ct(np.asarray(L0_attn_A_log, f).reshape(1, H)),
        "dtb": ct(np.asarray(L0_attn_dt_bias, f).reshape(1, H)),
        "wr": wr, "wkf": wkf, "wvf": wvf,
        "mur": ct(np.asarray(L0_ffn_mu_r, f).reshape(D, 1)),
        "muk": ct(np.asarray(L0_ffn_mu_k, f).reshape(D, 1)),
        "const": _CONSTS,
    }
    in_maps = []
    for b in range(B):
        m = dict(shared)
        m["x"] = ct(x[b].T)
        in_maps.append(m)
    nc = _get_nc()
    res = run_bass_kernel_spmd(nc, in_maps, core_ids=list(range(B)))
    out = np.stack([np.asarray(r["out"], np.float32).T for r in res.results], axis=0)
    return out
